# version 11
# baseline (speedup 1.0000x reference)
import numpy as np
import ml_dtypes
from contextlib import ExitStack

import concourse.bass as bass
import concourse.mybir as mybir
from concourse.bass_utils import run_bass_kernel_spmd

F32 = mybir.dt.float32
BF16 = mybir.dt.bfloat16
AF = mybir.ActivationFunctionType
ALU = mybir.AluOpType
AX = mybir.AxisListType

NCORES = 8
T = 2048
NT = 16
D = 1024
NJ = 8
FF = 4096
NH = 16
HD = 64
ES = 256
EXP = 16
SLOT = 16
MOE_H = 256
HALO = 32
TEXT = T + 128
NTE = 17

BF16NP = ml_dtypes.bfloat16
DEBUG_A = False
DEBUG_B1 = False


class Prog:

    ENGS = ("sync", "scalar", "vector", "tensor", "gpsimd")

    def __init__(self, name):
        self.name = name
        self.nc = bass.Bass("TRN2", target_bir_lowering=False)
        self.ctx = ExitStack()
        self.regions = [self.ctx]
        self.prefix = ""
        self.ops = {e: [] for e in self.ENGS}
        self.cnt = {e: 0 for e in self.ENGS}
        self.scnt = {}
        self.pending = {}

    def push_region(self, prefix):
        self.regions.append(ExitStack())
        self._prefixes = getattr(self, "_prefixes", [""])
        self._prefixes.append(self.prefix)
        self.prefix = self.prefix + prefix + "_"

    def pop_region(self):
        self.regions.pop().close()
        self.prefix = self._prefixes.pop()

    def barrier(self):
        snap = [(e, self.cnt[e]) for e in self.ENGS if self.cnt[e] > 0]
        snap += [(s, v) for s, v in self.scnt.items() if v > 0 and s != "cc"]
        for e in self.ENGS:
            self.pending.setdefault(e, [])
            self.pending[e] = snap

    def sb(self, name, shape, dt):
        return self.regions[-1].enter_context(
            self.nc.sbuf_tensor(self.prefix + name, shape, dt))

    def ps(self, name, shape, dt=F32):
        return self.regions[-1].enter_context(
            self.nc.psum_tensor(self.prefix + name, shape, dt))

    def din(self, name, shape, dt=F32):
        return self.nc.dram_tensor(name, shape, dt, kind="ExternalInput").ap()

    def dout(self, name, shape, dt=F32):
        return self.nc.dram_tensor(name, shape, dt, kind="ExternalOutput").ap()

    def dtmp(self, name, shape, dt=F32):
        return self.nc.dram_tensor(name, shape, dt).ap()

    def emit(self, eng, fn, waits=(), inc=1, sem=None):
        sem = sem or eng
        waits = list(waits) + self.pending.pop(eng, [])
        w = {}
        for tok in waits:
            if len(tok) == 3:
                _, we, wv = tok
            else:
                we, wv = tok
                if we in self.scnt:
                    wv = self.scnt[we]
            if we != eng and wv > 0:
                w[we] = max(w.get(we, 0), wv)
        self.ops[eng].append((tuple(w.items()), fn, inc, sem))
        if sem in self.scnt:
            self.scnt[sem] += inc
            return (sem, self.scnt[sem])
        self.cnt[eng] += inc
        return (eng, self.cnt[eng])

    def tok(self, eng):
        return (eng, self.cnt[eng])

    def stok(self, stream):
        return (stream, self.scnt.get(stream, 0))

    def const_col(self, value):
        if not hasattr(self, "_ccache"):
            self._ccache = {}
        if value not in self._ccache:
            t = self.regions[0].enter_context(self.nc.sbuf_tensor(
                f"cc_{len(self._ccache)}", [128, 1], F32))
            tok = self.emit("vector", lambda e, t=t, value=value: e.memset(
                t[:], float(value)))
            self._ccache[value] = (t, tok)
        return self._ccache[value]

    def dma_rr(self, stream, nways, idx, out, in_, waits=(), queue="sync", **kw):
        return self.dma(f"{stream}{idx % nways}", out, in_, waits, queue, **kw)

    def stok_rr(self, stream, nways):
        return [self.stok(f"{stream}{i}") for i in range(nways)]

    def dma(self, stream, out, in_, waits=(), queue="sync", **kw):
        if stream not in self.scnt:
            self.scnt[stream] = 0
        waits = list(waits)
        if self.scnt[stream] > 0:
            waits.append(("exact", stream, self.scnt[stream]))
        return self.emit(queue, lambda e: e.dma_start(out=out, in_=in_, **kw),
                         waits, 16, sem=stream)

    def finalize(self):
        while len(self.regions) > 1:
            self.pop_region()
        nc = self.nc
        sems = {e: self.ctx.enter_context(nc.semaphore("sem_" + e))
                for e in self.ENGS}
        for s in self.scnt:
            sems[s] = self.ctx.enter_context(nc.semaphore("sem_" + s))
        block = self.ctx.enter_context(nc.Block())
        for e in self.ENGS:
            ops = self.ops[e]
            if not ops and e != "sync":
                continue

            def body(engobj, ops=ops, e=e):
                hwm = {}
                selfcnt = 0
                selfsync = e in ("scalar", "vector", "gpsimd")
                for waits, fn, inc, sem in ops:
                    if selfsync and sem == e and selfcnt > 0:
                        engobj.wait_ge(sems[e], selfcnt)
                    for we, wv in waits:
                        if hwm.get(we, 0) < wv:
                            engobj.wait_ge(sems[we], wv)
                            hwm[we] = wv
                    ins = fn(engobj)
                    if inc:
                        ins.then_inc(sems[sem], inc)
                    if sem == e:
                        selfcnt += inc
                if e == "sync":
                    for oe in Prog.ENGS:
                        if self.cnt[oe] > 0:
                            engobj.wait_ge(sems[oe], self.cnt[oe])
                    for s, v in self.scnt.items():
                        if v > 0:
                            engobj.wait_ge(sems[s], v)

            getattr(block, e)(body)
        self.ctx.close()
        return nc



def common_consts(p):
    c = {}
    c["ident"] = p.sb("ident_sb", [128, 128], BF16)
    ident_in = p.din("ident", [128, 128], BF16)
    t_id = p.dma("in", c["ident"][:], ident_in[:])
    c["ones_bf"] = p.sb("ones_bf", [1, 2048], BF16)
    c["onescol_bf"] = p.sb("onescol_bf", [128, 1], BF16)
    c["ones_f"] = p.sb("ones_f", [1, 128], F32)
    t1 = p.emit("vector", lambda e: e.memset(c["ones_bf"][:], 1.0))
    t2 = p.emit("vector", lambda e: e.memset(c["onescol_bf"][:], 1.0))
    t3 = p.emit("vector", lambda e: e.memset(c["ones_f"][:], 1.0))
    c["ready"] = [t_id, t1, t2, t3]
    return c


def load_cast_weight(p, c, name, kt, n, stage, dest, brow_f, brow_h, waits=(),
                     win=None):
    if win is None:
        win = p.din(name, [kt * 128 + 1, n], F32)
    tw = p.dma("in", stage[:, 0:kt * n].rearrange("p (k f) -> p k f", f=n),
               win[0:kt * 128, :].rearrange("(k p) f -> p k f", p=128),
               waits=waits)
    tb = p.dma("in", brow_f[:, 0:n], win[kt * 128:kt * 128 + 1, :], waits=waits)
    t_in = p.stok("in")
    tcast = []
    for j in range(kt):
        tcast.append(p.emit("scalar", lambda e, j=j: e.activation(
            out=dest[:, j * n:(j + 1) * n], in_=stage[:, j * n:(j + 1) * n],
            func=AF.Copy), waits=[t_in]))
    tcb = p.emit("scalar", lambda e: e.activation(
        out=brow_h[:, 0:n], in_=brow_f[:, 0:n], func=AF.Copy), waits=[t_in])
    return tcast + [tcb]


def bcast_row_f32(p, c, dst, src_row, n, psA, waits=()):
    toks = []
    ev = []
    for n0 in range(0, n, 512):
        nn = min(512, n - n0)
        tmm = p.emit("tensor", lambda e, n0=n0, nn=nn: e.matmul(
            out=psA[:, n0 % 512:n0 % 512 + nn], lhsT=c["ones_f"][:, :],
            rhs=src_row[:, n0:n0 + nn], start=True, stop=True),
            waits=list(waits) + ev)
        tev = p.emit("vector", lambda e, n0=n0, nn=nn: e.tensor_copy(
            dst[:, n0:n0 + nn], psA[:, n0 % 512:n0 % 512 + nn]), waits=[tmm])
        ev = [tev]
        toks.append(tev)
    return toks


def ln_stats_apply(p, bufs, u, s1, out, eps, tw_u, tw_s2, tag):
    sq, mcol, vcol, tcol = bufs["sqw"], bufs["mcol"], bufs["vcol"], bufs["tcol"]
    t_s2 = p.emit("scalar", lambda e: e.activation(
        out=sq[:], in_=u[:], func=AF.Square, accum_out=vcol[:, 0:1]),
        waits=tw_u)
    t_m = p.emit("scalar", lambda e: e.mul(mcol[:, 0:1], s1[:, 0:1], 1.0 / D),
                 waits=tw_u)
    t_t = p.emit("scalar", lambda e: e.activation(
        out=tcol[:, 0:1], in_=mcol[:, 0:1], func=AF.Square), waits=[t_m])
    t_v = p.emit("vector", lambda e: e.scalar_tensor_tensor(
        out=vcol[:, 0:1], in0=vcol[:, 0:1], scalar=1.0 / D, in1=tcol[:, 0:1],
        op0=ALU.mult, op1=ALU.subtract), waits=[t_s2, t_t])
    eps_ap, eps_tok = p.const_col(float(eps))
    t_sq = p.emit("scalar", lambda e: e.activation(
        out=tcol[:, 0:1], in_=vcol[:, 0:1], func=AF.Sqrt,
        bias=eps_ap[:, 0:1]), waits=[t_v, eps_tok])
    t_r = p.emit("vector", lambda e: e.reciprocal(
        out=vcol[:, 0:1], in_=tcol[:, 0:1]), waits=[t_sq])
    t_o = p.emit("vector", lambda e: e.tensor_scalar(
        out=out[:], in0=u[:], scalar1=mcol[:, 0:1], scalar2=vcol[:, 0:1],
        op0=ALU.subtract, op1=ALU.mult), waits=[t_r] + tw_u)
    return t_o, t_sq


def transpose_tile(p, c, tps, src_bf, dst, waits_src, waits_ps, nblk=8,
                   waits_evac=()):
    tts = []
    for j in range(nblk):
        tts.append(p.emit("tensor", lambda e, j=j: e.transpose(
            out=tps[:, j * 128:(j + 1) * 128],
            in_=src_bf[:, j * 128:(j + 1) * 128],
            identity=c["ident"][:]),
            waits=(list(waits_src) + list(waits_ps)) if j == 0 else list(waits_src)))
    tev = p.emit("vector", lambda e: e.tensor_copy(
        dst[:], tps[:, 0:nblk * 128]), waits=[tts[-1]] + list(waits_evac))
    return tev




def interleave_pairs(bodies):
    live = list(bodies)
    while live:
        nxt = []
        for g in live:
            try:
                next(g)
                nxt.append(g)
            except StopIteration:
                pass
        live = nxt


def emit_A(p, c, io):
    x_in = io["x"]
    kvp_out = io["kvp"]
    eqT_res = io.get("eqT_res")
    rq_res = io.get("rq_res")
    eqT_out = io.get("eqT")
    rq_out = io.get("rq")

    wstage = [p.sb(f"wstage{i}", [128, 2 * D], F32) for i in range(2)]
    wb_f = p.sb("wb_f", [1, D], F32)
    wh = {k: p.sb(f"wh_{k}", [128, NJ * D], BF16) for k in "qkv"}
    wbh = {k: p.sb(f"wbh_{k}", [1, D], BF16) for k in "qkv"}

    xT = p.sb("xT", [128, NJ * T], BF16)
    NL = 2
    xf_t = [p.sb(f"xf_t{i}", [128, D], F32) for i in range(NL)]
    xh_t = [p.sb(f"xh_t{i}", [128, D], BF16) for i in range(NL)]
    qa = [p.sb(f"qa{i}", [128, D], F32) for i in range(NL)]
    qb = [p.sb(f"qb{i}", [128, D], F32) for i in range(NL)]
    qc = [p.sb(f"qc{i}", [128, D], F32) for i in range(NL)]
    eqbf = [p.sb(f"eqbf{i}", [128, D], BF16) for i in range(NL)]
    ekbf = [p.sb(f"ekbf{i}", [128, D], BF16) for i in range(NL)]
    vpbf = [p.sb(f"vpbf{i}", [128, D], BF16) for i in range(NL)]
    sqb0 = p.sb("sqb0", [128, D], F32)
    sqb = [sqb0] * NL
    ncol = [p.sb(f"ncol{i}", [128, NH], F32) for i in range(NL)]
    scol = [p.sb(f"scol{i}", [128, NH], F32) for i in range(NL)]
    rk = [p.sb(f"rk{i}", [128, NH], F32) for i in range(NL)]
    rkexp = [p.sb(f"rkexp{i}", [128, D], F32) for i in range(NL)]
    rqsb = ([p.sb(f"rqsb{i}", [128, NH], F32) for i in range(NL)]
            if io.get("rq_res") is None else None)
    eqstage = ([p.sb(f"eqstage{i}", [128, D], BF16) for i in range(NL)]
               if io.get("eqT_res") is None else None)
    kvsb = qa[0][0:64, 0:1024]

    tpsL = [p.ps("tpsA", [128, 1024], BF16), p.ps("tps2", [128, 1024], BF16)]
    qps = [p.ps(f"qps{i}", [128, 1024], F32) for i in range(NL)]
    kvps = p.ps("kvps", [64, 1024], F32)

    wready = []
    wcast = {}
    for idx, (k, h) in enumerate([(k, h) for k in "qkv" for h in range(4)]):
        tw = p.dma_rr("in_w", 2, idx,
                      wstage[idx % 2][:].rearrange("p (k f) -> p k f", f=D),
                      io["w%s_a" % k][h * 256:(h + 1) * 256, :]
                      .rearrange("(k p) f -> p k f", p=128),
                      waits=wcast.get(idx - 2, list(c["ready"])))
        cs = []
        for j in range(2):
            cs.append(p.emit("scalar", lambda e, k=k, h=h, j=j, idx=idx: e.activation(
                out=wh[k][:, (h * 2 + j) * D:(h * 2 + j + 1) * D],
                in_=wstage[idx % 2][:, j * D:(j + 1) * D],
                func=AF.Copy), waits=[tw]))
        wcast[idx] = cs
        wready += cs
    prevb = None
    for k in "qkv":
        tb = p.dma("in_b", wb_f[:, 0:D], io["w%s_a" % k][NJ * 128:NJ * 128 + 1, :],
                   waits=[prevb] if prevb else list(c["ready"]))
        prevb = p.emit("scalar", lambda e, k=k: e.activation(
            out=wbh[k][:, 0:D], in_=wb_f[:, 0:D], func=AF.Copy),
            waits=[tb])
        wready.append(prevb)

    pv = [dict(cast=None, xtp=None, eq_evac=None, vstt=None, kps=None,
               nq=None, sqq=None, rq=None) for _ in range(NL)]
    gkv = dict(ek=None, kvlast=None)

    def tile_body(t, L):
        v = pv[L]
        w_load = [v["cast"]] if v["cast"] else list(c["ready"])
        t_x = p.dma_rr("in_x", NL, L, xf_t[L][:],
                       x_in[t * 128:(t + 1) * 128, :], waits=w_load)
        yield
        w_c = [t_x] + ([v["xtp"]] if v["xtp"] else [])
        t_cast = p.emit("scalar", lambda e, L=L: e.activation(
            out=xh_t[L][:], in_=xf_t[L][:], func=AF.Copy), waits=w_c)
        v["cast"] = t_cast
        xT_view = xT[:].rearrange("p (j t) -> p j t", t=T)[:, :, t * 128:(t + 1) * 128]
        tev = transpose_tile(p, c, tpsL[L], xh_t[L], xT_view, [t_cast],
                             [v["xtp"]] if v["xtp"] else [], nblk=NJ)
        v["xtp"] = tev
        t_xT = tev
        yield

        def proj(k, waits_extra, t=t, t_xT=t_xT, L=L):
            last = None
            for n in range(2):
                for kk in range(NJ):
                    last = p.emit("tensor", lambda e, n=n, kk=kk, k=k, t=t, L=L: e.matmul(
                        out=qps[L][:, n * 512:(n + 1) * 512],
                        lhsT=xT[:, kk * T + t * 128: kk * T + (t + 1) * 128],
                        rhs=wh[k][:, kk * D + n * 512: kk * D + (n + 1) * 512],
                        start=(kk == 0), stop=False),
                        waits=([t_xT] + wready + waits_extra) if kk == 0 and n == 0 else [])
                last = p.emit("tensor", lambda e, n=n, k=k, L=L: e.matmul(
                    out=qps[L][:, n * 512:(n + 1) * 512],
                    lhsT=c["ones_bf"][0:1, 0:128],
                    rhs=wbh[k][:, n * 512:(n + 1) * 512],
                    start=False, stop=True), waits=[])
            return last

        def elu(out_bf, t_mm, war=(), L=L):
            ta = p.emit("scalar", lambda e, L=L: e.activation(
                out=qa[L][:], in_=qps[L][:], func=AF.Relu),
                waits=[t_mm] + list(war))
            tb = p.emit("scalar", lambda e, L=L: e.activation(
                out=qb[L][:], in_=qps[L][:], func=AF.Relu, scale=-1.0), waits=[])
            tc = p.emit("scalar", lambda e, L=L: e.activation(
                out=qc[L][:], in_=qb[L][:], func=AF.Exp, scale=-1.0), waits=[])
            te = p.emit("vector", lambda e, out_bf=out_bf, L=L: e.scalar_tensor_tensor(
                out=out_bf[:], in0=qa[L][:], scalar=-1.0, in1=qc[L][:],
                op0=ALU.add, op1=ALU.add), waits=[tc])
            return te, tb

        t_qmm = proj("q", [v["vstt"]] if v["vstt"] else [])
        yield
        war_q = [gkv["ek"]] if gkv["ek"] else []
        t_eq, t_q_psfree = elu(eqbf[L], t_qmm, war=war_q)
        yield
        t_eqsq = p.emit("scalar", lambda e, L=L: e.activation(
            out=sqb[L][:], in_=eqbf[L][:], func=AF.Square), waits=[t_eq])
        t_nq = p.emit("vector", lambda e, L=L: e.reduce_sum(
            out=ncol[L][:, 0:NH], in_=sqb[L][:].rearrange("p (h w) -> p h w", w=HD),
            axis=AX.X), waits=[t_eqsq])
        t_sqq = p.emit("scalar", lambda e, L=L: e.activation(
            out=scol[L][:, 0:NH], in_=ncol[L][:, 0:NH], func=AF.Sqrt),
            waits=[t_nq])
        if rq_res is not None:
            t_rq = p.emit("vector", lambda e, L=L, t=t: e.reciprocal(
                out=rq_res[:, t * NH:(t + 1) * NH], in_=scol[L][:, 0:NH]),
                waits=[t_sqq])
        else:
            t_rq = p.emit("vector", lambda e, L=L: e.reciprocal(
                out=rqsb[L][:, 0:NH], in_=scol[L][:, 0:NH]),
                waits=[t_sqq, p.stok(f"o_rq{L}")])
            p.dma_rr("o_rq", NL, L, rq_out[t, :, :], rqsb[L][:, 0:NH],
                     waits=[t_rq])
        yield

        w_ps2 = [v["eq_evac"]] if v["eq_evac"] else []
        if eqT_res is not None:
            dstv = eqT_res[:].rearrange("p (j t) -> p j t",
                                        t=T)[:, :, t * 128:(t + 1) * 128]
            tev_eq = transpose_tile(p, c, tpsL[L], eqbf[L], dstv, [t_eq],
                                    w_ps2, nblk=NJ)
        else:
            tev_eq = transpose_tile(p, c, tpsL[L], eqbf[L], eqstage[L][:],
                                    [t_eq], w_ps2, nblk=NJ,
                                    waits_evac=[p.stok(f"o_eq{L}")])
            p.dma_rr(
                "o_eq", NL, L,
                eqT_out[:, :, t * 128:(t + 1) * 128].rearrange("j p t -> p j t"),
                eqstage[L][:].rearrange("p (j t) -> p j t", t=128),
                waits=[tev_eq])
        v["eq_evac"] = tev_eq
        yield

        t_kmm = proj("k", [t_q_psfree])
        yield
        t_ek, t_k_psfree = elu(ekbf[L], t_kmm, war=[t_eq])
        v["kps"] = t_k_psfree
        yield
        t_eksq = p.emit("scalar", lambda e, L=L: e.activation(
            out=sqb[L][:], in_=ekbf[L][:], func=AF.Square),
            waits=[t_ek, t_nq])
        t_nk = p.emit("vector", lambda e, L=L: e.reduce_sum(
            out=ncol[L][:, 0:NH], in_=sqb[L][:].rearrange("p (h w) -> p h w", w=HD),
            axis=AX.X), waits=[t_eksq, t_sqq])
        t_sqk = p.emit("scalar", lambda e, L=L: e.activation(
            out=scol[L][:, 0:NH], in_=ncol[L][:, 0:NH], func=AF.Sqrt),
            waits=[t_nk, t_rq])
        t_rk = p.emit("vector", lambda e, L=L: e.reciprocal(
            out=rk[L][:, 0:NH], in_=scol[L][:, 0:NH]), waits=[t_sqk])
        t_rkexp = p.emit("vector", lambda e, L=L: e.tensor_copy(
            rkexp[L][:].rearrange("p (h w) -> p h w", w=HD),
            rk[L][:, 0:NH].unsqueeze(2).broadcast_to([128, NH, HD])),
            waits=[t_rk])
        yield

        t_vmm = proj("v", [t_k_psfree])
        yield
        t_vstt = p.emit("vector", lambda e, L=L: e.scalar_tensor_tensor(
            out=vpbf[L][:], in0=qps[L][:], scalar=1.0, in1=rkexp[L][:],
            op0=ALU.mult, op1=ALU.mult), waits=[t_vmm, t_rkexp])
        v["vstt"] = t_vstt
        yield

        t_kv = None
        for h in range(NH):
            t_kv = p.emit("tensor", lambda e, h=h, t=t, L=L: e.matmul(
                out=kvps[0:64, h * 64:(h + 1) * 64],
                lhsT=ekbf[L][:, h * 64:(h + 1) * 64],
                rhs=vpbf[L][:, h * 64:(h + 1) * 64],
                start=(t == 0 and h % 8 == 0),
                stop=(t == NT - 1 and h % 8 == 7)),
                waits=[t_ek, t_vstt] if h == 0 else [])
        gkv["ek"] = t_eq if False else t_ek
        gkv["kvlast"] = t_kv

    for t0 in range(0, NT, NL):
        interleave_pairs([tile_body(t, t % NL)
                          for t in range(t0, min(t0 + NL, NT))])
    t_kv_last = gkv["kvlast"]

    t_kvev = p.emit("vector", lambda e: e.tensor_copy(
        kvsb[:], kvps[0:64, :]), waits=[t_kv_last])
    p.dma("o_kv", kvp_out[:], kvsb[:], waits=[t_kvev])


def build_A():
    p = Prog("A")
    c = common_consts(p)
    io = dict(x=p.din("x", [T, D], F32),
              wq_a=p.din("wq_a", [D + 1, D], F32),
              wk_a=p.din("wk_a", [D + 1, D], F32),
              wv_a=p.din("wv_a", [D + 1, D], F32),
              kvp=p.dout("kvp", [64, 1024], F32),
              eqT=p.dout("eqT", [NJ, 128, T], BF16),
              rq=p.dout("rq", [NT, 128, NH], F32))
    emit_A(p, c, io)
    return p.finalize()



def emit_B1(p, c, io, ntiles=NT, stage=99, fused=False, kv_wait=None):
    x_in = io["x"]
    eqT_res = io.get("eqT_res")
    rq_res = io.get("rq_res")
    eqT_in = io.get("eqT")
    rq_in = io.get("rq")
    kvp_in = io["kvp"]
    g1_in = io["g1r"]
    be1_in = io["be1r"]
    ao_out = io["attn_out"]
    atT_out = io["atT"]

    eqT_sb = (p.sb("eqT_sb", [128, NJ * T], BF16) if eqT_res is None
              else eqT_res)
    kvacc = p.sb("kvacc", [64, 1024], F32)
    kvh = p.sb("kvh", [64, 1024], BF16)
    kvblk = p.sb("kvblk", [128, NJ * 128], BF16)
    wstage = p.sb("wstage", [128, NJ * D], F32)
    wb_f = p.sb("wb_f", [1, D], F32)
    woh = p.sb("woh", [128, NJ * D], BF16)
    wob_h = p.sb("wob_h", [1, D], BF16)
    g1row = p.sb("g1row", [1, D], F32)
    be1row = p.sb("be1row", [1, D], F32)
    gb1 = p.sb("gb1", [128, D], F32)
    bb1 = p.sb("bb1", [128, D], F32)
    NL = 2
    x_t = [p.sb(f"x_t{i}", [128, D], F32) for i in range(NL)]
    rq_t = [p.sb(f"rq_t{i}", [128, NH], F32) for i in range(NL)]
    rqx = [p.sb(f"rqx{i}", [128, NH], F32) for i in range(NL)]
    rqexp = [p.sb(f"rqexp{i}", [128, D], F32) for i in range(NL)]
    ctxn = [p.sb(f"ctxn{i}", [128, D], BF16) for i in range(NL)]
    ctxT = [p.sb(f"ctxT{i}", [128, D], BF16) for i in range(NL)]
    u = [p.sb(f"u{i}", [128, D], F32) for i in range(NL)]
    s1col = [p.sb(f"s1col{i}", [128, 1], F32) for i in range(NL)]
    lnb = [dict(sqw=p.sb(f"sqw{i}", [128, D], F32),
                mcol=p.sb(f"mcol{i}", [128, 1], F32),
                vcol=p.sb(f"vcol{i}", [128, 1], F32),
                tcol=p.sb(f"tcol{i}", [128, 1], F32)) for i in range(NL)]
    ao = [p.sb(f"ao{i}", [128, D], F32) for i in range(NL)]
    ao2 = [p.sb(f"ao2{i}", [128, D], F32) for i in range(NL)]
    aof = [p.sb(f"aof{i}", [128, D], F32) for i in range(NL)]
    at_bf = [p.sb(f"at_bf{i}", [128, D], BF16) for i in range(NL)]
    atT_stage = [p.sb(f"atT_stage{i}", [128, D], BF16) for i in range(NL)]

    cps = [p.ps(f"cps{i}", [128, 1024], F32) for i in range(NL)]
    tps = [p.ps(f"tps{i}", [128, 1024], BF16) for i in range(NL)]
    psA = p.ps("psA", [128, 512], F32)

    t_eqT_l = []
    if eqT_res is None:
        for ch in range(4):
            t_eqT_l.append(p.dma_rr(
                "in_eq", 4, ch,
                eqT_sb[:].rearrange("p (j t) -> p j t", t=T)[:, 2 * ch:2 * ch + 2, :],
                eqT_in[2 * ch:2 * ch + 2].rearrange("j p t -> p j t")))
    if fused:
        tprev = p.dma("in_kv", kvacc[:], kvp_in[:], waits=[kv_wait],
                      queue="gpsimd")
    else:
        tprev = None
        for cc in range(NCORES):
            tprev = p.dma("in_kv", kvacc[:], kvp_in[cc],
                          waits=[tprev] if tprev else [], queue="gpsimd",
                          accum_op=(ALU.add if cc else ALU.bypass))
    t_kvh0 = p.emit("scalar", lambda e: e.activation(
        out=kvh[0:64, :], in_=kvacc[:], func=AF.Copy), waits=[tprev])
    t_z = p.emit("vector", lambda e: e.memset(kvblk[:], 0.0))
    kv_toks = []
    for j in range(NJ):
        kv_toks.append(p.dma_rr("in_kv2", 4, 2 * j,
                                kvblk[0:64, j * 128:j * 128 + 64],
                                kvh[0:64, (2 * j) * 64:(2 * j + 1) * 64],
                                waits=[t_kvh0, t_z]))
        kv_toks.append(p.dma_rr("in_kv2", 4, 2 * j + 1,
                                kvblk[64:128, j * 128 + 64:(j + 1) * 128],
                                kvh[0:64, (2 * j + 1) * 64:(2 * j + 2) * 64],
                                waits=[t_kvh0, t_z]))
    t_kvh = kv_toks[-4:]
    wready = load_cast_weight(p, c, None, NJ, D, wstage, woh, wb_f, wob_h,
                              waits=c["ready"], win=io["wo_a"])
    t_g1 = p.dma("in", g1row[:], g1_in[:], waits=wready)
    t_be1 = p.dma("in", be1row[:], be1_in[:], waits=wready)
    t_in = p.stok("in")
    tb1 = bcast_row_f32(p, c, gb1, g1row, D, psA, waits=[t_in])
    tb2 = bcast_row_f32(p, c, bb1, be1row, D, psA, waits=[t_in] + tb1)
    t_bc = tb1 + tb2

    pv = [dict(rqx=None, lnsq=None, ctx_stt=None, ctxT_ev=None, u=None,
               lnout=None) for _ in range(NL)]

    def tile_body(t, L):
        v = pv[L]
        t_x = p.dma_rr("in_x", NL, L, x_t[L][:], x_in[t * 128:(t + 1) * 128, :],
                       waits=[v["u"]] if v["u"] else wready)
        if rq_res is not None:
            yield
            t_rqx = p.emit("scalar", lambda e, L=L, t=t: e.mul(
                rqx[L][:], rq_res[:, t * NH:(t + 1) * NH], 0.125), waits=[])
        else:
            t_rq = p.dma_rr("in_r", NL, L, rq_t[L][:], rq_in[t],
                            waits=[v["rqx"]] if v["rqx"] else wready)
            yield
            t_rqx = p.emit("scalar", lambda e, L=L: e.mul(
                rqx[L][:], rq_t[L][:], 0.125), waits=[t_rq])
        v["rqx"] = t_rqx
        t_rqe = p.emit("vector", lambda e, L=L: e.tensor_copy(
            rqexp[L][:].rearrange("p (h w) -> p h w", w=HD),
            rqx[L][:].unsqueeze(2).broadcast_to([128, NH, HD])), waits=[t_rqx])
        yield

        t_cmm = None
        for j in range(NJ):
            t_cmm = p.emit("tensor", lambda e, j=j, t=t, L=L: e.matmul(
                out=cps[L][:, j * 128:(j + 1) * 128],
                lhsT=eqT_sb[:, j * T + t * 128:j * T + (t + 1) * 128],
                rhs=kvblk[:, j * 128:(j + 1) * 128],
                start=(j % 4 == 0), stop=(j % 4 == 3)),
                waits=(t_eqT_l + t_kvh +
                       [x for x in (v["ctx_stt"], v["u"]) if x]) if j == 0 else [])
        t_ctx = p.emit("vector", lambda e, L=L: e.scalar_tensor_tensor(
            out=ctxn[L][:], in0=cps[L][:], scalar=1.0, in1=rqexp[L][:],
            op0=ALU.mult, op1=ALU.mult), waits=[t_cmm, t_rqe])
        yield

        t_ctxT = transpose_tile(p, c, tps[L], ctxn[L], ctxT[L][:], [t_ctx],
                                [v["ctxT_ev"]] if v["ctxT_ev"] else [],
                                nblk=NJ)
        yield

        t_omm = None
        for n in range(2):
            for kk in range(NJ):
                t_omm = p.emit("tensor", lambda e, n=n, kk=kk, L=L: e.matmul(
                    out=cps[L][:, n * 512:(n + 1) * 512],
                    lhsT=ctxT[L][:, kk * 128:(kk + 1) * 128],
                    rhs=woh[:, kk * D + n * 512:kk * D + (n + 1) * 512],
                    start=(kk == 0), stop=False),
                    waits=([t_ctxT] + wready +
                           ([v["u"]] if v["u"] else [])) if kk == 0 and n == 0 else [])
            t_omm = p.emit("tensor", lambda e, n=n, L=L: e.matmul(
                out=cps[L][:, n * 512:(n + 1) * 512],
                lhsT=c["ones_bf"][0:1, 0:128],
                rhs=wob_h[:, n * 512:(n + 1) * 512],
                start=False, stop=True), waits=[])
            yield

        t_u = p.emit("vector", lambda e, L=L: e.scalar_tensor_tensor(
            out=u[L][:], in0=cps[L][:], scalar=1.0, in1=x_t[L][:],
            op0=ALU.mult, op1=ALU.add, accum_out=s1col[L][:, 0:1]),
            waits=[t_omm, t_x] + ([v["lnsq"]] if v["lnsq"] else []))
        v["u"] = t_u
        yield

        w_ln = [t_u] + ([v["lnout"]] if v["lnout"] else [])
        t_ao, t_sq1 = ln_stats_apply(p, lnb[L], u[L], s1col[L], ao[L], 1e-12,
                                     w_ln, None, "ln1")
        v["lnsq"] = t_sq1
        yield
        t_m1 = p.emit("vector", lambda e, L=L: e.tensor_mul(
            ao2[L][:], ao[L][:], gb1[:]), waits=[t_ao] + t_bc)
        t_aof = p.emit("vector", lambda e, L=L: e.tensor_add(
            aof[L][:], ao2[L][:], bb1[:]),
            waits=[t_m1, p.stok(f"o_ao{L}")])
        v["lnout"] = t_aof
        yield

        t_cast = p.emit("scalar", lambda e, L=L: e.activation(
            out=at_bf[L][:], in_=aof[L][:], func=AF.Copy), waits=[t_aof])
        p.dma_rr("o_ao", NL, L, ao_out[t * 128:(t + 1) * 128, :], aof[L][:],
                 waits=[t_aof])
        yield
        t_atT = transpose_tile(p, c, tps[L], at_bf[L], atT_stage[L][:],
                               [t_cast], [t_ctxT], nblk=NJ,
                               waits_evac=[p.stok(f"o_at{L}")])
        v["ctxT_ev"] = t_atT
        v["ctx_stt"] = t_ctx
        p.dma_rr("o_at", NL, L,
                 atT_out[:, :, t * 128:(t + 1) * 128].rearrange("j p t -> p j t"),
                 atT_stage[L][:].rearrange("p (j t) -> p j t", t=128),
                 waits=[t_atT])

    for t0 in range(0, ntiles, NL):
        interleave_pairs([tile_body(t, t % NL)
                          for t in range(t0, min(t0 + NL, ntiles))])


def build_B1(ntiles=NT, stage=99):
    p = Prog("B1")
    c = common_consts(p)
    io = dict(x=p.din("x", [T, D], F32),
              eqT=p.din("eqT", [NJ, 128, T], BF16),
              rq=p.din("rq", [NT, 128, NH], F32),
              kvp=p.din("kvp", [NCORES, 64, 1024], F32),
              g1r=p.din("g1r", [1, D], F32),
              be1r=p.din("be1r", [1, D], F32),
              wo_a=p.din("wo_a", [D + 1, D], F32),
              attn_out=p.dout("attn_out", [T, D], F32),
              atT=p.dout("atT", [NJ, 128, T], BF16))
    emit_B1(p, c, io, ntiles=ntiles, stage=stage)
    return p.finalize()



def emit_B2a(p, c, io):
    atT_in = io["atT"]
    w1_in = io["w1_a"]
    b1c_in = io["b1cB"]
    hT_out = io["hT"]

    atT_sb = p.sb("atT_sb", [128, NJ * T], BF16)
    wstage = [p.sb(f"wstage{i}", [128, FF], F32) for i in range(2)]
    w1h = p.sb("w1h", [128, NJ * FF], BF16)
    b1cB = p.sb("b1cB_sb", [128, 32], F32)
    hbig = [p.sb(f"hbig{i}", [128, 8 * 512], BF16) for i in range(2)]
    hps = [p.ps(f"hps{i}", [128, 512], F32) for i in range(2)]

    t_atT_l = []
    for ach in range(4):
        t_atT_l.append(p.dma_rr(
            "in_at", 4, ach,
            atT_sb[:].rearrange("p (j t) -> p j t", t=T)[:, 2 * ach:2 * ach + 2, :],
            atT_in[2 * ach:2 * ach + 2].rearrange("j p t -> p j t")))
    tb = p.dma("in", b1cB[:], b1c_in[:])
    wready = [tb] + t_atT_l
    wcast = {}
    for ch in range(NJ):
        tw = p.dma_rr("in_w", 2, ch, wstage[ch % 2][:],
                      w1_in[ch * 128:(ch + 1) * 128, :],
                      waits=wcast.get(ch - 2, c["ready"]))
        cs = [p.emit("scalar", lambda e, ch=ch: e.activation(
            out=w1h[:, ch * FF:(ch + 1) * FF],
            in_=wstage[ch % 2][:], func=AF.Copy), waits=[tw])]
        wcast[ch] = cs
        wready += cs

    prev_ev = {}
    prev_st = {}
    i = 0
    for cch in range(4):
        for m in range(32):
            par = i % 2
            grp = (i // 8) % 2
            t_mm = None
            for kk in range(NJ):
                t_mm = p.emit("tensor", lambda e, cch=cch, m=m, kk=kk, par=par: e.matmul(
                    out=hps[par][:],
                    lhsT=w1h[:, kk * FF + m * 128:kk * FF + (m + 1) * 128],
                    rhs=atT_sb[:, kk * T + cch * 512:kk * T + (cch + 1) * 512],
                    start=(kk == 0), stop=(kk == NJ - 1)),
                    waits=(wready + list(prev_ev.get(par, []))) if kk == 0 else [])
            t_ev = p.emit("scalar", lambda e, par=par, grp=grp, m=m: e.activation(
                out=hbig[grp][:, (m % 8) * 512:(m % 8 + 1) * 512],
                in_=hps[par][:], func=AF.Gelu, bias=b1cB[:, m:m + 1]),
                waits=[t_mm] + (list(prev_st.get(grp, [])) if m % 8 < 2 else []))
            prev_ev[par] = [t_ev]
            if m % 8 == 7:
                t_st = p.dma_rr("o_h", 2, grp,
                                hT_out[m - 7:m + 1, :, cch * 512:(cch + 1) * 512]
                                .rearrange("m p tt -> p m tt"),
                                hbig[grp][:].rearrange("p (m tt) -> p m tt", tt=512),
                                waits=[t_ev])
                prev_st[grp] = [t_st]
            i += 1


def build_B2a():
    p = Prog("B2a")
    c = common_consts(p)
    io = dict(atT=p.din("atT", [NJ, 128, T], BF16),
              w1_a=p.din("w1_a", [NJ * 128 + 1, FF], F32),
              b1cB=p.din("b1cB", [128, 32], F32),
              hT=p.dout("hT", [32, 128, T], BF16))
    emit_B2a(p, c, io)
    return p.finalize()



def emit_B2b(p, c, io):
    hT_in = io["hT"]
    ao_in = io["attn_out"]
    x_in = io["x"]
    w2_in = io["w2_a"]
    g2_in = io["g2r"]
    be2_in = io["be2r"]
    x1_out = io["x1"]

    wstage = [p.sb(f"wstage{i}", [128, D], F32) for i in range(2)]
    w2h = p.sb("w2h", [128, 32 * D], BF16)
    w2b_f = wstage[0][0:1, 0:D]
    w2b_h = p.sb("w2b_h", [1, D], BF16)
    g2row = p.sb("g2row", [1, D], F32)
    be2row = p.sb("be2row", [1, D], F32)
    gb2 = p.sb("gb2", [128, D], F32)
    bb2 = p.sb("bb2", [128, D], F32)
    NL = 2
    hT_c = [p.sb(f"hT_c{i}", [128, 32 * 256], BF16) for i in range(2)]
    x_t = [p.sb(f"x_t{i}", [128, D], F32) for i in range(NL)]
    at_t = [p.sb(f"at_t{i}", [128, D], F32) for i in range(NL)]
    u = [p.sb(f"u{i}", [128, D], F32) for i in range(NL)]
    s1col = [p.sb(f"s1col{i}", [128, 1], F32) for i in range(NL)]
    sqw0 = p.sb("sqw0", [128, D], F32)
    lnb = [dict(sqw=sqw0,
                mcol=p.sb(f"mcol{i}", [128, 1], F32),
                vcol=p.sb(f"vcol{i}", [128, 1], F32),
                tcol=p.sb(f"tcol{i}", [128, 1], F32)) for i in range(NL)]
    ao0 = p.sb("ao0", [128, D], F32)
    ao = [ao0] * NL
    aof0 = p.sb("aof0", [128, D], F32)
    aof = [aof0] * NL
    u2 = at_t
    x1_t = [p.sb(f"x1_t{i}", [128, D], F32) for i in range(NL)]
    ao2 = u

    ops = [p.ps(f"ops{i}", [128, 1024], F32) for i in range(NL)]
    psA = p.ps("psA", [128, 512], F32)

    wready = []
    wcast = {}
    for ch in range(32):
        tw = p.dma_rr("in_w", 2, ch, wstage[ch % 2][:],
                      w2_in[ch * 128:(ch + 1) * 128, :],
                      waits=wcast.get(ch - 2, list(c["ready"])))
        cs = [p.emit("scalar", lambda e, ch=ch: e.activation(
            out=w2h[:, ch * D:(ch + 1) * D],
            in_=wstage[ch % 2][:], func=AF.Copy), waits=[tw])]
        wcast[ch] = cs
        wready += cs
    tb = p.dma("in", w2b_f[:], w2_in[32 * 128:32 * 128 + 1, :],
               waits=wcast[31])
    wready.append(p.emit("scalar", lambda e: e.activation(
        out=w2b_h[:], in_=w2b_f[:], func=AF.Copy), waits=[tb]))
    t_g2 = p.dma("in", g2row[:], g2_in[:], waits=wready)
    t_be2 = p.dma("in", be2row[:], be2_in[:], waits=wready)
    t_in = p.stok("in")
    tb1 = bcast_row_f32(p, c, gb2, g2row, D, psA, waits=[t_in])
    tb2 = bcast_row_f32(p, c, bb2, be2row, D, psA, waits=[t_in] + tb1)
    t_bc = tb1 + tb2

    pv = [dict(u=None, mm=None, lnout=None, lnsq=None, st=None)
          for _ in range(NL)]
    g_h = {}
    g_mm = {}

    def tile_body(t, L):
        v = pv[L]
        pr = t // 2
        if t % 2 == 0:
            war = [g_mm[pr - 2]] if (pr - 2) in g_mm else wready
            g_h[pr] = p.dma_rr(
                "in_h", 2, pr,
                hT_c[pr % 2][:].rearrange("p (k tt) -> p k tt", tt=256),
                hT_in[:, :, t * 128:(t + 2) * 128].rearrange("k p tt -> p k tt"),
                waits=war)
        t_h = g_h[pr]
        t_at = p.dma_rr("in_a", NL, L, at_t[L][:], ao_in[t * 128:(t + 1) * 128, :],
                     waits=[v["lnout"]] if v["lnout"] else wready)
        t_x = p.dma_rr("in_x", NL, L, x_t[L][:], x_in[t * 128:(t + 1) * 128, :],
                    waits=[v["lnout"]] if v["lnout"] else wready)
        yield

        t_mm = None
        for n in range(2):
            for kk in range(32):
                t_mm = p.emit("tensor", lambda e, n=n, kk=kk, L=L, t=t, pr=pr: e.matmul(
                    out=ops[L][:, n * 512:(n + 1) * 512],
                    lhsT=hT_c[pr % 2][:, kk * 256 + (t % 2) * 128:
                                      kk * 256 + (t % 2) * 128 + 128],
                    rhs=w2h[:, kk * D + n * 512:kk * D + (n + 1) * 512],
                    start=(kk == 0), stop=False),
                    waits=([t_h] + wready +
                           ([v["u"]] if v["u"] else [])) if kk == 0 and n == 0 else [])
            t_mm = p.emit("tensor", lambda e, n=n, L=L: e.matmul(
                out=ops[L][:, n * 512:(n + 1) * 512],
                lhsT=c["ones_bf"][0:1, 0:128],
                rhs=w2b_h[:, n * 512:(n + 1) * 512],
                start=False, stop=True), waits=[])
            yield
        v["mm"] = t_mm
        g_mm[pr] = t_mm

        t_u = p.emit("vector", lambda e, L=L: e.scalar_tensor_tensor(
            out=u[L][:], in0=ops[L][:], scalar=1.0, in1=at_t[L][:],
            op0=ALU.mult, op1=ALU.add, accum_out=s1col[L][:, 0:1]),
            waits=[t_mm, t_at] + ([v["lnsq"]] if v["lnsq"] else []))
        v["u"] = t_u
        yield
        w_ln = [t_u] + ([v["lnout"]] if v["lnout"] else [])
        t_ao, t_sq1 = ln_stats_apply(p, lnb[L], u[L], s1col[L], ao[L], 1e-12,
                                     w_ln, None, "ln2")
        t_m1 = p.emit("vector", lambda e, L=L: e.tensor_mul(
            ao2[L][:], ao[L][:], gb2[:]), waits=[t_ao] + t_bc)
        t_aof = p.emit("vector", lambda e, L=L: e.tensor_add(
            aof[L][:], ao2[L][:], bb2[:]), waits=[t_m1])
        t_u2 = p.emit("vector", lambda e, L=L: e.scalar_tensor_tensor(
            out=u2[L][:], in0=aof[L][:], scalar=1.0, in1=x_t[L][:],
            op0=ALU.mult, op1=ALU.add, accum_out=s1col[L][:, 0:1]),
            waits=[t_aof, t_x])
        yield
        t_x1, t_sq2 = ln_stats_apply(p, lnb[L], u2[L], s1col[L], x1_t[L],
                                     1e-5, [t_u2, p.stok(f"o_x1{L}")], None, "lnb")
        v["lnsq"] = t_sq2
        v["lnout"] = t_x1
        p.dma_rr("o_x1", NL, L, x1_out[t * 128:(t + 1) * 128, :], x1_t[L][:],
                 waits=[t_x1])

    for t0 in range(0, NT, NL):
        interleave_pairs([tile_body(t, t % NL)
                          for t in range(t0, min(t0 + NL, NT))])


def build_B2b():
    p = Prog("B2b")
    c = common_consts(p)
    io = dict(hT=p.din("hT", [32, 128, T], BF16),
              attn_out=p.din("attn_out", [T, D], F32),
              x=p.din("x", [T, D], F32),
              w2_a=p.din("w2_a", [32 * 128 + 1, D], F32),
              g2r=p.din("g2r", [1, D], F32),
              be2r=p.din("be2r", [1, D], F32),
              x1=p.dout("x1", [T, D], F32))
    emit_B2b(p, c, io)
    return p.finalize()



def emit_C(p, c, io):
    x1_in = io["x1"]
    gm_in = io["gmr"]
    bb_in = io["bbr"]
    mu_in = io["mur"]
    sc_in = io["scl"]
    Praw_out = io["Praw"]
    Sloc_out = io["Sloc"]
    E_out = io["E"]
    Zc_out = io["Zc"]

    onescol_f = p.sb("onescol_f", [128, 1], F32)
    gmrow = p.sb("gmrow", [1, D], F32)
    bbrow = p.sb("bbrow", [1, D], F32)
    gb = p.sb("gb", [128, D], F32)
    bb = p.sb("bb", [128, D], F32)
    mustage = p.sb("mustage", [128, NJ * ES], F32)
    musq = p.sb("musq", [128, NJ * ES], F32)
    srow = p.sb("srow", [1, ES], F32)
    srow2 = p.sb("srow2", [1, ES], F32)
    rmub = p.sb("rmub", [128, ES], F32)
    mun_h = p.sb("mun_h", [128, NJ * ES], BF16)
    sccol = p.sb("sccol", [128, 1], F32)
    sclrow = p.sb("sclrow", [1, 1], F32)
    NL = 2
    x1_t = [p.sb(f"x1_t{i}", [128, D], F32) for i in range(NL)]
    s1col = [p.sb(f"s1col{i}", [128, 1], F32) for i in range(NL)]
    lnb = [dict(sqw=p.sb(f"sqw{i}", [128, D], F32),
                mcol=p.sb(f"mcol{i}", [128, 1], F32),
                vcol=p.sb(f"vcol{i}", [128, 1], F32),
                tcol=p.sb(f"tcol{i}", [128, 1], F32)) for i in range(NL)]
    xn = [p.sb(f"xn{i}", [128, D], F32) for i in range(NL)]
    xn2a = [p.sb(f"xn2a{i}", [128, D], F32) for i in range(NL)]
    xn2 = [p.sb(f"xn2{i}", [128, D], F32) for i in range(NL)]
    nrm = [p.sb(f"nrm{i}", [128, 1], F32) for i in range(NL)]
    nrm2 = [p.sb(f"nrm2{i}", [128, 1], F32) for i in range(NL)]
    rn = [p.sb(f"rn{i}", [128, 1], F32) for i in range(NL)]
    xu_bf = [p.sb(f"xu_bf{i}", [128, D], BF16) for i in range(NL)]
    xnh = [p.sb(f"xnh{i}", [128, D], BF16) for i in range(NL)]
    xuT_t = [p.sb(f"xuT_t{i}", [128, D], BF16) for i in range(NL)]
    E_bf = [p.sb(f"E_bf{i}", [128, ES], BF16) for i in range(NL)]
    zc_col = [p.sb(f"zc_col{i}", [128, 1], F32) for i in range(NL)]
    Psb = p.sb("Psb", [128, NJ * ES], F32)
    slrow = p.sb("slrow", [1, ES], F32)

    SL_ps = p.ps("SL_ps", [1, ES], F32)
    Pps_a = p.ps("Pps_a", [128, 1024], F32)
    Pps_b = p.ps("Pps_b", [128, 1024], F32)
    L_ps = p.ps("L_ps", [128, ES], F32)
    p.push_region("pp")
    psA = p.ps("psA", [128, 512], F32)

    t_oc = p.emit("vector", lambda e: e.memset(onescol_f[:], 1.0))
    t_g = p.dma("in", gmrow[:], gm_in[:], waits=c["ready"])
    t_b = p.dma("in", bbrow[:], bb_in[:])
    t_mu = p.dma("in", mustage[:].rearrange("p (j s) -> p j s", s=ES),
                 mu_in[:].rearrange("(j p) s -> p j s", p=128))
    t_sc = p.dma("in", sclrow[:], sc_in[:])
    t_in = p.stok("in")
    tb1 = bcast_row_f32(p, c, gb, gmrow, D, psA, waits=[t_in])
    tb2 = bcast_row_f32(p, c, bb, bbrow, D, psA, waits=[t_in] + tb1)
    t_bc = tb1 + tb2
    t_scm = p.emit("tensor", lambda e: e.matmul(
        out=psA[:, 0:1], lhsT=c["ones_f"][:, :], rhs=sclrow[:, 0:1],
        start=True, stop=True), waits=[t_in] + t_bc)
    t_scc = p.emit("vector", lambda e: e.tensor_copy(sccol[:], psA[:, 0:1]),
                   waits=[t_scm])
    t_msq = p.emit("scalar", lambda e: e.activation(
        out=musq[:], in_=mustage[:], func=AF.Square), waits=[t_in])
    t_s2m = None
    for j in range(NJ):
        t_s2m = p.emit("tensor", lambda e, j=j: e.matmul(
            out=SL_ps[0:1, :], lhsT=onescol_f[:, 0:1],
            rhs=musq[:, j * ES:(j + 1) * ES],
            start=(j == 0), stop=(j == NJ - 1)),
            waits=[t_msq, t_oc, t_scm] if j == 0 else [])
    t_sr = p.emit("vector", lambda e: e.tensor_copy(srow[:], SL_ps[0:1, :]),
                  waits=[t_s2m])
    t_sq = p.emit("scalar", lambda e: e.activation(
        out=srow2[:], in_=srow[:], func=AF.Sqrt), waits=[t_sr])
    t_mx = p.emit("vector", lambda e: e.tensor_scalar_max(
        srow[:], srow2[:], 1e-12), waits=[t_sq])
    t_rc = p.emit("vector", lambda e: e.reciprocal(srow2[:], srow[:]),
                  waits=[t_mx])
    tbm = bcast_row_f32(p, c, rmub, srow2, ES, psA, waits=[t_rc])
    t_mn = []
    for j in range(NJ):
        t_mn.append(p.emit("vector", lambda e, j=j: e.tensor_mul(
            mun_h[:, j * ES:(j + 1) * ES], mustage[:, j * ES:(j + 1) * ES],
            rmub[:]), waits=tbm))
    p.pop_region()
    tps = [p.ps(f"tps{i}", [128, 1024], BF16) for i in range(NL)]

    t_prev_E = [None]
    pv = [dict(sq=None, xuT=None, P=None) for _ in range(NL)]

    def tile_body(t, L):
        v = pv[L]
        t_x1 = p.dma_rr("in_x", NL, L, x1_t[L][:],
                        x1_in[t * 128:(t + 1) * 128, :],
                        waits=[v["sq"]] if v["sq"] else [t_in])
        t_s1 = p.emit("vector", lambda e, L=L: e.reduce_sum(
            out=s1col[L][:, 0:1], in_=x1_t[L][:], axis=AX.X), waits=[t_x1])
        yield
        t_xn, t_sqa = ln_stats_apply(p, lnb[L], x1_t[L], s1col[L], xn[L], 1e-5,
                                     [t_x1, t_s1], None, "moe_ln")
        yield
        t_m1 = p.emit("vector", lambda e, L=L: e.tensor_mul(
            xn2a[L][:], xn[L][:], gb[:]), waits=[t_xn] + t_bc)
        t_xn2 = p.emit("vector", lambda e, L=L: e.tensor_add(
            xn2[L][:], xn2a[L][:], bb[:]), waits=[t_m1])
        yield
        t_sqb = p.emit("scalar", lambda e, L=L: e.activation(
            out=lnb[L]["sqw"][:], in_=xn2[L][:], func=AF.Square,
            accum_out=nrm[L][:, 0:1]), waits=[t_xn2, t_sqa])
        v["sq"] = t_sqb
        yield
        t_nsq = p.emit("scalar", lambda e, L=L: e.activation(
            out=nrm2[L][:, 0:1], in_=nrm[L][:, 0:1], func=AF.Sqrt),
            waits=[t_sqb])
        t_nmx = p.emit("vector", lambda e, L=L: e.tensor_scalar_max(
            nrm[L][:, 0:1], nrm2[L][:, 0:1], 1e-12), waits=[t_nsq])
        t_nrc = p.emit("vector", lambda e, L=L: e.reciprocal(
            nrm2[L][:, 0:1], nrm[L][:, 0:1]), waits=[t_nmx])
        t_rn = p.emit("vector", lambda e, L=L: e.tensor_mul(
            rn[L][:, 0:1], nrm2[L][:, 0:1], sccol[:, 0:1]),
            waits=[t_nrc, t_scc])
        yield
        t_xu = p.emit("vector", lambda e, L=L: e.tensor_scalar(
            out=xu_bf[L][:], in0=xn2[L][:], scalar1=rn[L][:, 0:1],
            scalar2=None, op0=ALU.mult), waits=[t_rn])
        t_xnh = p.emit("scalar", lambda e, L=L: e.activation(
            out=xnh[L][:], in_=xn2[L][:], func=AF.Copy),
            waits=[t_xn2] + ([v["P"]] if v["P"] else []))
        yield

        t_xuT = transpose_tile(p, c, tps[L], xu_bf[L], xuT_t[L][:], [t_xu],
                               [v["xuT"]] if v["xuT"] else [], nblk=NJ)
        v["xuT"] = t_xuT
        yield
        t_lg = None
        for j in range(NJ):
            t_lg = p.emit("tensor", lambda e, j=j, L=L: e.matmul(
                out=L_ps[:, :], lhsT=xuT_t[L][:, j * 128:(j + 1) * 128],
                rhs=mun_h[:, j * ES:(j + 1) * ES],
                start=(j == 0), stop=(j == NJ - 1)),
                waits=([t_xuT] + t_mn +
                       ([t_prev_E[0]] if t_prev_E[0] else [])) if j == 0 else [])
        t_E = p.emit("scalar", lambda e, L=L: e.activation(
            out=E_bf[L][:], in_=L_ps[:], func=AF.Exp,
            accum_out=zc_col[L][:, 0:1]),
            waits=[t_lg, p.stok(f"o_E{L}"), p.stok(f"o_Z{L}")])
        t_prev_E[0] = t_E
        p.dma_rr("o_E", NL, L, E_out[t], E_bf[L][:], waits=[t_E])
        p.dma_rr("o_Z", NL, L, Zc_out[t], zc_col[L][:, 0:1], waits=[t_E])
        yield
        t_P = None
        for jd in range(NJ):
            pps = Pps_a if jd < 4 else Pps_b
            col = (jd % 4) * ES
            t_P = p.emit("tensor", lambda e, jd=jd, pps=pps, col=col, t=t, L=L: e.matmul(
                out=pps[:, col:col + ES],
                lhsT=xnh[L][:, jd * 128:(jd + 1) * 128],
                rhs=E_bf[L][:],
                start=(t == 0 and jd % 2 == 0), stop=(t == NT - 1 and jd % 2 == 1)),
                waits=[t_E, t_xnh] if jd == 0 else [])
        v["P"] = t_P
        t_SL[0] = p.emit("tensor", lambda e, t=t, L=L: e.matmul(
            out=SL_ps[0:1, :], lhsT=c["onescol_bf"][:, 0:1], rhs=E_bf[L][:],
            start=(t == 0), stop=(t == NT - 1)),
            waits=[t_E] + ([t_sr] if t == 0 else []))

    t_SL = [None]
    for t0 in range(0, NT, NL):
        interleave_pairs([tile_body(t, t % NL)
                          for t in range(t0, min(t0 + NL, NT))])
    t_SL = t_SL[0]

    t_Pall = [pv[0]["P"], pv[1]["P"]]
    t_e1 = p.emit("vector", lambda e: e.tensor_copy(
        Psb[:, 0:1024], Pps_a[:]), waits=t_Pall)
    t_e2 = p.emit("vector", lambda e: e.tensor_copy(
        Psb[:, 1024:2048], Pps_b[:]), waits=t_Pall)
    t_e3 = p.emit("vector", lambda e: e.tensor_copy(
        slrow[:], SL_ps[0:1, :]), waits=[t_SL])
    p.dma("o_P", Praw_out[:], Psb[:], waits=[t_e1, t_e2])
    p.dma("o_S", Sloc_out[:], slrow[:], waits=[t_e3])


def build_C():
    p = Prog("C")
    c = common_consts(p)
    io = dict(x1=p.din("x1", [T, D], F32),
              gmr=p.din("gmr", [1, D], F32),
              bbr=p.din("bbr", [1, D], F32),
              mur=p.din("mur", [D, ES], F32),
              scl=p.din("scl", [1, 1], F32),
              Praw=p.dout("Praw", [128, NJ * ES], F32),
              Sloc=p.dout("Sloc", [1, ES], F32),
              E=p.dout("E", [NT, 128, ES], BF16),
              Zc=p.dout("Zc", [NT, 128, 1], F32))
    emit_C(p, c, io)
    return p.finalize()



def emit_D(p, c, io, fused=False, pr_wait=None, edges_out=None):
    x1_in = io["x1"]
    E_in = io["E"]
    Zc_in = io["Zc"]
    Praw_in = io["Praw"]
    Sloc_in = io["Sloc"]
    w1_in = io["w1e"]
    b1c_in = io["b1c"]
    w2_in = io["w2e"]
    x2_out = io["x2"]
    scratch = io["so_scratch"]

    Psb = p.sb("Psb", [128, NJ * ES], F32)
    Srow = p.sb("Srow", [1, ES], F32)
    rSrow = p.sb("rSrow", [1, ES], F32)
    rSb = p.sb("rSb", [128, ES], F32)
    slotin_h = p.sb("slotin_h", [128, NJ * ES], BF16)
    wstage = [p.sb(f"wstage{i}", [128, 2048], F32) for i in range(4)]
    whbuf = p.sb("whbuf", [128, EXP * 2048], BF16)
    b1cols = p.sb("b1cols", [128, EXP * 2], F32)
    b2stage = [p.sb(f"b2stage{i}", [1, D], F32) for i in range(2)]
    b2cur = [p.sb(f"b2cur{i}", [1, D], BF16) for i in range(2)]
    HT_sb = p.sb("HT_sb", [128, EXP * 32], BF16)
    so_stage = [p.sb(f"so_stage{i}", [16, D], BF16) for i in range(2)]
    soT_sb = p.sb("soT_sb", [128, 2 * D], BF16)
    NL = 2
    E_t = [p.sb(f"E_t{i}", [128, ES], BF16) for i in range(NL)]
    zc_t = [p.sb(f"zc_t{i}", [128, 1], F32) for i in range(NL)]
    rZ = [p.sb(f"rZ{i}", [128, 1], F32) for i in range(NL)]
    comb = [p.sb(f"comb{i}", [128, ES], BF16) for i in range(NL)]
    combT = [p.sb(f"combT{i}", [128, ES], BF16) for i in range(NL)]
    x1_t = [p.sb(f"x1_t{i}", [128, D], F32) for i in range(NL)]
    u = [p.sb(f"u{i}", [128, D], F32) for i in range(NL)]
    s1col = [p.sb(f"s1col{i}", [128, 1], F32) for i in range(NL)]
    lnb = [dict(sqw=p.sb(f"sqw{i}", [128, D], F32),
                mcol=p.sb(f"mcol{i}", [128, 1], F32),
                vcol=p.sb(f"vcol{i}", [128, 1], F32),
                tcol=p.sb(f"tcol{i}", [128, 1], F32)) for i in range(NL)]
    x2_t = [p.sb(f"x2_t{i}", [128, D], F32) for i in range(NL)]

    p.push_region("pp")
    psA = p.ps("psA", [128, 512], F32)
    HTps = [p.ps(f"HTps{i}", [128, 32], F32) for i in range(2)]
    SOps = [p.ps(f"SOps{i}", [16, D], F32) for i in range(2)]

    if fused:
        tprev = p.dma("in_P", Psb[:], Praw_in[:], waits=[pr_wait],
                      queue="gpsimd")
        tprevS = p.dma("in_S", Srow[:], Sloc_in[:], waits=[pr_wait],
                       queue="gpsimd")
    else:
        tprev = None
        for cc in range(NCORES):
            tprev = p.dma("in_P", Psb[:], Praw_in[cc],
                          waits=[tprev] if tprev else c["ready"], queue="gpsimd",
                          accum_op=(ALU.add if cc else ALU.bypass))
        tprevS = None
        for cc in range(NCORES):
            tprevS = p.dma("in_S", Srow[:], Sloc_in[cc],
                           waits=[tprevS] if tprevS else c["ready"], queue="gpsimd",
                           accum_op=(ALU.add if cc else ALU.bypass))
    t_rS = p.emit("vector", lambda e: e.reciprocal(rSrow[:], Srow[:]),
                  waits=[tprevS])
    tbs = bcast_row_f32(p, c, rSb, rSrow, ES, psA, waits=[t_rS])
    t_si = []
    for jd in range(NJ):
        t_si.append(p.emit("vector", lambda e, jd=jd: e.tensor_mul(
            slotin_h[:, jd * ES:(jd + 1) * ES], Psb[:, jd * ES:(jd + 1) * ES],
            rSb[:]), waits=tbs + [tprev]))

    t_b1 = p.dma("in", b1cols[:], b1c_in[:], waits=c["ready"])
    w1ready = [t_b1]
    wcast = {}
    for ee in range(EXP):
        tw = p.dma_rr("in_w", 4, ee,
                      wstage[ee % 4][:].rearrange("p (k h) -> p k h", h=MOE_H),
                      w1_in[ee].rearrange("(k p) h -> p k h", p=128),
                      waits=wcast.get(ee - 4, [t_b1]))
        cs = [p.emit("scalar", lambda e, ee=ee: e.activation(
            out=whbuf[:, ee * 2048:(ee + 1) * 2048], in_=wstage[ee % 4][:],
            func=AF.Copy), waits=[tw])]
        wcast[ee] = cs
        w1ready += cs

    t_H_ev = []
    hevs = {}
    t_mm = None
    for ee in range(EXP):
        for kk in range(2):
            for jd in range(NJ):
                t_mm = p.emit("tensor", lambda e, ee=ee, kk=kk, jd=jd: e.matmul(
                    out=HTps[ee % 2][:, kk * 16:(kk + 1) * 16],
                    lhsT=whbuf[:, ee * 2048 + jd * MOE_H + kk * 128:
                               ee * 2048 + jd * MOE_H + (kk + 1) * 128],
                    rhs=slotin_h[:, jd * ES + ee * SLOT:jd * ES + (ee + 1) * SLOT],
                    start=(kk == 0 and jd == 0), stop=(kk == 1 and jd == NJ - 1)),
                    waits=(t_si + w1ready + hevs.get(ee - 2, [])) if kk == 0 and jd == 0 else [])
        evs = []
        for kk in range(2):
            evs.append(p.emit("scalar", lambda e, ee=ee, kk=kk: e.activation(
                out=HT_sb[:, ee * 32 + kk * 16:ee * 32 + (kk + 1) * 16],
                in_=HTps[ee % 2][:, kk * 16:(kk + 1) * 16], func=AF.Gelu,
                bias=b1cols[:, ee * 2 + kk:ee * 2 + kk + 1]), waits=[t_mm]))
        hevs[ee] = evs
        t_H_ev += evs
    t_w1_done = t_mm
    prev_Hev = hevs[EXP - 1] + hevs[EXP - 2]

    w2ready = []
    wcast2 = {}
    for ee in range(EXP):
        tw = p.dma_rr("in_w", 4, ee,
                      wstage[ee % 4][:].rearrange("p (k d) -> p k d", d=D),
                      w2_in[ee, 0:256, :].rearrange("(k p) d -> p k d", p=128),
                      waits=wcast2.get(ee - 4, [t_w1_done] + prev_Hev))
        cs = [p.emit("scalar", lambda e, ee=ee: e.activation(
            out=whbuf[:, ee * 2048:(ee + 1) * 2048], in_=wstage[ee % 4][:],
            func=AF.Copy), waits=[tw] + ([t_w1_done] if ee < 4 else []))]
        wcast2[ee] = cs
        w2ready += cs

    soev = {}
    b2cast = {}
    t_so_st = []
    for ee in range(EXP):
        t_bd = p.dma_rr("in_b2", 2, ee, b2stage[ee % 2][:],
                        w2_in[ee, 256:257, :],
                        waits=b2cast.get(ee - 2, [w2ready[0]]))
        t_bc2 = p.emit("scalar", lambda e, ee=ee: e.activation(
            out=b2cur[ee % 2][:], in_=b2stage[ee % 2][:], func=AF.Copy),
            waits=[t_bd] + soev.get(ee - 2, []))
        b2cast[ee] = [t_bc2]
        t_mm = None
        for n in range(2):
            for kk in range(2):
                t_mm = p.emit("tensor", lambda e, ee=ee, n=n, kk=kk: e.matmul(
                    out=SOps[ee % 2][:, n * 512:(n + 1) * 512],
                    lhsT=HT_sb[:, ee * 32 + kk * 16:ee * 32 + (kk + 1) * 16],
                    rhs=whbuf[:, ee * 2048 + kk * D + n * 512:
                              ee * 2048 + kk * D + (n + 1) * 512],
                    start=(kk == 0), stop=False),
                    waits=(w2ready + [t_bc2] +
                           soev.get(ee - 2, [])) if kk == 0 and n == 0 else [])
            t_mm = p.emit("tensor", lambda e, ee=ee, n=n: e.matmul(
                out=SOps[ee % 2][:, n * 512:(n + 1) * 512],
                lhsT=c["ones_bf"][0:1, 0:SLOT],
                rhs=b2cur[ee % 2][0:1, n * 512:(n + 1) * 512],
                start=False, stop=True), waits=[])
        t_ev = p.emit("vector", lambda e, ee=ee: e.tensor_copy(
            so_stage[ee % 2][0:16, :], SOps[ee % 2][:, :]),
            waits=[t_mm, p.stok(f"so1{ee % 2}")])
        soev[ee] = [t_ev]
        t_st = p.dma_rr("so1", 2, ee, scratch[ee * 16:(ee + 1) * 16, :],
                        so_stage[ee % 2][0:16, :], waits=[t_ev])
        t_so_st.append(t_st)
    t_bi = p.dma("so2", soT_sb[:].rearrange("p (r d) -> p r d", d=D),
                 scratch.rearrange("(r p) d -> p r d", p=128), waits=t_so_st)

    p.pop_region()
    Ops = [p.ps(f"Ops{i}", [128, 1024], F32) for i in range(NL)]
    tpsL = [p.ps(f"tpsL{i}", [128, 256], BF16) for i in range(NL)]

    pv = [dict(mm=None, u=None, lnsq=None, lnout=None, comb=None, combT=None)
          for _ in range(NL)]

    def tile_body(t, L):
        v = pv[L]
        t_E = p.dma_rr("in_E", NL, L, E_t[L][:], E_in[t],
                       waits=[v["comb"]] if v["comb"] else w2ready)
        t_Z = p.dma_rr("in_Z", NL, L, zc_t[L][:], Zc_in[t],
                       waits=[v["comb"]] if v["comb"] else w2ready)
        t_x1 = p.dma_rr("in_x", NL, L, x1_t[L][:],
                        x1_in[t * 128:(t + 1) * 128, :],
                        waits=[v["u"]] if v["u"] else w2ready)
        yield
        t_rz = p.emit("vector", lambda e, L=L: e.reciprocal(
            rZ[L][:], zc_t[L][:]), waits=[t_Z])
        t_cb = p.emit("vector", lambda e, L=L: e.tensor_scalar(
            out=comb[L][:], in0=E_t[L][:], scalar1=rZ[L][:, 0:1], scalar2=None,
            op0=ALU.mult), waits=[t_rz, t_E])
        v["comb"] = t_cb
        yield
        t_tp2 = None
        for kk in range(2):
            t_tp2 = p.emit("tensor", lambda e, kk=kk, L=L: e.transpose(
                out=tpsL[L][:, kk * 128:(kk + 1) * 128],
                in_=comb[L][:, kk * 128:(kk + 1) * 128],
                identity=c["ident"][:]),
                waits=([t_cb] + ([v["combT"]] if v["combT"] else [])) if kk == 0 else [])
        t_cT = p.emit("vector", lambda e, L=L: e.tensor_copy(
            combT[L][:], tpsL[L][:]),
            waits=[t_tp2])
        v["combT"] = t_cT
        yield
        t_mm = None
        for n in range(2):
            for kk in range(2):
                t_mm = p.emit("tensor", lambda e, n=n, kk=kk, L=L: e.matmul(
                    out=Ops[L][:, n * 512:(n + 1) * 512],
                    lhsT=combT[L][:, kk * 128:(kk + 1) * 128],
                    rhs=soT_sb[:, kk * D + n * 512:kk * D + (n + 1) * 512],
                    start=(kk == 0), stop=(kk == 1)),
                    waits=([t_cT, t_bi] + ([v["u"]] if v["u"] else [])) if kk == 0 and n == 0 else [])
        v["mm"] = t_mm
        yield
        t_u = p.emit("vector", lambda e, L=L: e.scalar_tensor_tensor(
            out=u[L][:], in0=Ops[L][:], scalar=1.0, in1=x1_t[L][:],
            op0=ALU.mult, op1=ALU.add, accum_out=s1col[L][:, 0:1]),
            waits=[t_mm, t_x1] + ([v["lnsq"]] if v["lnsq"] else []))
        v["u"] = t_u
        yield
        w_ln = [t_u] + ([v["lnout"]] if v["lnout"] else [])
        t_x2, t_sq1 = ln_stats_apply(p, lnb[L], u[L], s1col[L], x2_t[L], 1e-5,
                                     w_ln + [p.stok(f"o_x2{L}"),
                                             p.stok("o_edge")], None, "lnx2")
        v["lnsq"] = t_sq1
        v["lnout"] = t_x2
        t_st2 = p.dma_rr("o_x2", NL, L, x2_out[t * 128:(t + 1) * 128, :],
                         x2_t[L][:], waits=[t_x2])
        if edges_out is not None and t == 0:
            p.dma("o_edge", edges_out[0:HALO, :], x2_t[L][0:HALO, :],
                  waits=[t_x2])
        if edges_out is not None and t == NT - 1:
            p.dma("o_edge", edges_out[HALO:2 * HALO, :],
                  x2_t[L][128 - HALO:128, :], waits=[t_x2])

    for t0 in range(0, NT, NL):
        interleave_pairs([tile_body(t, t % NL)
                          for t in range(t0, min(t0 + NL, NT))])


def build_D():
    p = Prog("D")
    c = common_consts(p)
    io = dict(x1=p.din("x1", [T, D], F32),
              E=p.din("E", [NT, 128, ES], BF16),
              Zc=p.din("Zc", [NT, 128, 1], F32),
              Praw=p.din("Praw", [NCORES, 128, NJ * ES], F32),
              Sloc=p.din("Sloc", [NCORES, 1, ES], F32),
              w1e=p.din("w1e", [EXP, D, MOE_H], F32),
              b1c=p.din("b1c", [128, EXP * 2], F32),
              w2e=p.din("w2e", [EXP, MOE_H + 1, D], F32),
              x2=p.dout("x2", [T, D], F32),
              so_scratch=p.dtmp("so_scratch", [ES, D], BF16))
    emit_D(p, c, io)
    return p.finalize()



def emit_E(p, c, io, xe_wait=()):
    TE = 2176
    xe_in = io["xe"]
    mA_in = io["maskA"]
    mB_in = io["maskB"]
    wq_in = io["wqT"]
    wk_in = io["wkT"]
    wv_in = io["wvT"]
    wo_in = io["woT_a"]
    x3_out = io["x3"]

    wqh = p.sb("wqh", [128, NJ * D], BF16)
    wkvh = p.sb("wkvh", [128, NJ * D], BF16)
    wob_f = p.sb("wob_f", [1, D], F32)
    wob_h = p.sb("wob_h", [1, D], BF16)
    xeT = p.sb("xeT", [128, NJ * TE], BF16)
    kT = p.sb("kT", [128, NJ * TE], BF16)
    v_sb = p.sb("v_sb", [128, NTE * D], BF16)
    NL = 2
    qz = [[p.sb(f"qz{i}_{L}", [128, NJ * 128], BF16) for i in range(2)]
          for L in range(NL)]
    EA = p.sb("EA", [128, 512], BF16)
    EB = p.sb("EB", [64, 512], BF16)
    rz_sb = [p.sb(f"rz_sb{L}", [128, NH], F32) for L in range(NL)]
    mA = p.sb("mA", [128, 128], F32)
    mB = p.sb("mB", [64, 128], F32)
    rzexp0 = p.sb("rzexp", [128, D], F32)
    rzexp = [rzexp0, rzexp0]
    actx0 = p.sb("actx", [128, D], BF16)
    actx = [actx0, actx0]
    ctxT = [p.sb(f"ctxT{L}", [128, D], BF16) for L in range(NL)]
    xr_t0 = p.sb("xr_t", [128, D], F32)
    xr_t = [xr_t0, xr_t0]
    u = [p.sb(f"uE{L}", [128, D], F32) for L in range(NL)]
    s1col = [p.sb(f"s1col{L}", [128, 1], F32) for L in range(NL)]
    sqw0 = p.sb("sqwE", [128, D], F32)
    lnb = [dict(sqw=sqw0,
                mcol=p.sb(f"mcolE{L}", [128, 1], F32),
                vcol=p.sb(f"vcolE{L}", [128, 1], F32),
                tcol=p.sb(f"tcolE{L}", [128, 1], F32)) for L in range(NL)]
    x3_t0 = p.sb("x3_t", [128, D], F32)
    x3_t = [x3_t0, x3_t0]

    tps = p.ps("tps", [128, 1024], BF16)
    qps = p.ps("qps", [128, 1024], F32)

    eighth, t_ei = p.const_col(0.125)
    t_mA = p.dma("in", mA[:], mA_in[:], waits=c["ready"])
    t_mB = p.dma("in", mB[:], mB_in[:], waits=c["ready"])

    p.push_region("w12")
    wstage = p.sb("wstage", [128, D], F32)
    xf_t = [p.sb(f"xf_t{i}", [128, D], F32) for i in range(2)]
    xh_t = [p.sb(f"xh_t{i}", [128, D], BF16) for i in range(2)]
    tpsB = p.ps("tpsB", [128, 1024], BF16)
    tps1 = [tps, tpsB]
    e_cast = {}
    e_ev = {}
    for te in range(NTE):
        LL = te % 2
        t_x = p.dma_rr("in_x", 2, LL, xf_t[LL][:],
                       xe_in[te * 128:(te + 1) * 128, :],
                       waits=([e_cast[te - 2]] if te >= 2 else c["ready"]) +
                             list(xe_wait))
        t_cast = p.emit("scalar", lambda e, LL=LL: e.activation(
            out=xh_t[LL][:], in_=xf_t[LL][:], func=AF.Copy),
            waits=[t_x] + ([e_ev[te - 2]] if te >= 2 else []))
        e_cast[te] = t_cast
        xeT_view = xeT[:].rearrange("p (j t) -> p j t", t=TE)[:, :, te * 128:(te + 1) * 128]
        e_ev[te] = transpose_tile(p, c, tps1[LL], xh_t[LL], xeT_view, [t_cast],
                                  [e_ev[te - 2]] if te >= 2 else [], nblk=NJ)
    t_xeT = p.tok("vector")

    def load_w(name_in, dest, waits):
        toks = []
        prev = list(waits)
        for ch in range(NJ):
            tw = p.dma("in", wstage[:],
                       name_in[ch * 128:(ch + 1) * 128, :], waits=prev)
            cs = [p.emit("scalar", lambda e, ch=ch, dest=dest: e.activation(
                out=dest[:, ch * D:(ch + 1) * D],
                in_=wstage[:], func=AF.Copy),
                waits=[p.stok("in")])]
            prev = cs
            toks += cs
        return toks

    t_wk = load_w(wk_in, wkvh, c["ready"])
    t_kml = None
    kev = []
    nch = [(i, min(512, TE - 512 * i)) for i in range((TE + 511) // 512)]
    ib = 0
    for m in range(NJ):
        for (ci, nn) in nch:
            b = ib % 2
            ib += 1
            t_kml = None
            for kk in range(NJ):
                t_kml = p.emit("tensor", lambda e, m=m, ci=ci, nn=nn, kk=kk, b=b: e.matmul(
                    out=qps[:, b * 512:b * 512 + nn],
                    lhsT=wkvh[:, kk * D + m * 128:kk * D + (m + 1) * 128],
                    rhs=xeT[:, kk * TE + ci * 512:kk * TE + ci * 512 + nn],
                    start=(kk == 0), stop=(kk == NJ - 1)),
                    waits=([t_xeT] + t_wk + kev[-2:-1]) if kk == 0 else [])
            kev.append(p.emit("vector", lambda e, m=m, ci=ci, nn=nn, b=b: e.tensor_copy(
                kT[:, m * TE + ci * 512:m * TE + ci * 512 + nn],
                qps[:, b * 512:b * 512 + nn]),
                waits=[t_kml]))
    t_wv = load_w(wv_in, wkvh, kev)
    vev = []
    for te in range(NTE):
        for n in range(2):
            t_vml = None
            for kk in range(NJ):
                t_vml = p.emit("tensor", lambda e, te=te, n=n, kk=kk: e.matmul(
                    out=qps[:, n * 512:(n + 1) * 512],
                    lhsT=xeT[:, kk * TE + te * 128:kk * TE + (te + 1) * 128],
                    rhs=wkvh[:, kk * D + n * 512:kk * D + (n + 1) * 512],
                    start=(kk == 0), stop=(kk == NJ - 1)),
                    waits=(t_wv + vev[-2:-1]) if kk == 0 else [])
            vev.append(p.emit("vector", lambda e, te=te, n=n: e.tensor_copy(
                v_sb[:, te * D + n * 512:te * D + (n + 1) * 512],
                qps[:, n * 512:(n + 1) * 512]), waits=[t_vml]))
    t_wq = load_w(wq_in, wqh, vev)
    woh = wkvh
    t_wo = load_w(wo_in, woh, t_wq)
    t_wob = p.dma("in", wob_f[:], wo_in[D:D + 1, :], waits=t_wo)
    t_wo = t_wo + [p.emit("scalar", lambda e: e.activation(
        out=wob_h[:], in_=wob_f[:], func=AF.Copy), waits=[p.stok("in")])]
    t_z0 = p.emit("vector", lambda e: e.memset(qz[0][0][64:128, :], 0.0))
    t_z1 = p.emit("vector", lambda e: e.memset(qz[0][1][0:64, :], 0.0))
    t_z2 = p.emit("vector", lambda e: e.memset(qz[1][0][64:128, :], 0.0))
    t_z3 = p.emit("vector", lambda e: e.memset(qz[1][1][0:64, :], 0.0))
    t_z0 = [t_z0, t_z1, t_z2, t_z3]
    p.pop_region()
    SAm = p.sb("SAm", [128, 512], F32)
    SBm = p.sb("SBm", [64, 512], F32)
    SA = p.ps("SA", [128, 512], F32)
    SB = p.ps("SB", [64, 512], F32)
    Ups = p.ps("Ups", [128, 1024], F32)
    Zps = p.ps("Zps", [128, 16], F32)

    g = dict(q0=None, q1=None, ea=[], u=None, ctxT=None, rz=None)
    pv = [dict(Slast=None, lnsq=None, lnout=None, u=None) for _ in range(NL)]

    def tile_body(t, L):
        v = pv[L]
        t_qml = None
        for jm in range(NJ):
            for kk in range(NJ):
                t_qml = p.emit("tensor", lambda e, jm=jm, kk=kk, t=t: e.matmul(
                    out=qps[:, jm * 128:(jm + 1) * 128],
                    lhsT=wqh[:, kk * D + jm * 128:kk * D + jm * 128 + 128],
                    rhs=xeT[:, kk * TE + 32 + t * 128:kk * TE + 32 + (t + 1) * 128],
                    start=(jm % 4 == 0 and kk == 0),
                    stop=(jm % 4 == 3 and kk == NJ - 1)),
                    waits=(t_wq + [x for x in (g["q0"], g["q1"], g["u"]) if x]) if jm == 0 and kk == 0 else [])
        t_q0 = p.emit("vector", lambda e, L=L: e.tensor_copy(
            qz[L][0][0:64, :], qps[0:64, :]),
            waits=[t_qml] + t_z0 + ([v["Slast"]] if v["Slast"] else []))
        t_q1 = p.emit("vector", lambda e, L=L: e.tensor_copy(
            qz[L][1][64:128, :], qps[64:128, :]), waits=[t_qml])
        g["q0"], g["q1"] = t_q0, t_q1
        yield

        t_U_last = None
        t_Z_last = None
        ea_war = list(g["ea"])
        for gg in range(4):
            t_SA = None
            t_SB = None
            for i in range(4):
                h = 4 * gg + i
                jh = h // 2
                t_SA = p.emit("tensor", lambda e, h=h, jh=jh, i=i, t=t, L=L: e.matmul(
                    out=SA[:, i * 128:(i + 1) * 128],
                    lhsT=kT[:, jh * TE + t * 128:jh * TE + (t + 1) * 128],
                    rhs=qz[L][h % 2][:, jh * 128:(jh + 1) * 128],
                    start=(i == 0), stop=(i == 3)),
                    waits=([t_q0, t_q1] + kev + ea_war) if i == 0 else [])
            for i in range(4):
                h = 4 * gg + i
                jh = h // 2
                t_SB = p.emit("tensor", lambda e, h=h, jh=jh, i=i, t=t, L=L: e.matmul(
                    out=SB[0:64, i * 128:(i + 1) * 128],
                    lhsT=kT[:, jh * TE + (t + 1) * 128:jh * TE + (t + 1) * 128 + 64],
                    rhs=qz[L][h % 2][:, jh * 128:(jh + 1) * 128],
                    start=(i == 0), stop=(i == 3)),
                    waits=[] if i > 0 else ea_war)
            t_MA = p.emit("vector", lambda e: e.tensor_add(
                SAm[:], SA[:],
                mA[:].unsqueeze(1).broadcast_to([128, 4, 128])),
                waits=[t_SA, t_mA] + ea_war)
            t_MB = p.emit("vector", lambda e: e.tensor_add(
                SBm[0:64, :], SB[0:64, :],
                mB[:].unsqueeze(1).broadcast_to([64, 4, 128])),
                waits=[t_SB, t_mB])
            t_EA = p.emit("scalar", lambda e: e.activation(
                out=EA[:], in_=SAm[:], func=AF.Exp, scale=eighth[:, 0:1]),
                waits=[t_MA, t_ei])
            t_EB = p.emit("scalar", lambda e: e.activation(
                out=EB[0:64, :], in_=SBm[0:64, :], func=AF.Exp,
                scale=eighth[0:64, 0:1]), waits=[t_MB])
            ea_war = []
            for i in range(4):
                h = 4 * gg + i
                wU = []
                if i == 0:
                    wU = [t_EA, t_EB] + vev + ([g["actx"]] if g.get("actx") else [])
                t_U_last = p.emit("tensor", lambda e, h=h, i=i, t=t: e.matmul(
                    out=Ups[:, h * 64:(h + 1) * 64],
                    lhsT=EA[:, i * 128:(i + 1) * 128],
                    rhs=v_sb[:, t * D + h * 64:t * D + (h + 1) * 64],
                    start=(h % 8 == 0), stop=False), waits=wU)
                t_U_last = p.emit("tensor", lambda e, h=h, i=i, t=t: e.matmul(
                    out=Ups[:, h * 64:(h + 1) * 64],
                    lhsT=EB[0:64, i * 128:(i + 1) * 128],
                    rhs=v_sb[0:64, (t + 1) * D + h * 64:(t + 1) * D + (h + 1) * 64],
                    start=False, stop=(h % 8 == 7)), waits=[])
                t_Z_last = p.emit("tensor", lambda e, h=h, i=i, t=t: e.matmul(
                    out=Zps[:, h:h + 1],
                    lhsT=EA[:, i * 128:(i + 1) * 128],
                    rhs=c["onescol_bf"][:, 0:1],
                    start=(h == 0), stop=False),
                    waits=([g["rz"]] if h == 0 and g["rz"] else []))
                t_Z_last = p.emit("tensor", lambda e, h=h, i=i, t=t: e.matmul(
                    out=Zps[:, h:h + 1],
                    lhsT=EB[0:64, i * 128:(i + 1) * 128],
                    rhs=c["onescol_bf"][0:64, 0:1],
                    start=False, stop=(h == 15)), waits=[])
            ea_war = [t_U_last, t_Z_last]
        v["Slast"] = t_SB
        g["ea"] = ea_war
        t_rz = p.emit("vector", lambda e, L=L: e.reciprocal(
            rz_sb[L][:, 0:NH], Zps[:, 0:NH]), waits=[t_Z_last])
        g["rz"] = t_rz
        t_rze = p.emit("vector", lambda e, L=L: e.tensor_copy(
            rzexp[L][:].rearrange("p (h w) -> p h w", w=HD),
            rz_sb[L][:, 0:NH].unsqueeze(2).broadcast_to([128, NH, HD])),
            waits=[t_rz])
        t_ac = p.emit("vector", lambda e, L=L: e.scalar_tensor_tensor(
            out=actx[L][:], in0=Ups[:], scalar=1.0, in1=rzexp[L][:],
            op0=ALU.mult, op1=ALU.mult), waits=[t_U_last, t_rze])
        g["actx"] = t_ac
        yield
        t_cT = transpose_tile(p, c, tps, actx[L], ctxT[L][:], [t_ac],
                              [g["ctxT"]] if g["ctxT"] else [t_xeT],
                              nblk=NJ)
        g["ctxT"] = t_cT
        yield
        t_omm = None
        for n in range(2):
            for kk in range(NJ):
                t_omm = p.emit("tensor", lambda e, n=n, kk=kk, L=L: e.matmul(
                    out=qps[:, n * 512:(n + 1) * 512],
                    lhsT=ctxT[L][:, kk * 128:(kk + 1) * 128],
                    rhs=woh[:, kk * D + n * 512:kk * D + (n + 1) * 512],
                    start=(kk == 0), stop=False),
                    waits=([t_cT] + t_wo +
                           [x for x in (g["q0"], g["q1"], g["u"]) if x]) if kk == 0 and n == 0 else [])
            t_omm = p.emit("tensor", lambda e, n=n: e.matmul(
                out=qps[:, n * 512:(n + 1) * 512],
                lhsT=c["ones_bf"][0:1, 0:128],
                rhs=wob_h[:, n * 512:(n + 1) * 512],
                start=False, stop=True), waits=[])
        t_xr = p.dma_rr("in_r", NL, L, xr_t[L][:],
                        xe_in[32 + t * 128:32 + (t + 1) * 128, :],
                        waits=[g["u"]] if g["u"] else t_wo)
        t_u = p.emit("vector", lambda e, L=L: e.scalar_tensor_tensor(
            out=u[L][:], in0=qps[:], scalar=1.0, in1=xr_t[L][:],
            op0=ALU.mult, op1=ALU.add, accum_out=s1col[L][:, 0:1]),
            waits=[t_omm, t_xr] + [x for x in (g.get("lnsq"),) if x])
        v["u"] = t_u
        g["u"] = t_u
        yield
        w_ln = [t_u] + ([v["lnout"]] if v["lnout"] else [])
        t_x3, t_sq1 = ln_stats_apply(p, lnb[L], u[L], s1col[L], x3_t[L], 1e-5,
                                     w_ln + [p.stok("o_x3")], None, "lnx3")
        v["lnsq"] = t_sq1
        g["lnsq"] = t_sq1
        v["lnout"] = t_x3
        p.dma("o_x3", x3_out[t * 128:(t + 1) * 128, :], x3_t[L][:],
              waits=[t_x3])

    for t0 in range(0, NT, NL):
        interleave_pairs([tile_body(t, t % NL)
                          for t in range(t0, min(t0 + NL, NT))])


def build_E():
    p = Prog("E")
    c = common_consts(p)
    io = dict(xe=p.din("xe", [2176, D], F32),
              maskA=p.din("maskA", [128, 128], F32),
              maskB=p.din("maskB", [64, 128], F32),
              wqT=p.din("wqT", [D, D], F32),
              wkT=p.din("wkT", [D, D], F32),
              wvT=p.din("wvT", [D, D], F32),
              woT_a=p.din("woT_a", [D + 1, D], F32),
              x3=p.dout("x3", [T, D], F32))
    emit_E(p, c, io)
    return p.finalize()



def emit_F1(p, c, io):
    x3_in = io["x3"]
    w1n_in = io["w1n"]
    w1T_in = io["w1T"]
    b1c_in = io["b1c"]
    hT_out = io["hT"]

    wn_t = [p.sb(f"wn_t{i}", [128, D], F32) for i in range(2)]
    absw = p.sb("absw", [128, D], F32)
    alpha1 = p.sb("alpha1", [128, 32], F32)
    alpha1s = p.sb("alpha1s", [128, 32], F32)
    b1c = p.sb("b1c_sb", [128, 32], F32)
    wstage = [p.sb(f"wstage{i}", [128, FF], F32) for i in range(2)]
    sw1T = p.sb("sw1T", [128, NJ * FF], BF16)
    xf_t = p.sb("xf_t", [128, D], F32)
    xh_t = p.sb("xh_t", [128, D], BF16)
    x3T = p.sb("x3T", [128, NJ * T], BF16)
    hbig = [p.sb(f"hbig{i}", [128, 8 * 512], BF16) for i in range(2)]

    tps = p.ps("tps", [128, 1024], BF16)
    hps = [p.ps(f"hps{i}", [128, 512], F32) for i in range(2)]

    t_abs = {}
    for i in range(32):
        t_w = p.dma_rr("in_a", 2, i, wn_t[i % 2][:],
                       w1n_in[i * 128:(i + 1) * 128, :],
                       waits=([t_abs[i - 2]] if i >= 2 else c["ready"]),
                       queue="gpsimd")
        t_abs[i] = p.emit("scalar", lambda e, i=i: e.activation(
            out=absw[:], in_=wn_t[i % 2][:], func=AF.Abs,
            accum_out=alpha1[:, i:i + 1]), waits=[t_w])
    t_ab = t_abs[31]
    t_al = p.emit("vector", lambda e: e.tensor_scalar_mul(
        alpha1s[:], alpha1[:], 1.0 / D), waits=[t_ab])
    t_b1 = p.dma("in", b1c[:], b1c_in[:], waits=c["ready"])

    t_sg = []
    prevs = {}
    for ch in range(NJ):
        t_w = p.dma_rr("in_s", 2, ch, wstage[ch % 2][:],
                       w1T_in[ch * 128:(ch + 1) * 128, :],
                       waits=prevs.get(ch - 2, c["ready"][:1]))
        cs = p.emit("scalar", lambda e, ch=ch: e.activation(
            out=sw1T[:, ch * FF:(ch + 1) * FF], in_=wstage[ch % 2][:],
            func=AF.Sign), waits=[t_w])
        prevs[ch] = [cs]
        t_sg.append(cs)

    t_prev_cast = None
    t_prev_ev = None
    for t in range(NT):
        t_x = p.dma("in_x", xf_t[:], x3_in[t * 128:(t + 1) * 128, :],
                    waits=[t_prev_cast] if t_prev_cast else c["ready"])
        t_cast = p.emit("scalar", lambda e: e.activation(
            out=xh_t[:], in_=xf_t[:], func=AF.Copy),
            waits=[t_x] + ([t_prev_ev] if t_prev_ev else []))
        t_prev_cast = t_cast
        xT_view = x3T[:].rearrange("p (j t) -> p j t", t=T)[:, :, t * 128:(t + 1) * 128]
        t_prev_ev = transpose_tile(p, c, tps, xh_t, xT_view, [t_cast],
                                   [t_prev_ev] if t_prev_ev else [], nblk=NJ)
    t_xT = t_prev_ev

    prev_ev = {}
    prev_st = {}
    i = 0
    for cch in range(4):
        for m in range(32):
            par = i % 2
            grp = (i // 8) % 2
            t_mm = None
            for kk in range(NJ):
                t_mm = p.emit("tensor", lambda e, cch=cch, m=m, kk=kk, par=par: e.matmul(
                    out=hps[par][:],
                    lhsT=sw1T[:, kk * FF + m * 128:kk * FF + (m + 1) * 128],
                    rhs=x3T[:, kk * T + cch * 512:kk * T + (cch + 1) * 512],
                    start=(kk == 0), stop=(kk == NJ - 1)),
                    waits=(t_sg + [t_xT] + list(prev_ev.get(par, []))) if kk == 0 else [])
            t_ev = p.emit("scalar", lambda e, m=m, par=par, grp=grp: e.activation(
                out=hbig[grp][:, (m % 8) * 512:(m % 8 + 1) * 512],
                in_=hps[par][:], func=AF.Gelu,
                scale=alpha1s[:, m:m + 1], bias=b1c[:, m:m + 1]),
                waits=[t_mm, t_al, t_b1] +
                      (list(prev_st.get(grp, [])) if m % 8 < 2 else []))
            prev_ev[par] = [t_ev]
            if m % 8 == 7:
                t_st = p.dma_rr("o_h", 2, grp,
                                hT_out[m - 7:m + 1, :, cch * 512:(cch + 1) * 512]
                                .rearrange("m p tt -> p m tt"),
                                hbig[grp][:].rearrange("p (m tt) -> p m tt", tt=512),
                                waits=[t_ev])
                prev_st[grp] = [t_st]
            i += 1


def build_F1():
    p = Prog("F1")
    c = common_consts(p)
    io = dict(x3=p.din("x3", [T, D], F32),
              w1n=p.din("w1n", [FF, D], F32),
              w1T=p.din("w1T", [D, FF], F32),
              b1c=p.din("b1c", [128, 32], F32),
              hT=p.dout("hT", [32, 128, T], BF16))
    emit_F1(p, c, io)
    return p.finalize()



def emit_F2(p, c, io):
    x3_in = io["x3"]
    hT_in = io["hT"]
    w2n_in = io["w2n"]
    w2T_in = io["w2T"]
    b2c_in = io["b2c"]
    y_out = io["y"]

    wn_t = [p.sb("wn_t0", [128, FF], F32)] * 2
    absw = p.sb("absw", [128, FF], F32)
    alpha2 = p.sb("alpha2", [128, NJ], F32)
    alpha2s = p.sb("alpha2s", [128, NJ], F32)
    b2c = p.sb("b2c_sb", [128, NJ], F32)
    wstage = [p.sb(f"wstage{i}", [128, D], F32) for i in range(2)]
    sw2T = p.sb("sw2T", [128, 32 * D], BF16)
    hT_c = [p.sb(f"hT_c{i}", [128, 32 * 512], BF16) for i in range(2)]
    preT = p.sb("preT", [128, NJ * 512], BF16)
    x3_t = [p.sb(f"x3_t{i}", [128, D], F32) for i in range(2)]
    u = [p.sb(f"u{i}", [128, D], F32) for i in range(2)]
    s1col = [p.sb(f"s1col{i}", [128, 1], F32) for i in range(2)]
    sqw0 = p.sb("sqw0", [128, D], F32)
    lnb = [dict(sqw=sqw0,
                mcol=p.sb(f"mcol{i}", [128, 1], F32),
                vcol=p.sb(f"vcol{i}", [128, 1], F32),
                tcol=p.sb(f"tcol{i}", [128, 1], F32)) for i in range(2)]
    y_t = [wn_t[0][:, 0:D], wn_t[0][:, D:2 * D]]

    tps = [p.ps(f"tps{i}", [128, 1024], BF16) for i in range(2)]
    o2ps = [p.ps(f"o2ps{i}", [128, 512], F32) for i in range(2)]

    t_abs = {}
    for i in range(NJ):
        t_w = p.dma_rr("in_a", 2, i, wn_t[i % 2][:],
                       w2n_in[i * 128:(i + 1) * 128, :],
                       waits=([t_abs[i - 1]] if i >= 1 else c["ready"]),
                       queue="gpsimd")
        t_abs[i] = p.emit("scalar", lambda e, i=i: e.activation(
            out=absw[:], in_=wn_t[i % 2][:], func=AF.Abs,
            accum_out=alpha2[:, i:i + 1]), waits=[t_w])
    t_ab = t_abs[NJ - 1]
    t_al = p.emit("vector", lambda e: e.tensor_scalar_mul(
        alpha2s[:], alpha2[:], 1.0 / FF), waits=[t_ab])
    t_b2 = p.dma("in", b2c[:], b2c_in[:], waits=c["ready"])

    t_sg = []
    prevs = {}
    for ch in range(32):
        t_w = p.dma_rr("in_s", 2, ch, wstage[ch % 2][:],
                       w2T_in[ch * 128:(ch + 1) * 128, :],
                       waits=prevs.get(ch - 2, c["ready"][:1]))
        cs = p.emit("scalar", lambda e, ch=ch: e.activation(
            out=sw2T[:, ch * D:(ch + 1) * D], in_=wstage[ch % 2][:],
            func=AF.Sign), waits=[t_w])
        prevs[ch] = [cs]
        t_sg.append(cs)

    t_prev_mm = None
    t_prev_pre = {}
    pvF = [dict(u=None, lnsq=None, lnout=None) for _ in range(2)]
    hload = {}
    hl_war = {}
    for cch in range(4):
        hload[cch] = p.dma_rr("in_h", 2, cch,
                    hT_c[cch % 2][:].rearrange("p (k tt) -> p k tt", tt=512),
                    hT_in[:, :, cch * 512:(cch + 1) * 512].rearrange(
                        "k p tt -> p k tt"),
                    waits=[hl_war[cch - 2]] if cch >= 2 else t_sg)
        t_h = hload[cch]
        for m in range(NJ):
            par = m % 2
            t_mm = None
            for kk in range(32):
                t_mm = p.emit("tensor", lambda e, m=m, kk=kk, par=par, cch=cch: e.matmul(
                    out=o2ps[par][:],
                    lhsT=sw2T[:, kk * D + m * 128:kk * D + (m + 1) * 128],
                    rhs=hT_c[cch % 2][:, kk * 512:(kk + 1) * 512],
                    start=(kk == 0), stop=(kk == 31)),
                    waits=([t_h] + t_sg + list(t_prev_pre.get(par, []))) if kk == 0 else [])
            t_pre = p.emit("scalar", lambda e, m=m, par=par: e.activation(
                out=preT[:, m * 512:(m + 1) * 512], in_=o2ps[par][:],
                func=AF.Identity, scale=alpha2s[:, m:m + 1],
                bias=b2c[:, m:m + 1]), waits=[t_mm, t_al, t_b2])
            t_prev_pre[par] = [t_pre]
            t_prev_mm = t_mm
        hl_war[cch] = t_prev_mm
        for tt in range(4):
            t = cch * 4 + tt
            L = t % 2
            v = pvF[L]
            t_x = p.dma_rr("in_x", 2, L, x3_t[L][:],
                           x3_in[t * 128:(t + 1) * 128, :],
                           waits=[v["u"]] if v["u"] else t_sg)
            t_tp = None
            for m in range(NJ):
                t_tp = p.emit("tensor", lambda e, m=m, tt=tt, L=L: e.transpose(
                    out=tps[L][:, m * 128:(m + 1) * 128],
                    in_=preT[:, m * 512 + tt * 128:m * 512 + (tt + 1) * 128],
                    identity=c["ident"][:]),
                    waits=(list(t_prev_pre[0]) + list(t_prev_pre[1]) +
                           ([v["u"]] if v["u"] else [])) if m == 0 else [])
            t_u = p.emit("vector", lambda e, L=L: e.scalar_tensor_tensor(
                out=u[L][:], in0=tps[L][:, 0:1024], scalar=1.0, in1=x3_t[L][:],
                op0=ALU.mult, op1=ALU.add, accum_out=s1col[L][:, 0:1]),
                waits=[t_tp, t_x] + ([v["lnsq"]] if v["lnsq"] else []))
            v["u"] = t_u
            w_ln = [t_u] + ([v["lnout"]] if v["lnout"] else [t_ab])
            t_y, t_sq1 = ln_stats_apply(p, lnb[L], u[L], s1col[L], y_t[L],
                                        1e-5, w_ln + [p.stok(f"o_y{L}")],
                                        None, "lny")
            v["lnsq"] = t_sq1
            v["lnout"] = t_y
            p.dma_rr("o_y", 2, L, y_out[t * 128:(t + 1) * 128, :], y_t[L][:],
                     waits=[t_y])


def build_F2():
    p = Prog("F2")
    c = common_consts(p)
    io = dict(x3=p.din("x3", [T, D], F32),
              hT=p.din("hT", [32, 128, T], BF16),
              w2n=p.din("w2n", [D, FF], F32),
              w2T=p.din("w2T", [FF, D], F32),
              b2c=p.din("b2c", [128, NJ], F32),
              y=p.dout("y", [T, D], F32))
    emit_F2(p, c, io)
    return p.finalize()



def build_fused(upto=None):
    p = Prog("FUSED")
    c = common_consts(p)
    for v in (1e-12, 1e-5, 0.125):
        p.const_col(v)
    G8 = [list(range(NCORES))]

    x = p.din("x", [T, D], F32)
    wq_a = p.din("wq_a", [D + 1, D], F32)
    wk_a = p.din("wk_a", [D + 1, D], F32)
    wv_a = p.din("wv_a", [D + 1, D], F32)
    wo_a = p.din("wo_a", [D + 1, D], F32)
    g1r = p.din("g1r", [1, D], F32)
    be1r = p.din("be1r", [1, D], F32)
    w1_a = p.din("w1_a", [D + 1, FF], F32)
    b1cB = p.din("b1cB", [128, 32], F32)
    w2_a = p.din("w2_a", [FF + 1, D], F32)
    g2r = p.din("g2r", [1, D], F32)
    be2r = p.din("be2r", [1, D], F32)
    gmr = p.din("gmr", [1, D], F32)
    bbr = p.din("bbr", [1, D], F32)
    mur = p.din("mur", [D, ES], F32)
    scl = p.din("scl", [1, 1], F32)
    w1e = p.din("w1e", [EXP, D, MOE_H], F32)
    b1c = p.din("b1c", [128, EXP * 2], F32)
    w2e = p.din("w2e", [EXP, MOE_H + 1, D], F32)
    wqT = p.din("wqT", [D, D], F32)
    wkT = p.din("wkT", [D, D], F32)
    wvT = p.din("wvT", [D, D], F32)
    woT_a = p.din("woT_a", [D + 1, D], F32)
    maskA = p.din("maskA", [128, 128], F32)
    maskB = p.din("maskB", [64, 128], F32)
    selL = p.din("selL", [512, 32], F32)
    selR = p.din("selR", [512, 32], F32)
    w1n = p.din("w1n", [FF, D], F32)
    w1T = p.din("w1T", [D, FF], F32)
    b1cf = p.din("b1cf", [128, 32], F32)
    w2n = p.din("w2n", [D, FF], F32)
    w2T = p.din("w2T", [FF, D], F32)
    b2cf = p.din("b2cf", [128, NJ], F32)
    y = p.dout("y", [T, D], F32)

    kvp = p.dtmp("s_kvp", [64, 1024], F32)
    kvs = p.dtmp("s_kvs", [64, 1024], F32)
    attn_out = p.dtmp("s_ao", [T, D], F32)
    atT = p.dtmp("s_atT", [NJ, 128, T], BF16)
    hT0 = p.dtmp("s_hT0", [32, 128, T], BF16)
    x1 = p.dtmp("s_x1", [T, D], F32)
    PrawP = p.dtmp("s_PrawP", [128, NJ * ES], F32)
    PrawS = p.dtmp("s_PrawS", [128, NJ * ES], F32)
    SlocP = p.dtmp("s_SlocP", [1, ES], F32)
    SlocS = p.dtmp("s_SlocS", [1, ES], F32)
    Emat = p.dtmp("s_E", [NT, 128, ES], BF16)
    Zc = p.dtmp("s_Zc", [NT, 128, 1], F32)
    xe = p.dtmp("s_xe", [2176, D], F32)
    edges = p.dtmp("s_edges", [2 * HALO, D], F32)
    gath = p.dtmp("s_gath", [NCORES * 2 * HALO, D], F32)
    x3 = p.dtmp("s_x3", [T, D], F32)
    hT1 = p.dtmp("s_hT1", [32, 128, T], BF16)
    soscr = p.dtmp("s_soscr", [ES, D], BF16)

    p.scnt.setdefault("cc", 0)

    def collective(kind, op, ins, outs, waits):
        return p.emit("gpsimd", lambda e: e.collective_compute(
            kind, op, replica_groups=G8, ins=[a.opt() for a in ins],
            outs=[a.opt() for a in outs]), waits=waits, inc=1, sem="cc")

    p.push_region("AB")
    eqT_res = p.sb("eqT_res", [128, NJ * T], BF16)
    rq_res = p.sb("rq_res", [128, NT * NH], F32)
    p.push_region("A")
    emit_A(p, c, dict(x=x, wq_a=wq_a, wk_a=wk_a, wv_a=wv_a, kvp=kvp,
                      eqT_res=eqT_res, rq_res=rq_res))
    t_cc_kv = collective("AllReduce", ALU.add, [kvp], [kvs],
                         [p.stok("o_kv")])
    p.pop_region()
    if upto == "A":
        return p.finalize()
    p.barrier()

    p.push_region("B1")
    emit_B1(p, c, dict(x=x, eqT_res=eqT_res, rq_res=rq_res, kvp=kvs,
                       g1r=g1r, be1r=be1r,
                       wo_a=wo_a, attn_out=attn_out, atT=atT), fused=True,
            kv_wait=t_cc_kv)
    p.pop_region()
    p.pop_region()
    if upto == "B1":
        return p.finalize()
    p.barrier()

    p.push_region("B2a")
    emit_B2a(p, c, dict(atT=atT, w1_a=w1_a, b1cB=b1cB, hT=hT0))
    p.pop_region()
    if upto == "B2a":
        return p.finalize()
    p.barrier()

    p.push_region("B2b")
    emit_B2b(p, c, dict(hT=hT0, attn_out=attn_out, x=x, w2_a=w2_a, g2r=g2r,
                        be2r=be2r, x1=x1))
    p.pop_region()
    if upto == "B2b":
        return p.finalize()
    p.barrier()

    p.push_region("C")
    emit_C(p, c, dict(x1=x1, gmr=gmr, bbr=bbr, mur=mur, scl=scl, Praw=PrawP,
                      Sloc=SlocP, E=Emat, Zc=Zc))
    collective("AllReduce", ALU.add, [PrawP], [PrawS], [p.stok("o_P")])
    t_cc_S = collective("AllReduce", ALU.add, [SlocP], [SlocS],
                        [p.stok("o_S")])
    p.pop_region()
    if upto == "C":
        return p.finalize()
    p.barrier()

    p.push_region("D")
    emit_D(p, c, dict(x1=x1, E=Emat, Zc=Zc, Praw=PrawS, Sloc=SlocS, w1e=w1e,
                      b1c=b1c, w2e=w2e, x2=xe[HALO:HALO + T, :],
                      so_scratch=soscr),
           fused=True, pr_wait=t_cc_S, edges_out=edges)
    t_cc_G = collective("AllGather", ALU.bypass, [edges], [gath],
                        [p.stok("o_edge")])
    p.pop_region()
    if upto == "D":
        return p.finalize()
    p.barrier()

    p.push_region("H")
    ztile = p.sb("ztile", [64, D], F32)
    gsb = p.sb("gsb", [128, 4 * D], F32)
    slsb = p.sb("slsb", [128, 4 * 32], F32)
    srsb = p.sb("srsb", [128, 4 * 32], F32)
    hsb = p.sb("hsb", [32, D], F32)
    hps = p.ps("hps", [32, 1024], F32)
    t_z = p.emit("vector", lambda e: e.memset(ztile[:], 0.0))
    if upto == "H0":
        return p.finalize()
    p.dma("o_xe", xe[2112:2176, :], ztile[:], waits=[t_z])
    if upto == "H1":
        return p.finalize()
    t_g = p.dma("in_g", gsb[:].rearrange("p (k d) -> p k d", d=D),
                gath[:].rearrange("(k p) d -> p k d", p=128),
                waits=[t_cc_G])
    t_sl = p.dma("in_g", slsb[:].rearrange("p (k m) -> p k m", m=32),
                 selL[:].rearrange("(k p) m -> p k m", p=128))
    t_sr = p.dma("in_g", srsb[:].rearrange("p (k m) -> p k m", m=32),
                 selR[:].rearrange("(k p) m -> p k m", p=128))
    t_gin = p.stok("in_g")
    if upto == "H2":
        return p.finalize()
    t_he = None
    for side, ssb, rows in (("L", slsb, slice(0, 32)), ("R", srsb, slice(2080, 2112))):
        t_hm = None
        for n in range(2):
            for kk in range(4):
                t_hm = p.emit("tensor", lambda e, n=n, kk=kk, ssb=ssb: e.matmul(
                    out=hps[0:32, n * 512:(n + 1) * 512],
                    lhsT=ssb[:, kk * 32:(kk + 1) * 32],
                    rhs=gsb[:, kk * D + n * 512:kk * D + (n + 1) * 512],
                    start=(kk == 0), stop=(kk == 3)),
                    waits=([t_gin] + ([t_he] if t_he else [])) if n == 0 and kk == 0 else [])
        t_he = p.emit("vector", lambda e: e.tensor_copy(hsb[0:32, :],
                                                        hps[0:32, :]),
                      waits=[t_hm, p.stok("o_xe")])
        t_hdma = p.dma("o_xe", xe[rows, :], hsb[0:32, :], waits=[t_he])
        if upto == "H3" and side == "L":
            return p.finalize()
    if upto == "H4":
        return p.finalize()
    p.pop_region()
    if upto == "H":
        return p.finalize()
    p.barrier()

    p.push_region("E")
    emit_E(p, c, dict(xe=xe, maskA=maskA, maskB=maskB, wqT=wqT, wkT=wkT,
                      wvT=wvT, woT_a=woT_a, x3=x3))
    p.pop_region()
    if upto == "E":
        return p.finalize()
    p.barrier()

    p.push_region("F1")
    emit_F1(p, c, dict(x3=x3, w1n=w1n, w1T=w1T, b1c=b1cf, hT=hT1))
    p.pop_region()
    if upto == "F1":
        return p.finalize()
    p.barrier()

    p.push_region("F2")
    emit_F2(p, c, dict(x3=x3, hT=hT1, w2n=w2n, w2T=w2T, b2c=b2cf, y=y))
    p.pop_region()

    return p.finalize()



def _aug(w, b):
    return np.ascontiguousarray(
        np.concatenate([w, b.reshape(1, -1)], axis=0), dtype=np.float32)


def _row(v):
    return np.ascontiguousarray(v.reshape(1, -1), dtype=np.float32)


IDENT = np.eye(128, dtype=BF16NP)


_NC_CACHE = {}


def _get_nc(name, builder):
    if name not in _NC_CACHE:
        _NC_CACHE[name] = builder()
    return _NC_CACHE[name]


def _run(name, builder, per_core, debug=None):
    nc = _get_nc(name, builder)
    res = run_bass_kernel_spmd(nc, per_core, core_ids=list(range(NCORES)))
    if debug is not None:
        debug[name] = res
    return res.results


def host_layouts(inp):
    x = np.ascontiguousarray(inp["L0_attn_x"].reshape(16384, 1024),
                             dtype=np.float32)
    com = dict(
        wq_a=_aug(inp["L0_attn_Wq"], inp["L0_attn_bq"]),
        wk_a=_aug(inp["L0_attn_Wk"], inp["L0_attn_bk"]),
        wv_a=_aug(inp["L0_attn_Wv"], inp["L0_attn_bv"]),
        wo_a=_aug(inp["L0_attn_Wo"], inp["L0_attn_bo"]),
        g1r=_row(inp["L0_attn_g1"]), be1r=_row(inp["L0_attn_be1"]),
        w1_a=_aug(inp["L0_attn_W1"], inp["L0_attn_b1"]),
        b1cB=np.ascontiguousarray(
            inp["L0_attn_b1"].reshape(32, 128).T, dtype=np.float32),
        w2_a=_aug(inp["L0_attn_W2"], inp["L0_attn_b2"]),
        g2r=_row(inp["L0_attn_g2"]), be2r=_row(inp["L0_attn_be2"]),
        gmr=_row(inp["L0_ffn_ln_gamma"]), bbr=_row(inp["L0_ffn_ln_beta"]),
        mur=np.ascontiguousarray(inp["L0_ffn_mu"].reshape(D, ES),
                                 dtype=np.float32),
        scl=np.ascontiguousarray(inp["L0_ffn_scale"].reshape(1, 1),
                                 dtype=np.float32),
        w1e=np.ascontiguousarray(inp["L0_ffn_w1"], dtype=np.float32),
        b1c=np.ascontiguousarray(
            inp["L0_ffn_b1"].reshape(EXP, 2, 128).transpose(2, 0, 1)
            .reshape(128, EXP * 2), dtype=np.float32),
        w2e=np.ascontiguousarray(
            np.concatenate([inp["L0_ffn_w2"],
                            inp["L0_ffn_b2"][:, None, :]], axis=1),
            dtype=np.float32),
        wqT=np.ascontiguousarray(inp["L1_attn_Wq"].T, dtype=np.float32),
        wkT=np.ascontiguousarray(inp["L1_attn_Wk"].T, dtype=np.float32),
        wvT=np.ascontiguousarray(inp["L1_attn_Wv"].T, dtype=np.float32),
        woT_a=_aug(np.ascontiguousarray(inp["L1_attn_Wo"].T),
                   inp["L1_attn_bo"]),
        w1n=np.ascontiguousarray(inp["L1_ffn_w1"], dtype=np.float32),
        w1T=np.ascontiguousarray(inp["L1_ffn_w1"].T, dtype=np.float32),
        b1cf=np.ascontiguousarray(inp["L1_ffn_b1"].reshape(32, 128).T,
                                  dtype=np.float32),
        w2n=np.ascontiguousarray(inp["L1_ffn_w2"], dtype=np.float32),
        w2T=np.ascontiguousarray(inp["L1_ffn_w2"].T, dtype=np.float32),
        b2cf=np.ascontiguousarray(inp["L1_ffn_b2"].reshape(NJ, 128).T,
                                  dtype=np.float32),
        ident=IDENT,
    )
    pp = np.arange(128)[:, None]
    rr = np.arange(128)[None, :]
    com["maskA"] = np.where((rr >= pp - 63) & (rr <= pp), 0.0,
                            -1e4).astype(np.float32)
    pb = np.arange(64)[:, None]
    com["maskB"] = np.where(rr >= pb + 65, 0.0, -1e4).astype(np.float32)
    maps = []
    for ci in range(NCORES):
        m = dict(com)
        m["x"] = np.ascontiguousarray(x[ci * T:(ci + 1) * T])
        sL = np.zeros((512, 32), np.float32)
        sR = np.zeros((512, 32), np.float32)
        if ci > 0:
            sL[np.arange(32) + 64 * (ci - 1) + 32, np.arange(32)] = 1.0
        if ci < NCORES - 1:
            sR[np.arange(32) + 64 * (ci + 1), np.arange(32)] = 1.0
        m["selL"] = sL
        m["selR"] = sR
        maps.append(m)
    return maps


def kernel_fused(**inp):
    inp = {k: np.asarray(v) for k, v in inp.items()}
    maps = host_layouts(inp)
    nc = _get_nc("FUSED", build_fused)
    res = run_bass_kernel_spmd(nc, maps, core_ids=list(range(NCORES)))
    yy = np.concatenate([res.results[ci]["y"] for ci in range(NCORES)], axis=0)
    return yy.reshape(1, 16384, 1024)


def kernel(**inp):
    if not inp.pop("_unfused", False) and "_upto" not in inp:
        inp.pop("_debug", None)
        return kernel_fused(**inp)
    debug = inp.pop("_debug", None)
    upto = inp.pop("_upto", None)
    inp = {k: np.asarray(v) for k, v in inp.items()}
    x = np.ascontiguousarray(inp["L0_attn_x"].reshape(16384, 1024),
                             dtype=np.float32)
    xs = [np.ascontiguousarray(x[ci * T:(ci + 1) * T]) for ci in range(NCORES)]

    wq_a = _aug(inp["L0_attn_Wq"], inp["L0_attn_bq"])
    wk_a = _aug(inp["L0_attn_Wk"], inp["L0_attn_bk"])
    wv_a = _aug(inp["L0_attn_Wv"], inp["L0_attn_bv"])

    rA = _run("A", build_A, [
        dict(x=xs[ci], wq_a=wq_a, wk_a=wk_a, wv_a=wv_a, ident=IDENT)
        for ci in range(NCORES)])
    if upto == "A":
        return rA

    kvp_all = np.ascontiguousarray(
        np.stack([rA[ci]["kvp"] for ci in range(NCORES)]))
    wo_a = _aug(inp["L0_attn_Wo"], inp["L0_attn_bo"])
    g1r = _row(inp["L0_attn_g1"])
    be1r = _row(inp["L0_attn_be1"])
    rB1 = _run("B1", build_B1, [
        dict(x=xs[ci], eqT=rA[ci]["eqT"], rq=rA[ci]["rq"], kvp=kvp_all,
             wo_a=wo_a, g1r=g1r, be1r=be1r, ident=IDENT)
        for ci in range(NCORES)])
    if upto == "B1":
        return rB1

    w1_a = _aug(inp["L0_attn_W1"], inp["L0_attn_b1"])
    b1cB = np.ascontiguousarray(
        inp["L0_attn_b1"].reshape(32, 128).T, dtype=np.float32)
    rB2a = _run("B2a", build_B2a, [
        dict(atT=rB1[ci]["atT"], w1_a=w1_a, b1cB=b1cB, ident=IDENT)
        for ci in range(NCORES)])

    w2_a = _aug(inp["L0_attn_W2"], inp["L0_attn_b2"])
    g2r = _row(inp["L0_attn_g2"])
    be2r = _row(inp["L0_attn_be2"])
    rB2b = _run("B2b", build_B2b, [
        dict(hT=rB2a[ci]["hT"], attn_out=rB1[ci]["attn_out"], x=xs[ci],
             w2_a=w2_a, g2r=g2r, be2r=be2r, ident=IDENT)
        for ci in range(NCORES)])
    if upto == "B2b":
        return rB2b

    gmr = _row(inp["L0_ffn_ln_gamma"])
    bbr = _row(inp["L0_ffn_ln_beta"])
    mur = np.ascontiguousarray(inp["L0_ffn_mu"].reshape(D, ES),
                               dtype=np.float32)
    scl = np.ascontiguousarray(inp["L0_ffn_scale"].reshape(1, 1),
                               dtype=np.float32)
    rC = _run("C", build_C, [
        dict(x1=rB2b[ci]["x1"], gmr=gmr, bbr=bbr, mur=mur, scl=scl,
             ident=IDENT) for ci in range(NCORES)])
    if upto == "C":
        return rC

    Praw_all = np.ascontiguousarray(np.stack([rC[ci]["Praw"] for ci in range(NCORES)]))
    Sloc_all = np.ascontiguousarray(np.stack([rC[ci]["Sloc"] for ci in range(NCORES)]))
    w1e = np.ascontiguousarray(inp["L0_ffn_w1"], dtype=np.float32)
    b1c = np.ascontiguousarray(
        inp["L0_ffn_b1"].reshape(EXP, 2, 128).transpose(2, 0, 1).reshape(128, EXP * 2),
        dtype=np.float32)
    w2e = np.ascontiguousarray(
        np.concatenate([inp["L0_ffn_w2"],
                        inp["L0_ffn_b2"][:, None, :]], axis=1), dtype=np.float32)
    rD = _run("D", build_D, [
        dict(x1=rB2b[ci]["x1"], E=rC[ci]["E"], Zc=rC[ci]["Zc"],
             Praw=Praw_all, Sloc=Sloc_all, w1e=w1e, b1c=b1c, w2e=w2e,
             ident=IDENT) for ci in range(NCORES)])
    if upto == "D":
        return rD

    x2 = np.concatenate([rD[ci]["x2"] for ci in range(NCORES)], axis=0)
    xe = np.zeros((NCORES, 2176, D), np.float32)
    for ci in range(NCORES):
        lo, hi = ci * T - HALO, ci * T + T + HALO
        slo, shi = max(lo, 0), min(hi, 16384)
        xe[ci, slo - lo:slo - lo + (shi - slo)] = x2[slo:shi]
    wqT = np.ascontiguousarray(inp["L1_attn_Wq"].T, dtype=np.float32)
    wkT = np.ascontiguousarray(inp["L1_attn_Wk"].T, dtype=np.float32)
    wvT = np.ascontiguousarray(inp["L1_attn_Wv"].T, dtype=np.float32)
    woT_a = _aug(np.ascontiguousarray(inp["L1_attn_Wo"].T), inp["L1_attn_bo"])
    pp = np.arange(128)[:, None]
    rr = np.arange(128)[None, :]
    maskA = np.where((rr >= pp - 63) & (rr <= pp), 0.0, -1e4).astype(np.float32)
    pb = np.arange(64)[:, None]
    maskB = np.where(rr >= pb + 65, 0.0, -1e4).astype(np.float32)
    rE = _run("E", build_E, [
        dict(xe=np.ascontiguousarray(xe[ci]), wqT=wqT, wkT=wkT, wvT=wvT,
             woT_a=woT_a, maskA=maskA, maskB=maskB, ident=IDENT)
        for ci in range(NCORES)])
    if upto == "E":
        return rE

    w1n = np.ascontiguousarray(inp["L1_ffn_w1"], dtype=np.float32)
    w1T = np.ascontiguousarray(inp["L1_ffn_w1"].T, dtype=np.float32)
    b1cf = np.ascontiguousarray(
        inp["L1_ffn_b1"].reshape(32, 128).T, dtype=np.float32)
    rF1 = _run("F1", build_F1, [
        dict(x3=rE[ci]["x3"], w1n=w1n, w1T=w1T, b1c=b1cf, ident=IDENT)
        for ci in range(NCORES)])

    w2n = np.ascontiguousarray(inp["L1_ffn_w2"], dtype=np.float32)
    w2T = np.ascontiguousarray(inp["L1_ffn_w2"].T, dtype=np.float32)
    b2cf = np.ascontiguousarray(
        inp["L1_ffn_b2"].reshape(NJ, 128).T, dtype=np.float32)
    rF2 = _run("F2", build_F2, [
        dict(x3=rE[ci]["x3"], hT=rF1[ci]["hT"], w2n=w2n, w2T=w2T, b2c=b2cf,
             ident=IDENT) for ci in range(NCORES)])

    y = np.concatenate([rF2[ci]["y"] for ci in range(NCORES)], axis=0)
    return y.reshape(1, 16384, 1024)


# version 12
# speedup vs baseline: 1.0984x; 1.0984x over previous
import numpy as np
import ml_dtypes
from contextlib import ExitStack

import concourse.bass as bass
import concourse.mybir as mybir
from concourse.bass_utils import run_bass_kernel_spmd

F32 = mybir.dt.float32
BF16 = mybir.dt.bfloat16
AF = mybir.ActivationFunctionType
ALU = mybir.AluOpType
AX = mybir.AxisListType

NCORES = 8
T = 2048
NT = 16
D = 1024
NJ = 8
FF = 4096
NH = 16
HD = 64
ES = 256
EXP = 16
SLOT = 16
MOE_H = 256
HALO = 32
TEXT = T + 128
NTE = 17

BF16NP = ml_dtypes.bfloat16
DEBUG_A = False
DEBUG_B1 = False


class Prog:

    ENGS = ("sync", "scalar", "vector", "tensor", "gpsimd")

    def __init__(self, name):
        self.name = name
        self.nc = bass.Bass("TRN2", target_bir_lowering=False)
        self.ctx = ExitStack()
        self.regions = [self.ctx]
        self.prefix = ""
        self.ops = {e: [] for e in self.ENGS}
        self.cnt = {e: 0 for e in self.ENGS}
        self.scnt = {}
        self.pending = {}

    def push_region(self, prefix):
        self.regions.append(ExitStack())
        self._prefixes = getattr(self, "_prefixes", [""])
        self._prefixes.append(self.prefix)
        self.prefix = self.prefix + prefix + "_"

    def pop_region(self):
        self.regions.pop().close()
        self.prefix = self._prefixes.pop()

    def barrier(self):
        snap = [(e, self.cnt[e]) for e in self.ENGS if self.cnt[e] > 0]
        snap += [(s, v) for s, v in self.scnt.items() if v > 0 and s != "cc"]
        for e in self.ENGS:
            self.pending.setdefault(e, [])
            self.pending[e] = snap

    def sb(self, name, shape, dt):
        return self.regions[-1].enter_context(
            self.nc.sbuf_tensor(self.prefix + name, shape, dt))

    def ps(self, name, shape, dt=F32):
        return self.regions[-1].enter_context(
            self.nc.psum_tensor(self.prefix + name, shape, dt))

    def din(self, name, shape, dt=F32):
        return self.nc.dram_tensor(name, shape, dt, kind="ExternalInput").ap()

    def dout(self, name, shape, dt=F32):
        return self.nc.dram_tensor(name, shape, dt, kind="ExternalOutput").ap()

    def dtmp(self, name, shape, dt=F32):
        return self.nc.dram_tensor(name, shape, dt).ap()

    def emit(self, eng, fn, waits=(), inc=1, sem=None):
        sem = sem or eng
        waits = list(waits) + self.pending.pop(eng, [])
        w = {}
        for tok in waits:
            if len(tok) == 3:
                _, we, wv = tok
            else:
                we, wv = tok
                if we in self.scnt:
                    wv = self.scnt[we]
            if we != eng and wv > 0:
                w[we] = max(w.get(we, 0), wv)
        self.ops[eng].append((tuple(w.items()), fn, inc, sem))
        if sem in self.scnt:
            self.scnt[sem] += inc
            return (sem, self.scnt[sem])
        self.cnt[eng] += inc
        return (eng, self.cnt[eng])

    def tok(self, eng):
        return (eng, self.cnt[eng])

    def stok(self, stream):
        return (stream, self.scnt.get(stream, 0))

    def const_col(self, value):
        if not hasattr(self, "_ccache"):
            self._ccache = {}
        if value not in self._ccache:
            t = self.regions[0].enter_context(self.nc.sbuf_tensor(
                f"cc_{len(self._ccache)}", [128, 1], F32))
            tok = self.emit("vector", lambda e, t=t, value=value: e.memset(
                t[:], float(value)))
            self._ccache[value] = (t, tok)
        return self._ccache[value]

    def dma_rr(self, stream, nways, idx, out, in_, waits=(), queue="sync", **kw):
        return self.dma(f"{stream}{idx % nways}", out, in_, waits, queue, **kw)

    def stok_rr(self, stream, nways):
        return [self.stok(f"{stream}{i}") for i in range(nways)]

    def dma(self, stream, out, in_, waits=(), queue="sync", **kw):
        if stream not in self.scnt:
            self.scnt[stream] = 0
        waits = list(waits)
        if self.scnt[stream] > 0:
            waits.append(("exact", stream, self.scnt[stream]))
        return self.emit(queue, lambda e: e.dma_start(out=out, in_=in_, **kw),
                         waits, 16, sem=stream)

    def finalize(self):
        while len(self.regions) > 1:
            self.pop_region()
        nc = self.nc
        sems = {e: self.ctx.enter_context(nc.semaphore("sem_" + e))
                for e in self.ENGS}
        for s in self.scnt:
            sems[s] = self.ctx.enter_context(nc.semaphore("sem_" + s))
        block = self.ctx.enter_context(nc.Block())
        for e in self.ENGS:
            ops = self.ops[e]
            if not ops and e != "sync":
                continue

            def body(engobj, ops=ops, e=e):
                hwm = {}
                selfcnt = 0
                selfsync = e in ("scalar", "vector", "gpsimd")
                for waits, fn, inc, sem in ops:
                    if selfsync and sem == e and selfcnt > 0:
                        engobj.wait_ge(sems[e], selfcnt)
                    for we, wv in waits:
                        if hwm.get(we, 0) < wv:
                            engobj.wait_ge(sems[we], wv)
                            hwm[we] = wv
                    ins = fn(engobj)
                    if inc:
                        ins.then_inc(sems[sem], inc)
                    if sem == e:
                        selfcnt += inc
                if e == "sync":
                    for oe in Prog.ENGS:
                        if self.cnt[oe] > 0:
                            engobj.wait_ge(sems[oe], self.cnt[oe])
                    for s, v in self.scnt.items():
                        if v > 0:
                            engobj.wait_ge(sems[s], v)

            getattr(block, e)(body)
        self.ctx.close()
        return nc



def common_consts(p):
    c = {}
    c["ident"] = p.sb("ident_sb", [128, 128], BF16)
    ident_in = p.din("ident", [128, 128], BF16)
    t_id = p.dma("in", c["ident"][:], ident_in[:])
    c["ones_bf"] = p.sb("ones_bf", [1, 2048], BF16)
    c["onescol_bf"] = p.sb("onescol_bf", [128, 1], BF16)
    c["ones_f"] = p.sb("ones_f", [1, 128], F32)
    t1 = p.emit("vector", lambda e: e.memset(c["ones_bf"][:], 1.0))
    t2 = p.emit("vector", lambda e: e.memset(c["onescol_bf"][:], 1.0))
    t3 = p.emit("vector", lambda e: e.memset(c["ones_f"][:], 1.0))
    c["ready"] = [t_id, t1, t2, t3]
    return c


def load_cast_weight(p, c, name, kt, n, stage, dest, brow_f, brow_h, waits=(),
                     win=None):
    if win is None:
        win = p.din(name, [kt * 128 + 1, n], F32)
    tw = p.dma("in", stage[:, 0:kt * n].rearrange("p (k f) -> p k f", f=n),
               win[0:kt * 128, :].rearrange("(k p) f -> p k f", p=128),
               waits=waits)
    tb = p.dma("in", brow_f[:, 0:n], win[kt * 128:kt * 128 + 1, :], waits=waits)
    t_in = p.stok("in")
    tcast = []
    for j in range(kt):
        tcast.append(p.emit("scalar", lambda e, j=j: e.activation(
            out=dest[:, j * n:(j + 1) * n], in_=stage[:, j * n:(j + 1) * n],
            func=AF.Copy), waits=[t_in]))
    tcb = p.emit("scalar", lambda e: e.activation(
        out=brow_h[:, 0:n], in_=brow_f[:, 0:n], func=AF.Copy), waits=[t_in])
    return tcast + [tcb]


def bcast_row_f32(p, c, dst, src_row, n, psA, waits=()):
    toks = []
    ev = []
    for n0 in range(0, n, 512):
        nn = min(512, n - n0)
        tmm = p.emit("tensor", lambda e, n0=n0, nn=nn: e.matmul(
            out=psA[:, n0 % 512:n0 % 512 + nn], lhsT=c["ones_f"][:, :],
            rhs=src_row[:, n0:n0 + nn], start=True, stop=True),
            waits=list(waits) + ev)
        tev = p.emit("vector", lambda e, n0=n0, nn=nn: e.tensor_copy(
            dst[:, n0:n0 + nn], psA[:, n0 % 512:n0 % 512 + nn]), waits=[tmm])
        ev = [tev]
        toks.append(tev)
    return toks


def ln_stats_apply(p, bufs, u, s1, out, eps, tw_u, tw_s2, tag):
    sq, mcol, vcol, tcol = bufs["sqw"], bufs["mcol"], bufs["vcol"], bufs["tcol"]
    t_s2 = p.emit("scalar", lambda e: e.activation(
        out=sq[:], in_=u[:], func=AF.Square, accum_out=vcol[:, 0:1]),
        waits=tw_u)
    t_m = p.emit("scalar", lambda e: e.mul(mcol[:, 0:1], s1[:, 0:1], 1.0 / D),
                 waits=tw_u)
    t_t = p.emit("scalar", lambda e: e.activation(
        out=tcol[:, 0:1], in_=mcol[:, 0:1], func=AF.Square), waits=[t_m])
    t_v = p.emit("vector", lambda e: e.scalar_tensor_tensor(
        out=vcol[:, 0:1], in0=vcol[:, 0:1], scalar=1.0 / D, in1=tcol[:, 0:1],
        op0=ALU.mult, op1=ALU.subtract), waits=[t_s2, t_t])
    eps_ap, eps_tok = p.const_col(float(eps))
    t_sq = p.emit("scalar", lambda e: e.activation(
        out=tcol[:, 0:1], in_=vcol[:, 0:1], func=AF.Sqrt,
        bias=eps_ap[:, 0:1]), waits=[t_v, eps_tok])
    t_r = p.emit("vector", lambda e: e.reciprocal(
        out=vcol[:, 0:1], in_=tcol[:, 0:1]), waits=[t_sq])
    t_o = p.emit("vector", lambda e: e.tensor_scalar(
        out=out[:], in0=u[:], scalar1=mcol[:, 0:1], scalar2=vcol[:, 0:1],
        op0=ALU.subtract, op1=ALU.mult), waits=[t_r] + tw_u)
    return t_o, t_sq


def transpose_tile(p, c, tps, src_bf, dst, waits_src, waits_ps, nblk=8,
                   waits_evac=()):
    tts = []
    for j in range(nblk):
        tts.append(p.emit("tensor", lambda e, j=j: e.transpose(
            out=tps[:, j * 128:(j + 1) * 128],
            in_=src_bf[:, j * 128:(j + 1) * 128],
            identity=c["ident"][:]),
            waits=(list(waits_src) + list(waits_ps)) if j == 0 else list(waits_src)))
    tev = p.emit("vector", lambda e: e.tensor_copy(
        dst[:], tps[:, 0:nblk * 128]), waits=[tts[-1]] + list(waits_evac))
    return tev




def interleave_pairs(bodies):
    live = list(bodies)
    while live:
        nxt = []
        for g in live:
            try:
                next(g)
                nxt.append(g)
            except StopIteration:
                pass
        live = nxt


def emit_A(p, c, io):
    x_in = io["x"]
    kvp_out = io["kvp"]
    eqT_res = io.get("eqT_res")
    rq_res = io.get("rq_res")
    eqT_out = io.get("eqT")
    rq_out = io.get("rq")

    wstage = [p.sb(f"wstage{i}", [128, 2 * D], F32) for i in range(2)]
    wb_f = p.sb("wb_f", [1, D], F32)
    wh = {k: p.sb(f"wh_{k}", [128, NJ * D], BF16) for k in "qkv"}
    wbh = {k: p.sb(f"wbh_{k}", [1, D], BF16) for k in "qkv"}

    xT = p.sb("xT", [128, NJ * T], BF16)
    NL = 2
    xf_t = [p.sb(f"xf_t{i}", [128, D], F32) for i in range(NL)]
    xh_t = [p.sb(f"xh_t{i}", [128, D], BF16) for i in range(NL)]
    qa = [p.sb(f"qa{i}", [128, D], F32) for i in range(NL)]
    qb = [p.sb(f"qb{i}", [128, D], F32) for i in range(NL)]
    qc = [p.sb(f"qc{i}", [128, D], F32) for i in range(NL)]
    eqbf = [p.sb(f"eqbf{i}", [128, D], BF16) for i in range(NL)]
    ekbf = [p.sb(f"ekbf{i}", [128, D], BF16) for i in range(NL)]
    vpbf = [p.sb(f"vpbf{i}", [128, D], BF16) for i in range(NL)]
    sqb0 = p.sb("sqb0", [128, D], F32)
    sqb = [sqb0] * NL
    ncol = [p.sb(f"ncol{i}", [128, NH], F32) for i in range(NL)]
    scol = [p.sb(f"scol{i}", [128, NH], F32) for i in range(NL)]
    rk = [p.sb(f"rk{i}", [128, NH], F32) for i in range(NL)]
    rkexp = [p.sb(f"rkexp{i}", [128, D], F32) for i in range(NL)]
    rqsb = ([p.sb(f"rqsb{i}", [128, NH], F32) for i in range(NL)]
            if io.get("rq_res") is None else None)
    eqstage = ([p.sb(f"eqstage{i}", [128, D], BF16) for i in range(NL)]
               if io.get("eqT_res") is None else None)
    kvsb = qa[0][0:64, 0:1024]

    tpsL = [p.ps("tpsA", [128, 1024], BF16), p.ps("tps2", [128, 1024], BF16)]
    qps = [p.ps(f"qps{i}", [128, 1024], F32) for i in range(NL)]
    kvps = p.ps("kvps", [64, 1024], F32)

    wready = []
    wcast = {}
    for idx, (k, h) in enumerate([(k, h) for k in "qkv" for h in range(4)]):
        tw = p.dma_rr("in_w", 2, idx,
                      wstage[idx % 2][:].rearrange("p (k f) -> p k f", f=D),
                      io["w%s_a" % k][h * 256:(h + 1) * 256, :]
                      .rearrange("(k p) f -> p k f", p=128),
                      waits=wcast.get(idx - 2, list(c["ready"])))
        cs = []
        for j in range(2):
            cs.append(p.emit("scalar", lambda e, k=k, h=h, j=j, idx=idx: e.activation(
                out=wh[k][:, (h * 2 + j) * D:(h * 2 + j + 1) * D],
                in_=wstage[idx % 2][:, j * D:(j + 1) * D],
                func=AF.Copy), waits=[tw]))
        wcast[idx] = cs
        wready += cs
    prevb = None
    for k in "qkv":
        tb = p.dma("in_b", wb_f[:, 0:D], io["w%s_a" % k][NJ * 128:NJ * 128 + 1, :],
                   waits=[prevb] if prevb else list(c["ready"]))
        prevb = p.emit("scalar", lambda e, k=k: e.activation(
            out=wbh[k][:, 0:D], in_=wb_f[:, 0:D], func=AF.Copy),
            waits=[tb])
        wready.append(prevb)

    pv = [dict(cast=None, xtp=None, eq_evac=None, vstt=None, kps=None,
               nq=None, sqq=None, rq=None) for _ in range(NL)]
    gkv = dict(ek=None, kvlast=None)

    def tile_body(t, L):
        v = pv[L]
        w_load = [v["cast"]] if v["cast"] else list(c["ready"])
        t_x = p.dma_rr("in_x", NL, L, xf_t[L][:],
                       x_in[t * 128:(t + 1) * 128, :], waits=w_load)
        yield
        w_c = [t_x] + ([v["xtp"]] if v["xtp"] else [])
        t_cast = p.emit("scalar", lambda e, L=L: e.activation(
            out=xh_t[L][:], in_=xf_t[L][:], func=AF.Copy), waits=w_c)
        v["cast"] = t_cast
        xT_view = xT[:].rearrange("p (j t) -> p j t", t=T)[:, :, t * 128:(t + 1) * 128]
        tev = transpose_tile(p, c, tpsL[L], xh_t[L], xT_view, [t_cast],
                             [v["xtp"]] if v["xtp"] else [], nblk=NJ)
        v["xtp"] = tev
        t_xT = tev
        yield

        def proj(k, waits_extra, t=t, t_xT=t_xT, L=L):
            last = None
            for n in range(2):
                for kk in range(NJ):
                    last = p.emit("tensor", lambda e, n=n, kk=kk, k=k, t=t, L=L: e.matmul(
                        out=qps[L][:, n * 512:(n + 1) * 512],
                        lhsT=xT[:, kk * T + t * 128: kk * T + (t + 1) * 128],
                        rhs=wh[k][:, kk * D + n * 512: kk * D + (n + 1) * 512],
                        start=(kk == 0), stop=False),
                        waits=([t_xT] + wready + waits_extra) if kk == 0 and n == 0 else [])
                last = p.emit("tensor", lambda e, n=n, k=k, L=L: e.matmul(
                    out=qps[L][:, n * 512:(n + 1) * 512],
                    lhsT=c["ones_bf"][0:1, 0:128],
                    rhs=wbh[k][:, n * 512:(n + 1) * 512],
                    start=False, stop=True), waits=[])
            return last

        def elu(out_bf, t_mm, war=(), L=L):
            ta = p.emit("scalar", lambda e, L=L: e.activation(
                out=qa[L][:], in_=qps[L][:], func=AF.Relu),
                waits=[t_mm] + list(war))
            tb = p.emit("scalar", lambda e, L=L: e.activation(
                out=qb[L][:], in_=qps[L][:], func=AF.Relu, scale=-1.0), waits=[])
            tc = p.emit("scalar", lambda e, L=L: e.activation(
                out=qc[L][:], in_=qb[L][:], func=AF.Exp, scale=-1.0), waits=[])
            te = p.emit("vector", lambda e, out_bf=out_bf, L=L: e.scalar_tensor_tensor(
                out=out_bf[:], in0=qa[L][:], scalar=-1.0, in1=qc[L][:],
                op0=ALU.add, op1=ALU.add), waits=[tc])
            return te, tb

        t_qmm = proj("q", [v["vstt"]] if v["vstt"] else [])
        yield
        war_q = [gkv["ek"]] if gkv["ek"] else []
        t_eq, t_q_psfree = elu(eqbf[L], t_qmm, war=war_q)
        yield
        t_eqsq = p.emit("scalar", lambda e, L=L: e.activation(
            out=sqb[L][:], in_=eqbf[L][:], func=AF.Square), waits=[t_eq])
        t_nq = p.emit("vector", lambda e, L=L: e.reduce_sum(
            out=ncol[L][:, 0:NH], in_=sqb[L][:].rearrange("p (h w) -> p h w", w=HD),
            axis=AX.X), waits=[t_eqsq])
        t_sqq = p.emit("scalar", lambda e, L=L: e.activation(
            out=scol[L][:, 0:NH], in_=ncol[L][:, 0:NH], func=AF.Sqrt),
            waits=[t_nq])
        if rq_res is not None:
            t_rq = p.emit("vector", lambda e, L=L, t=t: e.reciprocal(
                out=rq_res[:, t * NH:(t + 1) * NH], in_=scol[L][:, 0:NH]),
                waits=[t_sqq])
        else:
            t_rq = p.emit("vector", lambda e, L=L: e.reciprocal(
                out=rqsb[L][:, 0:NH], in_=scol[L][:, 0:NH]),
                waits=[t_sqq, p.stok(f"o_rq{L}")])
            p.dma_rr("o_rq", NL, L, rq_out[t, :, :], rqsb[L][:, 0:NH],
                     waits=[t_rq])
        yield

        w_ps2 = [v["eq_evac"]] if v["eq_evac"] else []
        if eqT_res is not None:
            dstv = eqT_res[:].rearrange("p (j t) -> p j t",
                                        t=T)[:, :, t * 128:(t + 1) * 128]
            tev_eq = transpose_tile(p, c, tpsL[L], eqbf[L], dstv, [t_eq],
                                    w_ps2, nblk=NJ)
        else:
            tev_eq = transpose_tile(p, c, tpsL[L], eqbf[L], eqstage[L][:],
                                    [t_eq], w_ps2, nblk=NJ,
                                    waits_evac=[p.stok(f"o_eq{L}")])
            p.dma_rr(
                "o_eq", NL, L,
                eqT_out[:, :, t * 128:(t + 1) * 128].rearrange("j p t -> p j t"),
                eqstage[L][:].rearrange("p (j t) -> p j t", t=128),
                waits=[tev_eq])
        v["eq_evac"] = tev_eq
        yield

        t_kmm = proj("k", [t_q_psfree])
        yield
        t_ek, t_k_psfree = elu(ekbf[L], t_kmm, war=[t_eq])
        v["kps"] = t_k_psfree
        yield
        t_eksq = p.emit("scalar", lambda e, L=L: e.activation(
            out=sqb[L][:], in_=ekbf[L][:], func=AF.Square),
            waits=[t_ek, t_nq])
        t_nk = p.emit("vector", lambda e, L=L: e.reduce_sum(
            out=ncol[L][:, 0:NH], in_=sqb[L][:].rearrange("p (h w) -> p h w", w=HD),
            axis=AX.X), waits=[t_eksq, t_sqq])
        t_sqk = p.emit("scalar", lambda e, L=L: e.activation(
            out=scol[L][:, 0:NH], in_=ncol[L][:, 0:NH], func=AF.Sqrt),
            waits=[t_nk, t_rq])
        t_rk = p.emit("vector", lambda e, L=L: e.reciprocal(
            out=rk[L][:, 0:NH], in_=scol[L][:, 0:NH]), waits=[t_sqk])
        t_rkexp = p.emit("vector", lambda e, L=L: e.tensor_copy(
            rkexp[L][:].rearrange("p (h w) -> p h w", w=HD),
            rk[L][:, 0:NH].unsqueeze(2).broadcast_to([128, NH, HD])),
            waits=[t_rk])
        yield

        t_vmm = proj("v", [t_k_psfree])
        yield
        t_vstt = p.emit("vector", lambda e, L=L: e.scalar_tensor_tensor(
            out=vpbf[L][:], in0=qps[L][:], scalar=1.0, in1=rkexp[L][:],
            op0=ALU.mult, op1=ALU.mult), waits=[t_vmm, t_rkexp])
        v["vstt"] = t_vstt
        yield

        t_kv = None
        for h in range(NH):
            t_kv = p.emit("tensor", lambda e, h=h, t=t, L=L: e.matmul(
                out=kvps[0:64, h * 64:(h + 1) * 64],
                lhsT=ekbf[L][:, h * 64:(h + 1) * 64],
                rhs=vpbf[L][:, h * 64:(h + 1) * 64],
                start=(t == 0 and h % 8 == 0),
                stop=(t == NT - 1 and h % 8 == 7)),
                waits=[t_ek, t_vstt] if h == 0 else [])
        gkv["ek"] = t_eq if False else t_ek
        gkv["kvlast"] = t_kv

    for t0 in range(0, NT, NL):
        interleave_pairs([tile_body(t, t % NL)
                          for t in range(t0, min(t0 + NL, NT))])
    t_kv_last = gkv["kvlast"]

    t_kvev = p.emit("vector", lambda e: e.tensor_copy(
        kvsb[:], kvps[0:64, :]), waits=[t_kv_last])
    p.dma("o_kv", kvp_out[:], kvsb[:], waits=[t_kvev])


def build_A():
    p = Prog("A")
    c = common_consts(p)
    io = dict(x=p.din("x", [T, D], F32),
              wq_a=p.din("wq_a", [D + 1, D], F32),
              wk_a=p.din("wk_a", [D + 1, D], F32),
              wv_a=p.din("wv_a", [D + 1, D], F32),
              kvp=p.dout("kvp", [64, 1024], F32),
              eqT=p.dout("eqT", [NJ, 128, T], BF16),
              rq=p.dout("rq", [NT, 128, NH], F32))
    emit_A(p, c, io)
    return p.finalize()



def emit_B1(p, c, io, ntiles=NT, stage=99, fused=False, kv_wait=None):
    x_in = io["x"]
    eqT_res = io.get("eqT_res")
    rq_res = io.get("rq_res")
    eqT_in = io.get("eqT")
    rq_in = io.get("rq")
    kvp_in = io["kvp"]
    g1_in = io["g1r"]
    be1_in = io["be1r"]
    ao_out = io["attn_out"]
    atT_out = io["atT"]

    eqT_sb = (p.sb("eqT_sb", [128, NJ * T], BF16) if eqT_res is None
              else eqT_res)
    kvacc = p.sb("kvacc", [64, 1024], F32)
    kvh = p.sb("kvh", [64, 1024], BF16)
    kvblk = p.sb("kvblk", [128, NJ * 128], BF16)
    wstage = p.sb("wstage", [128, NJ * D], F32)
    wb_f = p.sb("wb_f", [1, D], F32)
    woh = p.sb("woh", [128, NJ * D], BF16)
    wob_h = p.sb("wob_h", [1, D], BF16)
    g1row = p.sb("g1row", [1, D], F32)
    be1row = p.sb("be1row", [1, D], F32)
    gb1 = p.sb("gb1", [128, D], F32)
    bb1 = p.sb("bb1", [128, D], F32)
    NL = 2
    x_t = [p.sb(f"x_t{i}", [128, D], F32) for i in range(NL)]
    rq_t = [p.sb(f"rq_t{i}", [128, NH], F32) for i in range(NL)]
    rqx = [p.sb(f"rqx{i}", [128, NH], F32) for i in range(NL)]
    rqexp = [p.sb(f"rqexp{i}", [128, D], F32) for i in range(NL)]
    ctxn = [p.sb(f"ctxn{i}", [128, D], BF16) for i in range(NL)]
    ctxT = [p.sb(f"ctxT{i}", [128, D], BF16) for i in range(NL)]
    u = [p.sb(f"u{i}", [128, D], F32) for i in range(NL)]
    s1col = [p.sb(f"s1col{i}", [128, 1], F32) for i in range(NL)]
    lnb = [dict(sqw=p.sb(f"sqw{i}", [128, D], F32),
                mcol=p.sb(f"mcol{i}", [128, 1], F32),
                vcol=p.sb(f"vcol{i}", [128, 1], F32),
                tcol=p.sb(f"tcol{i}", [128, 1], F32)) for i in range(NL)]
    ao = [p.sb(f"ao{i}", [128, D], F32) for i in range(NL)]
    ao2 = [p.sb(f"ao2{i}", [128, D], F32) for i in range(NL)]
    aof = [p.sb(f"aof{i}", [128, D], F32) for i in range(NL)]
    at_bf = [p.sb(f"at_bf{i}", [128, D], BF16) for i in range(NL)]
    atT_stage = [p.sb(f"atT_stage{i}", [128, D], BF16) for i in range(NL)]

    cps = [p.ps(f"cps{i}", [128, 1024], F32) for i in range(NL)]
    tps = [p.ps(f"tps{i}", [128, 1024], BF16) for i in range(NL)]
    psA = p.ps("psA", [128, 512], F32)

    t_eqT_l = []
    if eqT_res is None:
        for ch in range(4):
            t_eqT_l.append(p.dma_rr(
                "in_eq", 4, ch,
                eqT_sb[:].rearrange("p (j t) -> p j t", t=T)[:, 2 * ch:2 * ch + 2, :],
                eqT_in[2 * ch:2 * ch + 2].rearrange("j p t -> p j t")))
    if fused:
        tprev = p.dma("in_kv", kvacc[:], kvp_in[:], waits=[kv_wait],
                      queue="gpsimd")
    else:
        tprev = None
        for cc in range(NCORES):
            tprev = p.dma("in_kv", kvacc[:], kvp_in[cc],
                          waits=[tprev] if tprev else [], queue="gpsimd",
                          accum_op=(ALU.add if cc else ALU.bypass))
    t_kvh0 = p.emit("scalar", lambda e: e.activation(
        out=kvh[0:64, :], in_=kvacc[:], func=AF.Copy), waits=[tprev])
    t_z = p.emit("vector", lambda e: e.memset(kvblk[:], 0.0))
    kv_toks = []
    for j in range(NJ):
        kv_toks.append(p.dma_rr("in_kv2", 4, 2 * j,
                                kvblk[0:64, j * 128:j * 128 + 64],
                                kvh[0:64, (2 * j) * 64:(2 * j + 1) * 64],
                                waits=[t_kvh0, t_z]))
        kv_toks.append(p.dma_rr("in_kv2", 4, 2 * j + 1,
                                kvblk[64:128, j * 128 + 64:(j + 1) * 128],
                                kvh[0:64, (2 * j + 1) * 64:(2 * j + 2) * 64],
                                waits=[t_kvh0, t_z]))
    t_kvh = kv_toks[-4:]
    wready = load_cast_weight(p, c, None, NJ, D, wstage, woh, wb_f, wob_h,
                              waits=c["ready"], win=io["wo_a"])
    t_g1 = p.dma("in", g1row[:], g1_in[:], waits=wready)
    t_be1 = p.dma("in", be1row[:], be1_in[:], waits=wready)
    t_in = p.stok("in")
    tb1 = bcast_row_f32(p, c, gb1, g1row, D, psA, waits=[t_in])
    tb2 = bcast_row_f32(p, c, bb1, be1row, D, psA, waits=[t_in] + tb1)
    t_bc = tb1 + tb2

    pv = [dict(rqx=None, lnsq=None, ctx_stt=None, ctxT_ev=None, u=None,
               lnout=None) for _ in range(NL)]

    def tile_body(t, L):
        v = pv[L]
        t_x = p.dma_rr("in_x", NL, L, x_t[L][:], x_in[t * 128:(t + 1) * 128, :],
                       waits=[v["u"]] if v["u"] else wready)
        if rq_res is not None:
            yield
            t_rqx = p.emit("scalar", lambda e, L=L, t=t: e.mul(
                rqx[L][:], rq_res[:, t * NH:(t + 1) * NH], 0.125), waits=[])
        else:
            t_rq = p.dma_rr("in_r", NL, L, rq_t[L][:], rq_in[t],
                            waits=[v["rqx"]] if v["rqx"] else wready)
            yield
            t_rqx = p.emit("scalar", lambda e, L=L: e.mul(
                rqx[L][:], rq_t[L][:], 0.125), waits=[t_rq])
        v["rqx"] = t_rqx
        t_rqe = p.emit("vector", lambda e, L=L: e.tensor_copy(
            rqexp[L][:].rearrange("p (h w) -> p h w", w=HD),
            rqx[L][:].unsqueeze(2).broadcast_to([128, NH, HD])), waits=[t_rqx])
        yield

        t_cmm = None
        for j in range(NJ):
            t_cmm = p.emit("tensor", lambda e, j=j, t=t, L=L: e.matmul(
                out=cps[L][:, j * 128:(j + 1) * 128],
                lhsT=eqT_sb[:, j * T + t * 128:j * T + (t + 1) * 128],
                rhs=kvblk[:, j * 128:(j + 1) * 128],
                start=(j % 4 == 0), stop=(j % 4 == 3)),
                waits=(t_eqT_l + t_kvh +
                       [x for x in (v["ctx_stt"], v["u"]) if x]) if j == 0 else [])
        t_ctx = p.emit("vector", lambda e, L=L: e.scalar_tensor_tensor(
            out=ctxn[L][:], in0=cps[L][:], scalar=1.0, in1=rqexp[L][:],
            op0=ALU.mult, op1=ALU.mult), waits=[t_cmm, t_rqe])
        yield

        t_ctxT = transpose_tile(p, c, tps[L], ctxn[L], ctxT[L][:], [t_ctx],
                                [v["ctxT_ev"]] if v["ctxT_ev"] else [],
                                nblk=NJ)
        yield

        t_omm = None
        for n in range(2):
            for kk in range(NJ):
                t_omm = p.emit("tensor", lambda e, n=n, kk=kk, L=L: e.matmul(
                    out=cps[L][:, n * 512:(n + 1) * 512],
                    lhsT=ctxT[L][:, kk * 128:(kk + 1) * 128],
                    rhs=woh[:, kk * D + n * 512:kk * D + (n + 1) * 512],
                    start=(kk == 0), stop=False),
                    waits=([t_ctxT] + wready +
                           ([v["u"]] if v["u"] else [])) if kk == 0 and n == 0 else [])
            t_omm = p.emit("tensor", lambda e, n=n, L=L: e.matmul(
                out=cps[L][:, n * 512:(n + 1) * 512],
                lhsT=c["ones_bf"][0:1, 0:128],
                rhs=wob_h[:, n * 512:(n + 1) * 512],
                start=False, stop=True), waits=[])
            yield

        t_u = p.emit("vector", lambda e, L=L: e.scalar_tensor_tensor(
            out=u[L][:], in0=cps[L][:], scalar=1.0, in1=x_t[L][:],
            op0=ALU.mult, op1=ALU.add, accum_out=s1col[L][:, 0:1]),
            waits=[t_omm, t_x] + ([v["lnsq"]] if v["lnsq"] else []))
        v["u"] = t_u
        yield

        w_ln = [t_u] + ([v["lnout"]] if v["lnout"] else [])
        t_ao, t_sq1 = ln_stats_apply(p, lnb[L], u[L], s1col[L], ao[L], 1e-12,
                                     w_ln, None, "ln1")
        v["lnsq"] = t_sq1
        yield
        t_m1 = p.emit("vector", lambda e, L=L: e.tensor_mul(
            ao2[L][:], ao[L][:], gb1[:]), waits=[t_ao] + t_bc)
        t_aof = p.emit("vector", lambda e, L=L: e.tensor_add(
            aof[L][:], ao2[L][:], bb1[:]),
            waits=[t_m1, p.stok(f"o_ao{L}")])
        v["lnout"] = t_aof
        yield

        t_cast = p.emit("scalar", lambda e, L=L: e.activation(
            out=at_bf[L][:], in_=aof[L][:], func=AF.Copy), waits=[t_aof])
        p.dma_rr("o_ao", NL, L, ao_out[t * 128:(t + 1) * 128, :], aof[L][:],
                 waits=[t_aof])
        yield
        t_atT = transpose_tile(p, c, tps[L], at_bf[L], atT_stage[L][:],
                               [t_cast], [t_ctxT], nblk=NJ,
                               waits_evac=[p.stok(f"o_at{L}")])
        v["ctxT_ev"] = t_atT
        v["ctx_stt"] = t_ctx
        p.dma_rr("o_at", NL, L,
                 atT_out[:, :, t * 128:(t + 1) * 128].rearrange("j p t -> p j t"),
                 atT_stage[L][:].rearrange("p (j t) -> p j t", t=128),
                 waits=[t_atT])

    for t0 in range(0, ntiles, NL):
        interleave_pairs([tile_body(t, t % NL)
                          for t in range(t0, min(t0 + NL, ntiles))])


def build_B1(ntiles=NT, stage=99):
    p = Prog("B1")
    c = common_consts(p)
    io = dict(x=p.din("x", [T, D], F32),
              eqT=p.din("eqT", [NJ, 128, T], BF16),
              rq=p.din("rq", [NT, 128, NH], F32),
              kvp=p.din("kvp", [NCORES, 64, 1024], F32),
              g1r=p.din("g1r", [1, D], F32),
              be1r=p.din("be1r", [1, D], F32),
              wo_a=p.din("wo_a", [D + 1, D], F32),
              attn_out=p.dout("attn_out", [T, D], F32),
              atT=p.dout("atT", [NJ, 128, T], BF16))
    emit_B1(p, c, io, ntiles=ntiles, stage=stage)
    return p.finalize()



def emit_B2a(p, c, io):
    atT_in = io["atT"]
    w1_in = io["w1_a"]
    b1c_in = io["b1cB"]
    hT_out = io["hT"]

    atT_sb = p.sb("atT_sb", [128, NJ * T], BF16)
    wstage = [p.sb(f"wstage{i}", [128, FF], F32) for i in range(2)]
    w1h = p.sb("w1h", [128, NJ * FF], BF16)
    b1cB = p.sb("b1cB_sb", [128, 32], F32)
    hbig = [p.sb(f"hbig{i}", [128, 8 * 512], BF16) for i in range(2)]
    hps = [p.ps(f"hps{i}", [128, 512], F32) for i in range(2)]

    t_atT_l = []
    for ach in range(4):
        t_atT_l.append(p.dma_rr(
            "in_at", 4, ach,
            atT_sb[:].rearrange("p (j t) -> p j t", t=T)[:, 2 * ach:2 * ach + 2, :],
            atT_in[2 * ach:2 * ach + 2].rearrange("j p t -> p j t")))
    tb = p.dma("in", b1cB[:], b1c_in[:])
    wready = [tb] + t_atT_l
    wcast = {}
    for ch in range(NJ):
        tw = p.dma_rr("in_w", 2, ch, wstage[ch % 2][:],
                      w1_in[ch * 128:(ch + 1) * 128, :],
                      waits=wcast.get(ch - 2, c["ready"]))
        cs = [p.emit("scalar", lambda e, ch=ch: e.activation(
            out=w1h[:, ch * FF:(ch + 1) * FF],
            in_=wstage[ch % 2][:], func=AF.Copy), waits=[tw])]
        wcast[ch] = cs
        wready += cs

    prev_ev = {}
    prev_st = {}
    i = 0
    for cch in range(4):
        for m in range(32):
            par = i % 2
            grp = (i // 8) % 2
            t_mm = None
            for kk in range(NJ):
                t_mm = p.emit("tensor", lambda e, cch=cch, m=m, kk=kk, par=par: e.matmul(
                    out=hps[par][:],
                    lhsT=w1h[:, kk * FF + m * 128:kk * FF + (m + 1) * 128],
                    rhs=atT_sb[:, kk * T + cch * 512:kk * T + (cch + 1) * 512],
                    start=(kk == 0), stop=(kk == NJ - 1)),
                    waits=(wready + list(prev_ev.get(par, []))) if kk == 0 else [])
            t_ev = p.emit("scalar", lambda e, par=par, grp=grp, m=m: e.activation(
                out=hbig[grp][:, (m % 8) * 512:(m % 8 + 1) * 512],
                in_=hps[par][:], func=AF.Gelu, bias=b1cB[:, m:m + 1]),
                waits=[t_mm] + (list(prev_st.get(grp, [])) if m % 8 < 2 else []))
            prev_ev[par] = [t_ev]
            if m % 8 == 7:
                t_st = p.dma_rr("o_h", 2, grp,
                                hT_out[m - 7:m + 1, :, cch * 512:(cch + 1) * 512]
                                .rearrange("m p tt -> p m tt"),
                                hbig[grp][:].rearrange("p (m tt) -> p m tt", tt=512),
                                waits=[t_ev])
                prev_st[grp] = [t_st]
            i += 1


def build_B2a():
    p = Prog("B2a")
    c = common_consts(p)
    io = dict(atT=p.din("atT", [NJ, 128, T], BF16),
              w1_a=p.din("w1_a", [NJ * 128 + 1, FF], F32),
              b1cB=p.din("b1cB", [128, 32], F32),
              hT=p.dout("hT", [32, 128, T], BF16))
    emit_B2a(p, c, io)
    return p.finalize()



def emit_B2b(p, c, io):
    hT_in = io["hT"]
    ao_in = io["attn_out"]
    x_in = io["x"]
    w2_in = io["w2_a"]
    g2_in = io["g2r"]
    be2_in = io["be2r"]
    x1_out = io["x1"]

    wstage = [p.sb(f"wstage{i}", [128, D], F32) for i in range(2)]
    w2h = p.sb("w2h", [128, 32 * D], BF16)
    w2b_f = wstage[0][0:1, 0:D]
    w2b_h = p.sb("w2b_h", [1, D], BF16)
    g2row = p.sb("g2row", [1, D], F32)
    be2row = p.sb("be2row", [1, D], F32)
    gb2 = p.sb("gb2", [128, D], F32)
    bb2 = p.sb("bb2", [128, D], F32)
    NL = 2
    hT_c = [p.sb(f"hT_c{i}", [128, 32 * 256], BF16) for i in range(2)]
    x_t = [p.sb(f"x_t{i}", [128, D], F32) for i in range(NL)]
    at_t = [p.sb(f"at_t{i}", [128, D], F32) for i in range(NL)]
    u = [p.sb(f"u{i}", [128, D], F32) for i in range(NL)]
    s1col = [p.sb(f"s1col{i}", [128, 1], F32) for i in range(NL)]
    sqw0 = p.sb("sqw0", [128, D], F32)
    lnb = [dict(sqw=sqw0,
                mcol=p.sb(f"mcol{i}", [128, 1], F32),
                vcol=p.sb(f"vcol{i}", [128, 1], F32),
                tcol=p.sb(f"tcol{i}", [128, 1], F32)) for i in range(NL)]
    ao0 = p.sb("ao0", [128, D], F32)
    ao = [ao0] * NL
    aof0 = p.sb("aof0", [128, D], F32)
    aof = [aof0] * NL
    u2 = at_t
    x1_t = [p.sb(f"x1_t{i}", [128, D], F32) for i in range(NL)]
    ao2 = u

    ops = [p.ps(f"ops{i}", [128, 1024], F32) for i in range(3)]
    psA = p.ps("psA", [128, 512], F32)

    wready = []
    wcast = {}
    for ch in range(32):
        tw = p.dma_rr("in_w", 2, ch, wstage[ch % 2][:],
                      w2_in[ch * 128:(ch + 1) * 128, :],
                      waits=wcast.get(ch - 2, list(c["ready"])))
        cs = [p.emit("scalar", lambda e, ch=ch: e.activation(
            out=w2h[:, ch * D:(ch + 1) * D],
            in_=wstage[ch % 2][:], func=AF.Copy), waits=[tw])]
        wcast[ch] = cs
        wready += cs
    tb = p.dma("in", w2b_f[:], w2_in[32 * 128:32 * 128 + 1, :],
               waits=wcast[31])
    wready.append(p.emit("scalar", lambda e: e.activation(
        out=w2b_h[:], in_=w2b_f[:], func=AF.Copy), waits=[tb]))
    t_g2 = p.dma("in", g2row[:], g2_in[:], waits=wready)
    t_be2 = p.dma("in", be2row[:], be2_in[:], waits=wready)
    t_in = p.stok("in")
    tb1 = bcast_row_f32(p, c, gb2, g2row, D, psA, waits=[t_in])
    tb2 = bcast_row_f32(p, c, bb2, be2row, D, psA, waits=[t_in] + tb1)
    t_bc = tb1 + tb2

    pv = [dict(u=None, mm=None, lnout=None, lnsq=None, st=None)
          for _ in range(NL)]
    g_h = {}
    g_mm = {}
    g_u = {}

    def tile_body(t, L):
        v = pv[L]
        pr = t // 2
        if t % 2 == 0:
            war = [g_mm[pr - 2]] if (pr - 2) in g_mm else wready
            g_h[pr] = p.dma_rr(
                "in_h", 2, pr,
                hT_c[pr % 2][:].rearrange("p (k tt) -> p k tt", tt=256),
                hT_in[:, :, t * 128:(t + 2) * 128].rearrange("k p tt -> p k tt"),
                waits=war)
        t_h = g_h[pr]
        t_at = p.dma_rr("in_a", NL, L, at_t[L][:], ao_in[t * 128:(t + 1) * 128, :],
                     waits=[v["lnout"]] if v["lnout"] else wready)
        t_x = p.dma_rr("in_x", NL, L, x_t[L][:], x_in[t * 128:(t + 1) * 128, :],
                    waits=[v["lnout"]] if v["lnout"] else wready)
        yield

        P = t % 3
        t_mm = None
        for n in range(2):
            for kk in range(32):
                t_mm = p.emit("tensor", lambda e, n=n, kk=kk, P=P, t=t, pr=pr: e.matmul(
                    out=ops[P][:, n * 512:(n + 1) * 512],
                    lhsT=hT_c[pr % 2][:, kk * 256 + (t % 2) * 128:
                                      kk * 256 + (t % 2) * 128 + 128],
                    rhs=w2h[:, kk * D + n * 512:kk * D + (n + 1) * 512],
                    start=(kk == 0), stop=False),
                    waits=([t_h] + wready +
                           ([g_u[t - 3]] if (t - 3) in g_u else [])) if kk == 0 and n == 0 else [])
            t_mm = p.emit("tensor", lambda e, n=n, P=P: e.matmul(
                out=ops[P][:, n * 512:(n + 1) * 512],
                lhsT=c["ones_bf"][0:1, 0:128],
                rhs=w2b_h[:, n * 512:(n + 1) * 512],
                start=False, stop=True), waits=[])
            yield
        v["mm"] = t_mm
        g_mm[pr] = t_mm

        t_u = p.emit("vector", lambda e, L=L, P=P: e.scalar_tensor_tensor(
            out=u[L][:], in0=ops[P][:], scalar=1.0, in1=at_t[L][:],
            op0=ALU.mult, op1=ALU.add, accum_out=s1col[L][:, 0:1]),
            waits=[t_mm, t_at] + ([v["lnsq"]] if v["lnsq"] else []))
        g_u[t] = t_u
        v["u"] = t_u
        yield
        w_ln = [t_u] + ([v["lnout"]] if v["lnout"] else [])
        t_ao, t_sq1 = ln_stats_apply(p, lnb[L], u[L], s1col[L], ao[L], 1e-12,
                                     w_ln, None, "ln2")
        t_m1 = p.emit("vector", lambda e, L=L: e.tensor_mul(
            ao2[L][:], ao[L][:], gb2[:]), waits=[t_ao] + t_bc)
        t_aof = p.emit("vector", lambda e, L=L: e.tensor_add(
            aof[L][:], ao2[L][:], bb2[:]), waits=[t_m1])
        t_u2 = p.emit("vector", lambda e, L=L: e.scalar_tensor_tensor(
            out=u2[L][:], in0=aof[L][:], scalar=1.0, in1=x_t[L][:],
            op0=ALU.mult, op1=ALU.add, accum_out=s1col[L][:, 0:1]),
            waits=[t_aof, t_x])
        yield
        t_x1, t_sq2 = ln_stats_apply(p, lnb[L], u2[L], s1col[L], x1_t[L],
                                     1e-5, [t_u2, p.stok(f"o_x1{L}")], None, "lnb")
        v["lnsq"] = t_sq2
        v["lnout"] = t_x1
        p.dma_rr("o_x1", NL, L, x1_out[t * 128:(t + 1) * 128, :], x1_t[L][:],
                 waits=[t_x1])

    for t0 in range(0, NT, NL):
        interleave_pairs([tile_body(t, t % NL)
                          for t in range(t0, min(t0 + NL, NT))])


def build_B2b():
    p = Prog("B2b")
    c = common_consts(p)
    io = dict(hT=p.din("hT", [32, 128, T], BF16),
              attn_out=p.din("attn_out", [T, D], F32),
              x=p.din("x", [T, D], F32),
              w2_a=p.din("w2_a", [32 * 128 + 1, D], F32),
              g2r=p.din("g2r", [1, D], F32),
              be2r=p.din("be2r", [1, D], F32),
              x1=p.dout("x1", [T, D], F32))
    emit_B2b(p, c, io)
    return p.finalize()



def emit_C(p, c, io):
    x1_in = io["x1"]
    gm_in = io["gmr"]
    bb_in = io["bbr"]
    mu_in = io["mur"]
    sc_in = io["scl"]
    Praw_out = io["Praw"]
    Sloc_out = io["Sloc"]
    E_out = io["E"]
    Zc_out = io["Zc"]

    onescol_f = p.sb("onescol_f", [128, 1], F32)
    gmrow = p.sb("gmrow", [1, D], F32)
    bbrow = p.sb("bbrow", [1, D], F32)
    gb = p.sb("gb", [128, D], F32)
    bb = p.sb("bb", [128, D], F32)
    mustage = p.sb("mustage", [128, NJ * ES], F32)
    musq = p.sb("musq", [128, NJ * ES], F32)
    srow = p.sb("srow", [1, ES], F32)
    srow2 = p.sb("srow2", [1, ES], F32)
    rmub = p.sb("rmub", [128, ES], F32)
    mun_h = p.sb("mun_h", [128, NJ * ES], BF16)
    sccol = p.sb("sccol", [128, 1], F32)
    sclrow = p.sb("sclrow", [1, 1], F32)
    NL = 2
    x1_t = [p.sb(f"x1_t{i}", [128, D], F32) for i in range(NL)]
    s1col = [p.sb(f"s1col{i}", [128, 1], F32) for i in range(NL)]
    lnb = [dict(sqw=p.sb(f"sqw{i}", [128, D], F32),
                mcol=p.sb(f"mcol{i}", [128, 1], F32),
                vcol=p.sb(f"vcol{i}", [128, 1], F32),
                tcol=p.sb(f"tcol{i}", [128, 1], F32)) for i in range(NL)]
    xn = [p.sb(f"xn{i}", [128, D], F32) for i in range(NL)]
    xn2a = [p.sb(f"xn2a{i}", [128, D], F32) for i in range(NL)]
    xn2 = [p.sb(f"xn2{i}", [128, D], F32) for i in range(NL)]
    nrm = [p.sb(f"nrm{i}", [128, 1], F32) for i in range(NL)]
    nrm2 = [p.sb(f"nrm2{i}", [128, 1], F32) for i in range(NL)]
    rn = [p.sb(f"rn{i}", [128, 1], F32) for i in range(NL)]
    xu_bf = [p.sb(f"xu_bf{i}", [128, D], BF16) for i in range(NL)]
    xnh = [p.sb(f"xnh{i}", [128, D], BF16) for i in range(NL)]
    xuT_t = [p.sb(f"xuT_t{i}", [128, D], BF16) for i in range(NL)]
    E_bf = [p.sb(f"E_bf{i}", [128, ES], BF16) for i in range(NL)]
    zc_col = [p.sb(f"zc_col{i}", [128, 1], F32) for i in range(NL)]
    Psb = p.sb("Psb", [128, NJ * ES], F32)
    slrow = p.sb("slrow", [1, ES], F32)

    SL_ps = p.ps("SL_ps", [1, ES], F32)
    Pps_a = p.ps("Pps_a", [128, 1024], F32)
    Pps_b = p.ps("Pps_b", [128, 1024], F32)
    L_ps = p.ps("L_ps", [128, ES], F32)
    p.push_region("pp")
    psA = p.ps("psA", [128, 512], F32)

    t_oc = p.emit("vector", lambda e: e.memset(onescol_f[:], 1.0))
    t_g = p.dma("in", gmrow[:], gm_in[:], waits=c["ready"])
    t_b = p.dma("in", bbrow[:], bb_in[:])
    t_mu = p.dma("in", mustage[:].rearrange("p (j s) -> p j s", s=ES),
                 mu_in[:].rearrange("(j p) s -> p j s", p=128))
    t_sc = p.dma("in", sclrow[:], sc_in[:])
    t_in = p.stok("in")
    tb1 = bcast_row_f32(p, c, gb, gmrow, D, psA, waits=[t_in])
    tb2 = bcast_row_f32(p, c, bb, bbrow, D, psA, waits=[t_in] + tb1)
    t_bc = tb1 + tb2
    t_scm = p.emit("tensor", lambda e: e.matmul(
        out=psA[:, 0:1], lhsT=c["ones_f"][:, :], rhs=sclrow[:, 0:1],
        start=True, stop=True), waits=[t_in] + t_bc)
    t_scc = p.emit("vector", lambda e: e.tensor_copy(sccol[:], psA[:, 0:1]),
                   waits=[t_scm])
    t_msq = p.emit("scalar", lambda e: e.activation(
        out=musq[:], in_=mustage[:], func=AF.Square), waits=[t_in])
    t_s2m = None
    for j in range(NJ):
        t_s2m = p.emit("tensor", lambda e, j=j: e.matmul(
            out=SL_ps[0:1, :], lhsT=onescol_f[:, 0:1],
            rhs=musq[:, j * ES:(j + 1) * ES],
            start=(j == 0), stop=(j == NJ - 1)),
            waits=[t_msq, t_oc, t_scm] if j == 0 else [])
    t_sr = p.emit("vector", lambda e: e.tensor_copy(srow[:], SL_ps[0:1, :]),
                  waits=[t_s2m])
    t_sq = p.emit("scalar", lambda e: e.activation(
        out=srow2[:], in_=srow[:], func=AF.Sqrt), waits=[t_sr])
    t_mx = p.emit("vector", lambda e: e.tensor_scalar_max(
        srow[:], srow2[:], 1e-12), waits=[t_sq])
    t_rc = p.emit("vector", lambda e: e.reciprocal(srow2[:], srow[:]),
                  waits=[t_mx])
    tbm = bcast_row_f32(p, c, rmub, srow2, ES, psA, waits=[t_rc])
    t_mn = []
    for j in range(NJ):
        t_mn.append(p.emit("vector", lambda e, j=j: e.tensor_mul(
            mun_h[:, j * ES:(j + 1) * ES], mustage[:, j * ES:(j + 1) * ES],
            rmub[:]), waits=tbm))
    p.pop_region()
    tps = [p.ps(f"tps{i}", [128, 1024], BF16) for i in range(NL)]

    t_prev_E = [None]
    pv = [dict(sq=None, xuT=None, P=None) for _ in range(NL)]

    def tile_body(t, L):
        v = pv[L]
        t_x1 = p.dma_rr("in_x", NL, L, x1_t[L][:],
                        x1_in[t * 128:(t + 1) * 128, :],
                        waits=[v["sq"]] if v["sq"] else [t_in])
        t_s1 = p.emit("vector", lambda e, L=L: e.reduce_sum(
            out=s1col[L][:, 0:1], in_=x1_t[L][:], axis=AX.X), waits=[t_x1])
        yield
        t_xn, t_sqa = ln_stats_apply(p, lnb[L], x1_t[L], s1col[L], xn[L], 1e-5,
                                     [t_x1, t_s1], None, "moe_ln")
        yield
        t_m1 = p.emit("vector", lambda e, L=L: e.tensor_mul(
            xn2a[L][:], xn[L][:], gb[:]), waits=[t_xn] + t_bc)
        t_xn2 = p.emit("vector", lambda e, L=L: e.tensor_add(
            xn2[L][:], xn2a[L][:], bb[:]), waits=[t_m1])
        yield
        t_sqb = p.emit("scalar", lambda e, L=L: e.activation(
            out=lnb[L]["sqw"][:], in_=xn2[L][:], func=AF.Square,
            accum_out=nrm[L][:, 0:1]), waits=[t_xn2, t_sqa])
        v["sq"] = t_sqb
        yield
        t_nsq = p.emit("scalar", lambda e, L=L: e.activation(
            out=nrm2[L][:, 0:1], in_=nrm[L][:, 0:1], func=AF.Sqrt),
            waits=[t_sqb])
        t_nmx = p.emit("vector", lambda e, L=L: e.tensor_scalar_max(
            nrm[L][:, 0:1], nrm2[L][:, 0:1], 1e-12), waits=[t_nsq])
        t_nrc = p.emit("vector", lambda e, L=L: e.reciprocal(
            nrm2[L][:, 0:1], nrm[L][:, 0:1]), waits=[t_nmx])
        t_rn = p.emit("vector", lambda e, L=L: e.tensor_mul(
            rn[L][:, 0:1], nrm2[L][:, 0:1], sccol[:, 0:1]),
            waits=[t_nrc, t_scc])
        yield
        t_xu = p.emit("vector", lambda e, L=L: e.tensor_scalar(
            out=xu_bf[L][:], in0=xn2[L][:], scalar1=rn[L][:, 0:1],
            scalar2=None, op0=ALU.mult), waits=[t_rn])
        t_xnh = p.emit("scalar", lambda e, L=L: e.activation(
            out=xnh[L][:], in_=xn2[L][:], func=AF.Copy),
            waits=[t_xn2] + ([v["P"]] if v["P"] else []))
        yield

        t_xuT = transpose_tile(p, c, tps[L], xu_bf[L], xuT_t[L][:], [t_xu],
                               [v["xuT"]] if v["xuT"] else [], nblk=NJ)
        v["xuT"] = t_xuT
        yield
        t_lg = None
        for j in range(NJ):
            t_lg = p.emit("tensor", lambda e, j=j, L=L: e.matmul(
                out=L_ps[:, :], lhsT=xuT_t[L][:, j * 128:(j + 1) * 128],
                rhs=mun_h[:, j * ES:(j + 1) * ES],
                start=(j == 0), stop=(j == NJ - 1)),
                waits=([t_xuT] + t_mn +
                       ([t_prev_E[0]] if t_prev_E[0] else [])) if j == 0 else [])
        t_E = p.emit("scalar", lambda e, L=L: e.activation(
            out=E_bf[L][:], in_=L_ps[:], func=AF.Exp,
            accum_out=zc_col[L][:, 0:1]),
            waits=[t_lg, p.stok(f"o_E{L}"), p.stok(f"o_Z{L}")])
        t_prev_E[0] = t_E
        p.dma_rr("o_E", NL, L, E_out[t], E_bf[L][:], waits=[t_E])
        p.dma_rr("o_Z", NL, L, Zc_out[t], zc_col[L][:, 0:1], waits=[t_E])
        yield
        t_P = None
        for jd in range(NJ):
            pps = Pps_a if jd < 4 else Pps_b
            col = (jd % 4) * ES
            t_P = p.emit("tensor", lambda e, jd=jd, pps=pps, col=col, t=t, L=L: e.matmul(
                out=pps[:, col:col + ES],
                lhsT=xnh[L][:, jd * 128:(jd + 1) * 128],
                rhs=E_bf[L][:],
                start=(t == 0 and jd % 2 == 0), stop=(t == NT - 1 and jd % 2 == 1)),
                waits=[t_E, t_xnh] if jd == 0 else [])
        v["P"] = t_P
        t_SL[0] = p.emit("tensor", lambda e, t=t, L=L: e.matmul(
            out=SL_ps[0:1, :], lhsT=c["onescol_bf"][:, 0:1], rhs=E_bf[L][:],
            start=(t == 0), stop=(t == NT - 1)),
            waits=[t_E] + ([t_sr] if t == 0 else []))

    t_SL = [None]
    for t0 in range(0, NT, NL):
        interleave_pairs([tile_body(t, t % NL)
                          for t in range(t0, min(t0 + NL, NT))])
    t_SL = t_SL[0]

    t_Pall = [pv[0]["P"], pv[1]["P"]]
    t_e1 = p.emit("vector", lambda e: e.tensor_copy(
        Psb[:, 0:1024], Pps_a[:]), waits=t_Pall)
    t_e2 = p.emit("vector", lambda e: e.tensor_copy(
        Psb[:, 1024:2048], Pps_b[:]), waits=t_Pall)
    t_e3 = p.emit("vector", lambda e: e.tensor_copy(
        slrow[:], SL_ps[0:1, :]), waits=[t_SL])
    p.dma("o_P", Praw_out[:], Psb[:], waits=[t_e1, t_e2])
    p.dma("o_S", Sloc_out[:], slrow[:], waits=[t_e3])


def build_C():
    p = Prog("C")
    c = common_consts(p)
    io = dict(x1=p.din("x1", [T, D], F32),
              gmr=p.din("gmr", [1, D], F32),
              bbr=p.din("bbr", [1, D], F32),
              mur=p.din("mur", [D, ES], F32),
              scl=p.din("scl", [1, 1], F32),
              Praw=p.dout("Praw", [128, NJ * ES], F32),
              Sloc=p.dout("Sloc", [1, ES], F32),
              E=p.dout("E", [NT, 128, ES], BF16),
              Zc=p.dout("Zc", [NT, 128, 1], F32))
    emit_C(p, c, io)
    return p.finalize()



def emit_D(p, c, io, fused=False, pr_wait=None, edges_out=None):
    x1_in = io["x1"]
    E_in = io["E"]
    Zc_in = io["Zc"]
    Praw_in = io["Praw"]
    Sloc_in = io["Sloc"]
    w1_in = io["w1e"]
    b1c_in = io["b1c"]
    w2_in = io["w2e"]
    x2_out = io["x2"]
    scratch = io["so_scratch"]

    Psb = p.sb("Psb", [128, NJ * ES], F32)
    Srow = p.sb("Srow", [1, ES], F32)
    rSrow = p.sb("rSrow", [1, ES], F32)
    rSb = p.sb("rSb", [128, ES], F32)
    slotin_h = p.sb("slotin_h", [128, NJ * ES], BF16)
    wstage = [p.sb(f"wstage{i}", [128, 2048], F32) for i in range(4)]
    whbuf = p.sb("whbuf", [128, EXP * 2048], BF16)
    b1cols = p.sb("b1cols", [128, EXP * 2], F32)
    b2stage = [p.sb(f"b2stage{i}", [1, D], F32) for i in range(2)]
    b2cur = [p.sb(f"b2cur{i}", [1, D], BF16) for i in range(2)]
    HT_sb = p.sb("HT_sb", [128, EXP * 32], BF16)
    so_stage = [p.sb(f"so_stage{i}", [16, D], BF16) for i in range(2)]
    soT_sb = p.sb("soT_sb", [128, 2 * D], BF16)
    NL = 2
    E_t = [p.sb(f"E_t{i}", [128, ES], BF16) for i in range(NL)]
    zc_t = [p.sb(f"zc_t{i}", [128, 1], F32) for i in range(NL)]
    rZ = [p.sb(f"rZ{i}", [128, 1], F32) for i in range(NL)]
    comb = [p.sb(f"comb{i}", [128, ES], BF16) for i in range(NL)]
    combT = [p.sb(f"combT{i}", [128, ES], BF16) for i in range(NL)]
    x1_t = [p.sb(f"x1_t{i}", [128, D], F32) for i in range(NL)]
    u = [p.sb(f"u{i}", [128, D], F32) for i in range(NL)]
    s1col = [p.sb(f"s1col{i}", [128, 1], F32) for i in range(NL)]
    lnb = [dict(sqw=p.sb(f"sqw{i}", [128, D], F32),
                mcol=p.sb(f"mcol{i}", [128, 1], F32),
                vcol=p.sb(f"vcol{i}", [128, 1], F32),
                tcol=p.sb(f"tcol{i}", [128, 1], F32)) for i in range(NL)]
    x2_t = [p.sb(f"x2_t{i}", [128, D], F32) for i in range(NL)]

    p.push_region("pp")
    psA = p.ps("psA", [128, 512], F32)
    HTps = [p.ps(f"HTps{i}", [128, 32], F32) for i in range(2)]
    SOps = [p.ps(f"SOps{i}", [16, D], F32) for i in range(2)]

    if fused:
        tprev = p.dma("in_P", Psb[:], Praw_in[:], waits=[pr_wait],
                      queue="gpsimd")
        tprevS = p.dma("in_S", Srow[:], Sloc_in[:], waits=[pr_wait],
                       queue="gpsimd")
    else:
        tprev = None
        for cc in range(NCORES):
            tprev = p.dma("in_P", Psb[:], Praw_in[cc],
                          waits=[tprev] if tprev else c["ready"], queue="gpsimd",
                          accum_op=(ALU.add if cc else ALU.bypass))
        tprevS = None
        for cc in range(NCORES):
            tprevS = p.dma("in_S", Srow[:], Sloc_in[cc],
                           waits=[tprevS] if tprevS else c["ready"], queue="gpsimd",
                           accum_op=(ALU.add if cc else ALU.bypass))
    t_rS = p.emit("vector", lambda e: e.reciprocal(rSrow[:], Srow[:]),
                  waits=[tprevS])
    tbs = bcast_row_f32(p, c, rSb, rSrow, ES, psA, waits=[t_rS])
    t_si = []
    for jd in range(NJ):
        t_si.append(p.emit("vector", lambda e, jd=jd: e.tensor_mul(
            slotin_h[:, jd * ES:(jd + 1) * ES], Psb[:, jd * ES:(jd + 1) * ES],
            rSb[:]), waits=tbs + [tprev]))

    t_b1 = p.dma("in", b1cols[:], b1c_in[:], waits=c["ready"])
    w1ready = [t_b1]
    wcast = {}
    for ee in range(EXP):
        tw = p.dma_rr("in_w", 4, ee,
                      wstage[ee % 4][:].rearrange("p (k h) -> p k h", h=MOE_H),
                      w1_in[ee].rearrange("(k p) h -> p k h", p=128),
                      waits=wcast.get(ee - 4, [t_b1]))
        cs = [p.emit("scalar", lambda e, ee=ee: e.activation(
            out=whbuf[:, ee * 2048:(ee + 1) * 2048], in_=wstage[ee % 4][:],
            func=AF.Copy), waits=[tw])]
        wcast[ee] = cs
        w1ready += cs

    t_H_ev = []
    hevs = {}
    t_mm = None
    for ee in range(EXP):
        for kk in range(2):
            for jd in range(NJ):
                t_mm = p.emit("tensor", lambda e, ee=ee, kk=kk, jd=jd: e.matmul(
                    out=HTps[ee % 2][:, kk * 16:(kk + 1) * 16],
                    lhsT=whbuf[:, ee * 2048 + jd * MOE_H + kk * 128:
                               ee * 2048 + jd * MOE_H + (kk + 1) * 128],
                    rhs=slotin_h[:, jd * ES + ee * SLOT:jd * ES + (ee + 1) * SLOT],
                    start=(kk == 0 and jd == 0), stop=(kk == 1 and jd == NJ - 1)),
                    waits=(t_si + w1ready + hevs.get(ee - 2, [])) if kk == 0 and jd == 0 else [])
        evs = []
        for kk in range(2):
            evs.append(p.emit("scalar", lambda e, ee=ee, kk=kk: e.activation(
                out=HT_sb[:, ee * 32 + kk * 16:ee * 32 + (kk + 1) * 16],
                in_=HTps[ee % 2][:, kk * 16:(kk + 1) * 16], func=AF.Gelu,
                bias=b1cols[:, ee * 2 + kk:ee * 2 + kk + 1]), waits=[t_mm]))
        hevs[ee] = evs
        t_H_ev += evs
    t_w1_done = t_mm
    prev_Hev = hevs[EXP - 1] + hevs[EXP - 2]

    w2ready = []
    wcast2 = {}
    for ee in range(EXP):
        tw = p.dma_rr("in_w", 4, ee,
                      wstage[ee % 4][:].rearrange("p (k d) -> p k d", d=D),
                      w2_in[ee, 0:256, :].rearrange("(k p) d -> p k d", p=128),
                      waits=wcast2.get(ee - 4, [t_w1_done] + prev_Hev))
        cs = [p.emit("scalar", lambda e, ee=ee: e.activation(
            out=whbuf[:, ee * 2048:(ee + 1) * 2048], in_=wstage[ee % 4][:],
            func=AF.Copy), waits=[tw] + ([t_w1_done] if ee < 4 else []))]
        wcast2[ee] = cs
        w2ready += cs

    soev = {}
    b2cast = {}
    t_so_st = []
    for ee in range(EXP):
        t_bd = p.dma_rr("in_b2", 2, ee, b2stage[ee % 2][:],
                        w2_in[ee, 256:257, :],
                        waits=b2cast.get(ee - 2, [w2ready[0]]))
        t_bc2 = p.emit("scalar", lambda e, ee=ee: e.activation(
            out=b2cur[ee % 2][:], in_=b2stage[ee % 2][:], func=AF.Copy),
            waits=[t_bd] + soev.get(ee - 2, []))
        b2cast[ee] = [t_bc2]
        t_mm = None
        for n in range(2):
            for kk in range(2):
                t_mm = p.emit("tensor", lambda e, ee=ee, n=n, kk=kk: e.matmul(
                    out=SOps[ee % 2][:, n * 512:(n + 1) * 512],
                    lhsT=HT_sb[:, ee * 32 + kk * 16:ee * 32 + (kk + 1) * 16],
                    rhs=whbuf[:, ee * 2048 + kk * D + n * 512:
                              ee * 2048 + kk * D + (n + 1) * 512],
                    start=(kk == 0), stop=False),
                    waits=(w2ready + [t_bc2] +
                           soev.get(ee - 2, [])) if kk == 0 and n == 0 else [])
            t_mm = p.emit("tensor", lambda e, ee=ee, n=n: e.matmul(
                out=SOps[ee % 2][:, n * 512:(n + 1) * 512],
                lhsT=c["ones_bf"][0:1, 0:SLOT],
                rhs=b2cur[ee % 2][0:1, n * 512:(n + 1) * 512],
                start=False, stop=True), waits=[])
        t_ev = p.emit("vector", lambda e, ee=ee: e.tensor_copy(
            so_stage[ee % 2][0:16, :], SOps[ee % 2][:, :]),
            waits=[t_mm, p.stok(f"so1{ee % 2}")])
        soev[ee] = [t_ev]
        t_st = p.dma_rr("so1", 2, ee, scratch[ee * 16:(ee + 1) * 16, :],
                        so_stage[ee % 2][0:16, :], waits=[t_ev])
        t_so_st.append(t_st)
    t_bi = p.dma("so2", soT_sb[:].rearrange("p (r d) -> p r d", d=D),
                 scratch.rearrange("(r p) d -> p r d", p=128), waits=t_so_st)

    p.pop_region()
    Ops = [p.ps(f"Ops{i}", [128, 1024], F32) for i in range(NL)]
    tpsL = [p.ps(f"tpsL{i}", [128, 256], BF16) for i in range(NL)]

    pv = [dict(mm=None, u=None, lnsq=None, lnout=None, comb=None, combT=None)
          for _ in range(NL)]

    def tile_body(t, L):
        v = pv[L]
        t_E = p.dma_rr("in_E", NL, L, E_t[L][:], E_in[t],
                       waits=[v["comb"]] if v["comb"] else w2ready)
        t_Z = p.dma_rr("in_Z", NL, L, zc_t[L][:], Zc_in[t],
                       waits=[v["comb"]] if v["comb"] else w2ready)
        t_x1 = p.dma_rr("in_x", NL, L, x1_t[L][:],
                        x1_in[t * 128:(t + 1) * 128, :],
                        waits=[v["u"]] if v["u"] else w2ready)
        yield
        t_rz = p.emit("vector", lambda e, L=L: e.reciprocal(
            rZ[L][:], zc_t[L][:]), waits=[t_Z])
        t_cb = p.emit("vector", lambda e, L=L: e.tensor_scalar(
            out=comb[L][:], in0=E_t[L][:], scalar1=rZ[L][:, 0:1], scalar2=None,
            op0=ALU.mult), waits=[t_rz, t_E])
        v["comb"] = t_cb
        yield
        t_tp2 = None
        for kk in range(2):
            t_tp2 = p.emit("tensor", lambda e, kk=kk, L=L: e.transpose(
                out=tpsL[L][:, kk * 128:(kk + 1) * 128],
                in_=comb[L][:, kk * 128:(kk + 1) * 128],
                identity=c["ident"][:]),
                waits=([t_cb] + ([v["combT"]] if v["combT"] else [])) if kk == 0 else [])
        t_cT = p.emit("vector", lambda e, L=L: e.tensor_copy(
            combT[L][:], tpsL[L][:]),
            waits=[t_tp2])
        v["combT"] = t_cT
        yield
        t_mm = None
        for n in range(2):
            for kk in range(2):
                t_mm = p.emit("tensor", lambda e, n=n, kk=kk, L=L: e.matmul(
                    out=Ops[L][:, n * 512:(n + 1) * 512],
                    lhsT=combT[L][:, kk * 128:(kk + 1) * 128],
                    rhs=soT_sb[:, kk * D + n * 512:kk * D + (n + 1) * 512],
                    start=(kk == 0), stop=(kk == 1)),
                    waits=([t_cT, t_bi] + ([v["u"]] if v["u"] else [])) if kk == 0 and n == 0 else [])
        v["mm"] = t_mm
        yield
        t_u = p.emit("vector", lambda e, L=L: e.scalar_tensor_tensor(
            out=u[L][:], in0=Ops[L][:], scalar=1.0, in1=x1_t[L][:],
            op0=ALU.mult, op1=ALU.add, accum_out=s1col[L][:, 0:1]),
            waits=[t_mm, t_x1] + ([v["lnsq"]] if v["lnsq"] else []))
        v["u"] = t_u
        yield
        w_ln = [t_u] + ([v["lnout"]] if v["lnout"] else [])
        t_x2, t_sq1 = ln_stats_apply(p, lnb[L], u[L], s1col[L], x2_t[L], 1e-5,
                                     w_ln + [p.stok(f"o_x2{L}"),
                                             p.stok("o_edge")], None, "lnx2")
        v["lnsq"] = t_sq1
        v["lnout"] = t_x2
        t_st2 = p.dma_rr("o_x2", NL, L, x2_out[t * 128:(t + 1) * 128, :],
                         x2_t[L][:], waits=[t_x2])
        if edges_out is not None and t == 0:
            p.dma("o_edge", edges_out[0:HALO, :], x2_t[L][0:HALO, :],
                  waits=[t_x2])
        if edges_out is not None and t == NT - 1:
            p.dma("o_edge", edges_out[HALO:2 * HALO, :],
                  x2_t[L][128 - HALO:128, :], waits=[t_x2])

    for t0 in range(0, NT, NL):
        interleave_pairs([tile_body(t, t % NL)
                          for t in range(t0, min(t0 + NL, NT))])


def build_D():
    p = Prog("D")
    c = common_consts(p)
    io = dict(x1=p.din("x1", [T, D], F32),
              E=p.din("E", [NT, 128, ES], BF16),
              Zc=p.din("Zc", [NT, 128, 1], F32),
              Praw=p.din("Praw", [NCORES, 128, NJ * ES], F32),
              Sloc=p.din("Sloc", [NCORES, 1, ES], F32),
              w1e=p.din("w1e", [EXP, D, MOE_H], F32),
              b1c=p.din("b1c", [128, EXP * 2], F32),
              w2e=p.din("w2e", [EXP, MOE_H + 1, D], F32),
              x2=p.dout("x2", [T, D], F32),
              so_scratch=p.dtmp("so_scratch", [ES, D], BF16))
    emit_D(p, c, io)
    return p.finalize()



def emit_E(p, c, io, xe_wait=()):
    TE = 2176
    xe_in = io["xe"]
    mA_in = io["maskA"]
    mB_in = io["maskB"]
    wq_in = io["wqT"]
    wk_in = io["wkT"]
    wv_in = io["wvT"]
    wo_in = io["woT_a"]
    x3_out = io["x3"]

    wqh = p.sb("wqh", [128, NJ * D], BF16)
    wkvh = p.sb("wkvh", [128, NJ * D], BF16)
    wob_f = p.sb("wob_f", [1, D], F32)
    wob_h = p.sb("wob_h", [1, D], BF16)
    xeT = p.sb("xeT", [128, NJ * TE], BF16)
    kT = p.sb("kT", [128, NJ * TE], BF16)
    v_sb = p.sb("v_sb", [128, NTE * D], BF16)
    NL = 2
    qz = [[p.sb(f"qz{i}_{L}", [128, NJ * 128], BF16) for i in range(2)]
          for L in range(NL)]
    EA = p.sb("EA", [128, 512], BF16)
    EB = p.sb("EB", [64, 512], BF16)
    rz_sb = [p.sb(f"rz_sb{L}", [128, NH], F32) for L in range(NL)]
    mA = p.sb("mA", [128, 128], F32)
    mB = p.sb("mB", [64, 128], F32)
    rzexp0 = p.sb("rzexp", [128, D], F32)
    rzexp = [rzexp0, rzexp0]
    actx0 = p.sb("actx", [128, D], BF16)
    actx = [actx0, actx0]
    ctxT = [p.sb(f"ctxT{L}", [128, D], BF16) for L in range(NL)]
    xr_t0 = p.sb("xr_t", [128, D], F32)
    xr_t = [xr_t0, xr_t0]
    u = [p.sb(f"uE{L}", [128, D], F32) for L in range(NL)]
    s1col = [p.sb(f"s1col{L}", [128, 1], F32) for L in range(NL)]
    sqw0 = p.sb("sqwE", [128, D], F32)
    lnb = [dict(sqw=sqw0,
                mcol=p.sb(f"mcolE{L}", [128, 1], F32),
                vcol=p.sb(f"vcolE{L}", [128, 1], F32),
                tcol=p.sb(f"tcolE{L}", [128, 1], F32)) for L in range(NL)]
    x3_t0 = p.sb("x3_t", [128, D], F32)
    x3_t = [x3_t0, x3_t0]

    tps = p.ps("tps", [128, 1024], BF16)
    qps = p.ps("qps", [128, 1024], F32)

    eighth, t_ei = p.const_col(0.125)
    t_mA = p.dma("in", mA[:], mA_in[:], waits=c["ready"])
    t_mB = p.dma("in", mB[:], mB_in[:], waits=c["ready"])
    mA4 = p.sb("mA4", [128, 512], BF16)
    mB4 = p.sb("mB4", [64, 512], BF16)
    t_m4 = []
    for r in range(4):
        t_m4.append(p.emit("scalar", lambda e, r=r: e.activation(
            out=mA4[:, r * 128:(r + 1) * 128], in_=mA[:], func=AF.Copy),
            waits=[t_mA]))
        t_m4.append(p.emit("scalar", lambda e, r=r: e.activation(
            out=mB4[0:64, r * 128:(r + 1) * 128], in_=mB[0:64, :],
            func=AF.Copy), waits=[t_mB]))

    p.push_region("w12")
    wstage = p.sb("wstage", [128, D], F32)
    xf_t = [p.sb(f"xf_t{i}", [128, D], F32) for i in range(2)]
    xh_t = [p.sb(f"xh_t{i}", [128, D], BF16) for i in range(2)]
    tpsB = p.ps("tpsB", [128, 1024], BF16)
    tps1 = [tps, tpsB]
    e_cast = {}
    e_ev = {}
    for te in range(NTE):
        LL = te % 2
        t_x = p.dma_rr("in_x", 2, LL, xf_t[LL][:],
                       xe_in[te * 128:(te + 1) * 128, :],
                       waits=([e_cast[te - 2]] if te >= 2 else c["ready"]) +
                             list(xe_wait))
        t_cast = p.emit("scalar", lambda e, LL=LL: e.activation(
            out=xh_t[LL][:], in_=xf_t[LL][:], func=AF.Copy),
            waits=[t_x] + ([e_ev[te - 2]] if te >= 2 else []))
        e_cast[te] = t_cast
        xeT_view = xeT[:].rearrange("p (j t) -> p j t", t=TE)[:, :, te * 128:(te + 1) * 128]
        e_ev[te] = transpose_tile(p, c, tps1[LL], xh_t[LL], xeT_view, [t_cast],
                                  [e_ev[te - 2]] if te >= 2 else [], nblk=NJ)
    t_xeT = p.tok("vector")

    def load_w(name_in, dest, waits):
        toks = []
        prev = list(waits)
        for ch in range(NJ):
            tw = p.dma("in", wstage[:],
                       name_in[ch * 128:(ch + 1) * 128, :], waits=prev)
            cs = [p.emit("scalar", lambda e, ch=ch, dest=dest: e.activation(
                out=dest[:, ch * D:(ch + 1) * D],
                in_=wstage[:], func=AF.Copy),
                waits=[p.stok("in")])]
            prev = cs
            toks += cs
        return toks

    t_wk = load_w(wk_in, wkvh, c["ready"])
    t_kml = None
    kev = []
    nch = [(i, min(512, TE - 512 * i)) for i in range((TE + 511) // 512)]
    ib = 0
    for m in range(NJ):
        for (ci, nn) in nch:
            b = ib % 2
            ib += 1
            t_kml = None
            for kk in range(NJ):
                t_kml = p.emit("tensor", lambda e, m=m, ci=ci, nn=nn, kk=kk, b=b: e.matmul(
                    out=qps[:, b * 512:b * 512 + nn],
                    lhsT=wkvh[:, kk * D + m * 128:kk * D + (m + 1) * 128],
                    rhs=xeT[:, kk * TE + ci * 512:kk * TE + ci * 512 + nn],
                    start=(kk == 0), stop=(kk == NJ - 1)),
                    waits=([t_xeT] + t_wk + kev[-2:-1]) if kk == 0 else [])
            kev.append(p.emit("vector", lambda e, m=m, ci=ci, nn=nn, b=b: e.tensor_copy(
                kT[:, m * TE + ci * 512:m * TE + ci * 512 + nn],
                qps[:, b * 512:b * 512 + nn]),
                waits=[t_kml]))
    t_wv = load_w(wv_in, wkvh, kev)
    vev = []
    for te in range(NTE):
        for n in range(2):
            t_vml = None
            for kk in range(NJ):
                t_vml = p.emit("tensor", lambda e, te=te, n=n, kk=kk: e.matmul(
                    out=qps[:, n * 512:(n + 1) * 512],
                    lhsT=xeT[:, kk * TE + te * 128:kk * TE + (te + 1) * 128],
                    rhs=wkvh[:, kk * D + n * 512:kk * D + (n + 1) * 512],
                    start=(kk == 0), stop=(kk == NJ - 1)),
                    waits=(t_wv + vev[-2:-1]) if kk == 0 else [])
            vev.append(p.emit("vector", lambda e, te=te, n=n: e.tensor_copy(
                v_sb[:, te * D + n * 512:te * D + (n + 1) * 512],
                qps[:, n * 512:(n + 1) * 512]), waits=[t_vml]))
    t_wq = load_w(wq_in, wqh, vev)
    woh = wkvh
    t_wo = load_w(wo_in, woh, t_wq)
    t_wob = p.dma("in", wob_f[:], wo_in[D:D + 1, :], waits=t_wo)
    t_wo = t_wo + [p.emit("scalar", lambda e: e.activation(
        out=wob_h[:], in_=wob_f[:], func=AF.Copy), waits=[p.stok("in")])]
    t_z0 = p.emit("vector", lambda e: e.memset(qz[0][0][64:128, :], 0.0))
    t_z1 = p.emit("vector", lambda e: e.memset(qz[0][1][0:64, :], 0.0))
    t_z2 = p.emit("vector", lambda e: e.memset(qz[1][0][64:128, :], 0.0))
    t_z3 = p.emit("vector", lambda e: e.memset(qz[1][1][0:64, :], 0.0))
    t_z0 = [t_z0, t_z1, t_z2, t_z3]
    p.pop_region()
    SA = p.ps("SA", [128, 512], F32)
    SB = p.ps("SB", [64, 512], F32)
    Ups = p.ps("Ups", [128, 1024], F32)
    Zps = p.ps("Zps", [128, 16], F32)

    g = dict(q0=None, q1=None, ea=[], u=None, ctxT=None, rz=None)
    pv = [dict(Slast=None, lnsq=None, lnout=None, u=None) for _ in range(NL)]

    def tile_body(t, L):
        v = pv[L]
        t_qml = None
        for jm in range(NJ):
            for kk in range(NJ):
                t_qml = p.emit("tensor", lambda e, jm=jm, kk=kk, t=t: e.matmul(
                    out=qps[:, jm * 128:(jm + 1) * 128],
                    lhsT=wqh[:, kk * D + jm * 128:kk * D + jm * 128 + 128],
                    rhs=xeT[:, kk * TE + 32 + t * 128:kk * TE + 32 + (t + 1) * 128],
                    start=(jm % 4 == 0 and kk == 0),
                    stop=(jm % 4 == 3 and kk == NJ - 1)),
                    waits=(t_wq + [x for x in (g["q0"], g["q1"], g["u"]) if x]) if jm == 0 and kk == 0 else [])
        t_q0 = p.emit("vector", lambda e, L=L: e.tensor_copy(
            qz[L][0][0:64, :], qps[0:64, :]),
            waits=[t_qml] + t_z0 + ([v["Slast"]] if v["Slast"] else []))
        t_q1 = p.emit("vector", lambda e, L=L: e.tensor_copy(
            qz[L][1][64:128, :], qps[64:128, :]), waits=[t_qml])
        g["q0"], g["q1"] = t_q0, t_q1
        yield

        t_U_last = None
        t_Z_last = None
        ea_war = list(g["ea"])
        for gg in range(4):
            t_SA = None
            t_SB = None
            for i in range(4):
                h = 4 * gg + i
                jh = h // 2
                t_SA = p.emit("tensor", lambda e, h=h, jh=jh, i=i, t=t, L=L: e.matmul(
                    out=SA[:, i * 128:(i + 1) * 128],
                    lhsT=kT[:, jh * TE + t * 128:jh * TE + (t + 1) * 128],
                    rhs=qz[L][h % 2][:, jh * 128:(jh + 1) * 128],
                    start=(i == 0), stop=False),
                    waits=([t_q0, t_q1] + kev + ea_war) if i == 0 else [])
            t_SA = p.emit("tensor", lambda e: e.matmul(
                out=SA[:, 0:512], lhsT=c["ident"][:], rhs=mA4[:, 0:512],
                start=False, stop=True), waits=t_m4 if t == 0 and gg == 0 else [])
            for i in range(4):
                h = 4 * gg + i
                jh = h // 2
                t_SB = p.emit("tensor", lambda e, h=h, jh=jh, i=i, t=t, L=L: e.matmul(
                    out=SB[0:64, i * 128:(i + 1) * 128],
                    lhsT=kT[:, jh * TE + (t + 1) * 128:jh * TE + (t + 1) * 128 + 64],
                    rhs=qz[L][h % 2][:, jh * 128:(jh + 1) * 128],
                    start=(i == 0), stop=False),
                    waits=[] if i > 0 else ea_war)
            t_SB = p.emit("tensor", lambda e: e.matmul(
                out=SB[0:64, 0:512], lhsT=c["ident"][0:64, 0:64],
                rhs=mB4[0:64, 0:512], start=False, stop=True), waits=[])
            t_EA = p.emit("scalar", lambda e: e.activation(
                out=EA[:], in_=SA[:], func=AF.Exp, scale=eighth[:, 0:1]),
                waits=[t_SA, t_ei] + ea_war)
            t_EB = p.emit("scalar", lambda e: e.activation(
                out=EB[0:64, :], in_=SB[0:64, :], func=AF.Exp,
                scale=eighth[0:64, 0:1]), waits=[t_SB])
            ea_war = []
            for i in range(4):
                h = 4 * gg + i
                wU = []
                if i == 0:
                    wU = [t_EA, t_EB] + vev + ([g["actx"]] if g.get("actx") else [])
                t_U_last = p.emit("tensor", lambda e, h=h, i=i, t=t: e.matmul(
                    out=Ups[:, h * 64:(h + 1) * 64],
                    lhsT=EA[:, i * 128:(i + 1) * 128],
                    rhs=v_sb[:, t * D + h * 64:t * D + (h + 1) * 64],
                    start=(h % 8 == 0), stop=False), waits=wU)
                t_U_last = p.emit("tensor", lambda e, h=h, i=i, t=t: e.matmul(
                    out=Ups[:, h * 64:(h + 1) * 64],
                    lhsT=EB[0:64, i * 128:(i + 1) * 128],
                    rhs=v_sb[0:64, (t + 1) * D + h * 64:(t + 1) * D + (h + 1) * 64],
                    start=False, stop=(h % 8 == 7)), waits=[])
                t_Z_last = p.emit("tensor", lambda e, h=h, i=i, t=t: e.matmul(
                    out=Zps[:, h:h + 1],
                    lhsT=EA[:, i * 128:(i + 1) * 128],
                    rhs=c["onescol_bf"][:, 0:1],
                    start=(h == 0), stop=False),
                    waits=([g["rz"]] if h == 0 and g["rz"] else []))
                t_Z_last = p.emit("tensor", lambda e, h=h, i=i, t=t: e.matmul(
                    out=Zps[:, h:h + 1],
                    lhsT=EB[0:64, i * 128:(i + 1) * 128],
                    rhs=c["onescol_bf"][0:64, 0:1],
                    start=False, stop=(h == 15)), waits=[])
            ea_war = [t_U_last, t_Z_last]
        v["Slast"] = t_SB
        g["ea"] = ea_war
        t_rz = p.emit("vector", lambda e, L=L: e.reciprocal(
            rz_sb[L][:, 0:NH], Zps[:, 0:NH]), waits=[t_Z_last])
        g["rz"] = t_rz
        t_rze = p.emit("vector", lambda e, L=L: e.tensor_copy(
            rzexp[L][:].rearrange("p (h w) -> p h w", w=HD),
            rz_sb[L][:, 0:NH].unsqueeze(2).broadcast_to([128, NH, HD])),
            waits=[t_rz])
        t_ac = p.emit("vector", lambda e, L=L: e.scalar_tensor_tensor(
            out=actx[L][:], in0=Ups[:], scalar=1.0, in1=rzexp[L][:],
            op0=ALU.mult, op1=ALU.mult), waits=[t_U_last, t_rze])
        g["actx"] = t_ac
        yield
        t_cT = transpose_tile(p, c, tps, actx[L], ctxT[L][:], [t_ac],
                              [g["ctxT"]] if g["ctxT"] else [t_xeT],
                              nblk=NJ)
        g["ctxT"] = t_cT
        yield
        t_omm = None
        for n in range(2):
            for kk in range(NJ):
                t_omm = p.emit("tensor", lambda e, n=n, kk=kk, L=L: e.matmul(
                    out=qps[:, n * 512:(n + 1) * 512],
                    lhsT=ctxT[L][:, kk * 128:(kk + 1) * 128],
                    rhs=woh[:, kk * D + n * 512:kk * D + (n + 1) * 512],
                    start=(kk == 0), stop=False),
                    waits=([t_cT] + t_wo +
                           [x for x in (g["q0"], g["q1"], g["u"]) if x]) if kk == 0 and n == 0 else [])
            t_omm = p.emit("tensor", lambda e, n=n: e.matmul(
                out=qps[:, n * 512:(n + 1) * 512],
                lhsT=c["ones_bf"][0:1, 0:128],
                rhs=wob_h[:, n * 512:(n + 1) * 512],
                start=False, stop=True), waits=[])
        t_xr = p.dma_rr("in_r", NL, L, xr_t[L][:],
                        xe_in[32 + t * 128:32 + (t + 1) * 128, :],
                        waits=[g["u"]] if g["u"] else t_wo)
        t_u = p.emit("vector", lambda e, L=L: e.scalar_tensor_tensor(
            out=u[L][:], in0=qps[:], scalar=1.0, in1=xr_t[L][:],
            op0=ALU.mult, op1=ALU.add, accum_out=s1col[L][:, 0:1]),
            waits=[t_omm, t_xr] + [x for x in (g.get("lnsq"),) if x])
        v["u"] = t_u
        g["u"] = t_u
        yield
        w_ln = [t_u] + ([v["lnout"]] if v["lnout"] else [])
        t_x3, t_sq1 = ln_stats_apply(p, lnb[L], u[L], s1col[L], x3_t[L], 1e-5,
                                     w_ln + [p.stok("o_x3")], None, "lnx3")
        v["lnsq"] = t_sq1
        g["lnsq"] = t_sq1
        v["lnout"] = t_x3
        p.dma("o_x3", x3_out[t * 128:(t + 1) * 128, :], x3_t[L][:],
              waits=[t_x3])

    for t0 in range(0, NT, NL):
        interleave_pairs([tile_body(t, t % NL)
                          for t in range(t0, min(t0 + NL, NT))])


def build_E():
    p = Prog("E")
    c = common_consts(p)
    io = dict(xe=p.din("xe", [2176, D], F32),
              maskA=p.din("maskA", [128, 128], F32),
              maskB=p.din("maskB", [64, 128], F32),
              wqT=p.din("wqT", [D, D], F32),
              wkT=p.din("wkT", [D, D], F32),
              wvT=p.din("wvT", [D, D], F32),
              woT_a=p.din("woT_a", [D + 1, D], F32),
              x3=p.dout("x3", [T, D], F32))
    emit_E(p, c, io)
    return p.finalize()



def emit_F1(p, c, io):
    x3_in = io["x3"]
    w1n_in = io["w1n"]
    w1T_in = io["w1T"]
    b1c_in = io["b1c"]
    hT_out = io["hT"]

    wn_t = [p.sb(f"wn_t{i}", [128, D], F32) for i in range(2)]
    absw = p.sb("absw", [128, D], F32)
    alpha1 = p.sb("alpha1", [128, 32], F32)
    alpha1s = p.sb("alpha1s", [128, 32], F32)
    b1c = p.sb("b1c_sb", [128, 32], F32)
    wstage = [p.sb(f"wstage{i}", [128, FF], F32) for i in range(2)]
    sw1T = p.sb("sw1T", [128, NJ * FF], BF16)
    xf_t = p.sb("xf_t", [128, D], F32)
    xh_t = p.sb("xh_t", [128, D], BF16)
    x3T = p.sb("x3T", [128, NJ * T], BF16)
    hbig = [p.sb(f"hbig{i}", [128, 8 * 512], BF16) for i in range(2)]

    tps = p.ps("tps", [128, 1024], BF16)
    hps = [p.ps(f"hps{i}", [128, 512], F32) for i in range(2)]

    t_abs = {}
    for i in range(32):
        t_w = p.dma_rr("in_a", 2, i, wn_t[i % 2][:],
                       w1n_in[i * 128:(i + 1) * 128, :],
                       waits=([t_abs[i - 2]] if i >= 2 else c["ready"]),
                       queue="gpsimd")
        t_abs[i] = p.emit("scalar", lambda e, i=i: e.activation(
            out=absw[:], in_=wn_t[i % 2][:], func=AF.Abs,
            accum_out=alpha1[:, i:i + 1]), waits=[t_w])
    t_ab = t_abs[31]
    t_al = p.emit("vector", lambda e: e.tensor_scalar_mul(
        alpha1s[:], alpha1[:], 1.0 / D), waits=[t_ab])
    t_b1 = p.dma("in", b1c[:], b1c_in[:], waits=c["ready"])

    t_sg = []
    prevs = {}
    for ch in range(NJ):
        t_w = p.dma_rr("in_s", 2, ch, wstage[ch % 2][:],
                       w1T_in[ch * 128:(ch + 1) * 128, :],
                       waits=prevs.get(ch - 2, c["ready"][:1]))
        cs = p.emit("scalar", lambda e, ch=ch: e.activation(
            out=sw1T[:, ch * FF:(ch + 1) * FF], in_=wstage[ch % 2][:],
            func=AF.Sign), waits=[t_w])
        prevs[ch] = [cs]
        t_sg.append(cs)

    t_prev_cast = None
    t_prev_ev = None
    for t in range(NT):
        t_x = p.dma("in_x", xf_t[:], x3_in[t * 128:(t + 1) * 128, :],
                    waits=[t_prev_cast] if t_prev_cast else c["ready"])
        t_cast = p.emit("scalar", lambda e: e.activation(
            out=xh_t[:], in_=xf_t[:], func=AF.Copy),
            waits=[t_x] + ([t_prev_ev] if t_prev_ev else []))
        t_prev_cast = t_cast
        xT_view = x3T[:].rearrange("p (j t) -> p j t", t=T)[:, :, t * 128:(t + 1) * 128]
        t_prev_ev = transpose_tile(p, c, tps, xh_t, xT_view, [t_cast],
                                   [t_prev_ev] if t_prev_ev else [], nblk=NJ)
    t_xT = t_prev_ev

    prev_ev = {}
    prev_st = {}
    i = 0
    for cch in range(4):
        for m in range(32):
            par = i % 2
            grp = (i // 8) % 2
            t_mm = None
            for kk in range(NJ):
                t_mm = p.emit("tensor", lambda e, cch=cch, m=m, kk=kk, par=par: e.matmul(
                    out=hps[par][:],
                    lhsT=sw1T[:, kk * FF + m * 128:kk * FF + (m + 1) * 128],
                    rhs=x3T[:, kk * T + cch * 512:kk * T + (cch + 1) * 512],
                    start=(kk == 0), stop=(kk == NJ - 1)),
                    waits=(t_sg + [t_xT] + list(prev_ev.get(par, []))) if kk == 0 else [])
            t_ev = p.emit("scalar", lambda e, m=m, par=par, grp=grp: e.activation(
                out=hbig[grp][:, (m % 8) * 512:(m % 8 + 1) * 512],
                in_=hps[par][:], func=AF.Gelu,
                scale=alpha1s[:, m:m + 1], bias=b1c[:, m:m + 1]),
                waits=[t_mm, t_al, t_b1] +
                      (list(prev_st.get(grp, [])) if m % 8 < 2 else []))
            prev_ev[par] = [t_ev]
            if m % 8 == 7:
                t_st = p.dma_rr("o_h", 2, grp,
                                hT_out[m - 7:m + 1, :, cch * 512:(cch + 1) * 512]
                                .rearrange("m p tt -> p m tt"),
                                hbig[grp][:].rearrange("p (m tt) -> p m tt", tt=512),
                                waits=[t_ev])
                prev_st[grp] = [t_st]
            i += 1


def build_F1():
    p = Prog("F1")
    c = common_consts(p)
    io = dict(x3=p.din("x3", [T, D], F32),
              w1n=p.din("w1n", [FF, D], F32),
              w1T=p.din("w1T", [D, FF], F32),
              b1c=p.din("b1c", [128, 32], F32),
              hT=p.dout("hT", [32, 128, T], BF16))
    emit_F1(p, c, io)
    return p.finalize()



def emit_F2(p, c, io):
    x3_in = io["x3"]
    hT_in = io["hT"]
    w2n_in = io["w2n"]
    w2T_in = io["w2T"]
    b2c_in = io["b2c"]
    y_out = io["y"]

    wn_t = [p.sb("wn_t0", [128, FF], F32)] * 2
    absw = p.sb("absw", [128, FF], F32)
    alpha2 = p.sb("alpha2", [128, NJ], F32)
    alpha2s = p.sb("alpha2s", [128, NJ], F32)
    b2c = p.sb("b2c_sb", [128, NJ], F32)
    wstage = [p.sb(f"wstage{i}", [128, D], F32) for i in range(2)]
    sw2T = p.sb("sw2T", [128, 32 * D], BF16)
    hT_c = [p.sb(f"hT_c{i}", [128, 32 * 512], BF16) for i in range(2)]
    preT = p.sb("preT", [128, NJ * 512], BF16)
    x3_t = [p.sb(f"x3_t{i}", [128, D], F32) for i in range(2)]
    u = [p.sb(f"u{i}", [128, D], F32) for i in range(2)]
    s1col = [p.sb(f"s1col{i}", [128, 1], F32) for i in range(2)]
    sqw0 = p.sb("sqw0", [128, D], F32)
    lnb = [dict(sqw=sqw0,
                mcol=p.sb(f"mcol{i}", [128, 1], F32),
                vcol=p.sb(f"vcol{i}", [128, 1], F32),
                tcol=p.sb(f"tcol{i}", [128, 1], F32)) for i in range(2)]
    y_t = [wn_t[0][:, 0:D], wn_t[0][:, D:2 * D]]

    tps = [p.ps(f"tps{i}", [128, 1024], BF16) for i in range(2)]
    o2ps = [p.ps(f"o2ps{i}", [128, 512], F32) for i in range(2)]

    t_abs = {}
    for i in range(NJ):
        t_w = p.dma_rr("in_a", 2, i, wn_t[i % 2][:],
                       w2n_in[i * 128:(i + 1) * 128, :],
                       waits=([t_abs[i - 1]] if i >= 1 else c["ready"]),
                       queue="gpsimd")
        t_abs[i] = p.emit("scalar", lambda e, i=i: e.activation(
            out=absw[:], in_=wn_t[i % 2][:], func=AF.Abs,
            accum_out=alpha2[:, i:i + 1]), waits=[t_w])
    t_ab = t_abs[NJ - 1]
    t_al = p.emit("vector", lambda e: e.tensor_scalar_mul(
        alpha2s[:], alpha2[:], 1.0 / FF), waits=[t_ab])
    t_b2 = p.dma("in", b2c[:], b2c_in[:], waits=c["ready"])

    t_sg = []
    prevs = {}
    for ch in range(32):
        t_w = p.dma_rr("in_s", 2, ch, wstage[ch % 2][:],
                       w2T_in[ch * 128:(ch + 1) * 128, :],
                       waits=prevs.get(ch - 2, c["ready"][:1]))
        cs = p.emit("scalar", lambda e, ch=ch: e.activation(
            out=sw2T[:, ch * D:(ch + 1) * D], in_=wstage[ch % 2][:],
            func=AF.Sign), waits=[t_w])
        prevs[ch] = [cs]
        t_sg.append(cs)

    preTs = [preT, absw[:, 0:2048].bitcast(BF16)]
    pvF = [dict(u=None, lnsq=None, lnout=None) for _ in range(2)]
    t_prev_pre = {}
    pre_done = {}
    hload = {}
    hl_war = {}

    def tail_body(cch):
        for tt in range(4):
            t = cch * 4 + tt
            L = t % 2
            v = pvF[L]
            t_x = p.dma_rr("in_x", 2, L, x3_t[L][:],
                           x3_in[t * 128:(t + 1) * 128, :],
                           waits=[v["u"]] if v["u"] else t_sg)
            t_tp = None
            for m in range(NJ):
                t_tp = p.emit("tensor", lambda e, m=m, tt=tt, L=L, cch=cch: e.transpose(
                    out=tps[L][:, m * 128:(m + 1) * 128],
                    in_=preTs[cch % 2][:, m * 512 + tt * 128:
                                       m * 512 + (tt + 1) * 128],
                    identity=c["ident"][:]),
                    waits=(pre_done[cch] +
                           ([v["u"]] if v["u"] else [])) if m == 0 else [])
            t_u = p.emit("vector", lambda e, L=L: e.scalar_tensor_tensor(
                out=u[L][:], in0=tps[L][:, 0:1024], scalar=1.0, in1=x3_t[L][:],
                op0=ALU.mult, op1=ALU.add, accum_out=s1col[L][:, 0:1]),
                waits=[t_tp, t_x] + ([v["lnsq"]] if v["lnsq"] else []))
            v["u"] = t_u
            w_ln = [t_u] + ([v["lnout"]] if v["lnout"] else [t_ab])
            t_y, t_sq1 = ln_stats_apply(p, lnb[L], u[L], s1col[L], y_t[L],
                                        1e-5, w_ln + [p.stok(f"o_y{L}")],
                                        None, "lny")
            v["lnsq"] = t_sq1
            v["lnout"] = t_y
            p.dma_rr("o_y", 2, L, y_out[t * 128:(t + 1) * 128, :], y_t[L][:],
                     waits=[t_y])

    for cch in range(4):
        hload[cch] = p.dma_rr("in_h", 2, cch,
                    hT_c[cch % 2][:].rearrange("p (k tt) -> p k tt", tt=512),
                    hT_in[:, :, cch * 512:(cch + 1) * 512].rearrange(
                        "k p tt -> p k tt"),
                    waits=[hl_war[cch - 2]] if cch >= 2 else t_sg)
        t_h = hload[cch]
        pre_done[cch] = []
        for m in range(NJ):
            par = m % 2
            t_mm = None
            for kk in range(32):
                t_mm = p.emit("tensor", lambda e, m=m, kk=kk, par=par, cch=cch: e.matmul(
                    out=o2ps[par][:],
                    lhsT=sw2T[:, kk * D + m * 128:kk * D + (m + 1) * 128],
                    rhs=hT_c[cch % 2][:, kk * 512:(kk + 1) * 512],
                    start=(kk == 0), stop=(kk == 31)),
                    waits=([t_h] + t_sg + list(t_prev_pre.get(par, []))) if kk == 0 else [])
            t_pre = p.emit("scalar", lambda e, m=m, par=par, cch=cch: e.activation(
                out=preTs[cch % 2][:, m * 512:(m + 1) * 512], in_=o2ps[par][:],
                func=AF.Identity, scale=alpha2s[:, m:m + 1],
                bias=b2c[:, m:m + 1]), waits=[t_mm, t_al, t_b2])
            t_prev_pre[par] = [t_pre]
            pre_done[cch].append(t_pre)
        hl_war[cch] = t_mm
        if cch > 0:
            tail_body(cch - 1)
    tail_body(3)


def build_F2():
    p = Prog("F2")
    c = common_consts(p)
    io = dict(x3=p.din("x3", [T, D], F32),
              hT=p.din("hT", [32, 128, T], BF16),
              w2n=p.din("w2n", [D, FF], F32),
              w2T=p.din("w2T", [FF, D], F32),
              b2c=p.din("b2c", [128, NJ], F32),
              y=p.dout("y", [T, D], F32))
    emit_F2(p, c, io)
    return p.finalize()



def build_fused(upto=None):
    p = Prog("FUSED")
    c = common_consts(p)
    for v in (1e-12, 1e-5, 0.125):
        p.const_col(v)
    G8 = [list(range(NCORES))]

    x = p.din("x", [T, D], F32)
    wq_a = p.din("wq_a", [D + 1, D], F32)
    wk_a = p.din("wk_a", [D + 1, D], F32)
    wv_a = p.din("wv_a", [D + 1, D], F32)
    wo_a = p.din("wo_a", [D + 1, D], F32)
    g1r = p.din("g1r", [1, D], F32)
    be1r = p.din("be1r", [1, D], F32)
    w1_a = p.din("w1_a", [D + 1, FF], F32)
    b1cB = p.din("b1cB", [128, 32], F32)
    w2_a = p.din("w2_a", [FF + 1, D], F32)
    g2r = p.din("g2r", [1, D], F32)
    be2r = p.din("be2r", [1, D], F32)
    gmr = p.din("gmr", [1, D], F32)
    bbr = p.din("bbr", [1, D], F32)
    mur = p.din("mur", [D, ES], F32)
    scl = p.din("scl", [1, 1], F32)
    w1e = p.din("w1e", [EXP, D, MOE_H], F32)
    b1c = p.din("b1c", [128, EXP * 2], F32)
    w2e = p.din("w2e", [EXP, MOE_H + 1, D], F32)
    wqT = p.din("wqT", [D, D], F32)
    wkT = p.din("wkT", [D, D], F32)
    wvT = p.din("wvT", [D, D], F32)
    woT_a = p.din("woT_a", [D + 1, D], F32)
    maskA = p.din("maskA", [128, 128], F32)
    maskB = p.din("maskB", [64, 128], F32)
    selL = p.din("selL", [512, 32], F32)
    selR = p.din("selR", [512, 32], F32)
    w1n = p.din("w1n", [FF, D], F32)
    w1T = p.din("w1T", [D, FF], F32)
    b1cf = p.din("b1cf", [128, 32], F32)
    w2n = p.din("w2n", [D, FF], F32)
    w2T = p.din("w2T", [FF, D], F32)
    b2cf = p.din("b2cf", [128, NJ], F32)
    y = p.dout("y", [T, D], F32)

    kvp = p.dtmp("s_kvp", [64, 1024], F32)
    kvs = p.dtmp("s_kvs", [64, 1024], F32)
    attn_out = p.dtmp("s_ao", [T, D], F32)
    atT = p.dtmp("s_atT", [NJ, 128, T], BF16)
    hT0 = p.dtmp("s_hT0", [32, 128, T], BF16)
    x1 = p.dtmp("s_x1", [T, D], F32)
    PrawP = p.dtmp("s_PrawP", [128, NJ * ES], F32)
    PrawS = p.dtmp("s_PrawS", [128, NJ * ES], F32)
    SlocP = p.dtmp("s_SlocP", [1, ES], F32)
    SlocS = p.dtmp("s_SlocS", [1, ES], F32)
    Emat = p.dtmp("s_E", [NT, 128, ES], BF16)
    Zc = p.dtmp("s_Zc", [NT, 128, 1], F32)
    xe = p.dtmp("s_xe", [2176, D], F32)
    edges = p.dtmp("s_edges", [2 * HALO, D], F32)
    gath = p.dtmp("s_gath", [NCORES * 2 * HALO, D], F32)
    x3 = p.dtmp("s_x3", [T, D], F32)
    hT1 = p.dtmp("s_hT1", [32, 128, T], BF16)
    soscr = p.dtmp("s_soscr", [ES, D], BF16)

    p.scnt.setdefault("cc", 0)

    def collective(kind, op, ins, outs, waits):
        return p.emit("gpsimd", lambda e: e.collective_compute(
            kind, op, replica_groups=G8, ins=[a.opt() for a in ins],
            outs=[a.opt() for a in outs]), waits=waits, inc=1, sem="cc")

    p.push_region("AB")
    eqT_res = p.sb("eqT_res", [128, NJ * T], BF16)
    rq_res = p.sb("rq_res", [128, NT * NH], F32)
    p.push_region("A")
    emit_A(p, c, dict(x=x, wq_a=wq_a, wk_a=wk_a, wv_a=wv_a, kvp=kvp,
                      eqT_res=eqT_res, rq_res=rq_res))
    t_cc_kv = collective("AllReduce", ALU.add, [kvp], [kvs],
                         [p.stok("o_kv")])
    p.pop_region()
    if upto == "A":
        return p.finalize()
    p.barrier()

    p.push_region("B1")
    emit_B1(p, c, dict(x=x, eqT_res=eqT_res, rq_res=rq_res, kvp=kvs,
                       g1r=g1r, be1r=be1r,
                       wo_a=wo_a, attn_out=attn_out, atT=atT), fused=True,
            kv_wait=t_cc_kv)
    p.pop_region()
    p.pop_region()
    if upto == "B1":
        return p.finalize()
    p.barrier()

    p.push_region("B2a")
    emit_B2a(p, c, dict(atT=atT, w1_a=w1_a, b1cB=b1cB, hT=hT0))
    p.pop_region()
    if upto == "B2a":
        return p.finalize()
    p.barrier()

    p.push_region("B2b")
    emit_B2b(p, c, dict(hT=hT0, attn_out=attn_out, x=x, w2_a=w2_a, g2r=g2r,
                        be2r=be2r, x1=x1))
    p.pop_region()
    if upto == "B2b":
        return p.finalize()
    p.barrier()

    p.push_region("C")
    emit_C(p, c, dict(x1=x1, gmr=gmr, bbr=bbr, mur=mur, scl=scl, Praw=PrawP,
                      Sloc=SlocP, E=Emat, Zc=Zc))
    collective("AllReduce", ALU.add, [PrawP], [PrawS], [p.stok("o_P")])
    t_cc_S = collective("AllReduce", ALU.add, [SlocP], [SlocS],
                        [p.stok("o_S")])
    p.pop_region()
    if upto == "C":
        return p.finalize()
    p.barrier()

    p.push_region("D")
    emit_D(p, c, dict(x1=x1, E=Emat, Zc=Zc, Praw=PrawS, Sloc=SlocS, w1e=w1e,
                      b1c=b1c, w2e=w2e, x2=xe[HALO:HALO + T, :],
                      so_scratch=soscr),
           fused=True, pr_wait=t_cc_S, edges_out=edges)
    t_cc_G = collective("AllGather", ALU.bypass, [edges], [gath],
                        [p.stok("o_edge")])
    p.pop_region()
    if upto == "D":
        return p.finalize()
    p.barrier()

    p.push_region("H")
    ztile = p.sb("ztile", [64, D], F32)
    gsb = p.sb("gsb", [128, 4 * D], F32)
    slsb = p.sb("slsb", [128, 4 * 32], F32)
    srsb = p.sb("srsb", [128, 4 * 32], F32)
    hsb = p.sb("hsb", [32, D], F32)
    hps = p.ps("hps", [32, 1024], F32)
    t_z = p.emit("vector", lambda e: e.memset(ztile[:], 0.0))
    if upto == "H0":
        return p.finalize()
    p.dma("o_xe", xe[2112:2176, :], ztile[:], waits=[t_z])
    if upto == "H1":
        return p.finalize()
    t_g = p.dma("in_g", gsb[:].rearrange("p (k d) -> p k d", d=D),
                gath[:].rearrange("(k p) d -> p k d", p=128),
                waits=[t_cc_G])
    t_sl = p.dma("in_g", slsb[:].rearrange("p (k m) -> p k m", m=32),
                 selL[:].rearrange("(k p) m -> p k m", p=128))
    t_sr = p.dma("in_g", srsb[:].rearrange("p (k m) -> p k m", m=32),
                 selR[:].rearrange("(k p) m -> p k m", p=128))
    t_gin = p.stok("in_g")
    if upto == "H2":
        return p.finalize()
    t_he = None
    for side, ssb, rows in (("L", slsb, slice(0, 32)), ("R", srsb, slice(2080, 2112))):
        t_hm = None
        for n in range(2):
            for kk in range(4):
                t_hm = p.emit("tensor", lambda e, n=n, kk=kk, ssb=ssb: e.matmul(
                    out=hps[0:32, n * 512:(n + 1) * 512],
                    lhsT=ssb[:, kk * 32:(kk + 1) * 32],
                    rhs=gsb[:, kk * D + n * 512:kk * D + (n + 1) * 512],
                    start=(kk == 0), stop=(kk == 3)),
                    waits=([t_gin] + ([t_he] if t_he else [])) if n == 0 and kk == 0 else [])
        t_he = p.emit("vector", lambda e: e.tensor_copy(hsb[0:32, :],
                                                        hps[0:32, :]),
                      waits=[t_hm, p.stok("o_xe")])
        t_hdma = p.dma("o_xe", xe[rows, :], hsb[0:32, :], waits=[t_he])
        if upto == "H3" and side == "L":
            return p.finalize()
    if upto == "H4":
        return p.finalize()
    p.pop_region()
    if upto == "H":
        return p.finalize()
    p.barrier()

    p.push_region("E")
    emit_E(p, c, dict(xe=xe, maskA=maskA, maskB=maskB, wqT=wqT, wkT=wkT,
                      wvT=wvT, woT_a=woT_a, x3=x3))
    p.pop_region()
    if upto == "E":
        return p.finalize()
    p.barrier()

    p.push_region("F1")
    emit_F1(p, c, dict(x3=x3, w1n=w1n, w1T=w1T, b1c=b1cf, hT=hT1))
    p.pop_region()
    if upto == "F1":
        return p.finalize()
    p.barrier()

    p.push_region("F2")
    emit_F2(p, c, dict(x3=x3, hT=hT1, w2n=w2n, w2T=w2T, b2c=b2cf, y=y))
    p.pop_region()

    return p.finalize()



def _aug(w, b):
    return np.ascontiguousarray(
        np.concatenate([w, b.reshape(1, -1)], axis=0), dtype=np.float32)


def _row(v):
    return np.ascontiguousarray(v.reshape(1, -1), dtype=np.float32)


IDENT = np.eye(128, dtype=BF16NP)


_NC_CACHE = {}


def _get_nc(name, builder):
    if name not in _NC_CACHE:
        _NC_CACHE[name] = builder()
    return _NC_CACHE[name]


def _run(name, builder, per_core, debug=None):
    nc = _get_nc(name, builder)
    res = run_bass_kernel_spmd(nc, per_core, core_ids=list(range(NCORES)))
    if debug is not None:
        debug[name] = res
    return res.results


def host_layouts(inp):
    x = np.ascontiguousarray(inp["L0_attn_x"].reshape(16384, 1024),
                             dtype=np.float32)
    com = dict(
        wq_a=_aug(inp["L0_attn_Wq"], inp["L0_attn_bq"]),
        wk_a=_aug(inp["L0_attn_Wk"], inp["L0_attn_bk"]),
        wv_a=_aug(inp["L0_attn_Wv"], inp["L0_attn_bv"]),
        wo_a=_aug(inp["L0_attn_Wo"], inp["L0_attn_bo"]),
        g1r=_row(inp["L0_attn_g1"]), be1r=_row(inp["L0_attn_be1"]),
        w1_a=_aug(inp["L0_attn_W1"], inp["L0_attn_b1"]),
        b1cB=np.ascontiguousarray(
            inp["L0_attn_b1"].reshape(32, 128).T, dtype=np.float32),
        w2_a=_aug(inp["L0_attn_W2"], inp["L0_attn_b2"]),
        g2r=_row(inp["L0_attn_g2"]), be2r=_row(inp["L0_attn_be2"]),
        gmr=_row(inp["L0_ffn_ln_gamma"]), bbr=_row(inp["L0_ffn_ln_beta"]),
        mur=np.ascontiguousarray(inp["L0_ffn_mu"].reshape(D, ES),
                                 dtype=np.float32),
        scl=np.ascontiguousarray(inp["L0_ffn_scale"].reshape(1, 1),
                                 dtype=np.float32),
        w1e=np.ascontiguousarray(inp["L0_ffn_w1"], dtype=np.float32),
        b1c=np.ascontiguousarray(
            inp["L0_ffn_b1"].reshape(EXP, 2, 128).transpose(2, 0, 1)
            .reshape(128, EXP * 2), dtype=np.float32),
        w2e=np.ascontiguousarray(
            np.concatenate([inp["L0_ffn_w2"],
                            inp["L0_ffn_b2"][:, None, :]], axis=1),
            dtype=np.float32),
        wqT=np.ascontiguousarray(inp["L1_attn_Wq"].T, dtype=np.float32),
        wkT=np.ascontiguousarray(inp["L1_attn_Wk"].T, dtype=np.float32),
        wvT=np.ascontiguousarray(inp["L1_attn_Wv"].T, dtype=np.float32),
        woT_a=_aug(np.ascontiguousarray(inp["L1_attn_Wo"].T),
                   inp["L1_attn_bo"]),
        w1n=np.ascontiguousarray(inp["L1_ffn_w1"], dtype=np.float32),
        w1T=np.ascontiguousarray(inp["L1_ffn_w1"].T, dtype=np.float32),
        b1cf=np.ascontiguousarray(inp["L1_ffn_b1"].reshape(32, 128).T,
                                  dtype=np.float32),
        w2n=np.ascontiguousarray(inp["L1_ffn_w2"], dtype=np.float32),
        w2T=np.ascontiguousarray(inp["L1_ffn_w2"].T, dtype=np.float32),
        b2cf=np.ascontiguousarray(inp["L1_ffn_b2"].reshape(NJ, 128).T,
                                  dtype=np.float32),
        ident=IDENT,
    )
    pp = np.arange(128)[:, None]
    rr = np.arange(128)[None, :]
    com["maskA"] = np.where((rr >= pp - 63) & (rr <= pp), 0.0,
                            -1e4).astype(np.float32)
    pb = np.arange(64)[:, None]
    com["maskB"] = np.where(rr >= pb + 65, 0.0, -1e4).astype(np.float32)
    maps = []
    for ci in range(NCORES):
        m = dict(com)
        m["x"] = np.ascontiguousarray(x[ci * T:(ci + 1) * T])
        sL = np.zeros((512, 32), np.float32)
        sR = np.zeros((512, 32), np.float32)
        if ci > 0:
            sL[np.arange(32) + 64 * (ci - 1) + 32, np.arange(32)] = 1.0
        if ci < NCORES - 1:
            sR[np.arange(32) + 64 * (ci + 1), np.arange(32)] = 1.0
        m["selL"] = sL
        m["selR"] = sR
        maps.append(m)
    return maps


def kernel_fused(**inp):
    inp = {k: np.asarray(v) for k, v in inp.items()}
    maps = host_layouts(inp)
    nc = _get_nc("FUSED", build_fused)
    res = run_bass_kernel_spmd(nc, maps, core_ids=list(range(NCORES)))
    yy = np.concatenate([res.results[ci]["y"] for ci in range(NCORES)], axis=0)
    return yy.reshape(1, 16384, 1024)


def kernel(**inp):
    if not inp.pop("_unfused", False) and "_upto" not in inp:
        inp.pop("_debug", None)
        return kernel_fused(**inp)
    debug = inp.pop("_debug", None)
    upto = inp.pop("_upto", None)
    inp = {k: np.asarray(v) for k, v in inp.items()}
    x = np.ascontiguousarray(inp["L0_attn_x"].reshape(16384, 1024),
                             dtype=np.float32)
    xs = [np.ascontiguousarray(x[ci * T:(ci + 1) * T]) for ci in range(NCORES)]

    wq_a = _aug(inp["L0_attn_Wq"], inp["L0_attn_bq"])
    wk_a = _aug(inp["L0_attn_Wk"], inp["L0_attn_bk"])
    wv_a = _aug(inp["L0_attn_Wv"], inp["L0_attn_bv"])

    rA = _run("A", build_A, [
        dict(x=xs[ci], wq_a=wq_a, wk_a=wk_a, wv_a=wv_a, ident=IDENT)
        for ci in range(NCORES)])
    if upto == "A":
        return rA

    kvp_all = np.ascontiguousarray(
        np.stack([rA[ci]["kvp"] for ci in range(NCORES)]))
    wo_a = _aug(inp["L0_attn_Wo"], inp["L0_attn_bo"])
    g1r = _row(inp["L0_attn_g1"])
    be1r = _row(inp["L0_attn_be1"])
    rB1 = _run("B1", build_B1, [
        dict(x=xs[ci], eqT=rA[ci]["eqT"], rq=rA[ci]["rq"], kvp=kvp_all,
             wo_a=wo_a, g1r=g1r, be1r=be1r, ident=IDENT)
        for ci in range(NCORES)])
    if upto == "B1":
        return rB1

    w1_a = _aug(inp["L0_attn_W1"], inp["L0_attn_b1"])
    b1cB = np.ascontiguousarray(
        inp["L0_attn_b1"].reshape(32, 128).T, dtype=np.float32)
    rB2a = _run("B2a", build_B2a, [
        dict(atT=rB1[ci]["atT"], w1_a=w1_a, b1cB=b1cB, ident=IDENT)
        for ci in range(NCORES)])

    w2_a = _aug(inp["L0_attn_W2"], inp["L0_attn_b2"])
    g2r = _row(inp["L0_attn_g2"])
    be2r = _row(inp["L0_attn_be2"])
    rB2b = _run("B2b", build_B2b, [
        dict(hT=rB2a[ci]["hT"], attn_out=rB1[ci]["attn_out"], x=xs[ci],
             w2_a=w2_a, g2r=g2r, be2r=be2r, ident=IDENT)
        for ci in range(NCORES)])
    if upto == "B2b":
        return rB2b

    gmr = _row(inp["L0_ffn_ln_gamma"])
    bbr = _row(inp["L0_ffn_ln_beta"])
    mur = np.ascontiguousarray(inp["L0_ffn_mu"].reshape(D, ES),
                               dtype=np.float32)
    scl = np.ascontiguousarray(inp["L0_ffn_scale"].reshape(1, 1),
                               dtype=np.float32)
    rC = _run("C", build_C, [
        dict(x1=rB2b[ci]["x1"], gmr=gmr, bbr=bbr, mur=mur, scl=scl,
             ident=IDENT) for ci in range(NCORES)])
    if upto == "C":
        return rC

    Praw_all = np.ascontiguousarray(np.stack([rC[ci]["Praw"] for ci in range(NCORES)]))
    Sloc_all = np.ascontiguousarray(np.stack([rC[ci]["Sloc"] for ci in range(NCORES)]))
    w1e = np.ascontiguousarray(inp["L0_ffn_w1"], dtype=np.float32)
    b1c = np.ascontiguousarray(
        inp["L0_ffn_b1"].reshape(EXP, 2, 128).transpose(2, 0, 1).reshape(128, EXP * 2),
        dtype=np.float32)
    w2e = np.ascontiguousarray(
        np.concatenate([inp["L0_ffn_w2"],
                        inp["L0_ffn_b2"][:, None, :]], axis=1), dtype=np.float32)
    rD = _run("D", build_D, [
        dict(x1=rB2b[ci]["x1"], E=rC[ci]["E"], Zc=rC[ci]["Zc"],
             Praw=Praw_all, Sloc=Sloc_all, w1e=w1e, b1c=b1c, w2e=w2e,
             ident=IDENT) for ci in range(NCORES)])
    if upto == "D":
        return rD

    x2 = np.concatenate([rD[ci]["x2"] for ci in range(NCORES)], axis=0)
    xe = np.zeros((NCORES, 2176, D), np.float32)
    for ci in range(NCORES):
        lo, hi = ci * T - HALO, ci * T + T + HALO
        slo, shi = max(lo, 0), min(hi, 16384)
        xe[ci, slo - lo:slo - lo + (shi - slo)] = x2[slo:shi]
    wqT = np.ascontiguousarray(inp["L1_attn_Wq"].T, dtype=np.float32)
    wkT = np.ascontiguousarray(inp["L1_attn_Wk"].T, dtype=np.float32)
    wvT = np.ascontiguousarray(inp["L1_attn_Wv"].T, dtype=np.float32)
    woT_a = _aug(np.ascontiguousarray(inp["L1_attn_Wo"].T), inp["L1_attn_bo"])
    pp = np.arange(128)[:, None]
    rr = np.arange(128)[None, :]
    maskA = np.where((rr >= pp - 63) & (rr <= pp), 0.0, -1e4).astype(np.float32)
    pb = np.arange(64)[:, None]
    maskB = np.where(rr >= pb + 65, 0.0, -1e4).astype(np.float32)
    rE = _run("E", build_E, [
        dict(xe=np.ascontiguousarray(xe[ci]), wqT=wqT, wkT=wkT, wvT=wvT,
             woT_a=woT_a, maskA=maskA, maskB=maskB, ident=IDENT)
        for ci in range(NCORES)])
    if upto == "E":
        return rE

    w1n = np.ascontiguousarray(inp["L1_ffn_w1"], dtype=np.float32)
    w1T = np.ascontiguousarray(inp["L1_ffn_w1"].T, dtype=np.float32)
    b1cf = np.ascontiguousarray(
        inp["L1_ffn_b1"].reshape(32, 128).T, dtype=np.float32)
    rF1 = _run("F1", build_F1, [
        dict(x3=rE[ci]["x3"], w1n=w1n, w1T=w1T, b1c=b1cf, ident=IDENT)
        for ci in range(NCORES)])

    w2n = np.ascontiguousarray(inp["L1_ffn_w2"], dtype=np.float32)
    w2T = np.ascontiguousarray(inp["L1_ffn_w2"].T, dtype=np.float32)
    b2cf = np.ascontiguousarray(
        inp["L1_ffn_b2"].reshape(NJ, 128).T, dtype=np.float32)
    rF2 = _run("F2", build_F2, [
        dict(x3=rE[ci]["x3"], hT=rF1[ci]["hT"], w2n=w2n, w2T=w2T, b2c=b2cf,
             ident=IDENT) for ci in range(NCORES)])

    y = np.concatenate([rF2[ci]["y"] for ci in range(NCORES)], axis=0)
    return y.reshape(1, 16384, 1024)


# version 13
# speedup vs baseline: 1.1139x; 1.0142x over previous
import numpy as np
import ml_dtypes
from contextlib import ExitStack

import concourse.bass as bass
import concourse.mybir as mybir
from concourse.bass_utils import run_bass_kernel_spmd

F32 = mybir.dt.float32
BF16 = mybir.dt.bfloat16
AF = mybir.ActivationFunctionType
ALU = mybir.AluOpType
AX = mybir.AxisListType

NCORES = 8
T = 2048
NT = 16
D = 1024
NJ = 8
FF = 4096
NH = 16
HD = 64
ES = 256
EXP = 16
SLOT = 16
MOE_H = 256
HALO = 32
TEXT = T + 128
NTE = 17

BF16NP = ml_dtypes.bfloat16
DEBUG_A = False
DEBUG_B1 = False


class Prog:

    ENGS = ("sync", "scalar", "vector", "tensor", "gpsimd")

    def __init__(self, name):
        self.name = name
        self.nc = bass.Bass("TRN2", target_bir_lowering=False)
        self.ctx = ExitStack()
        self.regions = [self.ctx]
        self.prefix = ""
        self.ops = {e: [] for e in self.ENGS}
        self.cnt = {e: 0 for e in self.ENGS}
        self.scnt = {}
        self.pending = {}

    def push_region(self, prefix):
        self.regions.append(ExitStack())
        self._prefixes = getattr(self, "_prefixes", [""])
        self._prefixes.append(self.prefix)
        self.prefix = self.prefix + prefix + "_"

    def pop_region(self):
        self.regions.pop().close()
        self.prefix = self._prefixes.pop()

    def barrier(self):
        snap = [(e, self.cnt[e]) for e in self.ENGS if self.cnt[e] > 0]
        snap += [(s, v) for s, v in self.scnt.items() if v > 0 and s != "cc"]
        for e in self.ENGS:
            self.pending.setdefault(e, [])
            self.pending[e] = snap

    def sb(self, name, shape, dt):
        return self.regions[-1].enter_context(
            self.nc.sbuf_tensor(self.prefix + name, shape, dt))

    def ps(self, name, shape, dt=F32):
        return self.regions[-1].enter_context(
            self.nc.psum_tensor(self.prefix + name, shape, dt))

    def din(self, name, shape, dt=F32):
        return self.nc.dram_tensor(name, shape, dt, kind="ExternalInput").ap()

    def dout(self, name, shape, dt=F32):
        return self.nc.dram_tensor(name, shape, dt, kind="ExternalOutput").ap()

    def dtmp(self, name, shape, dt=F32):
        return self.nc.dram_tensor(name, shape, dt).ap()

    def emit(self, eng, fn, waits=(), inc=1, sem=None):
        sem = sem or eng
        waits = list(waits) + self.pending.pop(eng, [])
        w = {}
        for tok in waits:
            if len(tok) == 3:
                _, we, wv = tok
            else:
                we, wv = tok
                if we in self.scnt:
                    wv = self.scnt[we]
            if we != eng and wv > 0:
                w[we] = max(w.get(we, 0), wv)
        self.ops[eng].append((tuple(w.items()), fn, inc, sem))
        if sem in self.scnt:
            self.scnt[sem] += inc
            return (sem, self.scnt[sem])
        self.cnt[eng] += inc
        return (eng, self.cnt[eng])

    def tok(self, eng):
        return (eng, self.cnt[eng])

    def stok(self, stream):
        return (stream, self.scnt.get(stream, 0))

    def const_col(self, value):
        if not hasattr(self, "_ccache"):
            self._ccache = {}
        if value not in self._ccache:
            t = self.regions[0].enter_context(self.nc.sbuf_tensor(
                f"cc_{len(self._ccache)}", [128, 1], F32))
            tok = self.emit("vector", lambda e, t=t, value=value: e.memset(
                t[:], float(value)))
            self._ccache[value] = (t, tok)
        return self._ccache[value]

    def dma_rr(self, stream, nways, idx, out, in_, waits=(), queue="sync", **kw):
        return self.dma(f"{stream}{idx % nways}", out, in_, waits, queue, **kw)

    def stok_rr(self, stream, nways):
        return [self.stok(f"{stream}{i}") for i in range(nways)]

    def dma(self, stream, out, in_, waits=(), queue="sync", **kw):
        if stream not in self.scnt:
            self.scnt[stream] = 0
        waits = list(waits)
        if self.scnt[stream] > 0:
            waits.append(("exact", stream, self.scnt[stream]))
        return self.emit(queue, lambda e: e.dma_start(out=out, in_=in_, **kw),
                         waits, 16, sem=stream)

    def finalize(self):
        while len(self.regions) > 1:
            self.pop_region()
        nc = self.nc
        sems = {e: self.ctx.enter_context(nc.semaphore("sem_" + e))
                for e in self.ENGS}
        for s in self.scnt:
            sems[s] = self.ctx.enter_context(nc.semaphore("sem_" + s))
        block = self.ctx.enter_context(nc.Block())
        for e in self.ENGS:
            ops = self.ops[e]
            if not ops and e != "sync":
                continue

            def body(engobj, ops=ops, e=e):
                hwm = {}
                selfcnt = 0
                selfsync = e in ("scalar", "vector", "gpsimd")
                for waits, fn, inc, sem in ops:
                    if selfsync and sem == e and selfcnt > 0:
                        engobj.wait_ge(sems[e], selfcnt)
                    for we, wv in waits:
                        if hwm.get(we, 0) < wv:
                            engobj.wait_ge(sems[we], wv)
                            hwm[we] = wv
                    ins = fn(engobj)
                    if inc:
                        ins.then_inc(sems[sem], inc)
                    if sem == e:
                        selfcnt += inc
                if e == "sync":
                    for oe in Prog.ENGS:
                        if self.cnt[oe] > 0:
                            engobj.wait_ge(sems[oe], self.cnt[oe])
                    for s, v in self.scnt.items():
                        if v > 0:
                            engobj.wait_ge(sems[s], v)

            getattr(block, e)(body)
        self.ctx.close()
        return nc



def common_consts(p):
    c = {}
    c["ident"] = p.sb("ident_sb", [128, 128], BF16)
    ident_in = p.din("ident", [128, 128], BF16)
    t_id = p.dma("in", c["ident"][:], ident_in[:])
    c["ones_bf"] = p.sb("ones_bf", [1, 2048], BF16)
    c["onescol_bf"] = p.sb("onescol_bf", [128, 1], BF16)
    c["ones_f"] = p.sb("ones_f", [1, 128], F32)
    t1 = p.emit("vector", lambda e: e.memset(c["ones_bf"][:], 1.0))
    t2 = p.emit("vector", lambda e: e.memset(c["onescol_bf"][:], 1.0))
    t3 = p.emit("vector", lambda e: e.memset(c["ones_f"][:], 1.0))
    c["ready"] = [t_id, t1, t2, t3]
    return c


def load_cast_weight(p, c, name, kt, n, stage, dest, brow_f, brow_h, waits=(),
                     win=None):
    if win is None:
        win = p.din(name, [kt * 128 + 1, n], F32)
    tw = p.dma("in", stage[:, 0:kt * n].rearrange("p (k f) -> p k f", f=n),
               win[0:kt * 128, :].rearrange("(k p) f -> p k f", p=128),
               waits=waits)
    tb = p.dma("in", brow_f[:, 0:n], win[kt * 128:kt * 128 + 1, :], waits=waits)
    t_in = p.stok("in")
    tcast = []
    for j in range(kt):
        tcast.append(p.emit("scalar", lambda e, j=j: e.activation(
            out=dest[:, j * n:(j + 1) * n], in_=stage[:, j * n:(j + 1) * n],
            func=AF.Copy), waits=[t_in]))
    tcb = p.emit("scalar", lambda e: e.activation(
        out=brow_h[:, 0:n], in_=brow_f[:, 0:n], func=AF.Copy), waits=[t_in])
    return tcast + [tcb]


def bcast_row_f32(p, c, dst, src_row, n, psA, waits=()):
    toks = []
    ev = []
    for n0 in range(0, n, 512):
        nn = min(512, n - n0)
        tmm = p.emit("tensor", lambda e, n0=n0, nn=nn: e.matmul(
            out=psA[:, n0 % 512:n0 % 512 + nn], lhsT=c["ones_f"][:, :],
            rhs=src_row[:, n0:n0 + nn], start=True, stop=True),
            waits=list(waits) + ev)
        tev = p.emit("vector", lambda e, n0=n0, nn=nn: e.tensor_copy(
            dst[:, n0:n0 + nn], psA[:, n0 % 512:n0 % 512 + nn]), waits=[tmm])
        ev = [tev]
        toks.append(tev)
    return toks


def ln_stats_apply(p, bufs, u, s1, out, eps, tw_u, tw_s2, tag):
    sq, mcol, vcol, tcol = bufs["sqw"], bufs["mcol"], bufs["vcol"], bufs["tcol"]
    t_s2 = p.emit("scalar", lambda e: e.activation(
        out=sq[:], in_=u[:], func=AF.Square, accum_out=vcol[:, 0:1]),
        waits=tw_u)
    t_m = p.emit("scalar", lambda e: e.mul(mcol[:, 0:1], s1[:, 0:1], 1.0 / D),
                 waits=tw_u)
    t_t = p.emit("scalar", lambda e: e.activation(
        out=tcol[:, 0:1], in_=mcol[:, 0:1], func=AF.Square), waits=[t_m])
    t_v = p.emit("vector", lambda e: e.scalar_tensor_tensor(
        out=vcol[:, 0:1], in0=vcol[:, 0:1], scalar=1.0 / D, in1=tcol[:, 0:1],
        op0=ALU.mult, op1=ALU.subtract), waits=[t_s2, t_t])
    eps_ap, eps_tok = p.const_col(float(eps))
    t_sq = p.emit("scalar", lambda e: e.activation(
        out=tcol[:, 0:1], in_=vcol[:, 0:1], func=AF.Sqrt,
        bias=eps_ap[:, 0:1]), waits=[t_v, eps_tok])
    t_r = p.emit("vector", lambda e: e.reciprocal(
        out=vcol[:, 0:1], in_=tcol[:, 0:1]), waits=[t_sq])
    t_o = p.emit("vector", lambda e: e.tensor_scalar(
        out=out[:], in0=u[:], scalar1=mcol[:, 0:1], scalar2=vcol[:, 0:1],
        op0=ALU.subtract, op1=ALU.mult), waits=[t_r] + tw_u)
    return t_o, t_sq


def transpose_tile(p, c, tps, src_bf, dst, waits_src, waits_ps, nblk=8,
                   waits_evac=()):
    tts = []
    for j in range(nblk):
        tts.append(p.emit("tensor", lambda e, j=j: e.transpose(
            out=tps[:, j * 128:(j + 1) * 128],
            in_=src_bf[:, j * 128:(j + 1) * 128],
            identity=c["ident"][:]),
            waits=(list(waits_src) + list(waits_ps)) if j == 0 else list(waits_src)))
    tev = p.emit("vector", lambda e: e.tensor_copy(
        dst[:], tps[:, 0:nblk * 128]), waits=[tts[-1]] + list(waits_evac))
    return tev




def interleave_pairs(bodies):
    live = list(bodies)
    while live:
        nxt = []
        for g in live:
            try:
                next(g)
                nxt.append(g)
            except StopIteration:
                pass
        live = nxt


def emit_A(p, c, io):
    x_in = io["x"]
    kvp_out = io["kvp"]
    eqT_res = io.get("eqT_res")
    rq_res = io.get("rq_res")
    eqT_out = io.get("eqT")
    rq_out = io.get("rq")

    wstage = [p.sb(f"wstage{i}", [128, 2 * D], F32) for i in range(2)]
    wb_f = p.sb("wb_f", [1, D], F32)
    wh = {k: p.sb(f"wh_{k}", [128, NJ * D], BF16) for k in "qkv"}
    wbh = {k: p.sb(f"wbh_{k}", [1, D], BF16) for k in "qkv"}

    xT = p.sb("xT", [128, NJ * T], BF16)
    NL = 2
    xf_t = [p.sb(f"xf_t{i}", [128, D], F32) for i in range(NL)]
    xh_t = [p.sb(f"xh_t{i}", [128, D], BF16) for i in range(NL)]
    qa = [p.sb(f"qa{i}", [128, D], F32) for i in range(NL)]
    qb = [p.sb(f"qb{i}", [128, D], F32) for i in range(NL)]
    qc = [p.sb(f"qc{i}", [128, D], F32) for i in range(NL)]
    eqbf = [p.sb(f"eqbf{i}", [128, D], BF16) for i in range(NL)]
    ekbf = [p.sb(f"ekbf{i}", [128, D], BF16) for i in range(NL)]
    vpbf = [p.sb(f"vpbf{i}", [128, D], BF16) for i in range(NL)]
    sqb0 = p.sb("sqb0", [128, D], F32)
    sqb = [sqb0] * NL
    ncol = [p.sb(f"ncol{i}", [128, NH], F32) for i in range(NL)]
    scol = [p.sb(f"scol{i}", [128, NH], F32) for i in range(NL)]
    rk = [p.sb(f"rk{i}", [128, NH], F32) for i in range(NL)]
    rqsb = ([p.sb(f"rqsb{i}", [128, NH], F32) for i in range(NL)]
            if io.get("rq_res") is None else None)
    eqstage = ([p.sb(f"eqstage{i}", [128, D], BF16) for i in range(NL)]
               if io.get("eqT_res") is None else None)
    kvsb = qa[0][0:64, 0:1024]

    tpsL = [p.ps("tpsA", [128, 1024], BF16), p.ps("tps2", [128, 1024], BF16)]
    qps = [p.ps(f"qps{i}", [128, 1024], F32) for i in range(NL)]
    kvps = p.ps("kvps", [64, 1024], F32)

    wready = []
    wcast = {}
    for idx, (k, h) in enumerate([(k, h) for k in "qkv" for h in range(4)]):
        tw = p.dma_rr("in_w", 2, idx,
                      wstage[idx % 2][:].rearrange("p (k f) -> p k f", f=D),
                      io["w%s_a" % k][h * 256:(h + 1) * 256, :]
                      .rearrange("(k p) f -> p k f", p=128),
                      waits=wcast.get(idx - 2, list(c["ready"])))
        cs = []
        for j in range(2):
            cs.append(p.emit("scalar", lambda e, k=k, h=h, j=j, idx=idx: e.activation(
                out=wh[k][:, (h * 2 + j) * D:(h * 2 + j + 1) * D],
                in_=wstage[idx % 2][:, j * D:(j + 1) * D],
                func=AF.Copy), waits=[tw]))
        wcast[idx] = cs
        wready += cs
    prevb = None
    for k in "qkv":
        tb = p.dma("in_b", wb_f[:, 0:D], io["w%s_a" % k][NJ * 128:NJ * 128 + 1, :],
                   waits=[prevb] if prevb else list(c["ready"]))
        prevb = p.emit("scalar", lambda e, k=k: e.activation(
            out=wbh[k][:, 0:D], in_=wb_f[:, 0:D], func=AF.Copy),
            waits=[tb])
        wready.append(prevb)

    pv = [dict(cast=None, xtp=None, eq_evac=None, vstt=None, kps=None,
               nq=None, sqq=None, rq=None) for _ in range(NL)]
    gkv = dict(ek=None, kvlast=None)

    def tile_body(t, L):
        v = pv[L]
        w_load = [v["cast"]] if v["cast"] else list(c["ready"])
        t_x = p.dma_rr("in_x", NL, L, xf_t[L][:],
                       x_in[t * 128:(t + 1) * 128, :], waits=w_load)
        yield
        w_c = [t_x] + ([v["xtp"]] if v["xtp"] else [])
        t_cast = p.emit("scalar", lambda e, L=L: e.activation(
            out=xh_t[L][:], in_=xf_t[L][:], func=AF.Copy), waits=w_c)
        v["cast"] = t_cast
        xT_view = xT[:].rearrange("p (j t) -> p j t", t=T)[:, :, t * 128:(t + 1) * 128]
        tev = transpose_tile(p, c, tpsL[L], xh_t[L], xT_view, [t_cast],
                             [v["xtp"]] if v["xtp"] else [], nblk=NJ)
        v["xtp"] = tev
        t_xT = tev
        yield

        def proj(k, waits_extra, t=t, t_xT=t_xT, L=L):
            last = None
            for n in range(2):
                for kk in range(NJ):
                    last = p.emit("tensor", lambda e, n=n, kk=kk, k=k, t=t, L=L: e.matmul(
                        out=qps[L][:, n * 512:(n + 1) * 512],
                        lhsT=xT[:, kk * T + t * 128: kk * T + (t + 1) * 128],
                        rhs=wh[k][:, kk * D + n * 512: kk * D + (n + 1) * 512],
                        start=(kk == 0), stop=False),
                        waits=([t_xT] + wready + waits_extra) if kk == 0 and n == 0 else [])
                last = p.emit("tensor", lambda e, n=n, k=k, L=L: e.matmul(
                    out=qps[L][:, n * 512:(n + 1) * 512],
                    lhsT=c["ones_bf"][0:1, 0:128],
                    rhs=wbh[k][:, n * 512:(n + 1) * 512],
                    start=False, stop=True), waits=[])
            return last

        def elu(out_bf, t_mm, war=(), L=L):
            ta = p.emit("vector", lambda e, L=L: e.tensor_scalar_max(
                qa[L][:], qps[L][:], 0.0), waits=[t_mm] + list(war))
            tb = p.emit("vector", lambda e, L=L: e.tensor_scalar(
                out=qb[L][:], in0=qps[L][:], scalar1=-1.0, scalar2=0.0,
                op0=ALU.mult, op1=ALU.max), waits=[])
            tc = p.emit("scalar", lambda e, L=L: e.activation(
                out=qc[L][:], in_=qb[L][:], func=AF.Exp, scale=-1.0),
                waits=[tb])
            te = p.emit("vector", lambda e, out_bf=out_bf, L=L: e.scalar_tensor_tensor(
                out=out_bf[:], in0=qa[L][:], scalar=-1.0, in1=qc[L][:],
                op0=ALU.add, op1=ALU.add), waits=[tc])
            return te, tb

        t_qmm = proj("q", [v["vstt"]] if v["vstt"] else [])
        yield
        war_q = [gkv["ek"]] if gkv["ek"] else []
        t_eq, t_q_psfree = elu(eqbf[L], t_qmm, war=war_q)
        yield
        t_eqsq = p.emit("scalar", lambda e, L=L: e.activation(
            out=sqb[L][:], in_=eqbf[L][:], func=AF.Square), waits=[t_eq])
        t_nq = p.emit("vector", lambda e, L=L: e.reduce_sum(
            out=ncol[L][:, 0:NH], in_=sqb[L][:].rearrange("p (h w) -> p h w", w=HD),
            axis=AX.X), waits=[t_eqsq])
        t_sqq = p.emit("scalar", lambda e, L=L: e.activation(
            out=scol[L][:, 0:NH], in_=ncol[L][:, 0:NH], func=AF.Sqrt),
            waits=[t_nq])
        if rq_res is not None:
            t_rq = p.emit("vector", lambda e, L=L, t=t: e.reciprocal(
                out=rq_res[:, t * NH:(t + 1) * NH], in_=scol[L][:, 0:NH]),
                waits=[t_sqq])
        else:
            t_rq = p.emit("vector", lambda e, L=L: e.reciprocal(
                out=rqsb[L][:, 0:NH], in_=scol[L][:, 0:NH]),
                waits=[t_sqq, p.stok(f"o_rq{L}")])
            p.dma_rr("o_rq", NL, L, rq_out[t, :, :], rqsb[L][:, 0:NH],
                     waits=[t_rq])
        yield

        w_ps2 = [v["eq_evac"]] if v["eq_evac"] else []
        if eqT_res is not None:
            dstv = eqT_res[:].rearrange("p (j t) -> p j t",
                                        t=T)[:, :, t * 128:(t + 1) * 128]
            tev_eq = transpose_tile(p, c, tpsL[L], eqbf[L], dstv, [t_eq],
                                    w_ps2, nblk=NJ)
        else:
            tev_eq = transpose_tile(p, c, tpsL[L], eqbf[L], eqstage[L][:],
                                    [t_eq], w_ps2, nblk=NJ,
                                    waits_evac=[p.stok(f"o_eq{L}")])
            p.dma_rr(
                "o_eq", NL, L,
                eqT_out[:, :, t * 128:(t + 1) * 128].rearrange("j p t -> p j t"),
                eqstage[L][:].rearrange("p (j t) -> p j t", t=128),
                waits=[tev_eq])
        v["eq_evac"] = tev_eq
        yield

        t_kmm = proj("k", [t_q_psfree])
        yield
        t_ek, t_k_psfree = elu(ekbf[L], t_kmm, war=[t_eq])
        v["kps"] = t_k_psfree
        yield
        t_eksq = p.emit("scalar", lambda e, L=L: e.activation(
            out=sqb[L][:], in_=ekbf[L][:], func=AF.Square),
            waits=[t_ek, t_nq])
        t_nk = p.emit("vector", lambda e, L=L: e.reduce_sum(
            out=ncol[L][:, 0:NH], in_=sqb[L][:].rearrange("p (h w) -> p h w", w=HD),
            axis=AX.X), waits=[t_eksq, t_sqq])
        t_sqk = p.emit("scalar", lambda e, L=L: e.activation(
            out=scol[L][:, 0:NH], in_=ncol[L][:, 0:NH], func=AF.Sqrt),
            waits=[t_nk, t_rq])
        t_rk = p.emit("vector", lambda e, L=L: e.reciprocal(
            out=rk[L][:, 0:NH], in_=scol[L][:, 0:NH]), waits=[t_sqk])
        yield

        t_vmm = proj("v", [t_k_psfree])
        yield
        t_vstt = p.emit("vector", lambda e, L=L: e.scalar_tensor_tensor(
            out=vpbf[L][:], in0=qps[L][:], scalar=1.0,
            in1=rk[L][:, 0:NH].unsqueeze(2).broadcast_to([128, NH, HD]),
            op0=ALU.mult, op1=ALU.mult), waits=[t_vmm, t_rk])
        v["vstt"] = t_vstt
        yield

        t_kv = None
        for h in range(NH):
            t_kv = p.emit("tensor", lambda e, h=h, t=t, L=L: e.matmul(
                out=kvps[0:64, h * 64:(h + 1) * 64],
                lhsT=ekbf[L][:, h * 64:(h + 1) * 64],
                rhs=vpbf[L][:, h * 64:(h + 1) * 64],
                start=(t == 0 and h % 8 == 0),
                stop=(t == NT - 1 and h % 8 == 7)),
                waits=[t_ek, t_vstt] if h == 0 else [])
        gkv["ek"] = t_eq if False else t_ek
        gkv["kvlast"] = t_kv

    for t0 in range(0, NT, NL):
        interleave_pairs([tile_body(t, t % NL)
                          for t in range(t0, min(t0 + NL, NT))])
    t_kv_last = gkv["kvlast"]

    t_kvev = p.emit("vector", lambda e: e.tensor_copy(
        kvsb[:], kvps[0:64, :]), waits=[t_kv_last])
    p.dma("o_kv", kvp_out[:], kvsb[:], waits=[t_kvev])


def build_A():
    p = Prog("A")
    c = common_consts(p)
    io = dict(x=p.din("x", [T, D], F32),
              wq_a=p.din("wq_a", [D + 1, D], F32),
              wk_a=p.din("wk_a", [D + 1, D], F32),
              wv_a=p.din("wv_a", [D + 1, D], F32),
              kvp=p.dout("kvp", [64, 1024], F32),
              eqT=p.dout("eqT", [NJ, 128, T], BF16),
              rq=p.dout("rq", [NT, 128, NH], F32))
    emit_A(p, c, io)
    return p.finalize()



def emit_B1(p, c, io, ntiles=NT, stage=99, fused=False, kv_wait=None):
    x_in = io["x"]
    eqT_res = io.get("eqT_res")
    rq_res = io.get("rq_res")
    eqT_in = io.get("eqT")
    rq_in = io.get("rq")
    kvp_in = io["kvp"]
    g1_in = io["g1r"]
    be1_in = io["be1r"]
    ao_out = io["attn_out"]
    atT_out = io["atT"]

    eqT_sb = (p.sb("eqT_sb", [128, NJ * T], BF16) if eqT_res is None
              else eqT_res)
    kvacc = p.sb("kvacc", [64, 1024], F32)
    kvh = p.sb("kvh", [64, 1024], BF16)
    kvblk = p.sb("kvblk", [128, NJ * 128], BF16)
    wstage = p.sb("wstage", [128, NJ * D], F32)
    wb_f = p.sb("wb_f", [1, D], F32)
    woh = p.sb("woh", [128, NJ * D], BF16)
    wob_h = p.sb("wob_h", [1, D], BF16)
    g1row = p.sb("g1row", [1, D], F32)
    be1row = p.sb("be1row", [1, D], F32)
    gb1 = p.sb("gb1", [128, D], F32)
    bb1 = p.sb("bb1", [128, D], F32)
    NL = 2
    x_t = [p.sb(f"x_t{i}", [128, D], F32) for i in range(NL)]
    rq_t = [p.sb(f"rq_t{i}", [128, NH], F32) for i in range(NL)]
    rqx = [p.sb(f"rqx{i}", [128, NH], F32) for i in range(NL)]
    ctxn = [p.sb(f"ctxn{i}", [128, D], BF16) for i in range(NL)]
    ctxT = [p.sb(f"ctxT{i}", [128, D], BF16) for i in range(NL)]
    u = [p.sb(f"u{i}", [128, D], F32) for i in range(NL)]
    s1col = [p.sb(f"s1col{i}", [128, 1], F32) for i in range(NL)]
    lnb = [dict(sqw=p.sb(f"sqw{i}", [128, D], F32),
                mcol=p.sb(f"mcol{i}", [128, 1], F32),
                vcol=p.sb(f"vcol{i}", [128, 1], F32),
                tcol=p.sb(f"tcol{i}", [128, 1], F32)) for i in range(NL)]
    ao = [p.sb(f"ao{i}", [128, D], F32) for i in range(NL)]
    ao2 = [p.sb(f"ao2{i}", [128, D], F32) for i in range(NL)]
    aof = [p.sb(f"aof{i}", [128, D], F32) for i in range(NL)]
    at_bf = [p.sb(f"at_bf{i}", [128, D], BF16) for i in range(NL)]
    atT_stage = [p.sb(f"atT_stage{i}", [128, D], BF16) for i in range(NL)]

    cps = [p.ps(f"cps{i}", [128, 1024], F32) for i in range(NL)]
    tps = [p.ps(f"tps{i}", [128, 1024], BF16) for i in range(NL)]
    psA = p.ps("psA", [128, 512], F32)

    t_eqT_l = []
    if eqT_res is None:
        for ch in range(4):
            t_eqT_l.append(p.dma_rr(
                "in_eq", 4, ch,
                eqT_sb[:].rearrange("p (j t) -> p j t", t=T)[:, 2 * ch:2 * ch + 2, :],
                eqT_in[2 * ch:2 * ch + 2].rearrange("j p t -> p j t")))
    if fused:
        tprev = p.dma("in_kv", kvacc[:], kvp_in[:], waits=[kv_wait],
                      queue="gpsimd")
    else:
        tprev = None
        for cc in range(NCORES):
            tprev = p.dma("in_kv", kvacc[:], kvp_in[cc],
                          waits=[tprev] if tprev else [], queue="gpsimd",
                          accum_op=(ALU.add if cc else ALU.bypass))
    t_kvh0 = p.emit("scalar", lambda e: e.activation(
        out=kvh[0:64, :], in_=kvacc[:], func=AF.Copy), waits=[tprev])
    t_z = p.emit("vector", lambda e: e.memset(kvblk[:], 0.0))
    kv_toks = []
    for j in range(NJ):
        kv_toks.append(p.dma_rr("in_kv2", 4, 2 * j,
                                kvblk[0:64, j * 128:j * 128 + 64],
                                kvh[0:64, (2 * j) * 64:(2 * j + 1) * 64],
                                waits=[t_kvh0, t_z]))
        kv_toks.append(p.dma_rr("in_kv2", 4, 2 * j + 1,
                                kvblk[64:128, j * 128 + 64:(j + 1) * 128],
                                kvh[0:64, (2 * j + 1) * 64:(2 * j + 2) * 64],
                                waits=[t_kvh0, t_z]))
    t_kvh = kv_toks[-4:]
    wready = load_cast_weight(p, c, None, NJ, D, wstage, woh, wb_f, wob_h,
                              waits=c["ready"], win=io["wo_a"])
    t_g1 = p.dma("in", g1row[:], g1_in[:], waits=wready)
    t_be1 = p.dma("in", be1row[:], be1_in[:], waits=wready)
    t_in = p.stok("in")
    tb1 = bcast_row_f32(p, c, gb1, g1row, D, psA, waits=[t_in])
    tb2 = bcast_row_f32(p, c, bb1, be1row, D, psA, waits=[t_in] + tb1)
    t_bc = tb1 + tb2

    pv = [dict(rqx=None, lnsq=None, ctx_stt=None, ctxT_ev=None, u=None,
               lnout=None) for _ in range(NL)]

    def tile_body(t, L):
        v = pv[L]
        t_x = p.dma_rr("in_x", NL, L, x_t[L][:], x_in[t * 128:(t + 1) * 128, :],
                       waits=[v["u"]] if v["u"] else wready)
        if rq_res is not None:
            yield
            t_rqx = p.emit("scalar", lambda e, L=L, t=t: e.mul(
                rqx[L][:], rq_res[:, t * NH:(t + 1) * NH], 0.125), waits=[])
        else:
            t_rq = p.dma_rr("in_r", NL, L, rq_t[L][:], rq_in[t],
                            waits=[v["rqx"]] if v["rqx"] else wready)
            yield
            t_rqx = p.emit("scalar", lambda e, L=L: e.mul(
                rqx[L][:], rq_t[L][:], 0.125), waits=[t_rq])
        v["rqx"] = t_rqx
        yield

        t_cmm = None
        for j in range(NJ):
            t_cmm = p.emit("tensor", lambda e, j=j, t=t, L=L: e.matmul(
                out=cps[L][:, j * 128:(j + 1) * 128],
                lhsT=eqT_sb[:, j * T + t * 128:j * T + (t + 1) * 128],
                rhs=kvblk[:, j * 128:(j + 1) * 128],
                start=(j % 4 == 0), stop=(j % 4 == 3)),
                waits=(t_eqT_l + t_kvh +
                       [x for x in (v["ctx_stt"], v["u"]) if x]) if j == 0 else [])
        t_ctx = p.emit("vector", lambda e, L=L: e.scalar_tensor_tensor(
            out=ctxn[L][:], in0=cps[L][:], scalar=1.0,
            in1=rqx[L][:].unsqueeze(2).broadcast_to([128, NH, HD]),
            op0=ALU.mult, op1=ALU.mult), waits=[t_cmm, t_rqx])
        yield

        t_ctxT = transpose_tile(p, c, tps[L], ctxn[L], ctxT[L][:], [t_ctx],
                                [v["ctxT_ev"]] if v["ctxT_ev"] else [],
                                nblk=NJ)
        yield

        t_omm = None
        for n in range(2):
            for kk in range(NJ):
                t_omm = p.emit("tensor", lambda e, n=n, kk=kk, L=L: e.matmul(
                    out=cps[L][:, n * 512:(n + 1) * 512],
                    lhsT=ctxT[L][:, kk * 128:(kk + 1) * 128],
                    rhs=woh[:, kk * D + n * 512:kk * D + (n + 1) * 512],
                    start=(kk == 0), stop=False),
                    waits=([t_ctxT] + wready +
                           ([v["u"]] if v["u"] else [])) if kk == 0 and n == 0 else [])
            t_omm = p.emit("tensor", lambda e, n=n, L=L: e.matmul(
                out=cps[L][:, n * 512:(n + 1) * 512],
                lhsT=c["ones_bf"][0:1, 0:128],
                rhs=wob_h[:, n * 512:(n + 1) * 512],
                start=False, stop=True), waits=[])
            yield

        t_u = p.emit("vector", lambda e, L=L: e.scalar_tensor_tensor(
            out=u[L][:], in0=cps[L][:], scalar=1.0, in1=x_t[L][:],
            op0=ALU.mult, op1=ALU.add, accum_out=s1col[L][:, 0:1]),
            waits=[t_omm, t_x] + ([v["lnsq"]] if v["lnsq"] else []))
        v["u"] = t_u
        yield

        w_ln = [t_u] + ([v["lnout"]] if v["lnout"] else [])
        t_ao, t_sq1 = ln_stats_apply(p, lnb[L], u[L], s1col[L], ao[L], 1e-12,
                                     w_ln, None, "ln1")
        v["lnsq"] = t_sq1
        yield
        t_m1 = p.emit("vector", lambda e, L=L: e.tensor_mul(
            ao2[L][:], ao[L][:], gb1[:]), waits=[t_ao] + t_bc)
        t_aof = p.emit("vector", lambda e, L=L: e.tensor_add(
            aof[L][:], ao2[L][:], bb1[:]),
            waits=[t_m1, p.stok(f"o_ao{L}")])
        v["lnout"] = t_aof
        yield

        t_cast = p.emit("scalar", lambda e, L=L: e.activation(
            out=at_bf[L][:], in_=aof[L][:], func=AF.Copy), waits=[t_aof])
        p.dma_rr("o_ao", NL, L, ao_out[t * 128:(t + 1) * 128, :], aof[L][:],
                 waits=[t_aof])
        yield
        t_atT = transpose_tile(p, c, tps[L], at_bf[L], atT_stage[L][:],
                               [t_cast], [t_ctxT], nblk=NJ,
                               waits_evac=[p.stok(f"o_at{L}")])
        v["ctxT_ev"] = t_atT
        v["ctx_stt"] = t_ctx
        p.dma_rr("o_at", NL, L,
                 atT_out[:, :, t * 128:(t + 1) * 128].rearrange("j p t -> p j t"),
                 atT_stage[L][:].rearrange("p (j t) -> p j t", t=128),
                 waits=[t_atT])

    for t0 in range(0, ntiles, NL):
        interleave_pairs([tile_body(t, t % NL)
                          for t in range(t0, min(t0 + NL, ntiles))])


def build_B1(ntiles=NT, stage=99):
    p = Prog("B1")
    c = common_consts(p)
    io = dict(x=p.din("x", [T, D], F32),
              eqT=p.din("eqT", [NJ, 128, T], BF16),
              rq=p.din("rq", [NT, 128, NH], F32),
              kvp=p.din("kvp", [NCORES, 64, 1024], F32),
              g1r=p.din("g1r", [1, D], F32),
              be1r=p.din("be1r", [1, D], F32),
              wo_a=p.din("wo_a", [D + 1, D], F32),
              attn_out=p.dout("attn_out", [T, D], F32),
              atT=p.dout("atT", [NJ, 128, T], BF16))
    emit_B1(p, c, io, ntiles=ntiles, stage=stage)
    return p.finalize()



def emit_B2a(p, c, io):
    atT_in = io["atT"]
    w1_in = io["w1_a"]
    b1c_in = io["b1cB"]
    hT_out = io["hT"]

    atT_sb = p.sb("atT_sb", [128, NJ * T], BF16)
    wstage = [p.sb(f"wstage{i}", [128, FF], F32) for i in range(2)]
    w1h = p.sb("w1h", [128, NJ * FF], BF16)
    b1cB = p.sb("b1cB_sb", [128, 32], F32)
    hbig = [p.sb(f"hbig{i}", [128, 8 * 512], BF16) for i in range(2)]
    hps = [p.ps(f"hps{i}", [128, 512], F32) for i in range(2)]

    t_atT_l = []
    for ach in range(4):
        t_atT_l.append(p.dma_rr(
            "in_at", 4, ach,
            atT_sb[:].rearrange("p (j t) -> p j t", t=T)[:, 2 * ach:2 * ach + 2, :],
            atT_in[2 * ach:2 * ach + 2].rearrange("j p t -> p j t")))
    tb = p.dma("in", b1cB[:], b1c_in[:])
    wready = [tb] + t_atT_l
    wcast = {}
    for ch in range(NJ):
        tw = p.dma_rr("in_w", 2, ch, wstage[ch % 2][:],
                      w1_in[ch * 128:(ch + 1) * 128, :],
                      waits=wcast.get(ch - 2, c["ready"]))
        cs = [p.emit("scalar", lambda e, ch=ch: e.activation(
            out=w1h[:, ch * FF:(ch + 1) * FF],
            in_=wstage[ch % 2][:], func=AF.Copy), waits=[tw])]
        wcast[ch] = cs
        wready += cs

    prev_ev = {}
    prev_st = {}
    i = 0
    for cch in range(4):
        for m in range(32):
            par = i % 2
            grp = (i // 8) % 2
            t_mm = None
            for kk in range(NJ):
                t_mm = p.emit("tensor", lambda e, cch=cch, m=m, kk=kk, par=par: e.matmul(
                    out=hps[par][:],
                    lhsT=w1h[:, kk * FF + m * 128:kk * FF + (m + 1) * 128],
                    rhs=atT_sb[:, kk * T + cch * 512:kk * T + (cch + 1) * 512],
                    start=(kk == 0), stop=(kk == NJ - 1)),
                    waits=(wready + list(prev_ev.get(par, []))) if kk == 0 else [])
            t_ev = p.emit("scalar", lambda e, par=par, grp=grp, m=m: e.activation(
                out=hbig[grp][:, (m % 8) * 512:(m % 8 + 1) * 512],
                in_=hps[par][:], func=AF.Gelu, bias=b1cB[:, m:m + 1]),
                waits=[t_mm] + (list(prev_st.get(grp, [])) if m % 8 < 2 else []))
            prev_ev[par] = [t_ev]
            if m % 8 == 7:
                t_st = p.dma_rr("o_h", 2, grp,
                                hT_out[m - 7:m + 1, :, cch * 512:(cch + 1) * 512]
                                .rearrange("m p tt -> p m tt"),
                                hbig[grp][:].rearrange("p (m tt) -> p m tt", tt=512),
                                waits=[t_ev])
                prev_st[grp] = [t_st]
            i += 1


def build_B2a():
    p = Prog("B2a")
    c = common_consts(p)
    io = dict(atT=p.din("atT", [NJ, 128, T], BF16),
              w1_a=p.din("w1_a", [NJ * 128 + 1, FF], F32),
              b1cB=p.din("b1cB", [128, 32], F32),
              hT=p.dout("hT", [32, 128, T], BF16))
    emit_B2a(p, c, io)
    return p.finalize()



def emit_B2b(p, c, io):
    hT_in = io["hT"]
    ao_in = io["attn_out"]
    x_in = io["x"]
    w2_in = io["w2_a"]
    g2_in = io["g2r"]
    be2_in = io["be2r"]
    x1_out = io["x1"]

    wstage = [p.sb(f"wstage{i}", [128, D], F32) for i in range(2)]
    w2h = p.sb("w2h", [128, 32 * D], BF16)
    w2b_f = wstage[0][0:1, 0:D]
    w2b_h = p.sb("w2b_h", [1, D], BF16)
    g2row = p.sb("g2row", [1, D], F32)
    be2row = p.sb("be2row", [1, D], F32)
    gb2 = p.sb("gb2", [128, D], F32)
    bb2 = p.sb("bb2", [128, D], F32)
    NL = 2
    hT_c = [p.sb(f"hT_c{i}", [128, 32 * 256], BF16) for i in range(2)]
    x_t = [p.sb(f"x_t{i}", [128, D], F32) for i in range(NL)]
    at_t = [p.sb(f"at_t{i}", [128, D], F32) for i in range(NL)]
    u = [p.sb(f"u{i}", [128, D], F32) for i in range(NL)]
    s1col = [p.sb(f"s1col{i}", [128, 1], F32) for i in range(NL)]
    sqw0 = p.sb("sqw0", [128, D], F32)
    lnb = [dict(sqw=sqw0,
                mcol=p.sb(f"mcol{i}", [128, 1], F32),
                vcol=p.sb(f"vcol{i}", [128, 1], F32),
                tcol=p.sb(f"tcol{i}", [128, 1], F32)) for i in range(NL)]
    ao0 = p.sb("ao0", [128, D], F32)
    ao = [ao0] * NL
    aof0 = p.sb("aof0", [128, D], F32)
    aof = [aof0] * NL
    u2 = at_t
    x1_t = [p.sb(f"x1_t{i}", [128, D], F32) for i in range(NL)]
    ao2 = u

    ops = [p.ps(f"ops{i}", [128, 1024], F32) for i in range(3)]
    psA = p.ps("psA", [128, 512], F32)

    wready = []
    wcast = {}
    for ch in range(32):
        tw = p.dma_rr("in_w", 2, ch, wstage[ch % 2][:],
                      w2_in[ch * 128:(ch + 1) * 128, :],
                      waits=wcast.get(ch - 2, list(c["ready"])))
        cs = [p.emit("scalar", lambda e, ch=ch: e.activation(
            out=w2h[:, ch * D:(ch + 1) * D],
            in_=wstage[ch % 2][:], func=AF.Copy), waits=[tw])]
        wcast[ch] = cs
        wready += cs
    tb = p.dma("in", w2b_f[:], w2_in[32 * 128:32 * 128 + 1, :],
               waits=wcast[31])
    wready.append(p.emit("scalar", lambda e: e.activation(
        out=w2b_h[:], in_=w2b_f[:], func=AF.Copy), waits=[tb]))
    t_g2 = p.dma("in", g2row[:], g2_in[:], waits=wready)
    t_be2 = p.dma("in", be2row[:], be2_in[:], waits=wready)
    t_in = p.stok("in")
    tb1 = bcast_row_f32(p, c, gb2, g2row, D, psA, waits=[t_in])
    tb2 = bcast_row_f32(p, c, bb2, be2row, D, psA, waits=[t_in] + tb1)
    t_bc = tb1 + tb2

    pv = [dict(u=None, mm=None, lnout=None, lnsq=None, st=None)
          for _ in range(NL)]
    g_h = {}
    g_mm = {}
    g_u = {}

    def tile_body(t, L):
        v = pv[L]
        pr = t // 2
        if t % 2 == 0:
            war = [g_mm[pr - 2]] if (pr - 2) in g_mm else wready
            g_h[pr] = p.dma_rr(
                "in_h", 2, pr,
                hT_c[pr % 2][:].rearrange("p (k tt) -> p k tt", tt=256),
                hT_in[:, :, t * 128:(t + 2) * 128].rearrange("k p tt -> p k tt"),
                waits=war)
        t_h = g_h[pr]
        t_at = p.dma_rr("in_a", NL, L, at_t[L][:], ao_in[t * 128:(t + 1) * 128, :],
                     waits=[v["lnout"]] if v["lnout"] else wready)
        t_x = p.dma_rr("in_x", NL, L, x_t[L][:], x_in[t * 128:(t + 1) * 128, :],
                    waits=[v["lnout"]] if v["lnout"] else wready)
        yield

        P = t % 3
        t_mm = None
        for n in range(2):
            for kk in range(32):
                t_mm = p.emit("tensor", lambda e, n=n, kk=kk, P=P, t=t, pr=pr: e.matmul(
                    out=ops[P][:, n * 512:(n + 1) * 512],
                    lhsT=hT_c[pr % 2][:, kk * 256 + (t % 2) * 128:
                                      kk * 256 + (t % 2) * 128 + 128],
                    rhs=w2h[:, kk * D + n * 512:kk * D + (n + 1) * 512],
                    start=(kk == 0), stop=False),
                    waits=([t_h] + wready +
                           ([g_u[t - 3]] if (t - 3) in g_u else [])) if kk == 0 and n == 0 else [])
            t_mm = p.emit("tensor", lambda e, n=n, P=P: e.matmul(
                out=ops[P][:, n * 512:(n + 1) * 512],
                lhsT=c["ones_bf"][0:1, 0:128],
                rhs=w2b_h[:, n * 512:(n + 1) * 512],
                start=False, stop=True), waits=[])
            yield
        v["mm"] = t_mm
        g_mm[pr] = t_mm

        t_u = p.emit("vector", lambda e, L=L, P=P: e.scalar_tensor_tensor(
            out=u[L][:], in0=ops[P][:], scalar=1.0, in1=at_t[L][:],
            op0=ALU.mult, op1=ALU.add, accum_out=s1col[L][:, 0:1]),
            waits=[t_mm, t_at] + ([v["lnsq"]] if v["lnsq"] else []))
        g_u[t] = t_u
        v["u"] = t_u
        yield
        w_ln = [t_u] + ([v["lnout"]] if v["lnout"] else [])
        t_ao, t_sq1 = ln_stats_apply(p, lnb[L], u[L], s1col[L], ao[L], 1e-12,
                                     w_ln, None, "ln2")
        t_m1 = p.emit("vector", lambda e, L=L: e.tensor_mul(
            ao2[L][:], ao[L][:], gb2[:]), waits=[t_ao] + t_bc)
        t_aof = p.emit("vector", lambda e, L=L: e.tensor_add(
            aof[L][:], ao2[L][:], bb2[:]), waits=[t_m1])
        t_u2 = p.emit("vector", lambda e, L=L: e.scalar_tensor_tensor(
            out=u2[L][:], in0=aof[L][:], scalar=1.0, in1=x_t[L][:],
            op0=ALU.mult, op1=ALU.add, accum_out=s1col[L][:, 0:1]),
            waits=[t_aof, t_x])
        yield
        t_x1, t_sq2 = ln_stats_apply(p, lnb[L], u2[L], s1col[L], x1_t[L],
                                     1e-5, [t_u2, p.stok(f"o_x1{L}")], None, "lnb")
        v["lnsq"] = t_sq2
        v["lnout"] = t_x1
        p.dma_rr("o_x1", NL, L, x1_out[t * 128:(t + 1) * 128, :], x1_t[L][:],
                 waits=[t_x1])

    for t0 in range(0, NT, NL):
        interleave_pairs([tile_body(t, t % NL)
                          for t in range(t0, min(t0 + NL, NT))])


def build_B2b():
    p = Prog("B2b")
    c = common_consts(p)
    io = dict(hT=p.din("hT", [32, 128, T], BF16),
              attn_out=p.din("attn_out", [T, D], F32),
              x=p.din("x", [T, D], F32),
              w2_a=p.din("w2_a", [32 * 128 + 1, D], F32),
              g2r=p.din("g2r", [1, D], F32),
              be2r=p.din("be2r", [1, D], F32),
              x1=p.dout("x1", [T, D], F32))
    emit_B2b(p, c, io)
    return p.finalize()



def emit_C(p, c, io):
    x1_in = io["x1"]
    gm_in = io["gmr"]
    bb_in = io["bbr"]
    mu_in = io["mur"]
    sc_in = io["scl"]
    Praw_out = io["Praw"]
    Sloc_out = io["Sloc"]
    E_out = io["E"]
    Zc_out = io["Zc"]

    onescol_f = p.sb("onescol_f", [128, 1], F32)
    gmrow = p.sb("gmrow", [1, D], F32)
    bbrow = p.sb("bbrow", [1, D], F32)
    gb = p.sb("gb", [128, D], F32)
    bb = p.sb("bb", [128, D], F32)
    mustage = p.sb("mustage", [128, NJ * ES], F32)
    musq = p.sb("musq", [128, NJ * ES], F32)
    srow = p.sb("srow", [1, ES], F32)
    srow2 = p.sb("srow2", [1, ES], F32)
    rmub = p.sb("rmub", [128, ES], F32)
    mun_h = p.sb("mun_h", [128, NJ * ES], BF16)
    sccol = p.sb("sccol", [128, 1], F32)
    sclrow = p.sb("sclrow", [1, 1], F32)
    NL = 2
    x1_t = [p.sb(f"x1_t{i}", [128, D], F32) for i in range(NL)]
    s1col = [p.sb(f"s1col{i}", [128, 1], F32) for i in range(NL)]
    lnb = [dict(sqw=p.sb(f"sqw{i}", [128, D], F32),
                mcol=p.sb(f"mcol{i}", [128, 1], F32),
                vcol=p.sb(f"vcol{i}", [128, 1], F32),
                tcol=p.sb(f"tcol{i}", [128, 1], F32)) for i in range(NL)]
    xn = [p.sb(f"xn{i}", [128, D], F32) for i in range(NL)]
    xn2a = [p.sb(f"xn2a{i}", [128, D], F32) for i in range(NL)]
    xn2 = [p.sb(f"xn2{i}", [128, D], F32) for i in range(NL)]
    nrm = [p.sb(f"nrm{i}", [128, 1], F32) for i in range(NL)]
    nrm2 = [p.sb(f"nrm2{i}", [128, 1], F32) for i in range(NL)]
    rn = [p.sb(f"rn{i}", [128, 1], F32) for i in range(NL)]
    xu_bf = [p.sb(f"xu_bf{i}", [128, D], BF16) for i in range(NL)]
    xnh = [p.sb(f"xnh{i}", [128, D], BF16) for i in range(NL)]
    xuT_t = [p.sb(f"xuT_t{i}", [128, D], BF16) for i in range(NL)]
    E_bf = [p.sb(f"E_bf{i}", [128, ES], BF16) for i in range(NL)]
    zc_col = [p.sb(f"zc_col{i}", [128, 1], F32) for i in range(NL)]
    Psb = p.sb("Psb", [128, NJ * ES], F32)
    slrow = p.sb("slrow", [1, ES], F32)

    SL_ps = p.ps("SL_ps", [1, ES], F32)
    Pps_a = p.ps("Pps_a", [128, 1024], F32)
    Pps_b = p.ps("Pps_b", [128, 1024], F32)
    L_ps = p.ps("L_ps", [128, ES], F32)
    p.push_region("pp")
    psA = p.ps("psA", [128, 512], F32)

    t_oc = p.emit("vector", lambda e: e.memset(onescol_f[:], 1.0))
    t_g = p.dma("in", gmrow[:], gm_in[:], waits=c["ready"])
    t_b = p.dma("in", bbrow[:], bb_in[:])
    t_mu = p.dma("in", mustage[:].rearrange("p (j s) -> p j s", s=ES),
                 mu_in[:].rearrange("(j p) s -> p j s", p=128))
    t_sc = p.dma("in", sclrow[:], sc_in[:])
    t_in = p.stok("in")
    tb1 = bcast_row_f32(p, c, gb, gmrow, D, psA, waits=[t_in])
    tb2 = bcast_row_f32(p, c, bb, bbrow, D, psA, waits=[t_in] + tb1)
    t_bc = tb1 + tb2
    t_scm = p.emit("tensor", lambda e: e.matmul(
        out=psA[:, 0:1], lhsT=c["ones_f"][:, :], rhs=sclrow[:, 0:1],
        start=True, stop=True), waits=[t_in] + t_bc)
    t_scc = p.emit("vector", lambda e: e.tensor_copy(sccol[:], psA[:, 0:1]),
                   waits=[t_scm])
    t_msq = p.emit("scalar", lambda e: e.activation(
        out=musq[:], in_=mustage[:], func=AF.Square), waits=[t_in])
    t_s2m = None
    for j in range(NJ):
        t_s2m = p.emit("tensor", lambda e, j=j: e.matmul(
            out=SL_ps[0:1, :], lhsT=onescol_f[:, 0:1],
            rhs=musq[:, j * ES:(j + 1) * ES],
            start=(j == 0), stop=(j == NJ - 1)),
            waits=[t_msq, t_oc, t_scm] if j == 0 else [])
    t_sr = p.emit("vector", lambda e: e.tensor_copy(srow[:], SL_ps[0:1, :]),
                  waits=[t_s2m])
    t_sq = p.emit("scalar", lambda e: e.activation(
        out=srow2[:], in_=srow[:], func=AF.Sqrt), waits=[t_sr])
    t_mx = p.emit("vector", lambda e: e.tensor_scalar_max(
        srow[:], srow2[:], 1e-12), waits=[t_sq])
    t_rc = p.emit("vector", lambda e: e.reciprocal(srow2[:], srow[:]),
                  waits=[t_mx])
    tbm = bcast_row_f32(p, c, rmub, srow2, ES, psA, waits=[t_rc])
    t_mn = []
    for j in range(NJ):
        t_mn.append(p.emit("vector", lambda e, j=j: e.tensor_mul(
            mun_h[:, j * ES:(j + 1) * ES], mustage[:, j * ES:(j + 1) * ES],
            rmub[:]), waits=tbm))
    p.pop_region()
    tps = [p.ps(f"tps{i}", [128, 1024], BF16) for i in range(NL)]

    t_prev_E = [None]
    pv = [dict(sq=None, xuT=None, P=None) for _ in range(NL)]

    def tile_body(t, L):
        v = pv[L]
        t_x1 = p.dma_rr("in_x", NL, L, x1_t[L][:],
                        x1_in[t * 128:(t + 1) * 128, :],
                        waits=[v["sq"]] if v["sq"] else [t_in])
        t_s1 = p.emit("vector", lambda e, L=L: e.reduce_sum(
            out=s1col[L][:, 0:1], in_=x1_t[L][:], axis=AX.X), waits=[t_x1])
        yield
        t_xn, t_sqa = ln_stats_apply(p, lnb[L], x1_t[L], s1col[L], xn[L], 1e-5,
                                     [t_x1, t_s1], None, "moe_ln")
        yield
        t_m1 = p.emit("vector", lambda e, L=L: e.tensor_mul(
            xn2a[L][:], xn[L][:], gb[:]), waits=[t_xn] + t_bc)
        t_xn2 = p.emit("vector", lambda e, L=L: e.tensor_add(
            xn2[L][:], xn2a[L][:], bb[:]), waits=[t_m1])
        yield
        t_sqb = p.emit("scalar", lambda e, L=L: e.activation(
            out=lnb[L]["sqw"][:], in_=xn2[L][:], func=AF.Square,
            accum_out=nrm[L][:, 0:1]), waits=[t_xn2, t_sqa])
        v["sq"] = t_sqb
        yield
        t_nsq = p.emit("scalar", lambda e, L=L: e.activation(
            out=nrm2[L][:, 0:1], in_=nrm[L][:, 0:1], func=AF.Sqrt),
            waits=[t_sqb])
        t_nmx = p.emit("vector", lambda e, L=L: e.tensor_scalar_max(
            nrm[L][:, 0:1], nrm2[L][:, 0:1], 1e-12), waits=[t_nsq])
        t_nrc = p.emit("vector", lambda e, L=L: e.reciprocal(
            nrm2[L][:, 0:1], nrm[L][:, 0:1]), waits=[t_nmx])
        t_rn = p.emit("vector", lambda e, L=L: e.tensor_mul(
            rn[L][:, 0:1], nrm2[L][:, 0:1], sccol[:, 0:1]),
            waits=[t_nrc, t_scc])
        yield
        t_xu = p.emit("vector", lambda e, L=L: e.tensor_scalar(
            out=xu_bf[L][:], in0=xn2[L][:], scalar1=rn[L][:, 0:1],
            scalar2=None, op0=ALU.mult), waits=[t_rn])
        t_xnh = p.emit("scalar", lambda e, L=L: e.activation(
            out=xnh[L][:], in_=xn2[L][:], func=AF.Copy),
            waits=[t_xn2] + ([v["P"]] if v["P"] else []))
        yield

        t_xuT = transpose_tile(p, c, tps[L], xu_bf[L], xuT_t[L][:], [t_xu],
                               [v["xuT"]] if v["xuT"] else [], nblk=NJ)
        v["xuT"] = t_xuT
        yield
        t_lg = None
        for j in range(NJ):
            t_lg = p.emit("tensor", lambda e, j=j, L=L: e.matmul(
                out=L_ps[:, :], lhsT=xuT_t[L][:, j * 128:(j + 1) * 128],
                rhs=mun_h[:, j * ES:(j + 1) * ES],
                start=(j == 0), stop=(j == NJ - 1)),
                waits=([t_xuT] + t_mn +
                       ([t_prev_E[0]] if t_prev_E[0] else [])) if j == 0 else [])
        t_E = p.emit("scalar", lambda e, L=L: e.activation(
            out=E_bf[L][:], in_=L_ps[:], func=AF.Exp,
            accum_out=zc_col[L][:, 0:1]),
            waits=[t_lg, p.stok(f"o_E{L}"), p.stok(f"o_Z{L}")])
        t_prev_E[0] = t_E
        p.dma_rr("o_E", NL, L, E_out[t], E_bf[L][:], waits=[t_E])
        p.dma_rr("o_Z", NL, L, Zc_out[t], zc_col[L][:, 0:1], waits=[t_E])
        yield
        t_P = None
        for jd in range(NJ):
            pps = Pps_a if jd < 4 else Pps_b
            col = (jd % 4) * ES
            t_P = p.emit("tensor", lambda e, jd=jd, pps=pps, col=col, t=t, L=L: e.matmul(
                out=pps[:, col:col + ES],
                lhsT=xnh[L][:, jd * 128:(jd + 1) * 128],
                rhs=E_bf[L][:],
                start=(t == 0 and jd % 2 == 0), stop=(t == NT - 1 and jd % 2 == 1)),
                waits=[t_E, t_xnh] if jd == 0 else [])
        v["P"] = t_P
        t_SL[0] = p.emit("tensor", lambda e, t=t, L=L: e.matmul(
            out=SL_ps[0:1, :], lhsT=c["onescol_bf"][:, 0:1], rhs=E_bf[L][:],
            start=(t == 0), stop=(t == NT - 1)),
            waits=[t_E] + ([t_sr] if t == 0 else []))

    t_SL = [None]
    for t0 in range(0, NT, NL):
        interleave_pairs([tile_body(t, t % NL)
                          for t in range(t0, min(t0 + NL, NT))])
    t_SL = t_SL[0]

    t_Pall = [pv[0]["P"], pv[1]["P"]]
    t_e1 = p.emit("vector", lambda e: e.tensor_copy(
        Psb[:, 0:1024], Pps_a[:]), waits=t_Pall)
    t_e2 = p.emit("vector", lambda e: e.tensor_copy(
        Psb[:, 1024:2048], Pps_b[:]), waits=t_Pall)
    t_e3 = p.emit("vector", lambda e: e.tensor_copy(
        slrow[:], SL_ps[0:1, :]), waits=[t_SL])
    p.dma("o_P", Praw_out[:], Psb[:], waits=[t_e1, t_e2])
    p.dma("o_S", Sloc_out[:], slrow[:], waits=[t_e3])


def build_C():
    p = Prog("C")
    c = common_consts(p)
    io = dict(x1=p.din("x1", [T, D], F32),
              gmr=p.din("gmr", [1, D], F32),
              bbr=p.din("bbr", [1, D], F32),
              mur=p.din("mur", [D, ES], F32),
              scl=p.din("scl", [1, 1], F32),
              Praw=p.dout("Praw", [128, NJ * ES], F32),
              Sloc=p.dout("Sloc", [1, ES], F32),
              E=p.dout("E", [NT, 128, ES], BF16),
              Zc=p.dout("Zc", [NT, 128, 1], F32))
    emit_C(p, c, io)
    return p.finalize()



def emit_D(p, c, io, fused=False, pr_wait=None, edges_out=None):
    x1_in = io["x1"]
    E_in = io["E"]
    Zc_in = io["Zc"]
    Praw_in = io["Praw"]
    Sloc_in = io["Sloc"]
    w1_in = io["w1e"]
    b1c_in = io["b1c"]
    w2_in = io["w2e"]
    x2_out = io["x2"]
    scratch = io["so_scratch"]

    Psb = p.sb("Psb", [128, NJ * ES], F32)
    Srow = p.sb("Srow", [1, ES], F32)
    rSrow = p.sb("rSrow", [1, ES], F32)
    rSb = p.sb("rSb", [128, ES], F32)
    slotin_h = p.sb("slotin_h", [128, NJ * ES], BF16)
    wstage = [p.sb(f"wstage{i}", [128, 2048], F32) for i in range(4)]
    whbuf = p.sb("whbuf", [128, EXP * 2048], BF16)
    b1cols = p.sb("b1cols", [128, EXP * 2], F32)
    b2stage = [p.sb(f"b2stage{i}", [1, D], F32) for i in range(2)]
    b2cur = [p.sb(f"b2cur{i}", [1, D], BF16) for i in range(2)]
    HT_sb = p.sb("HT_sb", [128, EXP * 32], BF16)
    so_stage = [p.sb(f"so_stage{i}", [16, D], BF16) for i in range(2)]
    soT_sb = p.sb("soT_sb", [128, 2 * D], BF16)
    NL = 2
    E_t = [p.sb(f"E_t{i}", [128, ES], BF16) for i in range(NL)]
    zc_t = [p.sb(f"zc_t{i}", [128, 1], F32) for i in range(NL)]
    rZ = [p.sb(f"rZ{i}", [128, 1], F32) for i in range(NL)]
    comb = [p.sb(f"comb{i}", [128, ES], BF16) for i in range(NL)]
    combT = [p.sb(f"combT{i}", [128, ES], BF16) for i in range(NL)]
    x1_t = [p.sb(f"x1_t{i}", [128, D], F32) for i in range(NL)]
    u = [p.sb(f"u{i}", [128, D], F32) for i in range(NL)]
    s1col = [p.sb(f"s1col{i}", [128, 1], F32) for i in range(NL)]
    lnb = [dict(sqw=p.sb(f"sqw{i}", [128, D], F32),
                mcol=p.sb(f"mcol{i}", [128, 1], F32),
                vcol=p.sb(f"vcol{i}", [128, 1], F32),
                tcol=p.sb(f"tcol{i}", [128, 1], F32)) for i in range(NL)]
    x2_t = [p.sb(f"x2_t{i}", [128, D], F32) for i in range(NL)]

    p.push_region("pp")
    psA = p.ps("psA", [128, 512], F32)
    HTps = [p.ps(f"HTps{i}", [128, 32], F32) for i in range(2)]
    SOps = [p.ps(f"SOps{i}", [16, D], F32) for i in range(2)]

    if fused:
        tprev = p.dma("in_P", Psb[:], Praw_in[:], waits=[pr_wait],
                      queue="gpsimd")
        tprevS = p.dma("in_S", Srow[:], Sloc_in[:], waits=[pr_wait],
                       queue="gpsimd")
    else:
        tprev = None
        for cc in range(NCORES):
            tprev = p.dma("in_P", Psb[:], Praw_in[cc],
                          waits=[tprev] if tprev else c["ready"], queue="gpsimd",
                          accum_op=(ALU.add if cc else ALU.bypass))
        tprevS = None
        for cc in range(NCORES):
            tprevS = p.dma("in_S", Srow[:], Sloc_in[cc],
                           waits=[tprevS] if tprevS else c["ready"], queue="gpsimd",
                           accum_op=(ALU.add if cc else ALU.bypass))
    t_rS = p.emit("vector", lambda e: e.reciprocal(rSrow[:], Srow[:]),
                  waits=[tprevS])
    tbs = bcast_row_f32(p, c, rSb, rSrow, ES, psA, waits=[t_rS])
    t_si = []
    for jd in range(NJ):
        t_si.append(p.emit("vector", lambda e, jd=jd: e.tensor_mul(
            slotin_h[:, jd * ES:(jd + 1) * ES], Psb[:, jd * ES:(jd + 1) * ES],
            rSb[:]), waits=tbs + [tprev]))

    t_b1 = p.dma("in", b1cols[:], b1c_in[:], waits=c["ready"])
    w1ready = [t_b1]
    wcast = {}
    for ee in range(EXP):
        tw = p.dma_rr("in_w", 4, ee,
                      wstage[ee % 4][:].rearrange("p (k h) -> p k h", h=MOE_H),
                      w1_in[ee].rearrange("(k p) h -> p k h", p=128),
                      waits=wcast.get(ee - 4, [t_b1]))
        cs = [p.emit("scalar", lambda e, ee=ee: e.activation(
            out=whbuf[:, ee * 2048:(ee + 1) * 2048], in_=wstage[ee % 4][:],
            func=AF.Copy), waits=[tw])]
        wcast[ee] = cs
        w1ready += cs

    t_H_ev = []
    hevs = {}
    t_mm = None
    for ee in range(EXP):
        for kk in range(2):
            for jd in range(NJ):
                t_mm = p.emit("tensor", lambda e, ee=ee, kk=kk, jd=jd: e.matmul(
                    out=HTps[ee % 2][:, kk * 16:(kk + 1) * 16],
                    lhsT=whbuf[:, ee * 2048 + jd * MOE_H + kk * 128:
                               ee * 2048 + jd * MOE_H + (kk + 1) * 128],
                    rhs=slotin_h[:, jd * ES + ee * SLOT:jd * ES + (ee + 1) * SLOT],
                    start=(kk == 0 and jd == 0), stop=(kk == 1 and jd == NJ - 1)),
                    waits=(t_si + w1ready + hevs.get(ee - 2, [])) if kk == 0 and jd == 0 else [])
        evs = []
        for kk in range(2):
            evs.append(p.emit("scalar", lambda e, ee=ee, kk=kk: e.activation(
                out=HT_sb[:, ee * 32 + kk * 16:ee * 32 + (kk + 1) * 16],
                in_=HTps[ee % 2][:, kk * 16:(kk + 1) * 16], func=AF.Gelu,
                bias=b1cols[:, ee * 2 + kk:ee * 2 + kk + 1]), waits=[t_mm]))
        hevs[ee] = evs
        t_H_ev += evs
    t_w1_done = t_mm
    prev_Hev = hevs[EXP - 1] + hevs[EXP - 2]

    w2ready = []
    wcast2 = {}
    for ee in range(EXP):
        tw = p.dma_rr("in_w", 4, ee,
                      wstage[ee % 4][:].rearrange("p (k d) -> p k d", d=D),
                      w2_in[ee, 0:256, :].rearrange("(k p) d -> p k d", p=128),
                      waits=wcast2.get(ee - 4, [t_w1_done] + prev_Hev))
        cs = [p.emit("scalar", lambda e, ee=ee: e.activation(
            out=whbuf[:, ee * 2048:(ee + 1) * 2048], in_=wstage[ee % 4][:],
            func=AF.Copy), waits=[tw] + ([t_w1_done] if ee < 4 else []))]
        wcast2[ee] = cs
        w2ready += cs

    soev = {}
    b2cast = {}
    t_so_st = []
    for ee in range(EXP):
        t_bd = p.dma_rr("in_b2", 2, ee, b2stage[ee % 2][:],
                        w2_in[ee, 256:257, :],
                        waits=b2cast.get(ee - 2, [w2ready[0]]))
        t_bc2 = p.emit("scalar", lambda e, ee=ee: e.activation(
            out=b2cur[ee % 2][:], in_=b2stage[ee % 2][:], func=AF.Copy),
            waits=[t_bd] + soev.get(ee - 2, []))
        b2cast[ee] = [t_bc2]
        t_mm = None
        for n in range(2):
            for kk in range(2):
                t_mm = p.emit("tensor", lambda e, ee=ee, n=n, kk=kk: e.matmul(
                    out=SOps[ee % 2][:, n * 512:(n + 1) * 512],
                    lhsT=HT_sb[:, ee * 32 + kk * 16:ee * 32 + (kk + 1) * 16],
                    rhs=whbuf[:, ee * 2048 + kk * D + n * 512:
                              ee * 2048 + kk * D + (n + 1) * 512],
                    start=(kk == 0), stop=False),
                    waits=(w2ready + [t_bc2] +
                           soev.get(ee - 2, [])) if kk == 0 and n == 0 else [])
            t_mm = p.emit("tensor", lambda e, ee=ee, n=n: e.matmul(
                out=SOps[ee % 2][:, n * 512:(n + 1) * 512],
                lhsT=c["ones_bf"][0:1, 0:SLOT],
                rhs=b2cur[ee % 2][0:1, n * 512:(n + 1) * 512],
                start=False, stop=True), waits=[])
        t_ev = p.emit("vector", lambda e, ee=ee: e.tensor_copy(
            so_stage[ee % 2][0:16, :], SOps[ee % 2][:, :]),
            waits=[t_mm, p.stok(f"so1{ee % 2}")])
        soev[ee] = [t_ev]
        t_st = p.dma_rr("so1", 2, ee, scratch[ee * 16:(ee + 1) * 16, :],
                        so_stage[ee % 2][0:16, :], waits=[t_ev])
        t_so_st.append(t_st)
    t_bi = p.dma("so2", soT_sb[:].rearrange("p (r d) -> p r d", d=D),
                 scratch.rearrange("(r p) d -> p r d", p=128), waits=t_so_st)

    p.pop_region()
    Ops = [p.ps(f"Ops{i}", [128, 1024], F32) for i in range(NL)]
    tpsL = [p.ps(f"tpsL{i}", [128, 256], BF16) for i in range(NL)]

    pv = [dict(mm=None, u=None, lnsq=None, lnout=None, comb=None, combT=None)
          for _ in range(NL)]

    def tile_body(t, L):
        v = pv[L]
        t_E = p.dma_rr("in_E", NL, L, E_t[L][:], E_in[t],
                       waits=[v["comb"]] if v["comb"] else w2ready)
        t_Z = p.dma_rr("in_Z", NL, L, zc_t[L][:], Zc_in[t],
                       waits=[v["comb"]] if v["comb"] else w2ready)
        t_x1 = p.dma_rr("in_x", NL, L, x1_t[L][:],
                        x1_in[t * 128:(t + 1) * 128, :],
                        waits=[v["u"]] if v["u"] else w2ready)
        yield
        t_rz = p.emit("vector", lambda e, L=L: e.reciprocal(
            rZ[L][:], zc_t[L][:]), waits=[t_Z])
        t_cb = p.emit("vector", lambda e, L=L: e.tensor_scalar(
            out=comb[L][:], in0=E_t[L][:], scalar1=rZ[L][:, 0:1], scalar2=None,
            op0=ALU.mult), waits=[t_rz, t_E])
        v["comb"] = t_cb
        yield
        t_tp2 = None
        for kk in range(2):
            t_tp2 = p.emit("tensor", lambda e, kk=kk, L=L: e.transpose(
                out=tpsL[L][:, kk * 128:(kk + 1) * 128],
                in_=comb[L][:, kk * 128:(kk + 1) * 128],
                identity=c["ident"][:]),
                waits=([t_cb] + ([v["combT"]] if v["combT"] else [])) if kk == 0 else [])
        t_cT = p.emit("vector", lambda e, L=L: e.tensor_copy(
            combT[L][:], tpsL[L][:]),
            waits=[t_tp2])
        v["combT"] = t_cT
        yield
        t_mm = None
        for n in range(2):
            for kk in range(2):
                t_mm = p.emit("tensor", lambda e, n=n, kk=kk, L=L: e.matmul(
                    out=Ops[L][:, n * 512:(n + 1) * 512],
                    lhsT=combT[L][:, kk * 128:(kk + 1) * 128],
                    rhs=soT_sb[:, kk * D + n * 512:kk * D + (n + 1) * 512],
                    start=(kk == 0), stop=(kk == 1)),
                    waits=([t_cT, t_bi] + ([v["u"]] if v["u"] else [])) if kk == 0 and n == 0 else [])
        v["mm"] = t_mm
        yield
        t_u = p.emit("vector", lambda e, L=L: e.scalar_tensor_tensor(
            out=u[L][:], in0=Ops[L][:], scalar=1.0, in1=x1_t[L][:],
            op0=ALU.mult, op1=ALU.add, accum_out=s1col[L][:, 0:1]),
            waits=[t_mm, t_x1] + ([v["lnsq"]] if v["lnsq"] else []))
        v["u"] = t_u
        yield
        w_ln = [t_u] + ([v["lnout"]] if v["lnout"] else [])
        t_x2, t_sq1 = ln_stats_apply(p, lnb[L], u[L], s1col[L], x2_t[L], 1e-5,
                                     w_ln + [p.stok(f"o_x2{L}"),
                                             p.stok("o_edge")], None, "lnx2")
        v["lnsq"] = t_sq1
        v["lnout"] = t_x2
        t_st2 = p.dma_rr("o_x2", NL, L, x2_out[t * 128:(t + 1) * 128, :],
                         x2_t[L][:], waits=[t_x2])
        if edges_out is not None and t == 0:
            p.dma("o_edge", edges_out[0:HALO, :], x2_t[L][0:HALO, :],
                  waits=[t_x2])
        if edges_out is not None and t == NT - 1:
            p.dma("o_edge", edges_out[HALO:2 * HALO, :],
                  x2_t[L][128 - HALO:128, :], waits=[t_x2])

    for t0 in range(0, NT, NL):
        interleave_pairs([tile_body(t, t % NL)
                          for t in range(t0, min(t0 + NL, NT))])


def build_D():
    p = Prog("D")
    c = common_consts(p)
    io = dict(x1=p.din("x1", [T, D], F32),
              E=p.din("E", [NT, 128, ES], BF16),
              Zc=p.din("Zc", [NT, 128, 1], F32),
              Praw=p.din("Praw", [NCORES, 128, NJ * ES], F32),
              Sloc=p.din("Sloc", [NCORES, 1, ES], F32),
              w1e=p.din("w1e", [EXP, D, MOE_H], F32),
              b1c=p.din("b1c", [128, EXP * 2], F32),
              w2e=p.din("w2e", [EXP, MOE_H + 1, D], F32),
              x2=p.dout("x2", [T, D], F32),
              so_scratch=p.dtmp("so_scratch", [ES, D], BF16))
    emit_D(p, c, io)
    return p.finalize()



def emit_E(p, c, io, xe_wait=()):
    TE = 2176
    xe_in = io["xe"]
    mA_in = io["maskA"]
    mB_in = io["maskB"]
    wq_in = io["wqT"]
    wk_in = io["wkT"]
    wv_in = io["wvT"]
    wo_in = io["woT_a"]
    x3_out = io["x3"]

    wqh = p.sb("wqh", [128, NJ * D], BF16)
    wkvh = p.sb("wkvh", [128, NJ * D], BF16)
    wob_f = p.sb("wob_f", [1, D], F32)
    wob_h = p.sb("wob_h", [1, D], BF16)
    xeT = p.sb("xeT", [128, NJ * TE], BF16)
    kT = p.sb("kT", [128, NJ * TE], BF16)
    v_sb = p.sb("v_sb", [128, NTE * D], BF16)
    NL = 2
    qz = [[p.sb(f"qz{i}_{L}", [128, NJ * 128], BF16) for i in range(2)]
          for L in range(NL)]
    EA = p.sb("EA", [128, 512], BF16)
    EB = p.sb("EB", [64, 512], BF16)
    rz_sb = [p.sb(f"rz_sb{L}", [128, NH], F32) for L in range(NL)]
    mA = p.sb("mA", [128, 128], F32)
    mB = p.sb("mB", [64, 128], F32)
    actx0 = p.sb("actx", [128, D], BF16)
    actx = [actx0, actx0]
    ctxT = [p.sb(f"ctxT{L}", [128, D], BF16) for L in range(NL)]
    xr_t0 = p.sb("xr_t", [128, D], F32)
    xr_t = [xr_t0, xr_t0]
    u = [p.sb(f"uE{L}", [128, D], F32) for L in range(NL)]
    s1col = [p.sb(f"s1col{L}", [128, 1], F32) for L in range(NL)]
    sqw0 = p.sb("sqwE", [128, D], F32)
    lnb = [dict(sqw=sqw0,
                mcol=p.sb(f"mcolE{L}", [128, 1], F32),
                vcol=p.sb(f"vcolE{L}", [128, 1], F32),
                tcol=p.sb(f"tcolE{L}", [128, 1], F32)) for L in range(NL)]
    x3_t0 = p.sb("x3_t", [128, D], F32)
    x3_t = [x3_t0, x3_t0]

    tps = p.ps("tps", [128, 1024], BF16)
    qps = p.ps("qps", [128, 1024], F32)

    eighth, t_ei = p.const_col(0.125)
    t_mA = p.dma("in", mA[:], mA_in[:], waits=c["ready"])
    t_mB = p.dma("in", mB[:], mB_in[:], waits=c["ready"])
    mA4 = p.sb("mA4", [128, 512], BF16)
    mB4 = p.sb("mB4", [64, 512], BF16)
    t_m4 = []
    for r in range(4):
        t_m4.append(p.emit("scalar", lambda e, r=r: e.activation(
            out=mA4[:, r * 128:(r + 1) * 128], in_=mA[:], func=AF.Copy),
            waits=[t_mA]))
        t_m4.append(p.emit("scalar", lambda e, r=r: e.activation(
            out=mB4[0:64, r * 128:(r + 1) * 128], in_=mB[0:64, :],
            func=AF.Copy), waits=[t_mB]))

    p.push_region("w12")
    wstage = p.sb("wstage", [128, D], F32)
    xf_t = [p.sb(f"xf_t{i}", [128, D], F32) for i in range(2)]
    xh_t = [p.sb(f"xh_t{i}", [128, D], BF16) for i in range(2)]
    tpsB = p.ps("tpsB", [128, 1024], BF16)
    tps1 = [tps, tpsB]
    e_cast = {}
    e_ev = {}
    for te in range(NTE):
        LL = te % 2
        t_x = p.dma_rr("in_x", 2, LL, xf_t[LL][:],
                       xe_in[te * 128:(te + 1) * 128, :],
                       waits=([e_cast[te - 2]] if te >= 2 else c["ready"]) +
                             list(xe_wait))
        t_cast = p.emit("scalar", lambda e, LL=LL: e.activation(
            out=xh_t[LL][:], in_=xf_t[LL][:], func=AF.Copy),
            waits=[t_x] + ([e_ev[te - 2]] if te >= 2 else []))
        e_cast[te] = t_cast
        xeT_view = xeT[:].rearrange("p (j t) -> p j t", t=TE)[:, :, te * 128:(te + 1) * 128]
        e_ev[te] = transpose_tile(p, c, tps1[LL], xh_t[LL], xeT_view, [t_cast],
                                  [e_ev[te - 2]] if te >= 2 else [], nblk=NJ)
    t_xeT = p.tok("vector")

    def load_w(name_in, dest, waits):
        toks = []
        prev = list(waits)
        for ch in range(NJ):
            tw = p.dma("in", wstage[:],
                       name_in[ch * 128:(ch + 1) * 128, :], waits=prev)
            cs = [p.emit("scalar", lambda e, ch=ch, dest=dest: e.activation(
                out=dest[:, ch * D:(ch + 1) * D],
                in_=wstage[:], func=AF.Copy),
                waits=[p.stok("in")])]
            prev = cs
            toks += cs
        return toks

    t_wk = load_w(wk_in, wkvh, c["ready"])
    t_kml = None
    kev = []
    nch = [(i, min(512, TE - 512 * i)) for i in range((TE + 511) // 512)]
    ib = 0
    for m in range(NJ):
        for (ci, nn) in nch:
            b = ib % 2
            ib += 1
            t_kml = None
            for kk in range(NJ):
                t_kml = p.emit("tensor", lambda e, m=m, ci=ci, nn=nn, kk=kk, b=b: e.matmul(
                    out=qps[:, b * 512:b * 512 + nn],
                    lhsT=wkvh[:, kk * D + m * 128:kk * D + (m + 1) * 128],
                    rhs=xeT[:, kk * TE + ci * 512:kk * TE + ci * 512 + nn],
                    start=(kk == 0), stop=(kk == NJ - 1)),
                    waits=([t_xeT] + t_wk + kev[-2:-1]) if kk == 0 else [])
            kev.append(p.emit("vector", lambda e, m=m, ci=ci, nn=nn, b=b: e.tensor_copy(
                kT[:, m * TE + ci * 512:m * TE + ci * 512 + nn],
                qps[:, b * 512:b * 512 + nn]),
                waits=[t_kml]))
    t_wv = load_w(wv_in, wkvh, kev)
    vev = []
    for te in range(NTE):
        for n in range(2):
            t_vml = None
            for kk in range(NJ):
                t_vml = p.emit("tensor", lambda e, te=te, n=n, kk=kk: e.matmul(
                    out=qps[:, n * 512:(n + 1) * 512],
                    lhsT=xeT[:, kk * TE + te * 128:kk * TE + (te + 1) * 128],
                    rhs=wkvh[:, kk * D + n * 512:kk * D + (n + 1) * 512],
                    start=(kk == 0), stop=(kk == NJ - 1)),
                    waits=(t_wv + vev[-2:-1]) if kk == 0 else [])
            vev.append(p.emit("vector", lambda e, te=te, n=n: e.tensor_copy(
                v_sb[:, te * D + n * 512:te * D + (n + 1) * 512],
                qps[:, n * 512:(n + 1) * 512]), waits=[t_vml]))
    t_wq = load_w(wq_in, wqh, vev)
    woh = wkvh
    t_wo = load_w(wo_in, woh, t_wq)
    t_wob = p.dma("in", wob_f[:], wo_in[D:D + 1, :], waits=t_wo)
    t_wo = t_wo + [p.emit("scalar", lambda e: e.activation(
        out=wob_h[:], in_=wob_f[:], func=AF.Copy), waits=[p.stok("in")])]
    t_z0 = p.emit("vector", lambda e: e.memset(qz[0][0][64:128, :], 0.0))
    t_z1 = p.emit("vector", lambda e: e.memset(qz[0][1][0:64, :], 0.0))
    t_z2 = p.emit("vector", lambda e: e.memset(qz[1][0][64:128, :], 0.0))
    t_z3 = p.emit("vector", lambda e: e.memset(qz[1][1][0:64, :], 0.0))
    t_z0 = [t_z0, t_z1, t_z2, t_z3]
    p.pop_region()
    SA = p.ps("SA", [128, 512], F32)
    SB = p.ps("SB", [64, 512], F32)
    Ups = p.ps("Ups", [128, 1024], F32)
    Zps = p.ps("Zps", [128, 16], F32)

    g = dict(q0=None, q1=None, ea=[], u=None, ctxT=None, rz=None)
    pv = [dict(Slast=None, lnsq=None, lnout=None, u=None) for _ in range(NL)]

    def tile_body(t, L):
        v = pv[L]
        t_qml = None
        for jm in range(NJ):
            for kk in range(NJ):
                t_qml = p.emit("tensor", lambda e, jm=jm, kk=kk, t=t: e.matmul(
                    out=qps[:, jm * 128:(jm + 1) * 128],
                    lhsT=wqh[:, kk * D + jm * 128:kk * D + jm * 128 + 128],
                    rhs=xeT[:, kk * TE + 32 + t * 128:kk * TE + 32 + (t + 1) * 128],
                    start=(jm % 4 == 0 and kk == 0),
                    stop=(jm % 4 == 3 and kk == NJ - 1)),
                    waits=(t_wq + [x for x in (g["q0"], g["q1"], g["u"]) if x]) if jm == 0 and kk == 0 else [])
        t_q0 = p.emit("vector", lambda e, L=L: e.tensor_copy(
            qz[L][0][0:64, :], qps[0:64, :]),
            waits=[t_qml] + t_z0 + ([v["Slast"]] if v["Slast"] else []))
        t_q1 = p.emit("vector", lambda e, L=L: e.tensor_copy(
            qz[L][1][64:128, :], qps[64:128, :]), waits=[t_qml])
        g["q0"], g["q1"] = t_q0, t_q1
        yield

        t_U_last = None
        t_Z_last = None
        ea_war = list(g["ea"])
        for gg in range(4):
            t_SA = None
            t_SB = None
            for i in range(4):
                h = 4 * gg + i
                jh = h // 2
                t_SA = p.emit("tensor", lambda e, h=h, jh=jh, i=i, t=t, L=L: e.matmul(
                    out=SA[:, i * 128:(i + 1) * 128],
                    lhsT=kT[:, jh * TE + t * 128:jh * TE + (t + 1) * 128],
                    rhs=qz[L][h % 2][:, jh * 128:(jh + 1) * 128],
                    start=(i == 0), stop=False),
                    waits=([t_q0, t_q1] + kev + ea_war) if i == 0 else [])
            t_SA = p.emit("tensor", lambda e: e.matmul(
                out=SA[:, 0:512], lhsT=c["ident"][:], rhs=mA4[:, 0:512],
                start=False, stop=True), waits=t_m4 if t == 0 and gg == 0 else [])
            for i in range(4):
                h = 4 * gg + i
                jh = h // 2
                t_SB = p.emit("tensor", lambda e, h=h, jh=jh, i=i, t=t, L=L: e.matmul(
                    out=SB[0:64, i * 128:(i + 1) * 128],
                    lhsT=kT[:, jh * TE + (t + 1) * 128:jh * TE + (t + 1) * 128 + 64],
                    rhs=qz[L][h % 2][:, jh * 128:(jh + 1) * 128],
                    start=(i == 0), stop=False),
                    waits=[] if i > 0 else ea_war)
            t_SB = p.emit("tensor", lambda e: e.matmul(
                out=SB[0:64, 0:512], lhsT=c["ident"][0:64, 0:64],
                rhs=mB4[0:64, 0:512], start=False, stop=True), waits=[])
            t_EA = p.emit("scalar", lambda e: e.activation(
                out=EA[:], in_=SA[:], func=AF.Exp, scale=eighth[:, 0:1]),
                waits=[t_SA, t_ei] + ea_war)
            t_EB = p.emit("scalar", lambda e: e.activation(
                out=EB[0:64, :], in_=SB[0:64, :], func=AF.Exp,
                scale=eighth[0:64, 0:1]), waits=[t_SB])
            ea_war = []
            for i in range(4):
                h = 4 * gg + i
                wU = []
                if i == 0:
                    wU = [t_EA, t_EB] + vev + ([g["actx"]] if g.get("actx") else [])
                t_U_last = p.emit("tensor", lambda e, h=h, i=i, t=t: e.matmul(
                    out=Ups[:, h * 64:(h + 1) * 64],
                    lhsT=EA[:, i * 128:(i + 1) * 128],
                    rhs=v_sb[:, t * D + h * 64:t * D + (h + 1) * 64],
                    start=(h % 8 == 0), stop=False), waits=wU)
                t_U_last = p.emit("tensor", lambda e, h=h, i=i, t=t: e.matmul(
                    out=Ups[:, h * 64:(h + 1) * 64],
                    lhsT=EB[0:64, i * 128:(i + 1) * 128],
                    rhs=v_sb[0:64, (t + 1) * D + h * 64:(t + 1) * D + (h + 1) * 64],
                    start=False, stop=(h % 8 == 7)), waits=[])
                t_Z_last = p.emit("tensor", lambda e, h=h, i=i, t=t: e.matmul(
                    out=Zps[:, h:h + 1],
                    lhsT=EA[:, i * 128:(i + 1) * 128],
                    rhs=c["onescol_bf"][:, 0:1],
                    start=(h == 0), stop=False),
                    waits=([g["rz"]] if h == 0 and g["rz"] else []))
                t_Z_last = p.emit("tensor", lambda e, h=h, i=i, t=t: e.matmul(
                    out=Zps[:, h:h + 1],
                    lhsT=EB[0:64, i * 128:(i + 1) * 128],
                    rhs=c["onescol_bf"][0:64, 0:1],
                    start=False, stop=(h == 15)), waits=[])
            ea_war = [t_U_last, t_Z_last]
        v["Slast"] = t_SB
        g["ea"] = ea_war
        t_rz = p.emit("vector", lambda e, L=L: e.reciprocal(
            rz_sb[L][:, 0:NH], Zps[:, 0:NH]), waits=[t_Z_last])
        g["rz"] = t_rz
        t_ac = p.emit("vector", lambda e, L=L: e.scalar_tensor_tensor(
            out=actx[L][:], in0=Ups[:], scalar=1.0,
            in1=rz_sb[L][:, 0:NH].unsqueeze(2).broadcast_to([128, NH, HD]),
            op0=ALU.mult, op1=ALU.mult), waits=[t_U_last, t_rz])
        g["actx"] = t_ac
        yield
        t_cT = transpose_tile(p, c, tps, actx[L], ctxT[L][:], [t_ac],
                              [g["ctxT"]] if g["ctxT"] else [t_xeT],
                              nblk=NJ)
        g["ctxT"] = t_cT
        yield
        t_omm = None
        for n in range(2):
            for kk in range(NJ):
                t_omm = p.emit("tensor", lambda e, n=n, kk=kk, L=L: e.matmul(
                    out=qps[:, n * 512:(n + 1) * 512],
                    lhsT=ctxT[L][:, kk * 128:(kk + 1) * 128],
                    rhs=woh[:, kk * D + n * 512:kk * D + (n + 1) * 512],
                    start=(kk == 0), stop=False),
                    waits=([t_cT] + t_wo +
                           [x for x in (g["q0"], g["q1"], g["u"]) if x]) if kk == 0 and n == 0 else [])
            t_omm = p.emit("tensor", lambda e, n=n: e.matmul(
                out=qps[:, n * 512:(n + 1) * 512],
                lhsT=c["ones_bf"][0:1, 0:128],
                rhs=wob_h[:, n * 512:(n + 1) * 512],
                start=False, stop=True), waits=[])
        t_xr = p.dma_rr("in_r", NL, L, xr_t[L][:],
                        xe_in[32 + t * 128:32 + (t + 1) * 128, :],
                        waits=[g["u"]] if g["u"] else t_wo)
        t_u = p.emit("vector", lambda e, L=L: e.scalar_tensor_tensor(
            out=u[L][:], in0=qps[:], scalar=1.0, in1=xr_t[L][:],
            op0=ALU.mult, op1=ALU.add, accum_out=s1col[L][:, 0:1]),
            waits=[t_omm, t_xr] + [x for x in (g.get("lnsq"),) if x])
        v["u"] = t_u
        g["u"] = t_u
        yield
        w_ln = [t_u] + ([v["lnout"]] if v["lnout"] else [])
        t_x3, t_sq1 = ln_stats_apply(p, lnb[L], u[L], s1col[L], x3_t[L], 1e-5,
                                     w_ln + [p.stok("o_x3")], None, "lnx3")
        v["lnsq"] = t_sq1
        g["lnsq"] = t_sq1
        v["lnout"] = t_x3
        p.dma("o_x3", x3_out[t * 128:(t + 1) * 128, :], x3_t[L][:],
              waits=[t_x3])

    for t0 in range(0, NT, NL):
        interleave_pairs([tile_body(t, t % NL)
                          for t in range(t0, min(t0 + NL, NT))])


def build_E():
    p = Prog("E")
    c = common_consts(p)
    io = dict(xe=p.din("xe", [2176, D], F32),
              maskA=p.din("maskA", [128, 128], F32),
              maskB=p.din("maskB", [64, 128], F32),
              wqT=p.din("wqT", [D, D], F32),
              wkT=p.din("wkT", [D, D], F32),
              wvT=p.din("wvT", [D, D], F32),
              woT_a=p.din("woT_a", [D + 1, D], F32),
              x3=p.dout("x3", [T, D], F32))
    emit_E(p, c, io)
    return p.finalize()



def emit_F1(p, c, io):
    x3_in = io["x3"]
    w1n_in = io["w1n"]
    w1T_in = io["w1T"]
    b1c_in = io["b1c"]
    hT_out = io["hT"]

    wn_t = [p.sb(f"wn_t{i}", [128, D], F32) for i in range(2)]
    absw = p.sb("absw", [128, D], F32)
    alpha1 = p.sb("alpha1", [128, 32], F32)
    alpha1s = p.sb("alpha1s", [128, 32], F32)
    b1c = p.sb("b1c_sb", [128, 32], F32)
    wstage = [p.sb(f"wstage{i}", [128, FF], F32) for i in range(2)]
    sw1T = p.sb("sw1T", [128, NJ * FF], BF16)
    xf_t = [p.sb(f"xf_t{i}", [128, D], F32) for i in range(2)]
    xh_t = [p.sb(f"xh_t{i}", [128, D], BF16) for i in range(2)]
    x3T = p.sb("x3T", [128, NJ * T], BF16)
    hbig = [p.sb(f"hbig{i}", [128, 8 * 512], BF16) for i in range(2)]

    tps = [p.ps(f"tpsF{i}", [128, 1024], BF16) for i in range(2)]
    hps = [p.ps(f"hps{i}", [128, 512], F32) for i in range(2)]

    t_abs = {}
    for i in range(32):
        t_w = p.dma_rr("in_a", 2, i, wn_t[i % 2][:],
                       w1n_in[i * 128:(i + 1) * 128, :],
                       waits=([t_abs[i - 2]] if i >= 2 else c["ready"]),
                       queue="gpsimd")
        t_abs[i] = p.emit("scalar", lambda e, i=i: e.activation(
            out=absw[:], in_=wn_t[i % 2][:], func=AF.Abs,
            accum_out=alpha1[:, i:i + 1]), waits=[t_w])
    t_ab = t_abs[31]
    t_al = p.emit("vector", lambda e: e.tensor_scalar_mul(
        alpha1s[:], alpha1[:], 1.0 / D), waits=[t_ab])
    t_b1 = p.dma("in", b1c[:], b1c_in[:], waits=c["ready"])

    t_sg = []
    prevs = {}
    for ch in range(NJ):
        t_w = p.dma_rr("in_s", 2, ch, wstage[ch % 2][:],
                       w1T_in[ch * 128:(ch + 1) * 128, :],
                       waits=prevs.get(ch - 2, c["ready"][:1]))
        cs = p.emit("scalar", lambda e, ch=ch: e.activation(
            out=sw1T[:, ch * FF:(ch + 1) * FF], in_=wstage[ch % 2][:],
            func=AF.Sign), waits=[t_w])
        prevs[ch] = [cs]
        t_sg.append(cs)

    xcast = {}
    xev = {}
    for t in range(NT):
        L = t % 2
        t_x = p.dma_rr("in_x", 2, L, xf_t[L][:],
                       x3_in[t * 128:(t + 1) * 128, :],
                       waits=[xcast[t - 2]] if t >= 2 else c["ready"])
        t_cast = p.emit("scalar", lambda e, L=L: e.activation(
            out=xh_t[L][:], in_=xf_t[L][:], func=AF.Copy),
            waits=[t_x] + ([xev[t - 2]] if t >= 2 else []))
        xcast[t] = t_cast
        xT_view = x3T[:].rearrange("p (j t) -> p j t", t=T)[:, :, t * 128:(t + 1) * 128]
        xev[t] = transpose_tile(p, c, tps[L], xh_t[L], xT_view, [t_cast],
                                [xev[t - 2]] if t >= 2 else [], nblk=NJ)

    prev_ev = {}
    prev_st = {}
    i = 0
    for cch in range(4):
        for m in range(32):
            par = i % 2
            grp = (i // 8) % 2
            t_mm = None
            for kk in range(NJ):
                t_mm = p.emit("tensor", lambda e, cch=cch, m=m, kk=kk, par=par: e.matmul(
                    out=hps[par][:],
                    lhsT=sw1T[:, kk * FF + m * 128:kk * FF + (m + 1) * 128],
                    rhs=x3T[:, kk * T + cch * 512:kk * T + (cch + 1) * 512],
                    start=(kk == 0), stop=(kk == NJ - 1)),
                    waits=(t_sg + [xev[cch * 4 + i] for i in range(4)] +
                           list(prev_ev.get(par, []))) if kk == 0 else [])
            t_ev = p.emit("scalar", lambda e, m=m, par=par, grp=grp: e.activation(
                out=hbig[grp][:, (m % 8) * 512:(m % 8 + 1) * 512],
                in_=hps[par][:], func=AF.Gelu,
                scale=alpha1s[:, m:m + 1], bias=b1c[:, m:m + 1]),
                waits=[t_mm, t_al, t_b1] +
                      (list(prev_st.get(grp, [])) if m % 8 < 2 else []))
            prev_ev[par] = [t_ev]
            if m % 8 == 7:
                t_st = p.dma_rr("o_h", 2, grp,
                                hT_out[m - 7:m + 1, :, cch * 512:(cch + 1) * 512]
                                .rearrange("m p tt -> p m tt"),
                                hbig[grp][:].rearrange("p (m tt) -> p m tt", tt=512),
                                waits=[t_ev])
                prev_st[grp] = [t_st]
            i += 1


def build_F1():
    p = Prog("F1")
    c = common_consts(p)
    io = dict(x3=p.din("x3", [T, D], F32),
              w1n=p.din("w1n", [FF, D], F32),
              w1T=p.din("w1T", [D, FF], F32),
              b1c=p.din("b1c", [128, 32], F32),
              hT=p.dout("hT", [32, 128, T], BF16))
    emit_F1(p, c, io)
    return p.finalize()



def emit_F2(p, c, io):
    x3_in = io["x3"]
    hT_in = io["hT"]
    w2n_in = io["w2n"]
    w2T_in = io["w2T"]
    b2c_in = io["b2c"]
    y_out = io["y"]

    wn_t = [p.sb("wn_t0", [128, FF], F32)] * 2
    absw = p.sb("absw", [128, FF], F32)
    alpha2 = p.sb("alpha2", [128, NJ], F32)
    alpha2s = p.sb("alpha2s", [128, NJ], F32)
    b2c = p.sb("b2c_sb", [128, NJ], F32)
    wstage = [p.sb(f"wstage{i}", [128, D], F32) for i in range(2)]
    sw2T = p.sb("sw2T", [128, 32 * D], BF16)
    hT_c = [p.sb(f"hT_c{i}", [128, 32 * 512], BF16) for i in range(2)]
    preT = p.sb("preT", [128, NJ * 512], BF16)
    x3_t = [p.sb(f"x3_t{i}", [128, D], F32) for i in range(2)]
    u = [p.sb(f"u{i}", [128, D], F32) for i in range(2)]
    s1col = [p.sb(f"s1col{i}", [128, 1], F32) for i in range(2)]
    sqw0 = p.sb("sqw0", [128, D], F32)
    lnb = [dict(sqw=sqw0,
                mcol=p.sb(f"mcol{i}", [128, 1], F32),
                vcol=p.sb(f"vcol{i}", [128, 1], F32),
                tcol=p.sb(f"tcol{i}", [128, 1], F32)) for i in range(2)]
    y_t = [wn_t[0][:, 0:D], wn_t[0][:, D:2 * D]]

    tps = [p.ps(f"tps{i}", [128, 1024], BF16) for i in range(2)]
    o2ps = [p.ps(f"o2ps{i}", [128, 512], F32) for i in range(2)]

    t_abs = {}
    for i in range(NJ):
        t_w = p.dma_rr("in_a", 2, i, wn_t[i % 2][:],
                       w2n_in[i * 128:(i + 1) * 128, :],
                       waits=([t_abs[i - 1]] if i >= 1 else c["ready"]),
                       queue="gpsimd")
        t_abs[i] = p.emit("scalar", lambda e, i=i: e.activation(
            out=absw[:], in_=wn_t[i % 2][:], func=AF.Abs,
            accum_out=alpha2[:, i:i + 1]), waits=[t_w])
    t_ab = t_abs[NJ - 1]
    t_al = p.emit("vector", lambda e: e.tensor_scalar_mul(
        alpha2s[:], alpha2[:], 1.0 / FF), waits=[t_ab])
    t_b2 = p.dma("in", b2c[:], b2c_in[:], waits=c["ready"])

    t_sg = []
    prevs = {}
    for ch in range(32):
        t_w = p.dma_rr("in_s", 2, ch, wstage[ch % 2][:],
                       w2T_in[ch * 128:(ch + 1) * 128, :],
                       waits=prevs.get(ch - 2, c["ready"][:1]))
        cs = p.emit("scalar", lambda e, ch=ch: e.activation(
            out=sw2T[:, ch * D:(ch + 1) * D], in_=wstage[ch % 2][:],
            func=AF.Sign), waits=[t_w])
        prevs[ch] = [cs]
        t_sg.append(cs)

    preTs = [preT, absw[:, 0:2048].bitcast(BF16)]
    pvF = [dict(u=None, lnsq=None, lnout=None) for _ in range(2)]
    t_prev_pre = {}
    pre_done = {}
    hload = {}
    hl_war = {}

    def tail_body(cch):
        for tt in range(4):
            t = cch * 4 + tt
            L = t % 2
            v = pvF[L]
            t_x = p.dma_rr("in_x", 2, L, x3_t[L][:],
                           x3_in[t * 128:(t + 1) * 128, :],
                           waits=[v["u"]] if v["u"] else t_sg)
            t_tp = None
            for m in range(NJ):
                t_tp = p.emit("tensor", lambda e, m=m, tt=tt, L=L, cch=cch: e.transpose(
                    out=tps[L][:, m * 128:(m + 1) * 128],
                    in_=preTs[cch % 2][:, m * 512 + tt * 128:
                                       m * 512 + (tt + 1) * 128],
                    identity=c["ident"][:]),
                    waits=(pre_done[cch] +
                           ([v["u"]] if v["u"] else [])) if m == 0 else [])
            t_u = p.emit("vector", lambda e, L=L: e.scalar_tensor_tensor(
                out=u[L][:], in0=tps[L][:, 0:1024], scalar=1.0, in1=x3_t[L][:],
                op0=ALU.mult, op1=ALU.add, accum_out=s1col[L][:, 0:1]),
                waits=[t_tp, t_x] + ([v["lnsq"]] if v["lnsq"] else []))
            v["u"] = t_u
            w_ln = [t_u] + ([v["lnout"]] if v["lnout"] else [t_ab])
            t_y, t_sq1 = ln_stats_apply(p, lnb[L], u[L], s1col[L], y_t[L],
                                        1e-5, w_ln + [p.stok(f"o_y{L}")],
                                        None, "lny")
            v["lnsq"] = t_sq1
            v["lnout"] = t_y
            p.dma_rr("o_y", 2, L, y_out[t * 128:(t + 1) * 128, :], y_t[L][:],
                     waits=[t_y])

    for cch in range(4):
        hload[cch] = p.dma_rr("in_h", 2, cch,
                    hT_c[cch % 2][:].rearrange("p (k tt) -> p k tt", tt=512),
                    hT_in[:, :, cch * 512:(cch + 1) * 512].rearrange(
                        "k p tt -> p k tt"),
                    waits=[hl_war[cch - 2]] if cch >= 2 else t_sg)
        t_h = hload[cch]
        pre_done[cch] = []
        for m in range(NJ):
            par = m % 2
            t_mm = None
            for kk in range(32):
                t_mm = p.emit("tensor", lambda e, m=m, kk=kk, par=par, cch=cch: e.matmul(
                    out=o2ps[par][:],
                    lhsT=sw2T[:, kk * D + m * 128:kk * D + (m + 1) * 128],
                    rhs=hT_c[cch % 2][:, kk * 512:(kk + 1) * 512],
                    start=(kk == 0), stop=(kk == 31)),
                    waits=([t_h] + t_sg + list(t_prev_pre.get(par, []))) if kk == 0 else [])
            t_pre = p.emit("scalar", lambda e, m=m, par=par, cch=cch: e.activation(
                out=preTs[cch % 2][:, m * 512:(m + 1) * 512], in_=o2ps[par][:],
                func=AF.Identity, scale=alpha2s[:, m:m + 1],
                bias=b2c[:, m:m + 1]), waits=[t_mm, t_al, t_b2])
            t_prev_pre[par] = [t_pre]
            pre_done[cch].append(t_pre)
        hl_war[cch] = t_mm
        if cch > 0:
            tail_body(cch - 1)
    tail_body(3)


def build_F2():
    p = Prog("F2")
    c = common_consts(p)
    io = dict(x3=p.din("x3", [T, D], F32),
              hT=p.din("hT", [32, 128, T], BF16),
              w2n=p.din("w2n", [D, FF], F32),
              w2T=p.din("w2T", [FF, D], F32),
              b2c=p.din("b2c", [128, NJ], F32),
              y=p.dout("y", [T, D], F32))
    emit_F2(p, c, io)
    return p.finalize()



def build_fused(upto=None):
    p = Prog("FUSED")
    c = common_consts(p)
    for v in (1e-12, 1e-5, 0.125):
        p.const_col(v)
    G8 = [list(range(NCORES))]

    x = p.din("x", [T, D], F32)
    wq_a = p.din("wq_a", [D + 1, D], F32)
    wk_a = p.din("wk_a", [D + 1, D], F32)
    wv_a = p.din("wv_a", [D + 1, D], F32)
    wo_a = p.din("wo_a", [D + 1, D], F32)
    g1r = p.din("g1r", [1, D], F32)
    be1r = p.din("be1r", [1, D], F32)
    w1_a = p.din("w1_a", [D + 1, FF], F32)
    b1cB = p.din("b1cB", [128, 32], F32)
    w2_a = p.din("w2_a", [FF + 1, D], F32)
    g2r = p.din("g2r", [1, D], F32)
    be2r = p.din("be2r", [1, D], F32)
    gmr = p.din("gmr", [1, D], F32)
    bbr = p.din("bbr", [1, D], F32)
    mur = p.din("mur", [D, ES], F32)
    scl = p.din("scl", [1, 1], F32)
    w1e = p.din("w1e", [EXP, D, MOE_H], F32)
    b1c = p.din("b1c", [128, EXP * 2], F32)
    w2e = p.din("w2e", [EXP, MOE_H + 1, D], F32)
    wqT = p.din("wqT", [D, D], F32)
    wkT = p.din("wkT", [D, D], F32)
    wvT = p.din("wvT", [D, D], F32)
    woT_a = p.din("woT_a", [D + 1, D], F32)
    maskA = p.din("maskA", [128, 128], F32)
    maskB = p.din("maskB", [64, 128], F32)
    selL = p.din("selL", [512, 32], F32)
    selR = p.din("selR", [512, 32], F32)
    w1n = p.din("w1n", [FF, D], F32)
    w1T = p.din("w1T", [D, FF], F32)
    b1cf = p.din("b1cf", [128, 32], F32)
    w2n = p.din("w2n", [D, FF], F32)
    w2T = p.din("w2T", [FF, D], F32)
    b2cf = p.din("b2cf", [128, NJ], F32)
    y = p.dout("y", [T, D], F32)

    kvp = p.dtmp("s_kvp", [64, 1024], F32)
    kvs = p.dtmp("s_kvs", [64, 1024], F32)
    attn_out = p.dtmp("s_ao", [T, D], F32)
    atT = p.dtmp("s_atT", [NJ, 128, T], BF16)
    hT0 = p.dtmp("s_hT0", [32, 128, T], BF16)
    x1 = p.dtmp("s_x1", [T, D], F32)
    PrawP = p.dtmp("s_PrawP", [128, NJ * ES], F32)
    PrawS = p.dtmp("s_PrawS", [128, NJ * ES], F32)
    SlocP = p.dtmp("s_SlocP", [1, ES], F32)
    SlocS = p.dtmp("s_SlocS", [1, ES], F32)
    Emat = p.dtmp("s_E", [NT, 128, ES], BF16)
    Zc = p.dtmp("s_Zc", [NT, 128, 1], F32)
    xe = p.dtmp("s_xe", [2176, D], F32)
    edges = p.dtmp("s_edges", [2 * HALO, D], F32)
    gath = p.dtmp("s_gath", [NCORES * 2 * HALO, D], F32)
    x3 = p.dtmp("s_x3", [T, D], F32)
    hT1 = p.dtmp("s_hT1", [32, 128, T], BF16)
    soscr = p.dtmp("s_soscr", [ES, D], BF16)

    p.scnt.setdefault("cc", 0)

    def collective(kind, op, ins, outs, waits):
        return p.emit("gpsimd", lambda e: e.collective_compute(
            kind, op, replica_groups=G8, ins=[a.opt() for a in ins],
            outs=[a.opt() for a in outs]), waits=waits, inc=1, sem="cc")

    p.push_region("AB")
    eqT_res = p.sb("eqT_res", [128, NJ * T], BF16)
    rq_res = p.sb("rq_res", [128, NT * NH], F32)
    p.push_region("A")
    emit_A(p, c, dict(x=x, wq_a=wq_a, wk_a=wk_a, wv_a=wv_a, kvp=kvp,
                      eqT_res=eqT_res, rq_res=rq_res))
    t_cc_kv = collective("AllReduce", ALU.add, [kvp], [kvs],
                         [p.stok("o_kv")])
    p.pop_region()
    if upto == "A":
        return p.finalize()
    p.barrier()

    p.push_region("B1")
    emit_B1(p, c, dict(x=x, eqT_res=eqT_res, rq_res=rq_res, kvp=kvs,
                       g1r=g1r, be1r=be1r,
                       wo_a=wo_a, attn_out=attn_out, atT=atT), fused=True,
            kv_wait=t_cc_kv)
    p.pop_region()
    p.pop_region()
    if upto == "B1":
        return p.finalize()
    p.barrier()

    p.push_region("B2a")
    emit_B2a(p, c, dict(atT=atT, w1_a=w1_a, b1cB=b1cB, hT=hT0))
    p.pop_region()
    if upto == "B2a":
        return p.finalize()
    p.barrier()

    p.push_region("B2b")
    emit_B2b(p, c, dict(hT=hT0, attn_out=attn_out, x=x, w2_a=w2_a, g2r=g2r,
                        be2r=be2r, x1=x1))
    p.pop_region()
    if upto == "B2b":
        return p.finalize()
    p.barrier()

    p.push_region("C")
    emit_C(p, c, dict(x1=x1, gmr=gmr, bbr=bbr, mur=mur, scl=scl, Praw=PrawP,
                      Sloc=SlocP, E=Emat, Zc=Zc))
    collective("AllReduce", ALU.add, [PrawP], [PrawS], [p.stok("o_P")])
    t_cc_S = collective("AllReduce", ALU.add, [SlocP], [SlocS],
                        [p.stok("o_S")])
    p.pop_region()
    if upto == "C":
        return p.finalize()
    p.barrier()

    p.push_region("D")
    emit_D(p, c, dict(x1=x1, E=Emat, Zc=Zc, Praw=PrawS, Sloc=SlocS, w1e=w1e,
                      b1c=b1c, w2e=w2e, x2=xe[HALO:HALO + T, :],
                      so_scratch=soscr),
           fused=True, pr_wait=t_cc_S, edges_out=edges)
    t_cc_G = collective("AllGather", ALU.bypass, [edges], [gath],
                        [p.stok("o_edge")])
    p.pop_region()
    if upto == "D":
        return p.finalize()
    p.barrier()

    p.push_region("H")
    ztile = p.sb("ztile", [64, D], F32)
    gsb = p.sb("gsb", [128, 4 * D], F32)
    slsb = p.sb("slsb", [128, 4 * 32], F32)
    srsb = p.sb("srsb", [128, 4 * 32], F32)
    hsb = p.sb("hsb", [32, D], F32)
    hps = p.ps("hps", [32, 1024], F32)
    t_z = p.emit("vector", lambda e: e.memset(ztile[:], 0.0))
    if upto == "H0":
        return p.finalize()
    p.dma("o_xe", xe[2112:2176, :], ztile[:], waits=[t_z])
    if upto == "H1":
        return p.finalize()
    t_g = p.dma("in_g", gsb[:].rearrange("p (k d) -> p k d", d=D),
                gath[:].rearrange("(k p) d -> p k d", p=128),
                waits=[t_cc_G])
    t_sl = p.dma("in_g", slsb[:].rearrange("p (k m) -> p k m", m=32),
                 selL[:].rearrange("(k p) m -> p k m", p=128))
    t_sr = p.dma("in_g", srsb[:].rearrange("p (k m) -> p k m", m=32),
                 selR[:].rearrange("(k p) m -> p k m", p=128))
    t_gin = p.stok("in_g")
    if upto == "H2":
        return p.finalize()
    t_he = None
    for side, ssb, rows in (("L", slsb, slice(0, 32)), ("R", srsb, slice(2080, 2112))):
        t_hm = None
        for n in range(2):
            for kk in range(4):
                t_hm = p.emit("tensor", lambda e, n=n, kk=kk, ssb=ssb: e.matmul(
                    out=hps[0:32, n * 512:(n + 1) * 512],
                    lhsT=ssb[:, kk * 32:(kk + 1) * 32],
                    rhs=gsb[:, kk * D + n * 512:kk * D + (n + 1) * 512],
                    start=(kk == 0), stop=(kk == 3)),
                    waits=([t_gin] + ([t_he] if t_he else [])) if n == 0 and kk == 0 else [])
        t_he = p.emit("vector", lambda e: e.tensor_copy(hsb[0:32, :],
                                                        hps[0:32, :]),
                      waits=[t_hm, p.stok("o_xe")])
        t_hdma = p.dma("o_xe", xe[rows, :], hsb[0:32, :], waits=[t_he])
        if upto == "H3" and side == "L":
            return p.finalize()
    if upto == "H4":
        return p.finalize()
    p.pop_region()
    if upto == "H":
        return p.finalize()
    p.barrier()

    p.push_region("E")
    emit_E(p, c, dict(xe=xe, maskA=maskA, maskB=maskB, wqT=wqT, wkT=wkT,
                      wvT=wvT, woT_a=woT_a, x3=x3))
    p.pop_region()
    if upto == "E":
        return p.finalize()
    p.barrier()

    p.push_region("F1")
    emit_F1(p, c, dict(x3=x3, w1n=w1n, w1T=w1T, b1c=b1cf, hT=hT1))
    p.pop_region()
    if upto == "F1":
        return p.finalize()
    p.barrier()

    p.push_region("F2")
    emit_F2(p, c, dict(x3=x3, hT=hT1, w2n=w2n, w2T=w2T, b2c=b2cf, y=y))
    p.pop_region()

    return p.finalize()



def _aug(w, b):
    return np.ascontiguousarray(
        np.concatenate([w, b.reshape(1, -1)], axis=0), dtype=np.float32)


def _row(v):
    return np.ascontiguousarray(v.reshape(1, -1), dtype=np.float32)


IDENT = np.eye(128, dtype=BF16NP)


_NC_CACHE = {}


def _get_nc(name, builder):
    if name not in _NC_CACHE:
        _NC_CACHE[name] = builder()
    return _NC_CACHE[name]


def _run(name, builder, per_core, debug=None):
    nc = _get_nc(name, builder)
    res = run_bass_kernel_spmd(nc, per_core, core_ids=list(range(NCORES)))
    if debug is not None:
        debug[name] = res
    return res.results


def host_layouts(inp):
    x = np.ascontiguousarray(inp["L0_attn_x"].reshape(16384, 1024),
                             dtype=np.float32)
    com = dict(
        wq_a=_aug(inp["L0_attn_Wq"], inp["L0_attn_bq"]),
        wk_a=_aug(inp["L0_attn_Wk"], inp["L0_attn_bk"]),
        wv_a=_aug(inp["L0_attn_Wv"], inp["L0_attn_bv"]),
        wo_a=_aug(inp["L0_attn_Wo"], inp["L0_attn_bo"]),
        g1r=_row(inp["L0_attn_g1"]), be1r=_row(inp["L0_attn_be1"]),
        w1_a=_aug(inp["L0_attn_W1"], inp["L0_attn_b1"]),
        b1cB=np.ascontiguousarray(
            inp["L0_attn_b1"].reshape(32, 128).T, dtype=np.float32),
        w2_a=_aug(inp["L0_attn_W2"], inp["L0_attn_b2"]),
        g2r=_row(inp["L0_attn_g2"]), be2r=_row(inp["L0_attn_be2"]),
        gmr=_row(inp["L0_ffn_ln_gamma"]), bbr=_row(inp["L0_ffn_ln_beta"]),
        mur=np.ascontiguousarray(inp["L0_ffn_mu"].reshape(D, ES),
                                 dtype=np.float32),
        scl=np.ascontiguousarray(inp["L0_ffn_scale"].reshape(1, 1),
                                 dtype=np.float32),
        w1e=np.ascontiguousarray(inp["L0_ffn_w1"], dtype=np.float32),
        b1c=np.ascontiguousarray(
            inp["L0_ffn_b1"].reshape(EXP, 2, 128).transpose(2, 0, 1)
            .reshape(128, EXP * 2), dtype=np.float32),
        w2e=np.ascontiguousarray(
            np.concatenate([inp["L0_ffn_w2"],
                            inp["L0_ffn_b2"][:, None, :]], axis=1),
            dtype=np.float32),
        wqT=np.ascontiguousarray(inp["L1_attn_Wq"].T, dtype=np.float32),
        wkT=np.ascontiguousarray(inp["L1_attn_Wk"].T, dtype=np.float32),
        wvT=np.ascontiguousarray(inp["L1_attn_Wv"].T, dtype=np.float32),
        woT_a=_aug(np.ascontiguousarray(inp["L1_attn_Wo"].T),
                   inp["L1_attn_bo"]),
        w1n=np.ascontiguousarray(inp["L1_ffn_w1"], dtype=np.float32),
        w1T=np.ascontiguousarray(inp["L1_ffn_w1"].T, dtype=np.float32),
        b1cf=np.ascontiguousarray(inp["L1_ffn_b1"].reshape(32, 128).T,
                                  dtype=np.float32),
        w2n=np.ascontiguousarray(inp["L1_ffn_w2"], dtype=np.float32),
        w2T=np.ascontiguousarray(inp["L1_ffn_w2"].T, dtype=np.float32),
        b2cf=np.ascontiguousarray(inp["L1_ffn_b2"].reshape(NJ, 128).T,
                                  dtype=np.float32),
        ident=IDENT,
    )
    pp = np.arange(128)[:, None]
    rr = np.arange(128)[None, :]
    com["maskA"] = np.where((rr >= pp - 63) & (rr <= pp), 0.0,
                            -1e4).astype(np.float32)
    pb = np.arange(64)[:, None]
    com["maskB"] = np.where(rr >= pb + 65, 0.0, -1e4).astype(np.float32)
    maps = []
    for ci in range(NCORES):
        m = dict(com)
        m["x"] = np.ascontiguousarray(x[ci * T:(ci + 1) * T])
        sL = np.zeros((512, 32), np.float32)
        sR = np.zeros((512, 32), np.float32)
        if ci > 0:
            sL[np.arange(32) + 64 * (ci - 1) + 32, np.arange(32)] = 1.0
        if ci < NCORES - 1:
            sR[np.arange(32) + 64 * (ci + 1), np.arange(32)] = 1.0
        m["selL"] = sL
        m["selR"] = sR
        maps.append(m)
    return maps


def kernel_fused(**inp):
    inp = {k: np.asarray(v) for k, v in inp.items()}
    maps = host_layouts(inp)
    nc = _get_nc("FUSED", build_fused)
    res = run_bass_kernel_spmd(nc, maps, core_ids=list(range(NCORES)))
    yy = np.concatenate([res.results[ci]["y"] for ci in range(NCORES)], axis=0)
    return yy.reshape(1, 16384, 1024)


def kernel(**inp):
    if not inp.pop("_unfused", False) and "_upto" not in inp:
        inp.pop("_debug", None)
        return kernel_fused(**inp)
    debug = inp.pop("_debug", None)
    upto = inp.pop("_upto", None)
    inp = {k: np.asarray(v) for k, v in inp.items()}
    x = np.ascontiguousarray(inp["L0_attn_x"].reshape(16384, 1024),
                             dtype=np.float32)
    xs = [np.ascontiguousarray(x[ci * T:(ci + 1) * T]) for ci in range(NCORES)]

    wq_a = _aug(inp["L0_attn_Wq"], inp["L0_attn_bq"])
    wk_a = _aug(inp["L0_attn_Wk"], inp["L0_attn_bk"])
    wv_a = _aug(inp["L0_attn_Wv"], inp["L0_attn_bv"])

    rA = _run("A", build_A, [
        dict(x=xs[ci], wq_a=wq_a, wk_a=wk_a, wv_a=wv_a, ident=IDENT)
        for ci in range(NCORES)])
    if upto == "A":
        return rA

    kvp_all = np.ascontiguousarray(
        np.stack([rA[ci]["kvp"] for ci in range(NCORES)]))
    wo_a = _aug(inp["L0_attn_Wo"], inp["L0_attn_bo"])
    g1r = _row(inp["L0_attn_g1"])
    be1r = _row(inp["L0_attn_be1"])
    rB1 = _run("B1", build_B1, [
        dict(x=xs[ci], eqT=rA[ci]["eqT"], rq=rA[ci]["rq"], kvp=kvp_all,
             wo_a=wo_a, g1r=g1r, be1r=be1r, ident=IDENT)
        for ci in range(NCORES)])
    if upto == "B1":
        return rB1

    w1_a = _aug(inp["L0_attn_W1"], inp["L0_attn_b1"])
    b1cB = np.ascontiguousarray(
        inp["L0_attn_b1"].reshape(32, 128).T, dtype=np.float32)
    rB2a = _run("B2a", build_B2a, [
        dict(atT=rB1[ci]["atT"], w1_a=w1_a, b1cB=b1cB, ident=IDENT)
        for ci in range(NCORES)])

    w2_a = _aug(inp["L0_attn_W2"], inp["L0_attn_b2"])
    g2r = _row(inp["L0_attn_g2"])
    be2r = _row(inp["L0_attn_be2"])
    rB2b = _run("B2b", build_B2b, [
        dict(hT=rB2a[ci]["hT"], attn_out=rB1[ci]["attn_out"], x=xs[ci],
             w2_a=w2_a, g2r=g2r, be2r=be2r, ident=IDENT)
        for ci in range(NCORES)])
    if upto == "B2b":
        return rB2b

    gmr = _row(inp["L0_ffn_ln_gamma"])
    bbr = _row(inp["L0_ffn_ln_beta"])
    mur = np.ascontiguousarray(inp["L0_ffn_mu"].reshape(D, ES),
                               dtype=np.float32)
    scl = np.ascontiguousarray(inp["L0_ffn_scale"].reshape(1, 1),
                               dtype=np.float32)
    rC = _run("C", build_C, [
        dict(x1=rB2b[ci]["x1"], gmr=gmr, bbr=bbr, mur=mur, scl=scl,
             ident=IDENT) for ci in range(NCORES)])
    if upto == "C":
        return rC

    Praw_all = np.ascontiguousarray(np.stack([rC[ci]["Praw"] for ci in range(NCORES)]))
    Sloc_all = np.ascontiguousarray(np.stack([rC[ci]["Sloc"] for ci in range(NCORES)]))
    w1e = np.ascontiguousarray(inp["L0_ffn_w1"], dtype=np.float32)
    b1c = np.ascontiguousarray(
        inp["L0_ffn_b1"].reshape(EXP, 2, 128).transpose(2, 0, 1).reshape(128, EXP * 2),
        dtype=np.float32)
    w2e = np.ascontiguousarray(
        np.concatenate([inp["L0_ffn_w2"],
                        inp["L0_ffn_b2"][:, None, :]], axis=1), dtype=np.float32)
    rD = _run("D", build_D, [
        dict(x1=rB2b[ci]["x1"], E=rC[ci]["E"], Zc=rC[ci]["Zc"],
             Praw=Praw_all, Sloc=Sloc_all, w1e=w1e, b1c=b1c, w2e=w2e,
             ident=IDENT) for ci in range(NCORES)])
    if upto == "D":
        return rD

    x2 = np.concatenate([rD[ci]["x2"] for ci in range(NCORES)], axis=0)
    xe = np.zeros((NCORES, 2176, D), np.float32)
    for ci in range(NCORES):
        lo, hi = ci * T - HALO, ci * T + T + HALO
        slo, shi = max(lo, 0), min(hi, 16384)
        xe[ci, slo - lo:slo - lo + (shi - slo)] = x2[slo:shi]
    wqT = np.ascontiguousarray(inp["L1_attn_Wq"].T, dtype=np.float32)
    wkT = np.ascontiguousarray(inp["L1_attn_Wk"].T, dtype=np.float32)
    wvT = np.ascontiguousarray(inp["L1_attn_Wv"].T, dtype=np.float32)
    woT_a = _aug(np.ascontiguousarray(inp["L1_attn_Wo"].T), inp["L1_attn_bo"])
    pp = np.arange(128)[:, None]
    rr = np.arange(128)[None, :]
    maskA = np.where((rr >= pp - 63) & (rr <= pp), 0.0, -1e4).astype(np.float32)
    pb = np.arange(64)[:, None]
    maskB = np.where(rr >= pb + 65, 0.0, -1e4).astype(np.float32)
    rE = _run("E", build_E, [
        dict(xe=np.ascontiguousarray(xe[ci]), wqT=wqT, wkT=wkT, wvT=wvT,
             woT_a=woT_a, maskA=maskA, maskB=maskB, ident=IDENT)
        for ci in range(NCORES)])
    if upto == "E":
        return rE

    w1n = np.ascontiguousarray(inp["L1_ffn_w1"], dtype=np.float32)
    w1T = np.ascontiguousarray(inp["L1_ffn_w1"].T, dtype=np.float32)
    b1cf = np.ascontiguousarray(
        inp["L1_ffn_b1"].reshape(32, 128).T, dtype=np.float32)
    rF1 = _run("F1", build_F1, [
        dict(x3=rE[ci]["x3"], w1n=w1n, w1T=w1T, b1c=b1cf, ident=IDENT)
        for ci in range(NCORES)])

    w2n = np.ascontiguousarray(inp["L1_ffn_w2"], dtype=np.float32)
    w2T = np.ascontiguousarray(inp["L1_ffn_w2"].T, dtype=np.float32)
    b2cf = np.ascontiguousarray(
        inp["L1_ffn_b2"].reshape(NJ, 128).T, dtype=np.float32)
    rF2 = _run("F2", build_F2, [
        dict(x3=rE[ci]["x3"], hT=rF1[ci]["hT"], w2n=w2n, w2T=w2T, b2c=b2cf,
             ident=IDENT) for ci in range(NCORES)])

    y = np.concatenate([rF2[ci]["y"] for ci in range(NCORES)], axis=0)
    return y.reshape(1, 16384, 1024)


# version 14
# speedup vs baseline: 1.1275x; 1.0122x over previous
import numpy as np
import ml_dtypes
from contextlib import ExitStack

import concourse.bass as bass
import concourse.mybir as mybir
from concourse.bass_utils import run_bass_kernel_spmd

F32 = mybir.dt.float32
BF16 = mybir.dt.bfloat16
AF = mybir.ActivationFunctionType
ALU = mybir.AluOpType
AX = mybir.AxisListType

NCORES = 8
T = 2048
NT = 16
D = 1024
NJ = 8
FF = 4096
NH = 16
HD = 64
ES = 256
EXP = 16
SLOT = 16
MOE_H = 256
HALO = 32
TEXT = T + 128
NTE = 17

BF16NP = ml_dtypes.bfloat16
DEBUG_A = False
DEBUG_B1 = False


class Prog:

    ENGS = ("sync", "scalar", "vector", "tensor", "gpsimd")

    def __init__(self, name):
        self.name = name
        self.nc = bass.Bass("TRN2", target_bir_lowering=False)
        self.ctx = ExitStack()
        self.regions = [self.ctx]
        self.prefix = ""
        self.ops = {e: [] for e in self.ENGS}
        self.cnt = {e: 0 for e in self.ENGS}
        self.scnt = {}
        self.pending = {}

    def push_region(self, prefix):
        self.regions.append(ExitStack())
        self._prefixes = getattr(self, "_prefixes", [""])
        self._prefixes.append(self.prefix)
        self.prefix = self.prefix + prefix + "_"

    def pop_region(self):
        self.regions.pop().close()
        self.prefix = self._prefixes.pop()

    def barrier(self):
        snap = [(e, self.cnt[e]) for e in self.ENGS if self.cnt[e] > 0]
        snap += [(s, v) for s, v in self.scnt.items() if v > 0 and s != "cc"]
        for e in self.ENGS:
            self.pending.setdefault(e, [])
            self.pending[e] = snap

    def sb(self, name, shape, dt):
        return self.regions[-1].enter_context(
            self.nc.sbuf_tensor(self.prefix + name, shape, dt))

    def ps(self, name, shape, dt=F32):
        return self.regions[-1].enter_context(
            self.nc.psum_tensor(self.prefix + name, shape, dt))

    def din(self, name, shape, dt=F32):
        return self.nc.dram_tensor(name, shape, dt, kind="ExternalInput").ap()

    def dout(self, name, shape, dt=F32):
        return self.nc.dram_tensor(name, shape, dt, kind="ExternalOutput").ap()

    def dtmp(self, name, shape, dt=F32):
        return self.nc.dram_tensor(name, shape, dt).ap()

    def emit(self, eng, fn, waits=(), inc=1, sem=None):
        sem = sem or eng
        waits = list(waits) + self.pending.pop(eng, [])
        w = {}
        for tok in waits:
            if len(tok) == 3:
                _, we, wv = tok
            else:
                we, wv = tok
                if we in self.scnt:
                    wv = self.scnt[we]
            if we != eng and wv > 0:
                w[we] = max(w.get(we, 0), wv)
        self.ops[eng].append((tuple(w.items()), fn, inc, sem))
        if sem in self.scnt:
            self.scnt[sem] += inc
            return (sem, self.scnt[sem])
        self.cnt[eng] += inc
        return (eng, self.cnt[eng])

    def tok(self, eng):
        return (eng, self.cnt[eng])

    def stok(self, stream):
        return (stream, self.scnt.get(stream, 0))

    def const_col(self, value):
        if not hasattr(self, "_ccache"):
            self._ccache = {}
        if value not in self._ccache:
            t = self.regions[0].enter_context(self.nc.sbuf_tensor(
                f"cc_{len(self._ccache)}", [128, 1], F32))
            tok = self.emit("vector", lambda e, t=t, value=value: e.memset(
                t[:], float(value)))
            self._ccache[value] = (t, tok)
        return self._ccache[value]

    def dma_rr(self, stream, nways, idx, out, in_, waits=(), queue="sync", **kw):
        return self.dma(f"{stream}{idx % nways}", out, in_, waits, queue, **kw)

    def stok_rr(self, stream, nways):
        return [self.stok(f"{stream}{i}") for i in range(nways)]

    def dma(self, stream, out, in_, waits=(), queue="sync", **kw):
        if stream not in self.scnt:
            self.scnt[stream] = 0
        waits = list(waits)
        if self.scnt[stream] > 0:
            waits.append(("exact", stream, self.scnt[stream]))
        return self.emit(queue, lambda e: e.dma_start(out=out, in_=in_, **kw),
                         waits, 16, sem=stream)

    def finalize(self):
        while len(self.regions) > 1:
            self.pop_region()
        nc = self.nc
        sems = {e: self.ctx.enter_context(nc.semaphore("sem_" + e))
                for e in self.ENGS}
        for s in self.scnt:
            sems[s] = self.ctx.enter_context(nc.semaphore("sem_" + s))
        block = self.ctx.enter_context(nc.Block())
        for e in self.ENGS:
            ops = self.ops[e]
            if not ops and e != "sync":
                continue

            def body(engobj, ops=ops, e=e):
                hwm = {}
                selfcnt = 0
                selfsync = e in ("scalar", "vector", "gpsimd")
                for waits, fn, inc, sem in ops:
                    if selfsync and sem == e and selfcnt > 0:
                        engobj.wait_ge(sems[e], selfcnt)
                    for we, wv in waits:
                        if hwm.get(we, 0) < wv:
                            engobj.wait_ge(sems[we], wv)
                            hwm[we] = wv
                    ins = fn(engobj)
                    if inc:
                        ins.then_inc(sems[sem], inc)
                    if sem == e:
                        selfcnt += inc
                if e == "sync":
                    for oe in Prog.ENGS:
                        if self.cnt[oe] > 0:
                            engobj.wait_ge(sems[oe], self.cnt[oe])
                    for s, v in self.scnt.items():
                        if v > 0:
                            engobj.wait_ge(sems[s], v)

            getattr(block, e)(body)
        self.ctx.close()
        return nc



def common_consts(p):
    c = {}
    c["ident"] = p.sb("ident_sb", [128, 128], BF16)
    ident_in = p.din("ident", [128, 128], BF16)
    t_id = p.dma("in", c["ident"][:], ident_in[:])
    c["ones_bf"] = p.sb("ones_bf", [1, 2048], BF16)
    c["onescol_bf"] = p.sb("onescol_bf", [128, 1], BF16)
    c["ones_f"] = p.sb("ones_f", [1, 128], F32)
    t1 = p.emit("vector", lambda e: e.memset(c["ones_bf"][:], 1.0))
    t2 = p.emit("vector", lambda e: e.memset(c["onescol_bf"][:], 1.0))
    t3 = p.emit("vector", lambda e: e.memset(c["ones_f"][:], 1.0))
    c["ready"] = [t_id, t1, t2, t3]
    return c


def load_cast_weight(p, c, name, kt, n, stage, dest, brow_f, brow_h, waits=(),
                     win=None):
    if win is None:
        win = p.din(name, [kt * 128 + 1, n], F32)
    tw = p.dma("in", stage[:, 0:kt * n].rearrange("p (k f) -> p k f", f=n),
               win[0:kt * 128, :].rearrange("(k p) f -> p k f", p=128),
               waits=waits)
    tb = p.dma("in", brow_f[:, 0:n], win[kt * 128:kt * 128 + 1, :], waits=waits)
    t_in = p.stok("in")
    tcast = []
    for j in range(kt):
        tcast.append(p.emit("scalar", lambda e, j=j: e.activation(
            out=dest[:, j * n:(j + 1) * n], in_=stage[:, j * n:(j + 1) * n],
            func=AF.Copy), waits=[t_in]))
    tcb = p.emit("scalar", lambda e: e.activation(
        out=brow_h[:, 0:n], in_=brow_f[:, 0:n], func=AF.Copy), waits=[t_in])
    return tcast + [tcb]


def bcast_row_f32(p, c, dst, src_row, n, psA, waits=()):
    toks = []
    ev = []
    for n0 in range(0, n, 512):
        nn = min(512, n - n0)
        tmm = p.emit("tensor", lambda e, n0=n0, nn=nn: e.matmul(
            out=psA[:, n0 % 512:n0 % 512 + nn], lhsT=c["ones_f"][:, :],
            rhs=src_row[:, n0:n0 + nn], start=True, stop=True),
            waits=list(waits) + ev)
        tev = p.emit("vector", lambda e, n0=n0, nn=nn: e.tensor_copy(
            dst[:, n0:n0 + nn], psA[:, n0 % 512:n0 % 512 + nn]), waits=[tmm])
        ev = [tev]
        toks.append(tev)
    return toks


def ln_stats_apply(p, bufs, u, s1, out, eps, tw_u, tw_s2, tag):
    sq, mcol, vcol, tcol = bufs["sqw"], bufs["mcol"], bufs["vcol"], bufs["tcol"]
    t_s2 = p.emit("scalar", lambda e: e.activation(
        out=sq[:], in_=u[:], func=AF.Square, accum_out=vcol[:, 0:1]),
        waits=tw_u)
    t_m = p.emit("scalar", lambda e: e.mul(mcol[:, 0:1], s1[:, 0:1], 1.0 / D),
                 waits=tw_u)
    t_t = p.emit("scalar", lambda e: e.activation(
        out=tcol[:, 0:1], in_=mcol[:, 0:1], func=AF.Square), waits=[t_m])
    t_v = p.emit("vector", lambda e: e.scalar_tensor_tensor(
        out=vcol[:, 0:1], in0=vcol[:, 0:1], scalar=1.0 / D, in1=tcol[:, 0:1],
        op0=ALU.mult, op1=ALU.subtract), waits=[t_s2, t_t])
    eps_ap, eps_tok = p.const_col(float(eps))
    t_sq = p.emit("scalar", lambda e: e.activation(
        out=tcol[:, 0:1], in_=vcol[:, 0:1], func=AF.Sqrt,
        bias=eps_ap[:, 0:1]), waits=[t_v, eps_tok])
    t_r = p.emit("vector", lambda e: e.reciprocal(
        out=vcol[:, 0:1], in_=tcol[:, 0:1]), waits=[t_sq])
    t_o = p.emit("vector", lambda e: e.tensor_scalar(
        out=out[:], in0=u[:], scalar1=mcol[:, 0:1], scalar2=vcol[:, 0:1],
        op0=ALU.subtract, op1=ALU.mult), waits=[t_r] + tw_u)
    return t_o, t_sq


def transpose_tile(p, c, tps, src_bf, dst, waits_src, waits_ps, nblk=8,
                   waits_evac=()):
    tts = []
    for j in range(nblk):
        tts.append(p.emit("tensor", lambda e, j=j: e.transpose(
            out=tps[:, j * 128:(j + 1) * 128],
            in_=src_bf[:, j * 128:(j + 1) * 128],
            identity=c["ident"][:]),
            waits=(list(waits_src) + list(waits_ps)) if j == 0 else list(waits_src)))
    tev = p.emit("vector", lambda e: e.tensor_copy(
        dst[:], tps[:, 0:nblk * 128]), waits=[tts[-1]] + list(waits_evac))
    return tev




def interleave_pairs(bodies):
    live = list(bodies)
    while live:
        nxt = []
        for g in live:
            try:
                next(g)
                nxt.append(g)
            except StopIteration:
                pass
        live = nxt


def emit_A(p, c, io):
    x_in = io["x"]
    kvp_out = io["kvp"]
    eqT_res = io.get("eqT_res")
    rq_res = io.get("rq_res")
    eqT_out = io.get("eqT")
    rq_out = io.get("rq")

    wstage = [p.sb(f"wstage{i}", [128, 2 * D], F32) for i in range(2)]
    wb_f = p.sb("wb_f", [1, D], F32)
    wh = {k: p.sb(f"wh_{k}", [128, NJ * D], BF16) for k in "qkv"}
    wbh = {k: p.sb(f"wbh_{k}", [1, D], BF16) for k in "qkv"}

    xT = p.sb("xT", [128, NJ * T], BF16)
    NL = 2
    xf_t = [p.sb(f"xf_t{i}", [128, D], F32) for i in range(NL)]
    xh_t = [p.sb(f"xh_t{i}", [128, D], BF16) for i in range(NL)]
    qa = [p.sb(f"qa{i}", [128, D], F32) for i in range(NL)]
    qb = [p.sb(f"qb{i}", [128, D], F32) for i in range(NL)]
    qc = [p.sb(f"qc{i}", [128, D], F32) for i in range(NL)]
    eqbf = [p.sb(f"eqbf{i}", [128, D], BF16) for i in range(NL)]
    ekbf = [p.sb(f"ekbf{i}", [128, D], BF16) for i in range(NL)]
    vpbf = [p.sb(f"vpbf{i}", [128, D], BF16) for i in range(NL)]
    sqb0 = p.sb("sqb0", [128, D], F32)
    sqb = [sqb0] * NL
    ncol = [p.sb(f"ncol{i}", [128, NH], F32) for i in range(NL)]
    scol = [p.sb(f"scol{i}", [128, NH], F32) for i in range(NL)]
    rk = [p.sb(f"rk{i}", [128, NH], F32) for i in range(NL)]
    rqsb = ([p.sb(f"rqsb{i}", [128, NH], F32) for i in range(NL)]
            if io.get("rq_res") is None else None)
    eqstage = ([p.sb(f"eqstage{i}", [128, D], BF16) for i in range(NL)]
               if io.get("eqT_res") is None else None)
    kvsb = qa[0][0:64, 0:1024]

    tpsL = [p.ps("tpsA", [128, 1024], BF16), p.ps("tps2", [128, 1024], BF16)]
    qps = [p.ps(f"qps{i}", [128, 1024], F32) for i in range(NL)]
    kvps = p.ps("kvps", [64, 1024], F32)

    wready = []
    wcast = {}
    for idx, (k, h) in enumerate([(k, h) for k in "qkv" for h in range(4)]):
        tw = p.dma_rr("in_w", 2, idx,
                      wstage[idx % 2][:].rearrange("p (k f) -> p k f", f=D),
                      io["w%s_a" % k][h * 256:(h + 1) * 256, :]
                      .rearrange("(k p) f -> p k f", p=128),
                      waits=wcast.get(idx - 2, list(c["ready"])))
        cs = []
        for j in range(2):
            cs.append(p.emit("scalar", lambda e, k=k, h=h, j=j, idx=idx: e.activation(
                out=wh[k][:, (h * 2 + j) * D:(h * 2 + j + 1) * D],
                in_=wstage[idx % 2][:, j * D:(j + 1) * D],
                func=AF.Copy), waits=[tw]))
        wcast[idx] = cs
        wready += cs
    prevb = None
    for k in "qkv":
        tb = p.dma("in_b", wb_f[:, 0:D], io["w%s_a" % k][NJ * 128:NJ * 128 + 1, :],
                   waits=[prevb] if prevb else list(c["ready"]))
        prevb = p.emit("scalar", lambda e, k=k: e.activation(
            out=wbh[k][:, 0:D], in_=wb_f[:, 0:D], func=AF.Copy),
            waits=[tb])
        wready.append(prevb)

    pv = [dict(cast=None, xtp=None, eq_evac=None, vstt=None, kps=None,
               nq=None, sqq=None, rq=None) for _ in range(NL)]
    gkv = dict(ek=None, kvlast=None)

    def tile_body(t, L):
        v = pv[L]
        w_load = [v["cast"]] if v["cast"] else list(c["ready"])
        t_x = p.dma_rr("in_x", NL, L, xf_t[L][:],
                       x_in[t * 128:(t + 1) * 128, :], waits=w_load)
        yield
        w_c = [t_x] + ([v["xtp"]] if v["xtp"] else [])
        t_cast = p.emit("scalar", lambda e, L=L: e.activation(
            out=xh_t[L][:], in_=xf_t[L][:], func=AF.Copy), waits=w_c)
        v["cast"] = t_cast
        xT_view = xT[:].rearrange("p (j t) -> p j t", t=T)[:, :, t * 128:(t + 1) * 128]
        tev = transpose_tile(p, c, tpsL[L], xh_t[L], xT_view, [t_cast],
                             [v["xtp"]] if v["xtp"] else [], nblk=NJ)
        v["xtp"] = tev
        t_xT = tev
        yield

        def proj(k, waits_extra, t=t, t_xT=t_xT, L=L):
            last = None
            for n in range(2):
                for kk in range(NJ):
                    last = p.emit("tensor", lambda e, n=n, kk=kk, k=k, t=t, L=L: e.matmul(
                        out=qps[L][:, n * 512:(n + 1) * 512],
                        lhsT=xT[:, kk * T + t * 128: kk * T + (t + 1) * 128],
                        rhs=wh[k][:, kk * D + n * 512: kk * D + (n + 1) * 512],
                        start=(kk == 0), stop=False),
                        waits=([t_xT] + wready + waits_extra) if kk == 0 and n == 0 else [])
                last = p.emit("tensor", lambda e, n=n, k=k, L=L: e.matmul(
                    out=qps[L][:, n * 512:(n + 1) * 512],
                    lhsT=c["ones_bf"][0:1, 0:128],
                    rhs=wbh[k][:, n * 512:(n + 1) * 512],
                    start=False, stop=True), waits=[])
            return last

        def elu(out_bf, t_mm, war=(), L=L):
            ta = p.emit("vector", lambda e, L=L: e.tensor_scalar_max(
                qa[L][:], qps[L][:], 0.0), waits=[t_mm] + list(war))
            tb = p.emit("vector", lambda e, L=L: e.tensor_scalar(
                out=qb[L][:], in0=qps[L][:], scalar1=-1.0, scalar2=0.0,
                op0=ALU.mult, op1=ALU.max), waits=[])
            tc = p.emit("scalar", lambda e, L=L: e.activation(
                out=qc[L][:], in_=qb[L][:], func=AF.Exp, scale=-1.0),
                waits=[tb])
            te = p.emit("vector", lambda e, out_bf=out_bf, L=L: e.scalar_tensor_tensor(
                out=out_bf[:], in0=qa[L][:], scalar=-1.0, in1=qc[L][:],
                op0=ALU.add, op1=ALU.add), waits=[tc])
            return te, tb

        t_qmm = proj("q", [v["vstt"]] if v["vstt"] else [])
        yield
        war_q = [gkv["ek"]] if gkv["ek"] else []
        t_eq, t_q_psfree = elu(eqbf[L], t_qmm, war=war_q)
        yield
        t_eqsq = p.emit("scalar", lambda e, L=L: e.activation(
            out=sqb[L][:], in_=eqbf[L][:], func=AF.Square), waits=[t_eq])
        t_nq = p.emit("vector", lambda e, L=L: e.reduce_sum(
            out=ncol[L][:, 0:NH], in_=sqb[L][:].rearrange("p (h w) -> p h w", w=HD),
            axis=AX.X), waits=[t_eqsq])
        t_sqq = p.emit("scalar", lambda e, L=L: e.activation(
            out=scol[L][:, 0:NH], in_=ncol[L][:, 0:NH], func=AF.Sqrt),
            waits=[t_nq])
        if rq_res is not None:
            t_rq = p.emit("vector", lambda e, L=L, t=t: e.reciprocal(
                out=rq_res[:, t * NH:(t + 1) * NH], in_=scol[L][:, 0:NH]),
                waits=[t_sqq])
        else:
            t_rq = p.emit("vector", lambda e, L=L: e.reciprocal(
                out=rqsb[L][:, 0:NH], in_=scol[L][:, 0:NH]),
                waits=[t_sqq, p.stok(f"o_rq{L}")])
            p.dma_rr("o_rq", NL, L, rq_out[t, :, :], rqsb[L][:, 0:NH],
                     waits=[t_rq])
        yield

        w_ps2 = [v["eq_evac"]] if v["eq_evac"] else []
        if eqT_res is not None:
            dstv = eqT_res[:].rearrange("p (j t) -> p j t",
                                        t=T)[:, :, t * 128:(t + 1) * 128]
            tev_eq = transpose_tile(p, c, tpsL[L], eqbf[L], dstv, [t_eq],
                                    w_ps2, nblk=NJ)
        else:
            tev_eq = transpose_tile(p, c, tpsL[L], eqbf[L], eqstage[L][:],
                                    [t_eq], w_ps2, nblk=NJ,
                                    waits_evac=[p.stok(f"o_eq{L}")])
            p.dma_rr(
                "o_eq", NL, L,
                eqT_out[:, :, t * 128:(t + 1) * 128].rearrange("j p t -> p j t"),
                eqstage[L][:].rearrange("p (j t) -> p j t", t=128),
                waits=[tev_eq])
        v["eq_evac"] = tev_eq
        yield

        t_kmm = proj("k", [t_q_psfree])
        yield
        t_ek, t_k_psfree = elu(ekbf[L], t_kmm, war=[t_eq])
        v["kps"] = t_k_psfree
        yield
        t_eksq = p.emit("scalar", lambda e, L=L: e.activation(
            out=sqb[L][:], in_=ekbf[L][:], func=AF.Square),
            waits=[t_ek, t_nq])
        t_nk = p.emit("vector", lambda e, L=L: e.reduce_sum(
            out=ncol[L][:, 0:NH], in_=sqb[L][:].rearrange("p (h w) -> p h w", w=HD),
            axis=AX.X), waits=[t_eksq, t_sqq])
        t_sqk = p.emit("scalar", lambda e, L=L: e.activation(
            out=scol[L][:, 0:NH], in_=ncol[L][:, 0:NH], func=AF.Sqrt),
            waits=[t_nk, t_rq])
        t_rk = p.emit("vector", lambda e, L=L: e.reciprocal(
            out=rk[L][:, 0:NH], in_=scol[L][:, 0:NH]), waits=[t_sqk])
        yield

        t_vmm = proj("v", [t_k_psfree])
        yield
        t_vstt = p.emit("vector", lambda e, L=L: e.scalar_tensor_tensor(
            out=vpbf[L][:], in0=qps[L][:], scalar=1.0,
            in1=rk[L][:, 0:NH].unsqueeze(2).broadcast_to([128, NH, HD]),
            op0=ALU.mult, op1=ALU.mult), waits=[t_vmm, t_rk])
        v["vstt"] = t_vstt
        yield

        t_kv = None
        for h in range(NH):
            t_kv = p.emit("tensor", lambda e, h=h, t=t, L=L: e.matmul(
                out=kvps[0:64, h * 64:(h + 1) * 64],
                lhsT=ekbf[L][:, h * 64:(h + 1) * 64],
                rhs=vpbf[L][:, h * 64:(h + 1) * 64],
                start=(t == 0 and h % 8 == 0),
                stop=(t == NT - 1 and h % 8 == 7)),
                waits=[t_ek, t_vstt] if h == 0 else [])
        gkv["ek"] = t_eq if False else t_ek
        gkv["kvlast"] = t_kv

    for t0 in range(0, NT, NL):
        interleave_pairs([tile_body(t, t % NL)
                          for t in range(t0, min(t0 + NL, NT))])
    t_kv_last = gkv["kvlast"]

    t_kvev = p.emit("vector", lambda e: e.tensor_copy(
        kvsb[:], kvps[0:64, :]), waits=[t_kv_last])
    p.dma("o_kv", kvp_out[:], kvsb[:], waits=[t_kvev])


def build_A():
    p = Prog("A")
    c = common_consts(p)
    io = dict(x=p.din("x", [T, D], F32),
              wq_a=p.din("wq_a", [D + 1, D], F32),
              wk_a=p.din("wk_a", [D + 1, D], F32),
              wv_a=p.din("wv_a", [D + 1, D], F32),
              kvp=p.dout("kvp", [64, 1024], F32),
              eqT=p.dout("eqT", [NJ, 128, T], BF16),
              rq=p.dout("rq", [NT, 128, NH], F32))
    emit_A(p, c, io)
    return p.finalize()



def emit_B1(p, c, io, ntiles=NT, stage=99, fused=False, kv_wait=None):
    x_in = io["x"]
    eqT_res = io.get("eqT_res")
    rq_res = io.get("rq_res")
    eqT_in = io.get("eqT")
    rq_in = io.get("rq")
    kvp_in = io["kvp"]
    g1_in = io["g1r"]
    be1_in = io["be1r"]
    ao_out = io["attn_out"]
    atT_out = io["atT"]

    eqT_sb = (p.sb("eqT_sb", [128, NJ * T], BF16) if eqT_res is None
              else eqT_res)
    kvacc = p.sb("kvacc", [64, 1024], F32)
    kvh = p.sb("kvh", [64, 1024], BF16)
    kvblk = p.sb("kvblk", [128, NJ * 128], BF16)
    wstage = p.sb("wstage", [128, NJ * D], F32)
    wb_f = p.sb("wb_f", [1, D], F32)
    woh = p.sb("woh", [128, NJ * D], BF16)
    wob_h = p.sb("wob_h", [1, D], BF16)
    g1row = p.sb("g1row", [1, D], F32)
    be1row = p.sb("be1row", [1, D], F32)
    gb1 = p.sb("gb1", [128, D], F32)
    bb1 = p.sb("bb1", [128, D], F32)
    NL = 2
    x_t = [p.sb(f"x_t{i}", [128, D], F32) for i in range(NL)]
    rq_t = [p.sb(f"rq_t{i}", [128, NH], F32) for i in range(NL)]
    rqx = [p.sb(f"rqx{i}", [128, NH], F32) for i in range(NL)]
    ctxn = [p.sb(f"ctxn{i}", [128, D], BF16) for i in range(NL)]
    ctxT = [p.sb(f"ctxT{i}", [128, D], BF16) for i in range(NL)]
    u = [p.sb(f"u{i}", [128, D], F32) for i in range(NL)]
    s1col = [p.sb(f"s1col{i}", [128, 1], F32) for i in range(NL)]
    lnb = [dict(sqw=p.sb(f"sqw{i}", [128, D], F32),
                mcol=p.sb(f"mcol{i}", [128, 1], F32),
                vcol=p.sb(f"vcol{i}", [128, 1], F32),
                tcol=p.sb(f"tcol{i}", [128, 1], F32)) for i in range(NL)]
    ao = [p.sb(f"ao{i}", [128, D], F32) for i in range(NL)]
    ao2 = [p.sb(f"ao2{i}", [128, D], F32) for i in range(NL)]
    aof = [p.sb(f"aof{i}", [128, D], F32) for i in range(NL)]
    at_bf = [p.sb(f"at_bf{i}", [128, D], BF16) for i in range(NL)]
    atT_stage = [p.sb(f"atT_stage{i}", [128, D], BF16) for i in range(NL)]

    cps = [p.ps(f"cps{i}", [128, 1024], F32) for i in range(NL)]
    tps = [p.ps(f"tps{i}", [128, 1024], BF16) for i in range(NL)]
    psA = p.ps("psA", [128, 512], F32)

    t_eqT_l = []
    if eqT_res is None:
        for ch in range(4):
            t_eqT_l.append(p.dma_rr(
                "in_eq", 4, ch,
                eqT_sb[:].rearrange("p (j t) -> p j t", t=T)[:, 2 * ch:2 * ch + 2, :],
                eqT_in[2 * ch:2 * ch + 2].rearrange("j p t -> p j t")))
    if fused:
        tprev = p.dma("in_kv", kvacc[:], kvp_in[:], waits=[kv_wait],
                      queue="gpsimd")
    else:
        tprev = None
        for cc in range(NCORES):
            tprev = p.dma("in_kv", kvacc[:], kvp_in[cc],
                          waits=[tprev] if tprev else [], queue="gpsimd",
                          accum_op=(ALU.add if cc else ALU.bypass))
    t_kvh0 = p.emit("scalar", lambda e: e.activation(
        out=kvh[0:64, :], in_=kvacc[:], func=AF.Copy), waits=[tprev])
    t_z = p.emit("vector", lambda e: e.memset(kvblk[:], 0.0))
    kv_toks = []
    for j in range(NJ):
        kv_toks.append(p.dma_rr("in_kv2", 4, 2 * j,
                                kvblk[0:64, j * 128:j * 128 + 64],
                                kvh[0:64, (2 * j) * 64:(2 * j + 1) * 64],
                                waits=[t_kvh0, t_z]))
        kv_toks.append(p.dma_rr("in_kv2", 4, 2 * j + 1,
                                kvblk[64:128, j * 128 + 64:(j + 1) * 128],
                                kvh[0:64, (2 * j + 1) * 64:(2 * j + 2) * 64],
                                waits=[t_kvh0, t_z]))
    t_kvh = kv_toks[-4:]
    wready = load_cast_weight(p, c, None, NJ, D, wstage, woh, wb_f, wob_h,
                              waits=c["ready"], win=io["wo_a"])
    t_g1 = p.dma("in", g1row[:], g1_in[:], waits=wready)
    t_be1 = p.dma("in", be1row[:], be1_in[:], waits=wready)
    t_in = p.stok("in")
    tb1 = bcast_row_f32(p, c, gb1, g1row, D, psA, waits=[t_in])
    tb2 = bcast_row_f32(p, c, bb1, be1row, D, psA, waits=[t_in] + tb1)
    t_bc = tb1 + tb2

    pv = [dict(rqx=None, lnsq=None, ctx_stt=None, ctxT_ev=None, u=None,
               lnout=None) for _ in range(NL)]

    def tile_body(t, L):
        v = pv[L]
        t_x = p.dma_rr("in_x", NL, L, x_t[L][:], x_in[t * 128:(t + 1) * 128, :],
                       waits=[v["u"]] if v["u"] else wready)
        if rq_res is not None:
            yield
            t_rqx = p.emit("scalar", lambda e, L=L, t=t: e.mul(
                rqx[L][:], rq_res[:, t * NH:(t + 1) * NH], 0.125), waits=[])
        else:
            t_rq = p.dma_rr("in_r", NL, L, rq_t[L][:], rq_in[t],
                            waits=[v["rqx"]] if v["rqx"] else wready)
            yield
            t_rqx = p.emit("scalar", lambda e, L=L: e.mul(
                rqx[L][:], rq_t[L][:], 0.125), waits=[t_rq])
        v["rqx"] = t_rqx
        yield

        t_cmm = None
        for j in range(NJ):
            t_cmm = p.emit("tensor", lambda e, j=j, t=t, L=L: e.matmul(
                out=cps[L][:, j * 128:(j + 1) * 128],
                lhsT=eqT_sb[:, j * T + t * 128:j * T + (t + 1) * 128],
                rhs=kvblk[:, j * 128:(j + 1) * 128],
                start=(j % 4 == 0), stop=(j % 4 == 3)),
                waits=(t_eqT_l + t_kvh +
                       [x for x in (v["ctx_stt"], v["u"]) if x]) if j == 0 else [])
        t_ctx = p.emit("vector", lambda e, L=L: e.scalar_tensor_tensor(
            out=ctxn[L][:], in0=cps[L][:], scalar=1.0,
            in1=rqx[L][:].unsqueeze(2).broadcast_to([128, NH, HD]),
            op0=ALU.mult, op1=ALU.mult), waits=[t_cmm, t_rqx])
        yield

        t_ctxT = transpose_tile(p, c, tps[L], ctxn[L], ctxT[L][:], [t_ctx],
                                [v["ctxT_ev"]] if v["ctxT_ev"] else [],
                                nblk=NJ)
        yield

        t_omm = None
        for n in range(2):
            for kk in range(NJ):
                t_omm = p.emit("tensor", lambda e, n=n, kk=kk, L=L: e.matmul(
                    out=cps[L][:, n * 512:(n + 1) * 512],
                    lhsT=ctxT[L][:, kk * 128:(kk + 1) * 128],
                    rhs=woh[:, kk * D + n * 512:kk * D + (n + 1) * 512],
                    start=(kk == 0), stop=False),
                    waits=([t_ctxT] + wready +
                           ([v["u"]] if v["u"] else [])) if kk == 0 and n == 0 else [])
            t_omm = p.emit("tensor", lambda e, n=n, L=L: e.matmul(
                out=cps[L][:, n * 512:(n + 1) * 512],
                lhsT=c["ones_bf"][0:1, 0:128],
                rhs=wob_h[:, n * 512:(n + 1) * 512],
                start=False, stop=True), waits=[])
            yield

        t_u = p.emit("vector", lambda e, L=L: e.scalar_tensor_tensor(
            out=u[L][:], in0=cps[L][:], scalar=1.0, in1=x_t[L][:],
            op0=ALU.mult, op1=ALU.add, accum_out=s1col[L][:, 0:1]),
            waits=[t_omm, t_x] + ([v["lnsq"]] if v["lnsq"] else []))
        v["u"] = t_u
        yield

        w_ln = [t_u] + ([v["lnout"]] if v["lnout"] else [])
        t_ao, t_sq1 = ln_stats_apply(p, lnb[L], u[L], s1col[L], ao[L], 1e-12,
                                     w_ln, None, "ln1")
        v["lnsq"] = t_sq1
        yield
        t_m1 = p.emit("vector", lambda e, L=L: e.tensor_mul(
            ao2[L][:], ao[L][:], gb1[:]), waits=[t_ao] + t_bc)
        t_aof = p.emit("vector", lambda e, L=L: e.tensor_add(
            aof[L][:], ao2[L][:], bb1[:]),
            waits=[t_m1, p.stok(f"o_ao{L}")])
        v["lnout"] = t_aof
        yield

        t_cast = p.emit("scalar", lambda e, L=L: e.activation(
            out=at_bf[L][:], in_=aof[L][:], func=AF.Copy), waits=[t_aof])
        p.dma_rr("o_ao", NL, L, ao_out[t * 128:(t + 1) * 128, :], aof[L][:],
                 waits=[t_aof])
        yield
        t_atT = transpose_tile(p, c, tps[L], at_bf[L], atT_stage[L][:],
                               [t_cast], [t_ctxT], nblk=NJ,
                               waits_evac=[p.stok(f"o_at{L}")])
        v["ctxT_ev"] = t_atT
        v["ctx_stt"] = t_ctx
        p.dma_rr("o_at", NL, L,
                 atT_out[:, :, t * 128:(t + 1) * 128].rearrange("j p t -> p j t"),
                 atT_stage[L][:].rearrange("p (j t) -> p j t", t=128),
                 waits=[t_atT])

    for t0 in range(0, ntiles, NL):
        interleave_pairs([tile_body(t, t % NL)
                          for t in range(t0, min(t0 + NL, ntiles))])


def build_B1(ntiles=NT, stage=99):
    p = Prog("B1")
    c = common_consts(p)
    io = dict(x=p.din("x", [T, D], F32),
              eqT=p.din("eqT", [NJ, 128, T], BF16),
              rq=p.din("rq", [NT, 128, NH], F32),
              kvp=p.din("kvp", [NCORES, 64, 1024], F32),
              g1r=p.din("g1r", [1, D], F32),
              be1r=p.din("be1r", [1, D], F32),
              wo_a=p.din("wo_a", [D + 1, D], F32),
              attn_out=p.dout("attn_out", [T, D], F32),
              atT=p.dout("atT", [NJ, 128, T], BF16))
    emit_B1(p, c, io, ntiles=ntiles, stage=stage)
    return p.finalize()



def emit_B2a(p, c, io):
    atT_in = io["atT"]
    w1_in = io["w1_a"]
    b1c_in = io["b1cB"]
    hT_out = io["hT"]

    atT_sb = p.sb("atT_sb", [128, NJ * T], BF16)
    wstage = [p.sb(f"wstage{i}", [128, FF], F32) for i in range(2)]
    w1h = p.sb("w1h", [128, NJ * FF], BF16)
    b1cB = p.sb("b1cB_sb", [128, 32], F32)
    hbig = [p.sb(f"hbig{i}", [128, 8 * 512], BF16) for i in range(2)]
    hps = [p.ps(f"hps{i}", [128, 512], F32) for i in range(2)]

    t_atT_l = []
    for ach in range(4):
        t_atT_l.append(p.dma_rr(
            "in_at", 4, ach,
            atT_sb[:].rearrange("p (j t) -> p j t", t=T)[:, 2 * ach:2 * ach + 2, :],
            atT_in[2 * ach:2 * ach + 2].rearrange("j p t -> p j t")))
    tb = p.dma("in", b1cB[:], b1c_in[:])
    wready = [tb] + t_atT_l
    wcast = {}
    for ch in range(NJ):
        tw = p.dma_rr("in_w", 2, ch, wstage[ch % 2][:],
                      w1_in[ch * 128:(ch + 1) * 128, :],
                      waits=wcast.get(ch - 2, c["ready"]))
        cs = [p.emit("scalar", lambda e, ch=ch: e.activation(
            out=w1h[:, ch * FF:(ch + 1) * FF],
            in_=wstage[ch % 2][:], func=AF.Copy), waits=[tw])]
        wcast[ch] = cs
        wready += cs

    prev_ev = {}
    prev_st = {}
    i = 0
    for cch in range(4):
        for m in range(32):
            par = i % 2
            grp = (i // 8) % 2
            t_mm = None
            for kk in range(NJ):
                t_mm = p.emit("tensor", lambda e, cch=cch, m=m, kk=kk, par=par: e.matmul(
                    out=hps[par][:],
                    lhsT=w1h[:, kk * FF + m * 128:kk * FF + (m + 1) * 128],
                    rhs=atT_sb[:, kk * T + cch * 512:kk * T + (cch + 1) * 512],
                    start=(kk == 0), stop=(kk == NJ - 1)),
                    waits=(wready + list(prev_ev.get(par, []))) if kk == 0 else [])
            t_ev = p.emit("scalar", lambda e, par=par, grp=grp, m=m: e.activation(
                out=hbig[grp][:, (m % 8) * 512:(m % 8 + 1) * 512],
                in_=hps[par][:], func=AF.Gelu, bias=b1cB[:, m:m + 1]),
                waits=[t_mm] + (list(prev_st.get(grp, [])) if m % 8 < 2 else []))
            prev_ev[par] = [t_ev]
            if m % 8 == 7:
                t_st = p.dma_rr("o_h", 2, grp,
                                hT_out[m - 7:m + 1, :, cch * 512:(cch + 1) * 512]
                                .rearrange("m p tt -> p m tt"),
                                hbig[grp][:].rearrange("p (m tt) -> p m tt", tt=512),
                                waits=[t_ev])
                prev_st[grp] = [t_st]
            i += 1


def build_B2a():
    p = Prog("B2a")
    c = common_consts(p)
    io = dict(atT=p.din("atT", [NJ, 128, T], BF16),
              w1_a=p.din("w1_a", [NJ * 128 + 1, FF], F32),
              b1cB=p.din("b1cB", [128, 32], F32),
              hT=p.dout("hT", [32, 128, T], BF16))
    emit_B2a(p, c, io)
    return p.finalize()



def emit_B2b(p, c, io):
    hT_in = io["hT"]
    ao_in = io["attn_out"]
    x_in = io["x"]
    w2_in = io["w2_a"]
    g2_in = io["g2r"]
    be2_in = io["be2r"]
    x1_out = io["x1"]

    wstage = [p.sb(f"wstage{i}", [128, D], F32) for i in range(2)]
    w2h = p.sb("w2h", [128, 32 * D], BF16)
    w2b_f = wstage[0][0:1, 0:D]
    w2b_h = p.sb("w2b_h", [1, D], BF16)
    g2row = p.sb("g2row", [1, D], F32)
    be2row = p.sb("be2row", [1, D], F32)
    gb2 = p.sb("gb2", [128, D], F32)
    bb2 = p.sb("bb2", [128, D], F32)
    NL = 2
    hT_c = [p.sb(f"hT_c{i}", [128, 32 * 256], BF16) for i in range(2)]
    x_t = [p.sb(f"x_t{i}", [128, D], F32) for i in range(NL)]
    at_t = [p.sb(f"at_t{i}", [128, D], F32) for i in range(NL)]
    u = [p.sb(f"u{i}", [128, D], F32) for i in range(NL)]
    s1col = [p.sb(f"s1col{i}", [128, 1], F32) for i in range(NL)]
    sqw0 = p.sb("sqw0", [128, D], F32)
    lnb = [dict(sqw=sqw0,
                mcol=p.sb(f"mcol{i}", [128, 1], F32),
                vcol=p.sb(f"vcol{i}", [128, 1], F32),
                tcol=p.sb(f"tcol{i}", [128, 1], F32)) for i in range(NL)]
    ao0 = p.sb("ao0", [128, D], F32)
    ao = [ao0] * NL
    aof0 = p.sb("aof0", [128, D], F32)
    aof = [aof0] * NL
    u2 = at_t
    x1_t = [p.sb(f"x1_t{i}", [128, D], F32) for i in range(NL)]
    ao2 = u

    ops = [p.ps(f"ops{i}", [128, 1024], F32) for i in range(3)]
    psA = p.ps("psA", [128, 512], F32)

    wready = []
    wcast = {}
    for ch in range(32):
        tw = p.dma_rr("in_w", 2, ch, wstage[ch % 2][:],
                      w2_in[ch * 128:(ch + 1) * 128, :],
                      waits=wcast.get(ch - 2, list(c["ready"])))
        cs = [p.emit("scalar", lambda e, ch=ch: e.activation(
            out=w2h[:, ch * D:(ch + 1) * D],
            in_=wstage[ch % 2][:], func=AF.Copy), waits=[tw])]
        wcast[ch] = cs
        wready += cs
    tb = p.dma("in", w2b_f[:], w2_in[32 * 128:32 * 128 + 1, :],
               waits=wcast[31])
    wready.append(p.emit("scalar", lambda e: e.activation(
        out=w2b_h[:], in_=w2b_f[:], func=AF.Copy), waits=[tb]))
    t_g2 = p.dma("in", g2row[:], g2_in[:], waits=wready)
    t_be2 = p.dma("in", be2row[:], be2_in[:], waits=wready)
    t_in = p.stok("in")
    tb1 = bcast_row_f32(p, c, gb2, g2row, D, psA, waits=[t_in])
    tb2 = bcast_row_f32(p, c, bb2, be2row, D, psA, waits=[t_in] + tb1)
    t_bc = tb1 + tb2

    pv = [dict(u=None, mm=None, lnout=None, lnsq=None, st=None)
          for _ in range(NL)]
    g_h = {}
    g_mm = {}
    g_u = {}

    def tile_body(t, L):
        v = pv[L]
        pr = t // 2
        if t % 2 == 0:
            war = [g_mm[pr - 2]] if (pr - 2) in g_mm else wready
            g_h[pr] = p.dma_rr(
                "in_h", 2, pr,
                hT_c[pr % 2][:].rearrange("p (k tt) -> p k tt", tt=256),
                hT_in[:, :, t * 128:(t + 2) * 128].rearrange("k p tt -> p k tt"),
                waits=war)
        t_h = g_h[pr]
        t_at = p.dma_rr("in_a", NL, L, at_t[L][:], ao_in[t * 128:(t + 1) * 128, :],
                     waits=[v["lnout"]] if v["lnout"] else wready)
        t_x = p.dma_rr("in_x", NL, L, x_t[L][:], x_in[t * 128:(t + 1) * 128, :],
                    waits=[v["lnout"]] if v["lnout"] else wready)
        yield

        P = t % 3
        t_mm = None
        for n in range(2):
            for kk in range(32):
                t_mm = p.emit("tensor", lambda e, n=n, kk=kk, P=P, t=t, pr=pr: e.matmul(
                    out=ops[P][:, n * 512:(n + 1) * 512],
                    lhsT=hT_c[pr % 2][:, kk * 256 + (t % 2) * 128:
                                      kk * 256 + (t % 2) * 128 + 128],
                    rhs=w2h[:, kk * D + n * 512:kk * D + (n + 1) * 512],
                    start=(kk == 0), stop=False),
                    waits=([t_h] + wready +
                           ([g_u[t - 3]] if (t - 3) in g_u else [])) if kk == 0 and n == 0 else [])
            t_mm = p.emit("tensor", lambda e, n=n, P=P: e.matmul(
                out=ops[P][:, n * 512:(n + 1) * 512],
                lhsT=c["ones_bf"][0:1, 0:128],
                rhs=w2b_h[:, n * 512:(n + 1) * 512],
                start=False, stop=True), waits=[])
            yield
        v["mm"] = t_mm
        g_mm[pr] = t_mm

        t_u = p.emit("vector", lambda e, L=L, P=P: e.scalar_tensor_tensor(
            out=u[L][:], in0=ops[P][:], scalar=1.0, in1=at_t[L][:],
            op0=ALU.mult, op1=ALU.add, accum_out=s1col[L][:, 0:1]),
            waits=[t_mm, t_at] + ([v["lnsq"]] if v["lnsq"] else []))
        g_u[t] = t_u
        v["u"] = t_u
        yield
        w_ln = [t_u] + ([v["lnout"]] if v["lnout"] else [])
        t_ao, t_sq1 = ln_stats_apply(p, lnb[L], u[L], s1col[L], ao[L], 1e-12,
                                     w_ln, None, "ln2")
        t_m1 = p.emit("vector", lambda e, L=L: e.tensor_mul(
            ao2[L][:], ao[L][:], gb2[:]), waits=[t_ao] + t_bc)
        t_aof = p.emit("vector", lambda e, L=L: e.tensor_add(
            aof[L][:], ao2[L][:], bb2[:]), waits=[t_m1])
        t_u2 = p.emit("vector", lambda e, L=L: e.scalar_tensor_tensor(
            out=u2[L][:], in0=aof[L][:], scalar=1.0, in1=x_t[L][:],
            op0=ALU.mult, op1=ALU.add, accum_out=s1col[L][:, 0:1]),
            waits=[t_aof, t_x])
        yield
        t_x1, t_sq2 = ln_stats_apply(p, lnb[L], u2[L], s1col[L], x1_t[L],
                                     1e-5, [t_u2, p.stok(f"o_x1{L}")], None, "lnb")
        v["lnsq"] = t_sq2
        v["lnout"] = t_x1
        p.dma_rr("o_x1", NL, L, x1_out[t * 128:(t + 1) * 128, :], x1_t[L][:],
                 waits=[t_x1])

    for t0 in range(0, NT, NL):
        interleave_pairs([tile_body(t, t % NL)
                          for t in range(t0, min(t0 + NL, NT))])


def build_B2b():
    p = Prog("B2b")
    c = common_consts(p)
    io = dict(hT=p.din("hT", [32, 128, T], BF16),
              attn_out=p.din("attn_out", [T, D], F32),
              x=p.din("x", [T, D], F32),
              w2_a=p.din("w2_a", [32 * 128 + 1, D], F32),
              g2r=p.din("g2r", [1, D], F32),
              be2r=p.din("be2r", [1, D], F32),
              x1=p.dout("x1", [T, D], F32))
    emit_B2b(p, c, io)
    return p.finalize()



def emit_C(p, c, io):
    x1_in = io["x1"]
    gm_in = io["gmr"]
    bb_in = io["bbr"]
    mu_in = io["mur"]
    sc_in = io["scl"]
    Praw_out = io["Praw"]
    Sloc_out = io["Sloc"]
    E_out = io["E"]
    Zc_out = io["Zc"]

    onescol_f = p.sb("onescol_f", [128, 1], F32)
    gmrow = p.sb("gmrow", [1, D], F32)
    bbrow = p.sb("bbrow", [1, D], F32)
    gb = p.sb("gb", [128, D], F32)
    bb = p.sb("bb", [128, D], F32)
    mustage = p.sb("mustage", [128, NJ * ES], F32)
    musq = p.sb("musq", [128, NJ * ES], F32)
    srow = p.sb("srow", [1, ES], F32)
    srow2 = p.sb("srow2", [1, ES], F32)
    rmub = p.sb("rmub", [128, ES], F32)
    mun_h = p.sb("mun_h", [128, NJ * ES], BF16)
    sccol = p.sb("sccol", [128, 1], F32)
    sclrow = p.sb("sclrow", [1, 1], F32)
    NL = 2
    x1_t = [p.sb(f"x1_t{i}", [128, D], F32) for i in range(NL)]
    s1col = [p.sb(f"s1col{i}", [128, 1], F32) for i in range(NL)]
    lnb = [dict(sqw=p.sb(f"sqw{i}", [128, D], F32),
                mcol=p.sb(f"mcol{i}", [128, 1], F32),
                vcol=p.sb(f"vcol{i}", [128, 1], F32),
                tcol=p.sb(f"tcol{i}", [128, 1], F32)) for i in range(NL)]
    xn = [p.sb(f"xn{i}", [128, D], F32) for i in range(NL)]
    xn2a = [p.sb(f"xn2a{i}", [128, D], F32) for i in range(NL)]
    xn2 = [p.sb(f"xn2{i}", [128, D], F32) for i in range(NL)]
    nrm = [p.sb(f"nrm{i}", [128, 1], F32) for i in range(NL)]
    nrm2 = [p.sb(f"nrm2{i}", [128, 1], F32) for i in range(NL)]
    rn = [p.sb(f"rn{i}", [128, 1], F32) for i in range(NL)]
    xu_bf = [p.sb(f"xu_bf{i}", [128, D], BF16) for i in range(NL)]
    xnh = [p.sb(f"xnh{i}", [128, D], BF16) for i in range(NL)]
    xuT_t = [p.sb(f"xuT_t{i}", [128, D], BF16) for i in range(NL)]
    E_bf = [p.sb(f"E_bf{i}", [128, ES], BF16) for i in range(NL)]
    zc_col = [p.sb(f"zc_col{i}", [128, 1], F32) for i in range(NL)]
    Psb = p.sb("Psb", [128, NJ * ES], F32)
    slrow = p.sb("slrow", [1, ES], F32)

    SL_ps = p.ps("SL_ps", [1, ES], F32)
    Pps_a = p.ps("Pps_a", [128, 1024], F32)
    Pps_b = p.ps("Pps_b", [128, 1024], F32)
    L_ps = p.ps("L_ps", [128, ES], F32)
    p.push_region("pp")
    psA = p.ps("psA", [128, 512], F32)

    t_oc = p.emit("vector", lambda e: e.memset(onescol_f[:], 1.0))
    t_g = p.dma("in", gmrow[:], gm_in[:], waits=c["ready"])
    t_b = p.dma("in", bbrow[:], bb_in[:])
    t_mu = p.dma("in", mustage[:].rearrange("p (j s) -> p j s", s=ES),
                 mu_in[:].rearrange("(j p) s -> p j s", p=128))
    t_sc = p.dma("in", sclrow[:], sc_in[:])
    t_in = p.stok("in")
    tb1 = bcast_row_f32(p, c, gb, gmrow, D, psA, waits=[t_in])
    tb2 = bcast_row_f32(p, c, bb, bbrow, D, psA, waits=[t_in] + tb1)
    t_bc = tb1 + tb2
    t_scm = p.emit("tensor", lambda e: e.matmul(
        out=psA[:, 0:1], lhsT=c["ones_f"][:, :], rhs=sclrow[:, 0:1],
        start=True, stop=True), waits=[t_in] + t_bc)
    t_scc = p.emit("vector", lambda e: e.tensor_copy(sccol[:], psA[:, 0:1]),
                   waits=[t_scm])
    t_msq = p.emit("scalar", lambda e: e.activation(
        out=musq[:], in_=mustage[:], func=AF.Square), waits=[t_in])
    t_s2m = None
    for j in range(NJ):
        t_s2m = p.emit("tensor", lambda e, j=j: e.matmul(
            out=SL_ps[0:1, :], lhsT=onescol_f[:, 0:1],
            rhs=musq[:, j * ES:(j + 1) * ES],
            start=(j == 0), stop=(j == NJ - 1)),
            waits=[t_msq, t_oc, t_scm] if j == 0 else [])
    t_sr = p.emit("vector", lambda e: e.tensor_copy(srow[:], SL_ps[0:1, :]),
                  waits=[t_s2m])
    t_sq = p.emit("scalar", lambda e: e.activation(
        out=srow2[:], in_=srow[:], func=AF.Sqrt), waits=[t_sr])
    t_mx = p.emit("vector", lambda e: e.tensor_scalar_max(
        srow[:], srow2[:], 1e-12), waits=[t_sq])
    t_rc = p.emit("vector", lambda e: e.reciprocal(srow2[:], srow[:]),
                  waits=[t_mx])
    tbm = bcast_row_f32(p, c, rmub, srow2, ES, psA, waits=[t_rc])
    t_mn = []
    for j in range(NJ):
        t_mn.append(p.emit("vector", lambda e, j=j: e.tensor_mul(
            mun_h[:, j * ES:(j + 1) * ES], mustage[:, j * ES:(j + 1) * ES],
            rmub[:]), waits=tbm))
    p.pop_region()
    tps = [p.ps(f"tps{i}", [128, 1024], BF16) for i in range(NL)]

    t_prev_E = [None]
    pv = [dict(sq=None, xuT=None, P=None) for _ in range(NL)]

    def tile_body(t, L):
        v = pv[L]
        t_x1 = p.dma_rr("in_x", NL, L, x1_t[L][:],
                        x1_in[t * 128:(t + 1) * 128, :],
                        waits=[v["sq"]] if v["sq"] else [t_in])
        t_s1 = p.emit("vector", lambda e, L=L: e.reduce_sum(
            out=s1col[L][:, 0:1], in_=x1_t[L][:], axis=AX.X), waits=[t_x1])
        yield
        t_xn, t_sqa = ln_stats_apply(p, lnb[L], x1_t[L], s1col[L], xn[L], 1e-5,
                                     [t_x1, t_s1], None, "moe_ln")
        yield
        t_m1 = p.emit("vector", lambda e, L=L: e.tensor_mul(
            xn2a[L][:], xn[L][:], gb[:]), waits=[t_xn] + t_bc)
        t_xn2 = p.emit("vector", lambda e, L=L: e.tensor_add(
            xn2[L][:], xn2a[L][:], bb[:]), waits=[t_m1])
        yield
        t_sqb = p.emit("scalar", lambda e, L=L: e.activation(
            out=lnb[L]["sqw"][:], in_=xn2[L][:], func=AF.Square,
            accum_out=nrm[L][:, 0:1]), waits=[t_xn2, t_sqa])
        v["sq"] = t_sqb
        yield
        t_nsq = p.emit("scalar", lambda e, L=L: e.activation(
            out=nrm2[L][:, 0:1], in_=nrm[L][:, 0:1], func=AF.Sqrt),
            waits=[t_sqb])
        t_nmx = p.emit("vector", lambda e, L=L: e.tensor_scalar_max(
            nrm[L][:, 0:1], nrm2[L][:, 0:1], 1e-12), waits=[t_nsq])
        t_nrc = p.emit("vector", lambda e, L=L: e.reciprocal(
            nrm2[L][:, 0:1], nrm[L][:, 0:1]), waits=[t_nmx])
        t_rn = p.emit("vector", lambda e, L=L: e.tensor_mul(
            rn[L][:, 0:1], nrm2[L][:, 0:1], sccol[:, 0:1]),
            waits=[t_nrc, t_scc])
        yield
        t_xu = p.emit("vector", lambda e, L=L: e.tensor_scalar(
            out=xu_bf[L][:], in0=xn2[L][:], scalar1=rn[L][:, 0:1],
            scalar2=None, op0=ALU.mult), waits=[t_rn])
        t_xnh = p.emit("scalar", lambda e, L=L: e.activation(
            out=xnh[L][:], in_=xn2[L][:], func=AF.Copy),
            waits=[t_xn2] + ([v["P"]] if v["P"] else []))
        yield

        t_xuT = transpose_tile(p, c, tps[L], xu_bf[L], xuT_t[L][:], [t_xu],
                               [v["xuT"]] if v["xuT"] else [], nblk=NJ)
        v["xuT"] = t_xuT
        yield
        t_lg = None
        for j in range(NJ):
            t_lg = p.emit("tensor", lambda e, j=j, L=L: e.matmul(
                out=L_ps[:, :], lhsT=xuT_t[L][:, j * 128:(j + 1) * 128],
                rhs=mun_h[:, j * ES:(j + 1) * ES],
                start=(j == 0), stop=(j == NJ - 1)),
                waits=([t_xuT] + t_mn +
                       ([t_prev_E[0]] if t_prev_E[0] else [])) if j == 0 else [])
        t_E = p.emit("scalar", lambda e, L=L: e.activation(
            out=E_bf[L][:], in_=L_ps[:], func=AF.Exp,
            accum_out=zc_col[L][:, 0:1]),
            waits=[t_lg, p.stok(f"o_E{L}"), p.stok(f"o_Z{L}")])
        t_prev_E[0] = t_E
        p.dma_rr("o_E", NL, L, E_out[t], E_bf[L][:], waits=[t_E])
        p.dma_rr("o_Z", NL, L, Zc_out[t], zc_col[L][:, 0:1], waits=[t_E])
        yield
        t_P = None
        for jd in range(NJ):
            pps = Pps_a if jd < 4 else Pps_b
            col = (jd % 4) * ES
            t_P = p.emit("tensor", lambda e, jd=jd, pps=pps, col=col, t=t, L=L: e.matmul(
                out=pps[:, col:col + ES],
                lhsT=xnh[L][:, jd * 128:(jd + 1) * 128],
                rhs=E_bf[L][:],
                start=(t == 0 and jd % 2 == 0), stop=(t == NT - 1 and jd % 2 == 1)),
                waits=[t_E, t_xnh] if jd == 0 else [])
        v["P"] = t_P
        t_SL[0] = p.emit("tensor", lambda e, t=t, L=L: e.matmul(
            out=SL_ps[0:1, :], lhsT=c["onescol_bf"][:, 0:1], rhs=E_bf[L][:],
            start=(t == 0), stop=(t == NT - 1)),
            waits=[t_E] + ([t_sr] if t == 0 else []))

    t_SL = [None]
    for t0 in range(0, NT, NL):
        interleave_pairs([tile_body(t, t % NL)
                          for t in range(t0, min(t0 + NL, NT))])
    t_SL = t_SL[0]

    t_Pall = [pv[0]["P"], pv[1]["P"]]
    t_e1 = p.emit("vector", lambda e: e.tensor_copy(
        Psb[:, 0:1024], Pps_a[:]), waits=t_Pall)
    t_e2 = p.emit("vector", lambda e: e.tensor_copy(
        Psb[:, 1024:2048], Pps_b[:]), waits=t_Pall)
    t_e3 = p.emit("vector", lambda e: e.tensor_copy(
        slrow[:], SL_ps[0:1, :]), waits=[t_SL])
    p.dma("o_P", Praw_out[:], Psb[:], waits=[t_e1, t_e2])
    p.dma("o_S", Sloc_out[:], slrow[:], waits=[t_e3])


def build_C():
    p = Prog("C")
    c = common_consts(p)
    io = dict(x1=p.din("x1", [T, D], F32),
              gmr=p.din("gmr", [1, D], F32),
              bbr=p.din("bbr", [1, D], F32),
              mur=p.din("mur", [D, ES], F32),
              scl=p.din("scl", [1, 1], F32),
              Praw=p.dout("Praw", [128, NJ * ES], F32),
              Sloc=p.dout("Sloc", [1, ES], F32),
              E=p.dout("E", [NT, 128, ES], BF16),
              Zc=p.dout("Zc", [NT, 128, 1], F32))
    emit_C(p, c, io)
    return p.finalize()



def emit_D(p, c, io, fused=False, pr_wait=None, edges_out=None):
    x1_in = io["x1"]
    E_in = io["E"]
    Zc_in = io["Zc"]
    Praw_in = io["Praw"]
    Sloc_in = io["Sloc"]
    w1_in = io["w1e"]
    b1c_in = io["b1c"]
    w2_in = io["w2e"]
    x2_out = io["x2"]
    scratch = io["so_scratch"]

    Psb = p.sb("Psb", [128, NJ * ES], F32)
    Srow = p.sb("Srow", [1, ES], F32)
    rSrow = p.sb("rSrow", [1, ES], F32)
    rSb = p.sb("rSb", [128, ES], F32)
    slotin_h = p.sb("slotin_h", [128, NJ * ES], BF16)
    wstage = [p.sb(f"wstage{i}", [128, 2048], F32) for i in range(4)]
    whbuf = p.sb("whbuf", [128, EXP * 2048], BF16)
    b1cols = p.sb("b1cols", [128, EXP * 2], F32)
    b2stage = [p.sb(f"b2stage{i}", [1, D], F32) for i in range(2)]
    b2cur = [p.sb(f"b2cur{i}", [1, D], BF16) for i in range(2)]
    HT_sb = p.sb("HT_sb", [128, EXP * 32], BF16)
    so_stage = [p.sb(f"so_stage{i}", [16, D], BF16) for i in range(2)]
    soT_sb = p.sb("soT_sb", [128, 2 * D], BF16)
    NL = 2
    E_t = [p.sb(f"E_t{i}", [128, ES], BF16) for i in range(NL)]
    zc_t = [p.sb(f"zc_t{i}", [128, 1], F32) for i in range(NL)]
    rZ = [p.sb(f"rZ{i}", [128, 1], F32) for i in range(NL)]
    comb = [p.sb(f"comb{i}", [128, ES], BF16) for i in range(NL)]
    combT = [p.sb(f"combT{i}", [128, ES], BF16) for i in range(NL)]
    x1_t = [p.sb(f"x1_t{i}", [128, D], F32) for i in range(NL)]
    u = [p.sb(f"u{i}", [128, D], F32) for i in range(NL)]
    s1col = [p.sb(f"s1col{i}", [128, 1], F32) for i in range(NL)]
    lnb = [dict(sqw=p.sb(f"sqw{i}", [128, D], F32),
                mcol=p.sb(f"mcol{i}", [128, 1], F32),
                vcol=p.sb(f"vcol{i}", [128, 1], F32),
                tcol=p.sb(f"tcol{i}", [128, 1], F32)) for i in range(NL)]
    x2_t = [p.sb(f"x2_t{i}", [128, D], F32) for i in range(NL)]

    p.push_region("pp")
    psA = p.ps("psA", [128, 512], F32)
    HTps = [p.ps(f"HTps{i}", [128, 32], F32) for i in range(2)]
    SOps = [p.ps(f"SOps{i}", [16, D], F32) for i in range(2)]

    if fused:
        tprev = p.dma("in_P", Psb[:], Praw_in[:], waits=[pr_wait],
                      queue="gpsimd")
        tprevS = p.dma("in_S", Srow[:], Sloc_in[:], waits=[pr_wait],
                       queue="gpsimd")
    else:
        tprev = None
        for cc in range(NCORES):
            tprev = p.dma("in_P", Psb[:], Praw_in[cc],
                          waits=[tprev] if tprev else c["ready"], queue="gpsimd",
                          accum_op=(ALU.add if cc else ALU.bypass))
        tprevS = None
        for cc in range(NCORES):
            tprevS = p.dma("in_S", Srow[:], Sloc_in[cc],
                           waits=[tprevS] if tprevS else c["ready"], queue="gpsimd",
                           accum_op=(ALU.add if cc else ALU.bypass))
    t_rS = p.emit("vector", lambda e: e.reciprocal(rSrow[:], Srow[:]),
                  waits=[tprevS])
    tbs = bcast_row_f32(p, c, rSb, rSrow, ES, psA, waits=[t_rS])
    t_si = []
    for jd in range(NJ):
        t_si.append(p.emit("vector", lambda e, jd=jd: e.tensor_mul(
            slotin_h[:, jd * ES:(jd + 1) * ES], Psb[:, jd * ES:(jd + 1) * ES],
            rSb[:]), waits=tbs + [tprev]))

    t_b1 = p.dma("in", b1cols[:], b1c_in[:], waits=c["ready"])
    w1ready = [t_b1]
    wcast = {}
    for ee in range(EXP):
        tw = p.dma_rr("in_w", 4, ee,
                      wstage[ee % 4][:].rearrange("p (k h) -> p k h", h=MOE_H),
                      w1_in[ee].rearrange("(k p) h -> p k h", p=128),
                      waits=wcast.get(ee - 4, [t_b1]))
        cs = [p.emit("scalar", lambda e, ee=ee: e.activation(
            out=whbuf[:, ee * 2048:(ee + 1) * 2048], in_=wstage[ee % 4][:],
            func=AF.Copy), waits=[tw])]
        wcast[ee] = cs
        w1ready += cs

    t_H_ev = []
    hevs = {}
    t_mm = None
    for ee in range(EXP):
        for kk in range(2):
            for jd in range(NJ):
                t_mm = p.emit("tensor", lambda e, ee=ee, kk=kk, jd=jd: e.matmul(
                    out=HTps[ee % 2][:, kk * 16:(kk + 1) * 16],
                    lhsT=whbuf[:, ee * 2048 + jd * MOE_H + kk * 128:
                               ee * 2048 + jd * MOE_H + (kk + 1) * 128],
                    rhs=slotin_h[:, jd * ES + ee * SLOT:jd * ES + (ee + 1) * SLOT],
                    start=(kk == 0 and jd == 0), stop=(kk == 1 and jd == NJ - 1)),
                    waits=(t_si + w1ready + hevs.get(ee - 2, [])) if kk == 0 and jd == 0 else [])
        evs = []
        for kk in range(2):
            evs.append(p.emit("scalar", lambda e, ee=ee, kk=kk: e.activation(
                out=HT_sb[:, ee * 32 + kk * 16:ee * 32 + (kk + 1) * 16],
                in_=HTps[ee % 2][:, kk * 16:(kk + 1) * 16], func=AF.Gelu,
                bias=b1cols[:, ee * 2 + kk:ee * 2 + kk + 1]), waits=[t_mm]))
        hevs[ee] = evs
        t_H_ev += evs
    t_w1_done = t_mm
    prev_Hev = hevs[EXP - 1] + hevs[EXP - 2]

    w2ready = []
    wcast2 = {}
    for ee in range(EXP):
        tw = p.dma_rr("in_w", 4, ee,
                      wstage[ee % 4][:].rearrange("p (k d) -> p k d", d=D),
                      w2_in[ee, 0:256, :].rearrange("(k p) d -> p k d", p=128),
                      waits=wcast2.get(ee - 4, [t_w1_done] + prev_Hev))
        cs = [p.emit("scalar", lambda e, ee=ee: e.activation(
            out=whbuf[:, ee * 2048:(ee + 1) * 2048], in_=wstage[ee % 4][:],
            func=AF.Copy), waits=[tw] + ([t_w1_done] if ee < 4 else []))]
        wcast2[ee] = cs
        w2ready += cs

    soev = {}
    b2cast = {}
    t_so_st = []
    for ee in range(EXP):
        t_bd = p.dma_rr("in_b2", 2, ee, b2stage[ee % 2][:],
                        w2_in[ee, 256:257, :],
                        waits=b2cast.get(ee - 2, [w2ready[0]]))
        t_bc2 = p.emit("scalar", lambda e, ee=ee: e.activation(
            out=b2cur[ee % 2][:], in_=b2stage[ee % 2][:], func=AF.Copy),
            waits=[t_bd] + soev.get(ee - 2, []))
        b2cast[ee] = [t_bc2]
        t_mm = None
        for n in range(2):
            for kk in range(2):
                t_mm = p.emit("tensor", lambda e, ee=ee, n=n, kk=kk: e.matmul(
                    out=SOps[ee % 2][:, n * 512:(n + 1) * 512],
                    lhsT=HT_sb[:, ee * 32 + kk * 16:ee * 32 + (kk + 1) * 16],
                    rhs=whbuf[:, ee * 2048 + kk * D + n * 512:
                              ee * 2048 + kk * D + (n + 1) * 512],
                    start=(kk == 0), stop=False),
                    waits=(w2ready + [t_bc2] +
                           soev.get(ee - 2, [])) if kk == 0 and n == 0 else [])
            t_mm = p.emit("tensor", lambda e, ee=ee, n=n: e.matmul(
                out=SOps[ee % 2][:, n * 512:(n + 1) * 512],
                lhsT=c["ones_bf"][0:1, 0:SLOT],
                rhs=b2cur[ee % 2][0:1, n * 512:(n + 1) * 512],
                start=False, stop=True), waits=[])
        t_ev = p.emit("vector", lambda e, ee=ee: e.tensor_copy(
            so_stage[ee % 2][0:16, :], SOps[ee % 2][:, :]),
            waits=[t_mm, p.stok(f"so1{ee % 2}")])
        soev[ee] = [t_ev]
        t_st = p.dma_rr("so1", 2, ee, scratch[ee * 16:(ee + 1) * 16, :],
                        so_stage[ee % 2][0:16, :], waits=[t_ev])
        t_so_st.append(t_st)
    t_bi = p.dma("so2", soT_sb[:].rearrange("p (r d) -> p r d", d=D),
                 scratch.rearrange("(r p) d -> p r d", p=128), waits=t_so_st)

    p.pop_region()
    Ops = [p.ps(f"Ops{i}", [128, 1024], F32) for i in range(NL)]
    tpsL = [p.ps(f"tpsL{i}", [128, 256], BF16) for i in range(NL)]

    pv = [dict(mm=None, u=None, lnsq=None, lnout=None, comb=None, combT=None)
          for _ in range(NL)]

    def tile_body(t, L):
        v = pv[L]
        t_E = p.dma_rr("in_E", NL, L, E_t[L][:], E_in[t],
                       waits=[v["comb"]] if v["comb"] else w2ready)
        t_Z = p.dma_rr("in_Z", NL, L, zc_t[L][:], Zc_in[t],
                       waits=[v["comb"]] if v["comb"] else w2ready)
        t_x1 = p.dma_rr("in_x", NL, L, x1_t[L][:],
                        x1_in[t * 128:(t + 1) * 128, :],
                        waits=[v["u"]] if v["u"] else w2ready)
        yield
        t_rz = p.emit("vector", lambda e, L=L: e.reciprocal(
            rZ[L][:], zc_t[L][:]), waits=[t_Z])
        t_cb = p.emit("vector", lambda e, L=L: e.tensor_scalar(
            out=comb[L][:], in0=E_t[L][:], scalar1=rZ[L][:, 0:1], scalar2=None,
            op0=ALU.mult), waits=[t_rz, t_E])
        v["comb"] = t_cb
        yield
        t_tp2 = None
        for kk in range(2):
            t_tp2 = p.emit("tensor", lambda e, kk=kk, L=L: e.transpose(
                out=tpsL[L][:, kk * 128:(kk + 1) * 128],
                in_=comb[L][:, kk * 128:(kk + 1) * 128],
                identity=c["ident"][:]),
                waits=([t_cb] + ([v["combT"]] if v["combT"] else [])) if kk == 0 else [])
        t_cT = p.emit("vector", lambda e, L=L: e.tensor_copy(
            combT[L][:], tpsL[L][:]),
            waits=[t_tp2])
        v["combT"] = t_cT
        yield
        t_mm = None
        for n in range(2):
            for kk in range(2):
                t_mm = p.emit("tensor", lambda e, n=n, kk=kk, L=L: e.matmul(
                    out=Ops[L][:, n * 512:(n + 1) * 512],
                    lhsT=combT[L][:, kk * 128:(kk + 1) * 128],
                    rhs=soT_sb[:, kk * D + n * 512:kk * D + (n + 1) * 512],
                    start=(kk == 0), stop=(kk == 1)),
                    waits=([t_cT, t_bi] + ([v["u"]] if v["u"] else [])) if kk == 0 and n == 0 else [])
        v["mm"] = t_mm
        yield
        t_u = p.emit("vector", lambda e, L=L: e.scalar_tensor_tensor(
            out=u[L][:], in0=Ops[L][:], scalar=1.0, in1=x1_t[L][:],
            op0=ALU.mult, op1=ALU.add, accum_out=s1col[L][:, 0:1]),
            waits=[t_mm, t_x1] + ([v["lnsq"]] if v["lnsq"] else []))
        v["u"] = t_u
        yield
        w_ln = [t_u] + ([v["lnout"]] if v["lnout"] else [])
        t_x2, t_sq1 = ln_stats_apply(p, lnb[L], u[L], s1col[L], x2_t[L], 1e-5,
                                     w_ln + [p.stok(f"o_x2{L}"),
                                             p.stok("o_edge")], None, "lnx2")
        v["lnsq"] = t_sq1
        v["lnout"] = t_x2
        t_st2 = p.dma_rr("o_x2", NL, L, x2_out[t * 128:(t + 1) * 128, :],
                         x2_t[L][:], waits=[t_x2])
        if edges_out is not None and t == 0:
            p.dma("o_edge", edges_out[0:HALO, :], x2_t[L][0:HALO, :],
                  waits=[t_x2])
        if edges_out is not None and t == NT - 1:
            p.dma("o_edge", edges_out[HALO:2 * HALO, :],
                  x2_t[L][128 - HALO:128, :], waits=[t_x2])

    for t0 in range(0, NT, NL):
        interleave_pairs([tile_body(t, t % NL)
                          for t in range(t0, min(t0 + NL, NT))])


def build_D():
    p = Prog("D")
    c = common_consts(p)
    io = dict(x1=p.din("x1", [T, D], F32),
              E=p.din("E", [NT, 128, ES], BF16),
              Zc=p.din("Zc", [NT, 128, 1], F32),
              Praw=p.din("Praw", [NCORES, 128, NJ * ES], F32),
              Sloc=p.din("Sloc", [NCORES, 1, ES], F32),
              w1e=p.din("w1e", [EXP, D, MOE_H], F32),
              b1c=p.din("b1c", [128, EXP * 2], F32),
              w2e=p.din("w2e", [EXP, MOE_H + 1, D], F32),
              x2=p.dout("x2", [T, D], F32),
              so_scratch=p.dtmp("so_scratch", [ES, D], BF16))
    emit_D(p, c, io)
    return p.finalize()



def emit_E(p, c, io, xe_wait=()):
    TE = 2176
    xe_in = io["xe"]
    mA_in = io["maskA"]
    mB_in = io["maskB"]
    wq_in = io["wqT"]
    wk_in = io["wkT"]
    wv_in = io["wvT"]
    wo_in = io["woT_a"]
    x3_out = io["x3"]

    wqh = p.sb("wqh", [128, NJ * D], BF16)
    wkvh = p.sb("wkvh", [128, NJ * D], BF16)
    wob_f = p.sb("wob_f", [1, D], F32)
    wob_h = p.sb("wob_h", [1, D], BF16)
    xeT = p.sb("xeT", [128, NJ * TE], BF16)
    kT = p.sb("kT", [128, NJ * TE], BF16)
    v_sb = p.sb("v_sb", [128, NTE * D], BF16)
    NL = 2
    qz = [[p.sb(f"qz{i}_{L}", [128, NJ * 128], BF16) for i in range(2)]
          for L in range(NL)]
    EA = p.sb("EA", [128, 512], BF16)
    EB = p.sb("EB", [64, 512], BF16)
    rz_sb = [p.sb(f"rz_sb{L}", [128, NH], F32) for L in range(NL)]
    mA = p.sb("mA", [128, 128], F32)
    mB = p.sb("mB", [64, 128], F32)
    actx0 = p.sb("actx", [128, D], BF16)
    actx = [actx0, actx0]
    ctxT = [p.sb(f"ctxT{L}", [128, D], BF16) for L in range(NL)]
    xr_t0 = p.sb("xr_t", [128, D], F32)
    xr_t = [xr_t0, xr_t0]
    u = [p.sb(f"uE{L}", [128, D], F32) for L in range(NL)]
    s1col = [p.sb(f"s1col{L}", [128, 1], F32) for L in range(NL)]
    sqw0 = p.sb("sqwE", [128, D], F32)
    lnb = [dict(sqw=sqw0,
                mcol=p.sb(f"mcolE{L}", [128, 1], F32),
                vcol=p.sb(f"vcolE{L}", [128, 1], F32),
                tcol=p.sb(f"tcolE{L}", [128, 1], F32)) for L in range(NL)]
    x3_t0 = p.sb("x3_t", [128, D], F32)
    x3_t = [x3_t0, x3_t0]

    tps = p.ps("tps", [128, 1024], BF16)
    qps = p.ps("qps", [128, 1024], F32)

    eighth, t_ei = p.const_col(0.125)
    t_mA = p.dma("in", mA[:], mA_in[:], waits=c["ready"])
    t_mB = p.dma("in", mB[:], mB_in[:], waits=c["ready"])
    mA4 = p.sb("mA4", [128, 512], BF16)
    mB4 = p.sb("mB4", [64, 512], BF16)
    t_m4 = []
    for r in range(4):
        t_m4.append(p.emit("scalar", lambda e, r=r: e.activation(
            out=mA4[:, r * 128:(r + 1) * 128], in_=mA[:], func=AF.Copy),
            waits=[t_mA]))
        t_m4.append(p.emit("scalar", lambda e, r=r: e.activation(
            out=mB4[0:64, r * 128:(r + 1) * 128], in_=mB[0:64, :],
            func=AF.Copy), waits=[t_mB]))

    p.push_region("w12")
    wstage = p.sb("wstage", [128, D], F32)
    xf_t = [p.sb(f"xf_t{i}", [128, D], F32) for i in range(2)]
    xh_t = [p.sb(f"xh_t{i}", [128, D], BF16) for i in range(2)]
    tpsB = p.ps("tpsB", [128, 1024], BF16)
    tps1 = [tps, tpsB]
    e_cast = {}
    e_ev = {}
    for te in range(NTE):
        LL = te % 2
        t_x = p.dma_rr("in_x", 2, LL, xf_t[LL][:],
                       xe_in[te * 128:(te + 1) * 128, :],
                       waits=([e_cast[te - 2]] if te >= 2 else c["ready"]) +
                             list(xe_wait))
        t_cast = p.emit("scalar", lambda e, LL=LL: e.activation(
            out=xh_t[LL][:], in_=xf_t[LL][:], func=AF.Copy),
            waits=[t_x] + ([e_ev[te - 2]] if te >= 2 else []))
        e_cast[te] = t_cast
        xeT_view = xeT[:].rearrange("p (j t) -> p j t", t=TE)[:, :, te * 128:(te + 1) * 128]
        e_ev[te] = transpose_tile(p, c, tps1[LL], xh_t[LL], xeT_view, [t_cast],
                                  [e_ev[te - 2]] if te >= 2 else [], nblk=NJ)
    t_xeT = p.tok("vector")

    def load_w(name_in, dest, waits):
        toks = []
        prev = list(waits)
        for ch in range(NJ):
            tw = p.dma("in", wstage[:],
                       name_in[ch * 128:(ch + 1) * 128, :], waits=prev)
            cs = [p.emit("scalar", lambda e, ch=ch, dest=dest: e.activation(
                out=dest[:, ch * D:(ch + 1) * D],
                in_=wstage[:], func=AF.Copy),
                waits=[p.stok("in")])]
            prev = cs
            toks += cs
        return toks

    t_wk = load_w(wk_in, wkvh, c["ready"])
    t_kml = None
    kev = []
    nch = [(i, min(512, TE - 512 * i)) for i in range((TE + 511) // 512)]
    ib = 0
    for m in range(NJ):
        for (ci, nn) in nch:
            b = ib % 2
            ib += 1
            t_kml = None
            for kk in range(NJ):
                t_kml = p.emit("tensor", lambda e, m=m, ci=ci, nn=nn, kk=kk, b=b: e.matmul(
                    out=qps[:, b * 512:b * 512 + nn],
                    lhsT=wkvh[:, kk * D + m * 128:kk * D + (m + 1) * 128],
                    rhs=xeT[:, kk * TE + ci * 512:kk * TE + ci * 512 + nn],
                    start=(kk == 0), stop=(kk == NJ - 1)),
                    waits=([t_xeT] + t_wk + kev[-2:-1]) if kk == 0 else [])
            kev.append(p.emit("vector", lambda e, m=m, ci=ci, nn=nn, b=b: e.tensor_copy(
                kT[:, m * TE + ci * 512:m * TE + ci * 512 + nn],
                qps[:, b * 512:b * 512 + nn]),
                waits=[t_kml]))
    t_wv = load_w(wv_in, wkvh, kev)
    vev = []
    for te in range(NTE):
        for n in range(2):
            t_vml = None
            for kk in range(NJ):
                t_vml = p.emit("tensor", lambda e, te=te, n=n, kk=kk: e.matmul(
                    out=qps[:, n * 512:(n + 1) * 512],
                    lhsT=xeT[:, kk * TE + te * 128:kk * TE + (te + 1) * 128],
                    rhs=wkvh[:, kk * D + n * 512:kk * D + (n + 1) * 512],
                    start=(kk == 0), stop=(kk == NJ - 1)),
                    waits=(t_wv + vev[-2:-1]) if kk == 0 else [])
            vev.append(p.emit("vector", lambda e, te=te, n=n: e.tensor_copy(
                v_sb[:, te * D + n * 512:te * D + (n + 1) * 512],
                qps[:, n * 512:(n + 1) * 512]), waits=[t_vml]))
    t_wq = load_w(wq_in, wqh, vev)
    woh = wkvh
    t_wo = load_w(wo_in, woh, t_wq)
    t_wob = p.dma("in", wob_f[:], wo_in[D:D + 1, :], waits=t_wo)
    t_wo = t_wo + [p.emit("scalar", lambda e: e.activation(
        out=wob_h[:], in_=wob_f[:], func=AF.Copy), waits=[p.stok("in")])]
    t_z0 = p.emit("vector", lambda e: e.memset(qz[0][0][64:128, :], 0.0))
    t_z1 = p.emit("vector", lambda e: e.memset(qz[0][1][0:64, :], 0.0))
    t_z2 = p.emit("vector", lambda e: e.memset(qz[1][0][64:128, :], 0.0))
    t_z3 = p.emit("vector", lambda e: e.memset(qz[1][1][0:64, :], 0.0))
    t_z0 = [t_z0, t_z1, t_z2, t_z3]
    p.pop_region()
    SA = p.ps("SA", [128, 512], F32)
    SB = p.ps("SB", [64, 512], F32)
    Ups = p.ps("Ups", [128, 1024], F32)
    Zps = p.ps("Zps", [128, 16], F32)

    g = dict(q0=None, q1=None, ea=[], u=None, ctxT=None, rz=None)
    pv = [dict(Slast=None, lnsq=None, lnout=None, u=None) for _ in range(NL)]

    def tile_body(t, L):
        v = pv[L]
        t_qml = None
        for jm in range(NJ):
            for kk in range(NJ):
                t_qml = p.emit("tensor", lambda e, jm=jm, kk=kk, t=t: e.matmul(
                    out=qps[:, jm * 128:(jm + 1) * 128],
                    lhsT=wqh[:, kk * D + jm * 128:kk * D + jm * 128 + 128],
                    rhs=xeT[:, kk * TE + 32 + t * 128:kk * TE + 32 + (t + 1) * 128],
                    start=(jm % 4 == 0 and kk == 0),
                    stop=(jm % 4 == 3 and kk == NJ - 1)),
                    waits=(t_wq + [x for x in (g["q0"], g["q1"], g["u"]) if x]) if jm == 0 and kk == 0 else [])
        t_q0 = p.emit("vector", lambda e, L=L: e.tensor_copy(
            qz[L][0][0:64, :], qps[0:64, :]),
            waits=[t_qml] + t_z0 + ([v["Slast"]] if v["Slast"] else []))
        t_q1 = p.emit("vector", lambda e, L=L: e.tensor_copy(
            qz[L][1][64:128, :], qps[64:128, :]), waits=[t_qml])
        g["q0"], g["q1"] = t_q0, t_q1
        yield

        t_U_last = None
        t_Z_last = None
        ea_war = list(g["ea"])
        for gg in range(4):
            t_SA = None
            t_SB = None
            for i in range(4):
                h = 4 * gg + i
                jh = h // 2
                t_SA = p.emit("tensor", lambda e, h=h, jh=jh, i=i, t=t, L=L: e.matmul(
                    out=SA[:, i * 128:(i + 1) * 128],
                    lhsT=kT[:, jh * TE + t * 128:jh * TE + (t + 1) * 128],
                    rhs=qz[L][h % 2][:, jh * 128:(jh + 1) * 128],
                    start=(i == 0), stop=False),
                    waits=([t_q0, t_q1] + kev + ea_war) if i == 0 else [])
            t_SA = p.emit("tensor", lambda e: e.matmul(
                out=SA[:, 0:512], lhsT=c["ident"][:], rhs=mA4[:, 0:512],
                start=False, stop=True), waits=t_m4 if t == 0 and gg == 0 else [])
            for i in range(4):
                h = 4 * gg + i
                jh = h // 2
                t_SB = p.emit("tensor", lambda e, h=h, jh=jh, i=i, t=t, L=L: e.matmul(
                    out=SB[0:64, i * 128:(i + 1) * 128],
                    lhsT=kT[:, jh * TE + (t + 1) * 128:jh * TE + (t + 1) * 128 + 64],
                    rhs=qz[L][h % 2][:, jh * 128:(jh + 1) * 128],
                    start=(i == 0), stop=False),
                    waits=[] if i > 0 else ea_war)
            t_SB = p.emit("tensor", lambda e: e.matmul(
                out=SB[0:64, 0:512], lhsT=c["ident"][0:64, 0:64],
                rhs=mB4[0:64, 0:512], start=False, stop=True), waits=[])
            t_EA = p.emit("scalar", lambda e: e.activation(
                out=EA[:], in_=SA[:], func=AF.Exp, scale=eighth[:, 0:1]),
                waits=[t_SA, t_ei] + ea_war)
            t_EB = p.emit("scalar", lambda e: e.activation(
                out=EB[0:64, :], in_=SB[0:64, :], func=AF.Exp,
                scale=eighth[0:64, 0:1]), waits=[t_SB])
            ea_war = []
            for i in range(4):
                h = 4 * gg + i
                wU = []
                if i == 0:
                    wU = [t_EA, t_EB] + vev + ([g["actx"]] if g.get("actx") else [])
                t_U_last = p.emit("tensor", lambda e, h=h, i=i, t=t: e.matmul(
                    out=Ups[:, h * 64:(h + 1) * 64],
                    lhsT=EA[:, i * 128:(i + 1) * 128],
                    rhs=v_sb[:, t * D + h * 64:t * D + (h + 1) * 64],
                    start=(h % 8 == 0), stop=False), waits=wU)
                t_U_last = p.emit("tensor", lambda e, h=h, i=i, t=t: e.matmul(
                    out=Ups[:, h * 64:(h + 1) * 64],
                    lhsT=EB[0:64, i * 128:(i + 1) * 128],
                    rhs=v_sb[0:64, (t + 1) * D + h * 64:(t + 1) * D + (h + 1) * 64],
                    start=False, stop=(h % 8 == 7)), waits=[])
                t_Z_last = p.emit("tensor", lambda e, h=h, i=i, t=t: e.matmul(
                    out=Zps[:, h:h + 1],
                    lhsT=EA[:, i * 128:(i + 1) * 128],
                    rhs=c["onescol_bf"][:, 0:1],
                    start=(h == 0), stop=False),
                    waits=([g["rz"]] if h == 0 and g["rz"] else []))
                t_Z_last = p.emit("tensor", lambda e, h=h, i=i, t=t: e.matmul(
                    out=Zps[:, h:h + 1],
                    lhsT=EB[0:64, i * 128:(i + 1) * 128],
                    rhs=c["onescol_bf"][0:64, 0:1],
                    start=False, stop=(h == 15)), waits=[])
            ea_war = [t_U_last, t_Z_last]
        v["Slast"] = t_SB
        g["ea"] = ea_war
        t_rz = p.emit("vector", lambda e, L=L: e.reciprocal(
            rz_sb[L][:, 0:NH], Zps[:, 0:NH]), waits=[t_Z_last])
        g["rz"] = t_rz
        t_ac = p.emit("vector", lambda e, L=L: e.scalar_tensor_tensor(
            out=actx[L][:], in0=Ups[:], scalar=1.0,
            in1=rz_sb[L][:, 0:NH].unsqueeze(2).broadcast_to([128, NH, HD]),
            op0=ALU.mult, op1=ALU.mult), waits=[t_U_last, t_rz])
        g["actx"] = t_ac
        yield
        t_cT = transpose_tile(p, c, tps, actx[L], ctxT[L][:], [t_ac],
                              [g["ctxT"]] if g["ctxT"] else [t_xeT],
                              nblk=NJ)
        g["ctxT"] = t_cT
        yield
        t_omm = None
        for n in range(2):
            for kk in range(NJ):
                t_omm = p.emit("tensor", lambda e, n=n, kk=kk, L=L: e.matmul(
                    out=qps[:, n * 512:(n + 1) * 512],
                    lhsT=ctxT[L][:, kk * 128:(kk + 1) * 128],
                    rhs=woh[:, kk * D + n * 512:kk * D + (n + 1) * 512],
                    start=(kk == 0), stop=False),
                    waits=([t_cT] + t_wo +
                           [x for x in (g["q0"], g["q1"], g["u"]) if x]) if kk == 0 and n == 0 else [])
            t_omm = p.emit("tensor", lambda e, n=n: e.matmul(
                out=qps[:, n * 512:(n + 1) * 512],
                lhsT=c["ones_bf"][0:1, 0:128],
                rhs=wob_h[:, n * 512:(n + 1) * 512],
                start=False, stop=True), waits=[])
        t_xr = p.dma_rr("in_r", NL, L, xr_t[L][:],
                        xe_in[32 + t * 128:32 + (t + 1) * 128, :],
                        waits=[g["u"]] if g["u"] else t_wo)
        t_u = p.emit("vector", lambda e, L=L: e.scalar_tensor_tensor(
            out=u[L][:], in0=qps[:], scalar=1.0, in1=xr_t[L][:],
            op0=ALU.mult, op1=ALU.add, accum_out=s1col[L][:, 0:1]),
            waits=[t_omm, t_xr] + [x for x in (g.get("lnsq"),) if x])
        v["u"] = t_u
        g["u"] = t_u
        yield
        w_ln = [t_u] + ([v["lnout"]] if v["lnout"] else [])
        t_x3, t_sq1 = ln_stats_apply(p, lnb[L], u[L], s1col[L], x3_t[L], 1e-5,
                                     w_ln + [p.stok("o_x3")], None, "lnx3")
        v["lnsq"] = t_sq1
        g["lnsq"] = t_sq1
        v["lnout"] = t_x3
        p.dma("o_x3", x3_out[t * 128:(t + 1) * 128, :], x3_t[L][:],
              waits=[t_x3])

    for t0 in range(0, NT, NL):
        interleave_pairs([tile_body(t, t % NL)
                          for t in range(t0, min(t0 + NL, NT))])


def build_E():
    p = Prog("E")
    c = common_consts(p)
    io = dict(xe=p.din("xe", [2176, D], F32),
              maskA=p.din("maskA", [128, 128], F32),
              maskB=p.din("maskB", [64, 128], F32),
              wqT=p.din("wqT", [D, D], F32),
              wkT=p.din("wkT", [D, D], F32),
              wvT=p.din("wvT", [D, D], F32),
              woT_a=p.din("woT_a", [D + 1, D], F32),
              x3=p.dout("x3", [T, D], F32))
    emit_E(p, c, io)
    return p.finalize()



def emit_F1(p, c, io):
    x3_in = io["x3"]
    w1T_in = io["w1T"]
    b1c_in = io["b1c"]
    hT_out = io["hT"]

    absw = p.sb("absw", [128, FF], F32)
    onescol_f = p.sb("onescol_f1", [128, 1], F32)
    alpha1s = p.sb("alpha1s", [128, 32], F32)
    b1c = p.sb("b1c_sb", [128, 32], F32)
    wstage = [p.sb(f"wstage{i}", [128, FF], F32) for i in range(2)]
    sw1T = p.sb("sw1T", [128, NJ * FF], BF16)
    xf_t = [p.sb(f"xf_t{i}", [128, D], F32) for i in range(2)]
    xh_t = [p.sb(f"xh_t{i}", [128, D], BF16) for i in range(2)]
    x3T = p.sb("x3T", [128, NJ * T], BF16)
    hbig = [p.sb(f"hbig{i}", [128, 8 * 512], BF16) for i in range(2)]

    tps = [p.ps(f"tpsF{i}", [128, 1024], BF16) for i in range(2)]
    hps = [p.ps(f"hps{i}", [128, 512], F32) for i in range(2)]
    aps = p.ps("aps1", [128, 32], F32)

    t_oc = p.emit("vector", lambda e: e.memset(onescol_f[:], 1.0))
    t_b1 = p.dma("in", b1c[:], b1c_in[:], waits=c["ready"])

    t_sg = []
    prevs = {}
    t_amm = None
    for ch in range(NJ):
        t_w = p.dma_rr("in_s", 2, ch, wstage[ch % 2][:],
                       w1T_in[ch * 128:(ch + 1) * 128, :],
                       waits=prevs.get(ch - 2, c["ready"][:1]))
        cs = p.emit("scalar", lambda e, ch=ch: e.activation(
            out=sw1T[:, ch * FF:(ch + 1) * FF], in_=wstage[ch % 2][:],
            func=AF.Sign), waits=[t_w])
        t_av = p.emit("vector", lambda e, ch=ch: e.scalar_tensor_tensor(
            out=absw[:], in0=wstage[ch % 2][:], scalar=-1.0,
            in1=wstage[ch % 2][:], op0=ALU.mult, op1=ALU.max),
            waits=[t_w] + ([t_amm] if t_amm else []))
        for m in range(32):
            t_amm = p.emit("tensor", lambda e, ch=ch, m=m: e.matmul(
                out=aps[:, m:m + 1],
                lhsT=absw[:, m * 128:(m + 1) * 128],
                rhs=onescol_f[:, 0:1],
                start=(ch == 0 and m == 0), stop=(ch == NJ - 1 and m == 31)),
                waits=[t_av, t_oc] if m == 0 else [])
        prevs[ch] = [cs, t_av]
        t_sg.append(cs)
    t_al = p.emit("vector", lambda e: e.tensor_scalar_mul(
        alpha1s[:], aps[:], 1.0 / D), waits=[t_amm])
    t_ab = t_al

    xcast = {}
    xev = {}
    for t in range(NT):
        L = t % 2
        t_x = p.dma_rr("in_x", 2, L, xf_t[L][:],
                       x3_in[t * 128:(t + 1) * 128, :],
                       waits=[xcast[t - 2]] if t >= 2 else c["ready"])
        t_cast = p.emit("scalar", lambda e, L=L: e.activation(
            out=xh_t[L][:], in_=xf_t[L][:], func=AF.Copy),
            waits=[t_x] + ([xev[t - 2]] if t >= 2 else []))
        xcast[t] = t_cast
        xT_view = x3T[:].rearrange("p (j t) -> p j t", t=T)[:, :, t * 128:(t + 1) * 128]
        xev[t] = transpose_tile(p, c, tps[L], xh_t[L], xT_view, [t_cast],
                                [xev[t - 2]] if t >= 2 else [], nblk=NJ)

    prev_ev = {}
    prev_st = {}
    i = 0
    for cch in range(4):
        for m in range(32):
            par = i % 2
            grp = (i // 8) % 2
            t_mm = None
            for kk in range(NJ):
                t_mm = p.emit("tensor", lambda e, cch=cch, m=m, kk=kk, par=par: e.matmul(
                    out=hps[par][:],
                    lhsT=sw1T[:, kk * FF + m * 128:kk * FF + (m + 1) * 128],
                    rhs=x3T[:, kk * T + cch * 512:kk * T + (cch + 1) * 512],
                    start=(kk == 0), stop=(kk == NJ - 1)),
                    waits=(t_sg + [xev[cch * 4 + i] for i in range(4)] +
                           list(prev_ev.get(par, []))) if kk == 0 else [])
            t_ev = p.emit("scalar", lambda e, m=m, par=par, grp=grp: e.activation(
                out=hbig[grp][:, (m % 8) * 512:(m % 8 + 1) * 512],
                in_=hps[par][:], func=AF.Gelu,
                scale=alpha1s[:, m:m + 1], bias=b1c[:, m:m + 1]),
                waits=[t_mm, t_al, t_b1] +
                      (list(prev_st.get(grp, [])) if m % 8 < 2 else []))
            prev_ev[par] = [t_ev]
            if m % 8 == 7:
                t_st = p.dma_rr("o_h", 2, grp,
                                hT_out[m - 7:m + 1, :, cch * 512:(cch + 1) * 512]
                                .rearrange("m p tt -> p m tt"),
                                hbig[grp][:].rearrange("p (m tt) -> p m tt", tt=512),
                                waits=[t_ev])
                prev_st[grp] = [t_st]
            i += 1


def build_F1():
    p = Prog("F1")
    c = common_consts(p)
    io = dict(x3=p.din("x3", [T, D], F32),
              w1T=p.din("w1T", [D, FF], F32),
              b1c=p.din("b1c", [128, 32], F32),
              hT=p.dout("hT", [32, 128, T], BF16))
    emit_F1(p, c, io)
    return p.finalize()



def emit_F2(p, c, io):
    x3_in = io["x3"]
    hT_in = io["hT"]
    w2T_in = io["w2T"]
    b2c_in = io["b2c"]
    y_out = io["y"]

    absw = p.sb("absw", [128, FF], F32)
    onescol_f = p.sb("onescol_f2", [128, 1], F32)
    alpha2s = p.sb("alpha2s", [128, NJ], F32)
    b2c = p.sb("b2c_sb", [128, NJ], F32)
    wstage = [p.sb(f"wstage{i}", [128, D], F32) for i in range(2)]
    sw2T = p.sb("sw2T", [128, 32 * D], BF16)
    hT_c = [p.sb(f"hT_c{i}", [128, 32 * 512], BF16) for i in range(2)]
    preT = p.sb("preT", [128, NJ * 512], BF16)
    x3_t = [p.sb(f"x3_t{i}", [128, D], F32) for i in range(2)]
    u = [p.sb(f"u{i}", [128, D], F32) for i in range(2)]
    s1col = [p.sb(f"s1col{i}", [128, 1], F32) for i in range(2)]
    sqw0 = p.sb("sqw0", [128, D], F32)
    lnb = [dict(sqw=sqw0,
                mcol=p.sb(f"mcol{i}", [128, 1], F32),
                vcol=p.sb(f"vcol{i}", [128, 1], F32),
                tcol=p.sb(f"tcol{i}", [128, 1], F32)) for i in range(2)]
    y_t = [p.sb(f"y_t{i}", [128, D], F32) for i in range(2)]

    tps = [p.ps(f"tps{i}", [128, 1024], BF16) for i in range(2)]
    o2ps = [p.ps(f"o2ps{i}", [128, 512], F32) for i in range(2)]
    aps = p.ps("aps2", [128, NJ], F32)

    t_oc = p.emit("vector", lambda e: e.memset(onescol_f[:], 1.0))
    t_b2 = p.dma("in", b2c[:], b2c_in[:], waits=c["ready"])

    absv = absw[:, 2048:3072]
    t_sg = []
    prevs = {}
    t_amm = None
    for ch in range(32):
        t_w = p.dma_rr("in_s", 2, ch, wstage[ch % 2][:],
                       w2T_in[ch * 128:(ch + 1) * 128, :],
                       waits=prevs.get(ch - 2, c["ready"][:1]))
        cs = p.emit("scalar", lambda e, ch=ch: e.activation(
            out=sw2T[:, ch * D:(ch + 1) * D], in_=wstage[ch % 2][:],
            func=AF.Sign), waits=[t_w])
        t_av = p.emit("vector", lambda e, ch=ch: e.scalar_tensor_tensor(
            out=absv[:, 0:D], in0=wstage[ch % 2][:], scalar=-1.0,
            in1=wstage[ch % 2][:], op0=ALU.mult, op1=ALU.max),
            waits=[t_w] + ([t_amm] if t_amm else []))
        for m in range(NJ):
            t_amm = p.emit("tensor", lambda e, ch=ch, m=m: e.matmul(
                out=aps[:, m:m + 1],
                lhsT=absv[:, m * 128:(m + 1) * 128],
                rhs=onescol_f[:, 0:1],
                start=(ch == 0 and m == 0), stop=(ch == 31 and m == NJ - 1)),
                waits=[t_av, t_oc] if m == 0 else [])
        prevs[ch] = [cs, t_av]
        t_sg.append(cs)
    t_al = p.emit("vector", lambda e: e.tensor_scalar_mul(
        alpha2s[:], aps[:], 1.0 / FF), waits=[t_amm])
    t_ab = t_al

    preTs = [preT, absw[:, 0:2048].bitcast(BF16)]
    pvF = [dict(u=None, lnsq=None, lnout=None) for _ in range(2)]
    t_prev_pre = {}
    pre_done = {}
    hload = {}
    hl_war = {}

    def tail_body(cch):
        for tt in range(4):
            t = cch * 4 + tt
            L = t % 2
            v = pvF[L]
            t_x = p.dma_rr("in_x", 2, L, x3_t[L][:],
                           x3_in[t * 128:(t + 1) * 128, :],
                           waits=[v["u"]] if v["u"] else t_sg)
            t_tp = None
            for m in range(NJ):
                t_tp = p.emit("tensor", lambda e, m=m, tt=tt, L=L, cch=cch: e.transpose(
                    out=tps[L][:, m * 128:(m + 1) * 128],
                    in_=preTs[cch % 2][:, m * 512 + tt * 128:
                                       m * 512 + (tt + 1) * 128],
                    identity=c["ident"][:]),
                    waits=(pre_done[cch] +
                           ([v["u"]] if v["u"] else [])) if m == 0 else [])
            t_u = p.emit("vector", lambda e, L=L: e.scalar_tensor_tensor(
                out=u[L][:], in0=tps[L][:, 0:1024], scalar=1.0, in1=x3_t[L][:],
                op0=ALU.mult, op1=ALU.add, accum_out=s1col[L][:, 0:1]),
                waits=[t_tp, t_x] + ([v["lnsq"]] if v["lnsq"] else []))
            v["u"] = t_u
            w_ln = [t_u] + ([v["lnout"]] if v["lnout"] else [t_ab])
            t_y, t_sq1 = ln_stats_apply(p, lnb[L], u[L], s1col[L], y_t[L],
                                        1e-5, w_ln + [p.stok(f"o_y{L}")],
                                        None, "lny")
            v["lnsq"] = t_sq1
            v["lnout"] = t_y
            p.dma_rr("o_y", 2, L, y_out[t * 128:(t + 1) * 128, :], y_t[L][:],
                     waits=[t_y])

    for cch in range(4):
        hload[cch] = p.dma_rr("in_h", 2, cch,
                    hT_c[cch % 2][:].rearrange("p (k tt) -> p k tt", tt=512),
                    hT_in[:, :, cch * 512:(cch + 1) * 512].rearrange(
                        "k p tt -> p k tt"),
                    waits=[hl_war[cch - 2]] if cch >= 2 else t_sg)
        t_h = hload[cch]
        pre_done[cch] = []
        for m in range(NJ):
            par = m % 2
            t_mm = None
            for kk in range(32):
                t_mm = p.emit("tensor", lambda e, m=m, kk=kk, par=par, cch=cch: e.matmul(
                    out=o2ps[par][:],
                    lhsT=sw2T[:, kk * D + m * 128:kk * D + (m + 1) * 128],
                    rhs=hT_c[cch % 2][:, kk * 512:(kk + 1) * 512],
                    start=(kk == 0), stop=(kk == 31)),
                    waits=([t_h] + t_sg + list(t_prev_pre.get(par, []))) if kk == 0 else [])
            t_pre = p.emit("scalar", lambda e, m=m, par=par, cch=cch: e.activation(
                out=preTs[cch % 2][:, m * 512:(m + 1) * 512], in_=o2ps[par][:],
                func=AF.Identity, scale=alpha2s[:, m:m + 1],
                bias=b2c[:, m:m + 1]), waits=[t_mm, t_al, t_b2])
            t_prev_pre[par] = [t_pre]
            pre_done[cch].append(t_pre)
        hl_war[cch] = t_mm
        if cch > 0:
            tail_body(cch - 1)
    tail_body(3)


def build_F2():
    p = Prog("F2")
    c = common_consts(p)
    io = dict(x3=p.din("x3", [T, D], F32),
              hT=p.din("hT", [32, 128, T], BF16),
              w2T=p.din("w2T", [FF, D], F32),
              b2c=p.din("b2c", [128, NJ], F32),
              y=p.dout("y", [T, D], F32))
    emit_F2(p, c, io)
    return p.finalize()



def build_fused(upto=None):
    p = Prog("FUSED")
    c = common_consts(p)
    for v in (1e-12, 1e-5, 0.125):
        p.const_col(v)
    G8 = [list(range(NCORES))]

    x = p.din("x", [T, D], F32)
    wq_a = p.din("wq_a", [D + 1, D], F32)
    wk_a = p.din("wk_a", [D + 1, D], F32)
    wv_a = p.din("wv_a", [D + 1, D], F32)
    wo_a = p.din("wo_a", [D + 1, D], F32)
    g1r = p.din("g1r", [1, D], F32)
    be1r = p.din("be1r", [1, D], F32)
    w1_a = p.din("w1_a", [D + 1, FF], F32)
    b1cB = p.din("b1cB", [128, 32], F32)
    w2_a = p.din("w2_a", [FF + 1, D], F32)
    g2r = p.din("g2r", [1, D], F32)
    be2r = p.din("be2r", [1, D], F32)
    gmr = p.din("gmr", [1, D], F32)
    bbr = p.din("bbr", [1, D], F32)
    mur = p.din("mur", [D, ES], F32)
    scl = p.din("scl", [1, 1], F32)
    w1e = p.din("w1e", [EXP, D, MOE_H], F32)
    b1c = p.din("b1c", [128, EXP * 2], F32)
    w2e = p.din("w2e", [EXP, MOE_H + 1, D], F32)
    wqT = p.din("wqT", [D, D], F32)
    wkT = p.din("wkT", [D, D], F32)
    wvT = p.din("wvT", [D, D], F32)
    woT_a = p.din("woT_a", [D + 1, D], F32)
    maskA = p.din("maskA", [128, 128], F32)
    maskB = p.din("maskB", [64, 128], F32)
    selL = p.din("selL", [512, 32], F32)
    selR = p.din("selR", [512, 32], F32)
    w1T = p.din("w1T", [D, FF], F32)
    b1cf = p.din("b1cf", [128, 32], F32)
    w2T = p.din("w2T", [FF, D], F32)
    b2cf = p.din("b2cf", [128, NJ], F32)
    y = p.dout("y", [T, D], F32)

    kvp = p.dtmp("s_kvp", [64, 1024], F32)
    kvs = p.dtmp("s_kvs", [64, 1024], F32)
    attn_out = p.dtmp("s_ao", [T, D], F32)
    atT = p.dtmp("s_atT", [NJ, 128, T], BF16)
    hT0 = p.dtmp("s_hT0", [32, 128, T], BF16)
    x1 = p.dtmp("s_x1", [T, D], F32)
    PrawP = p.dtmp("s_PrawP", [128, NJ * ES], F32)
    PrawS = p.dtmp("s_PrawS", [128, NJ * ES], F32)
    SlocP = p.dtmp("s_SlocP", [1, ES], F32)
    SlocS = p.dtmp("s_SlocS", [1, ES], F32)
    Emat = p.dtmp("s_E", [NT, 128, ES], BF16)
    Zc = p.dtmp("s_Zc", [NT, 128, 1], F32)
    xe = p.dtmp("s_xe", [2176, D], F32)
    edges = p.dtmp("s_edges", [2 * HALO, D], F32)
    gath = p.dtmp("s_gath", [NCORES * 2 * HALO, D], F32)
    x3 = p.dtmp("s_x3", [T, D], F32)
    hT1 = p.dtmp("s_hT1", [32, 128, T], BF16)
    soscr = p.dtmp("s_soscr", [ES, D], BF16)

    p.scnt.setdefault("cc", 0)

    def collective(kind, op, ins, outs, waits):
        return p.emit("gpsimd", lambda e: e.collective_compute(
            kind, op, replica_groups=G8, ins=[a.opt() for a in ins],
            outs=[a.opt() for a in outs]), waits=waits, inc=1, sem="cc")

    p.push_region("AB")
    eqT_res = p.sb("eqT_res", [128, NJ * T], BF16)
    rq_res = p.sb("rq_res", [128, NT * NH], F32)
    p.push_region("A")
    emit_A(p, c, dict(x=x, wq_a=wq_a, wk_a=wk_a, wv_a=wv_a, kvp=kvp,
                      eqT_res=eqT_res, rq_res=rq_res))
    t_cc_kv = collective("AllReduce", ALU.add, [kvp], [kvs],
                         [p.stok("o_kv")])
    p.pop_region()
    if upto == "A":
        return p.finalize()
    p.barrier()

    p.push_region("B1")
    emit_B1(p, c, dict(x=x, eqT_res=eqT_res, rq_res=rq_res, kvp=kvs,
                       g1r=g1r, be1r=be1r,
                       wo_a=wo_a, attn_out=attn_out, atT=atT), fused=True,
            kv_wait=t_cc_kv)
    p.pop_region()
    p.pop_region()
    if upto == "B1":
        return p.finalize()
    p.barrier()

    p.push_region("B2a")
    emit_B2a(p, c, dict(atT=atT, w1_a=w1_a, b1cB=b1cB, hT=hT0))
    p.pop_region()
    if upto == "B2a":
        return p.finalize()
    p.barrier()

    p.push_region("B2b")
    emit_B2b(p, c, dict(hT=hT0, attn_out=attn_out, x=x, w2_a=w2_a, g2r=g2r,
                        be2r=be2r, x1=x1))
    p.pop_region()
    if upto == "B2b":
        return p.finalize()
    p.barrier()

    p.push_region("C")
    emit_C(p, c, dict(x1=x1, gmr=gmr, bbr=bbr, mur=mur, scl=scl, Praw=PrawP,
                      Sloc=SlocP, E=Emat, Zc=Zc))
    collective("AllReduce", ALU.add, [PrawP], [PrawS], [p.stok("o_P")])
    t_cc_S = collective("AllReduce", ALU.add, [SlocP], [SlocS],
                        [p.stok("o_S")])
    p.pop_region()
    if upto == "C":
        return p.finalize()
    p.barrier()

    p.push_region("D")
    emit_D(p, c, dict(x1=x1, E=Emat, Zc=Zc, Praw=PrawS, Sloc=SlocS, w1e=w1e,
                      b1c=b1c, w2e=w2e, x2=xe[HALO:HALO + T, :],
                      so_scratch=soscr),
           fused=True, pr_wait=t_cc_S, edges_out=edges)
    t_cc_G = collective("AllGather", ALU.bypass, [edges], [gath],
                        [p.stok("o_edge")])
    p.pop_region()
    if upto == "D":
        return p.finalize()
    p.barrier()

    p.push_region("H")
    ztile = p.sb("ztile", [64, D], F32)
    gsb = p.sb("gsb", [128, 4 * D], F32)
    slsb = p.sb("slsb", [128, 4 * 32], F32)
    srsb = p.sb("srsb", [128, 4 * 32], F32)
    hsb = p.sb("hsb", [32, D], F32)
    hps = p.ps("hps", [32, 1024], F32)
    t_z = p.emit("vector", lambda e: e.memset(ztile[:], 0.0))
    if upto == "H0":
        return p.finalize()
    p.dma("o_xe", xe[2112:2176, :], ztile[:], waits=[t_z])
    if upto == "H1":
        return p.finalize()
    t_g = p.dma("in_g", gsb[:].rearrange("p (k d) -> p k d", d=D),
                gath[:].rearrange("(k p) d -> p k d", p=128),
                waits=[t_cc_G])
    t_sl = p.dma("in_g", slsb[:].rearrange("p (k m) -> p k m", m=32),
                 selL[:].rearrange("(k p) m -> p k m", p=128))
    t_sr = p.dma("in_g", srsb[:].rearrange("p (k m) -> p k m", m=32),
                 selR[:].rearrange("(k p) m -> p k m", p=128))
    t_gin = p.stok("in_g")
    if upto == "H2":
        return p.finalize()
    t_he = None
    for side, ssb, rows in (("L", slsb, slice(0, 32)), ("R", srsb, slice(2080, 2112))):
        t_hm = None
        for n in range(2):
            for kk in range(4):
                t_hm = p.emit("tensor", lambda e, n=n, kk=kk, ssb=ssb: e.matmul(
                    out=hps[0:32, n * 512:(n + 1) * 512],
                    lhsT=ssb[:, kk * 32:(kk + 1) * 32],
                    rhs=gsb[:, kk * D + n * 512:kk * D + (n + 1) * 512],
                    start=(kk == 0), stop=(kk == 3)),
                    waits=([t_gin] + ([t_he] if t_he else [])) if n == 0 and kk == 0 else [])
        t_he = p.emit("vector", lambda e: e.tensor_copy(hsb[0:32, :],
                                                        hps[0:32, :]),
                      waits=[t_hm, p.stok("o_xe")])
        t_hdma = p.dma("o_xe", xe[rows, :], hsb[0:32, :], waits=[t_he])
        if upto == "H3" and side == "L":
            return p.finalize()
    if upto == "H4":
        return p.finalize()
    p.pop_region()
    if upto == "H":
        return p.finalize()
    p.barrier()

    p.push_region("E")
    emit_E(p, c, dict(xe=xe, maskA=maskA, maskB=maskB, wqT=wqT, wkT=wkT,
                      wvT=wvT, woT_a=woT_a, x3=x3))
    p.pop_region()
    if upto == "E":
        return p.finalize()
    p.barrier()

    p.push_region("F1")
    emit_F1(p, c, dict(x3=x3, w1T=w1T, b1c=b1cf, hT=hT1))
    p.pop_region()
    if upto == "F1":
        return p.finalize()
    p.barrier()

    p.push_region("F2")
    emit_F2(p, c, dict(x3=x3, hT=hT1, w2T=w2T, b2c=b2cf, y=y))
    p.pop_region()

    return p.finalize()



def _aug(w, b):
    return np.ascontiguousarray(
        np.concatenate([w, b.reshape(1, -1)], axis=0), dtype=np.float32)


def _row(v):
    return np.ascontiguousarray(v.reshape(1, -1), dtype=np.float32)


IDENT = np.eye(128, dtype=BF16NP)


_NC_CACHE = {}


def _get_nc(name, builder):
    if name not in _NC_CACHE:
        _NC_CACHE[name] = builder()
    return _NC_CACHE[name]


def _run(name, builder, per_core, debug=None):
    nc = _get_nc(name, builder)
    res = run_bass_kernel_spmd(nc, per_core, core_ids=list(range(NCORES)))
    if debug is not None:
        debug[name] = res
    return res.results


def host_layouts(inp):
    x = np.ascontiguousarray(inp["L0_attn_x"].reshape(16384, 1024),
                             dtype=np.float32)
    com = dict(
        wq_a=_aug(inp["L0_attn_Wq"], inp["L0_attn_bq"]),
        wk_a=_aug(inp["L0_attn_Wk"], inp["L0_attn_bk"]),
        wv_a=_aug(inp["L0_attn_Wv"], inp["L0_attn_bv"]),
        wo_a=_aug(inp["L0_attn_Wo"], inp["L0_attn_bo"]),
        g1r=_row(inp["L0_attn_g1"]), be1r=_row(inp["L0_attn_be1"]),
        w1_a=_aug(inp["L0_attn_W1"], inp["L0_attn_b1"]),
        b1cB=np.ascontiguousarray(
            inp["L0_attn_b1"].reshape(32, 128).T, dtype=np.float32),
        w2_a=_aug(inp["L0_attn_W2"], inp["L0_attn_b2"]),
        g2r=_row(inp["L0_attn_g2"]), be2r=_row(inp["L0_attn_be2"]),
        gmr=_row(inp["L0_ffn_ln_gamma"]), bbr=_row(inp["L0_ffn_ln_beta"]),
        mur=np.ascontiguousarray(inp["L0_ffn_mu"].reshape(D, ES),
                                 dtype=np.float32),
        scl=np.ascontiguousarray(inp["L0_ffn_scale"].reshape(1, 1),
                                 dtype=np.float32),
        w1e=np.ascontiguousarray(inp["L0_ffn_w1"], dtype=np.float32),
        b1c=np.ascontiguousarray(
            inp["L0_ffn_b1"].reshape(EXP, 2, 128).transpose(2, 0, 1)
            .reshape(128, EXP * 2), dtype=np.float32),
        w2e=np.ascontiguousarray(
            np.concatenate([inp["L0_ffn_w2"],
                            inp["L0_ffn_b2"][:, None, :]], axis=1),
            dtype=np.float32),
        wqT=np.ascontiguousarray(inp["L1_attn_Wq"].T, dtype=np.float32),
        wkT=np.ascontiguousarray(inp["L1_attn_Wk"].T, dtype=np.float32),
        wvT=np.ascontiguousarray(inp["L1_attn_Wv"].T, dtype=np.float32),
        woT_a=_aug(np.ascontiguousarray(inp["L1_attn_Wo"].T),
                   inp["L1_attn_bo"]),
        w1T=np.ascontiguousarray(inp["L1_ffn_w1"].T, dtype=np.float32),
        b1cf=np.ascontiguousarray(inp["L1_ffn_b1"].reshape(32, 128).T,
                                  dtype=np.float32),
        w2T=np.ascontiguousarray(inp["L1_ffn_w2"].T, dtype=np.float32),
        b2cf=np.ascontiguousarray(inp["L1_ffn_b2"].reshape(NJ, 128).T,
                                  dtype=np.float32),
        ident=IDENT,
    )
    pp = np.arange(128)[:, None]
    rr = np.arange(128)[None, :]
    com["maskA"] = np.where((rr >= pp - 63) & (rr <= pp), 0.0,
                            -1e4).astype(np.float32)
    pb = np.arange(64)[:, None]
    com["maskB"] = np.where(rr >= pb + 65, 0.0, -1e4).astype(np.float32)
    maps = []
    for ci in range(NCORES):
        m = dict(com)
        m["x"] = np.ascontiguousarray(x[ci * T:(ci + 1) * T])
        sL = np.zeros((512, 32), np.float32)
        sR = np.zeros((512, 32), np.float32)
        if ci > 0:
            sL[np.arange(32) + 64 * (ci - 1) + 32, np.arange(32)] = 1.0
        if ci < NCORES - 1:
            sR[np.arange(32) + 64 * (ci + 1), np.arange(32)] = 1.0
        m["selL"] = sL
        m["selR"] = sR
        maps.append(m)
    return maps


def kernel_fused(**inp):
    inp = {k: np.asarray(v) for k, v in inp.items()}
    maps = host_layouts(inp)
    nc = _get_nc("FUSED", build_fused)
    res = run_bass_kernel_spmd(nc, maps, core_ids=list(range(NCORES)))
    yy = np.concatenate([res.results[ci]["y"] for ci in range(NCORES)], axis=0)
    return yy.reshape(1, 16384, 1024)


def kernel(**inp):
    if not inp.pop("_unfused", False) and "_upto" not in inp:
        inp.pop("_debug", None)
        return kernel_fused(**inp)
    debug = inp.pop("_debug", None)
    upto = inp.pop("_upto", None)
    inp = {k: np.asarray(v) for k, v in inp.items()}
    x = np.ascontiguousarray(inp["L0_attn_x"].reshape(16384, 1024),
                             dtype=np.float32)
    xs = [np.ascontiguousarray(x[ci * T:(ci + 1) * T]) for ci in range(NCORES)]

    wq_a = _aug(inp["L0_attn_Wq"], inp["L0_attn_bq"])
    wk_a = _aug(inp["L0_attn_Wk"], inp["L0_attn_bk"])
    wv_a = _aug(inp["L0_attn_Wv"], inp["L0_attn_bv"])

    rA = _run("A", build_A, [
        dict(x=xs[ci], wq_a=wq_a, wk_a=wk_a, wv_a=wv_a, ident=IDENT)
        for ci in range(NCORES)])
    if upto == "A":
        return rA

    kvp_all = np.ascontiguousarray(
        np.stack([rA[ci]["kvp"] for ci in range(NCORES)]))
    wo_a = _aug(inp["L0_attn_Wo"], inp["L0_attn_bo"])
    g1r = _row(inp["L0_attn_g1"])
    be1r = _row(inp["L0_attn_be1"])
    rB1 = _run("B1", build_B1, [
        dict(x=xs[ci], eqT=rA[ci]["eqT"], rq=rA[ci]["rq"], kvp=kvp_all,
             wo_a=wo_a, g1r=g1r, be1r=be1r, ident=IDENT)
        for ci in range(NCORES)])
    if upto == "B1":
        return rB1

    w1_a = _aug(inp["L0_attn_W1"], inp["L0_attn_b1"])
    b1cB = np.ascontiguousarray(
        inp["L0_attn_b1"].reshape(32, 128).T, dtype=np.float32)
    rB2a = _run("B2a", build_B2a, [
        dict(atT=rB1[ci]["atT"], w1_a=w1_a, b1cB=b1cB, ident=IDENT)
        for ci in range(NCORES)])

    w2_a = _aug(inp["L0_attn_W2"], inp["L0_attn_b2"])
    g2r = _row(inp["L0_attn_g2"])
    be2r = _row(inp["L0_attn_be2"])
    rB2b = _run("B2b", build_B2b, [
        dict(hT=rB2a[ci]["hT"], attn_out=rB1[ci]["attn_out"], x=xs[ci],
             w2_a=w2_a, g2r=g2r, be2r=be2r, ident=IDENT)
        for ci in range(NCORES)])
    if upto == "B2b":
        return rB2b

    gmr = _row(inp["L0_ffn_ln_gamma"])
    bbr = _row(inp["L0_ffn_ln_beta"])
    mur = np.ascontiguousarray(inp["L0_ffn_mu"].reshape(D, ES),
                               dtype=np.float32)
    scl = np.ascontiguousarray(inp["L0_ffn_scale"].reshape(1, 1),
                               dtype=np.float32)
    rC = _run("C", build_C, [
        dict(x1=rB2b[ci]["x1"], gmr=gmr, bbr=bbr, mur=mur, scl=scl,
             ident=IDENT) for ci in range(NCORES)])
    if upto == "C":
        return rC

    Praw_all = np.ascontiguousarray(np.stack([rC[ci]["Praw"] for ci in range(NCORES)]))
    Sloc_all = np.ascontiguousarray(np.stack([rC[ci]["Sloc"] for ci in range(NCORES)]))
    w1e = np.ascontiguousarray(inp["L0_ffn_w1"], dtype=np.float32)
    b1c = np.ascontiguousarray(
        inp["L0_ffn_b1"].reshape(EXP, 2, 128).transpose(2, 0, 1).reshape(128, EXP * 2),
        dtype=np.float32)
    w2e = np.ascontiguousarray(
        np.concatenate([inp["L0_ffn_w2"],
                        inp["L0_ffn_b2"][:, None, :]], axis=1), dtype=np.float32)
    rD = _run("D", build_D, [
        dict(x1=rB2b[ci]["x1"], E=rC[ci]["E"], Zc=rC[ci]["Zc"],
             Praw=Praw_all, Sloc=Sloc_all, w1e=w1e, b1c=b1c, w2e=w2e,
             ident=IDENT) for ci in range(NCORES)])
    if upto == "D":
        return rD

    x2 = np.concatenate([rD[ci]["x2"] for ci in range(NCORES)], axis=0)
    xe = np.zeros((NCORES, 2176, D), np.float32)
    for ci in range(NCORES):
        lo, hi = ci * T - HALO, ci * T + T + HALO
        slo, shi = max(lo, 0), min(hi, 16384)
        xe[ci, slo - lo:slo - lo + (shi - slo)] = x2[slo:shi]
    wqT = np.ascontiguousarray(inp["L1_attn_Wq"].T, dtype=np.float32)
    wkT = np.ascontiguousarray(inp["L1_attn_Wk"].T, dtype=np.float32)
    wvT = np.ascontiguousarray(inp["L1_attn_Wv"].T, dtype=np.float32)
    woT_a = _aug(np.ascontiguousarray(inp["L1_attn_Wo"].T), inp["L1_attn_bo"])
    pp = np.arange(128)[:, None]
    rr = np.arange(128)[None, :]
    maskA = np.where((rr >= pp - 63) & (rr <= pp), 0.0, -1e4).astype(np.float32)
    pb = np.arange(64)[:, None]
    maskB = np.where(rr >= pb + 65, 0.0, -1e4).astype(np.float32)
    rE = _run("E", build_E, [
        dict(xe=np.ascontiguousarray(xe[ci]), wqT=wqT, wkT=wkT, wvT=wvT,
             woT_a=woT_a, maskA=maskA, maskB=maskB, ident=IDENT)
        for ci in range(NCORES)])
    if upto == "E":
        return rE

    w1n = np.ascontiguousarray(inp["L1_ffn_w1"], dtype=np.float32)
    w1T = np.ascontiguousarray(inp["L1_ffn_w1"].T, dtype=np.float32)
    b1cf = np.ascontiguousarray(
        inp["L1_ffn_b1"].reshape(32, 128).T, dtype=np.float32)
    rF1 = _run("F1", build_F1, [
        dict(x3=rE[ci]["x3"], w1n=w1n, w1T=w1T, b1c=b1cf, ident=IDENT)
        for ci in range(NCORES)])

    w2n = np.ascontiguousarray(inp["L1_ffn_w2"], dtype=np.float32)
    w2T = np.ascontiguousarray(inp["L1_ffn_w2"].T, dtype=np.float32)
    b2cf = np.ascontiguousarray(
        inp["L1_ffn_b2"].reshape(NJ, 128).T, dtype=np.float32)
    rF2 = _run("F2", build_F2, [
        dict(x3=rE[ci]["x3"], hT=rF1[ci]["hT"], w2n=w2n, w2T=w2T, b2c=b2cf,
             ident=IDENT) for ci in range(NCORES)])

    y = np.concatenate([rF2[ci]["y"] for ci in range(NCORES)], axis=0)
    return y.reshape(1, 16384, 1024)
